# version 03
# baseline (speedup 1.0000x reference)
import numpy as np
import ml_dtypes
from contextlib import ExitStack

import concourse.bass as bass
import concourse.mybir as mybir

F32 = mybir.dt.float32
BF16 = mybir.dt.bfloat16
ALU = mybir.AluOpType
ACT = mybir.ActivationFunctionType

NKH, NVH, DK, DV = 8, 16, 64, 64
KEY, VAL, CDIM, KSZ = 512, 1024, 2048, 4
CH = 64


def ts(i, n):
    return slice(i * n, (i + 1) * n)


def register_custom_ops():
    return


class Sem:
    def __init__(self, nc, stack, name):
        self.h = stack.enter_context(nc.semaphore(name))
        self.val = 0
    def inc(self, instr, n=1):
        self.val += n
        instr.then_inc(self.h, n)
        return instr
    def wait(self, eng, val=None):
        eng.wait_ge(self.h, self.val if val is None else val)


class KB:
    def __init__(self, T=2048, SS=256, debug=(), ncores=8):
        register_custom_ops()
        assert T % SS == 0 and SS % CH == 0 and SS % 128 == 0
        self.T, self.SS = T, SS
        self.TH = T + 3
        self.SSH = SS + 3
        self.NSS = T // SS
        self.NCH = T // CH
        self.NCORE = ncores
        self.ISH = 2816 // ncores
        self.NR = (self.ISH + 127) // 128
        self.ISHP = self.NR * 128
        self.NJ = (ncores * self.ISHP) // 128
        self.FS = min(512, T)
        self.NFS = T // self.FS
        self.NU = self.FS // 128
        self.SCROFF = 544
        self.debug = set(debug)
        self.nc = bass.Bass("TRN2", target_bir_lowering=False)
        self.gstack = ExitStack()
        self.dbg_bufs = {}

    def inp(self, name, shape, dtype=F32):
        return self.nc.dram_tensor(name, shape, dtype, kind="ExternalInput")
    def outp(self, name, shape, dtype=F32):
        return self.nc.dram_tensor(name, shape, dtype, kind="ExternalOutput")
    def sb(self, stack, name, shape, dtype):
        return stack.enter_context(self.nc.sbuf_tensor(name, shape, dtype))
    def ps(self, stack, name, shape, dtype=F32):
        return stack.enter_context(self.nc.psum_tensor(name, shape, dtype))
    def sem(self, name):
        if not hasattr(self, "_semcache"):
            self._semcache = {}
        if name not in self._semcache:
            self._semcache[name] = Sem(self.nc, self.gstack, name)
        return self._semcache[name]
    def V(self, fn): self.blk.vector(fn)
    def A(self, fn): self.blk.scalar(fn)
    def P(self, fn): self.blk.tensor(fn)
    def G_(self, fn): self.blk.gpsimd(fn)
    def S_(self, fn): self.blk.sync(fn)

    def dbg(self, name, shape, dtype=F32):
        if name in self.debug:
            self.dbg_bufs[name] = self.outp("dbg_" + name, shape, dtype)

    def dump(self, name, src_ap, s_after):
        if name not in self.dbg_bufs:
            return
        h = self.dbg_bufs[name]
        s_d = self.sem("s_dbg_" + name)
        def f(g):
            s_after.wait(g)
            s_d.inc(g.dma_start(h[:, :], src_ap), 16)
            s_d.wait(g)
        self.G_(f)

    def declare_global(self):
        T, TH, SS, SSH = self.T, self.TH, self.SS, self.SSH
        gs = self.gstack
        self.i_x = self.inp("x", [128, 8 * TH])
        self.i_wqkvT = self.inp("wqkvT", [128, 8 * 2048])
        self.i_wabT = self.inp("wabT", [128, 8 * 64])
        self.i_wzT = self.inp("wzT", [128, 8 * 1024])
        self.i_woutT = self.inp("woutT", [128, 8 * 1024])
        self.i_convw = self.inp("convw", [128, 16 * 4])
        self.i_wmod = self.inp("wmod", [128, 8 * 6144])
        self.i_crow = self.inp("crow", [128, 8])
        self.i_vec = self.inp("vec", [16, 4])
        self.i_nw = self.inp("nw", [128, 2])
        self.i_masks = self.inp("masks", [128, 3 * 64])
        self.i_eyed = self.inp("eyed", [128, 64], BF16)
        self.i_eyeb = self.inp("eyeb", [128, 128], BF16)
        self.i_eyef = self.inp("eyef", [128, 128])
        self.i_ones = self.inp("ones", [128, 129])
        self.i_cvec = self.inp("cvec", [128, 8])
        self.i_gnsel = self.inp("gnsel", [128, 8 * 16])
        self.i_gnselT = self.inp("gnselT", [16, 8 * 128])
        self.i_wg = self.inp("wg_sh", [128, self.NR * 1024])
        self.i_wu = self.inp("wu_sh", [128, self.NR * 1024])
        self.i_wd = self.inp("wd_sh", [128, 8 * self.ISHP])
        self.o_y = self.outp("y", [128, 8 * T])
        self.dbc = self.nc.dram_tensor("dbc", [8192], F32)
        self.cc_in = self.nc.dram_tensor("cc_in", [64, 1024], F32)
        self.cc_out = self.nc.dram_tensor("cc_out", [128, 1024], F32)
        self.d_xp = self.nc.dram_tensor("d_xp", [128, 8 * T], F32)
        self.d_opre = self.nc.dram_tensor("d_opre", [128, 8 * T], BF16)
        self.d_EpT = self.nc.dram_tensor("d_EpT", [128, 8 * T], BF16)
        PB = self.ISHP
        self.cc_g_in = self.nc.dram_tensor("cc_g_in", [1024, PB], BF16)
        self.cc_u_in = self.nc.dram_tensor("cc_u_in", [1024, PB], BF16)
        self.cc_d_in = self.nc.dram_tensor("cc_d_in", [PB, 1024], BF16)
        self.g_gT = self.nc.dram_tensor("g_gT", [self.NCORE * 1024, PB], BF16)
        self.g_uT = self.nc.dram_tensor("g_uT", [self.NCORE * 1024, PB], BF16)
        self.g_dT = self.nc.dram_tensor("g_dT", [self.NCORE * PB, 1024], BF16)
        self.Sst = [self.sb(gs, "g_S0", [64, 16, DV], F32),
                    self.sb(gs, "g_S1", [64, 16, DV], F32)]
        self.Wb = [self.sb(gs, "g_W0", [64, 16, DK], BF16),
                   self.sb(gs, "g_W1", [64, 16, DK], BF16)]
        self.statpack = self.sb(gs, "g_stats", [33, TH], F32)
        self.stats = self.statpack
        self.statsm = self.statpack
        self.convw = self.sb(gs, "g_convw", [128, 16, 4], F32)
        self.crow = self.sb(gs, "g_crow", [128, 8], F32)
        self.vecs = self.sb(gs, "g_vecs", [16, 4], F32)
        self.nw = self.sb(gs, "g_nw", [128, 2], F32)
        self.masks = self.sb(gs, "g_masks", [128, 3 * 64], F32)
        self.eyed = self.sb(gs, "g_eyed", [128, 64], BF16)
        self.eyeb = self.sb(gs, "g_eyeb", [128, 128], BF16)
        self.eyef = self.sb(gs, "g_eyef", [128, 128], F32)
        self.onesf = self.sb(gs, "g_onesf", [128, 129], F32)
        self.cvec = self.sb(gs, "g_cvec", [128, 8], F32)
        self.gnsel = self.sb(gs, "g_gnsel", [128, 8, 16], F32)
        self.gnselT = self.sb(gs, "g_gnselT", [16, 8, 128], F32)
        self.adaln = self.sb(gs, "g_adaln", [128, 8, 6], F32)
        self.negea = self.sb(gs, "g_negea", [16, 1], F32)
        self.scanmask = self.sb(gs, "g_scanmask", [16, SS], F32)
        self.scr = self.sb(gs, "g_scr", [128, 1088], F32)
        self.scrb = self.sb(gs, "g_scrb", [128, 576], BF16)

    def declare_phaseA(self, stk):
        SS, SSH = self.SS, self.SSH
        self.xs = self.sb(stk, "a_xs", [128, 8, SSH], F32)
        self.wqb = [self.sb(stk, "a_wq0", [128, 8, 128], BF16),
                    self.sb(stk, "a_wq1", [128, 8, 128], BF16)]
        self.wabT = self.sb(stk, "a_wabT", [128, 8, 64], BF16)
        self.hT = self.sb(stk, "a_hT", [128, 8, SSH], BF16)
        self.mixed = self.sb(stk, "a_mixed", [128, 16, SSH], F32)
        self.cv1 = self.sb(stk, "a_cv1", [128, SS], F32)
        self.cv2 = self.sb(stk, "a_cv2", [128, SS], F32)
        self.qkvs = self.sb(stk, "a_qkvs", [128, 16, SS], BF16)
        self.kTs = self.sb(stk, "a_kTs", [128, SS // 128, KEY], BF16)
        self.vTs = self.sb(stk, "a_vTs", [128, SS // 128, VAL], BF16)
        self.dec = self.sb(stk, "a_dec", [128, SS], F32)
        self.dec2 = self.sb(stk, "a_dec2", [96, SS], F32)
        self.decT = self.sb(stk, "a_decT", [128, SS // 128, 128], F32)
        self.decT2 = self.sb(stk, "a_decT2", [128, SS // 128, 96], F32)
        self.sA = self.sb(stk, "a_sA", [16, SS], F32)
        self.sB = self.sb(stk, "a_sB", [16, SS], F32)
        self.sC = self.sb(stk, "a_sC", [16, SS], F32)
        self.r_Gp = self.dec[0:16, :]
        self.r_G = self.dec[32:48, :]
        self.r_lamq = self.dec[64:80, :]
        self.r_blam = self.dec[96:112, :]
        self.r_bet = self.dec2[0:16, :]
        self.r_erev = self.dec2[32:48, :]
        self.r_lamr = self.dec2[64:80, :]

    def build_p0(self, stk):
        s_in = self.s_in = self.sem("s_in")
        s_w = self.s_w = self.sem("s_w")
        s_msk = self.s_msk = self.sem("s_msk")
        s_p0 = self.s_p0 = self.sem("s_p0")
        wmbuf = self.sb(stk, "p0_wmod", [128, 6144], BF16)
        modacc = self.sb(stk, "p0_modacc", [128, 48], F32)
        self.pR1 = self.ps(stk, "p_p0mod", [128, 512])

        def f_sync(s):
            for dst, src in ((self.convw, self.i_convw),
                             (self.vecs, self.i_vec), (self.nw, self.i_nw),
                             (self.masks, self.i_masks), (self.eyed, self.i_eyed), (self.eyeb, self.i_eyeb),
                             (self.eyef, self.i_eyef), (self.onesf, self.i_ones),
                             (self.cvec, self.i_cvec), (self.gnsel, self.i_gnsel), (self.gnselT, self.i_gnselT),
                             (self.crow, self.i_crow)):
                d = dst[:, :, :].rearrange("p a b -> p (a b)") if len(dst.shape) == 3 else dst[:, :]
                s_in.inc(s.dma_start(d, src[:, :]), 16)
        self.S_(f_sync)

        def f_g(g):
            s_w.inc(g.dma_start(self.wabT[:, :, :].rearrange("p a b -> p (a b)"),
                                self.i_wabT[:, :]), 16)
            g.memset(self.scanmask[:, :], 1.0)
            g.drain()
            g.memset(bass.AP(self.scanmask, 0, [[self.SS, 16], [CH, self.SS // CH]]), 0.0)
            g.drain()
            g.memset(self.dec[:, :], 0.0)
            g.drain()
            g.memset(self.dec2[:, :], 0.0)
            g.drain()
            g.memset(self.Sst[0][:, :, :].rearrange("p a b -> p (a b)"), 0.0)
            g.drain()
            s_msk.inc(g.nop(), 1)
        self.G_(f_g)

        def f_wi(v):
            s_in.wait(v)
            for j in range(16):
                v.tensor_copy(self.Wb[0][:, j, :], self.eyed[0:64, :])
            v.drain()
            s_msk.inc(v.nop(), 1)
        self.V(f_wi)
        def f_a(a):
            s_in.wait(a)
            a.activation(self.scr[:, 0:8], self.crow[:, :], ACT.Silu)
            a.drain()
            s_p0.inc(a.nop(), 1)
        self.A(f_a)
        def f_v(v):
            s_p0.wait(v)
            v.tensor_copy(self.scrb[:, 0:8], self.scr[:, 0:8])
            v.drain()
            s_p0.inc(v.nop(), 1)
        self.V(f_v)

        s_wmd = self.sem("s_wmd")
        s_wmc = self.sem("s_wmc")
        for k in range(8):
            def f_gk(g, k=k):
                if k > 0:
                    g.wait_ge(s_wmc.h, 2 * k - 1)
                s_wmd.inc(g.dma_start(wmbuf[:, :], self.i_wmod[:, ts(k, 6144)]), 16)
            self.G_(f_gk)
            def f_tk(t, k=k):
                s_wmd.wait(t)
                s_p0.wait(t, 2)
                if k > 0:
                    t.wait_ge(s_wmc.h, 2 * k)
                for j in range(48):
                    t.matmul(self.pR1[:, j:j + 1], wmbuf[:, ts(j, 128)], self.scrb[:, k:k + 1],
                             start=True, stop=True)
                t.drain()
                s_wmc.inc(t.nop(), 1)
            self.P(f_tk)
            def f_vk(v, k=k):
                s_wmc.wait(v)
                if k == 0:
                    v.tensor_copy(modacc[:, :], self.pR1[:, 0:48])
                else:
                    v.tensor_tensor(modacc[:, :], modacc[:, :], self.pR1[:, 0:48], ALU.add)
                v.drain()
                s_wmc.inc(v.nop(), 1)
            self.V(f_vk)
        s_wm = s_wmc
        def f_v2(v):
            s_wm.wait(v)
            v.tensor_copy(self.adaln[:, :, :].transpose([0, 2, 1]), modacc[:, :])
            v.drain()
            v.tensor_scalar(self.adaln[:, :, 1], self.adaln[:, :, 1], 1.0, None, ALU.add)
            v.tensor_scalar(self.adaln[:, :, 4], self.adaln[:, :, 4], 1.0, None, ALU.add)
            v.drain()
            s_p0.inc(v.nop(), 1)
        self.V(f_v2)
        def f_a2(a):
            s_p0.wait(a)
            a.activation(self.scr[:, 0:8], self.adaln[:, :, 2], ACT.Abs)
            a.activation(self.scr[:, 8:16], self.adaln[:, :, 5], ACT.Abs)
            a.activation(self.scr[:, 16:24], self.adaln[:, :, 2], ACT.Sign)
            a.activation(self.scr[:, 24:32], self.adaln[:, :, 5], ACT.Sign)
            a.activation(self.negea[:, 0:1], self.vecs[:, 1:2], ACT.Exp)
            a.drain()
            s_p0.inc(a.nop(), 1)
        self.A(f_a2)
        def f_v3(v):
            s_p0.wait(v)
            v.tensor_scalar(self.scr[:, 0:16], self.scr[:, 0:16], 0.25, None, ALU.max)
            v.tensor_scalar(self.negea[:, 0:1], self.negea[:, 0:1], -1.0, None, ALU.mult)
            v.drain()
            v.tensor_tensor(self.adaln[:, :, 2], self.scr[:, 0:8], self.scr[:, 16:24], ALU.mult)
            v.tensor_tensor(self.adaln[:, :, 5], self.scr[:, 8:16], self.scr[:, 24:32], ALU.mult)
            v.drain()
            s_p0.inc(v.nop(), 1)
        self.V(f_v3)

    def ln_apply(self, which, w0, n, s_pre, sq_scratch, stats_r, stats_m, dst, halo_mask,
                 xsrc, x0):
        ai_s, ai_c = (0, 1) if which == 0 else (3, 4)
        s_loc = self.sem("s_ln%d" % which)
        def f_sq(a):
            s_pre.wait(a)
            for d in range(8):
                a.activation(sq_scratch[:, d, 0:n], xsrc[:, d, x0:x0 + n], ACT.Square)
            a.drain()
            s_loc.inc(a.nop(), 1)
        self.A(f_sq)
        def f_mm(t):
            s_loc.wait(t)
            s_pre.wait(t)
            for d in range(8):
                t.matmul(self.pS[0:1, 0:n], self.onesf[:, 0:1], xsrc[:, d, x0:x0 + n],
                         start=(d == 0), stop=(d == 7))
            for d in range(8):
                t.matmul(self.pQ[0:1, 0:n], self.onesf[:, 0:1], sq_scratch[:, d, 0:n],
                         start=(d == 0), stop=(d == 7))
            t.drain()
            s_loc.inc(t.nop(), 1)
        self.P(f_mm)
        def f_vst(v):
            s_loc.wait(v)
            v.tensor_scalar(self.scr[0:1, 0:n], self.pS[0:1, 0:n], 1.0 / 1024.0, None, ALU.mult)
            v.drain()
            v.tensor_tensor(self.scr[0:1, 0:n], self.scr[0:1, 0:n], self.scr[0:1, 0:n], ALU.mult)
            v.drain()
            v.scalar_tensor_tensor(self.scr[0:1, self.SCROFF:self.SCROFF + n], self.pQ[0:1, 0:n], 1.0 / 1024.0,
                                   self.scr[0:1, 0:n], ALU.mult, ALU.subtract)
            v.drain()
            s_loc.inc(v.nop(), 1)
        self.V(f_vst)
        def f_ast(a):
            s_loc.wait(a)
            a.activation(self.scr[0:1, 0:n], self.scr[0:1, self.SCROFF:self.SCROFF + n], ACT.Ln, bias=self.cvec[0:1, 0:1])
            a.drain()
            a.activation(stats_r[0:1, w0:w0 + n], self.scr[0:1, 0:n], ACT.Exp, scale=-0.5)
            a.drain()
            s_loc.inc(a.nop(), 1)
        self.A(f_ast)
        def f_vst2(v):
            s_loc.wait(v)
            v.scalar_tensor_tensor(stats_m[32:33, w0:w0 + n], self.pS[0:1, 0:n], 1.0 / 1024.0,
                                   stats_r[0:1, w0:w0 + n], ALU.mult, ALU.mult)
            v.drain()
            s_loc.inc(v.nop(), 1)
        self.V(f_vst2)
        self.ln_rep_apply(which, w0, n, s_loc, stats_r, stats_m, dst, halo_mask, xsrc, x0)
        return s_loc

    def ln_rep_apply(self, which, w0, n, s_loc, stats_r, stats_m, dst, halo_mask, xsrc, x0):
        ai_s, ai_c = (0, 1) if which == 0 else (3, 4)
        def f_rep(t):
            s_loc.wait(t)
            t.matmul(self.pR1[:, 0:n], self.onesf[0:1, 0:128], stats_r[0:1, w0:w0 + n],
                     start=True, stop=True)
            t.drain()
            t.matmul(self.pR2[:, 0:n], self.onesf[32:33, 0:128], stats_m[32:33, w0:w0 + n],
                     start=True, stop=True)
            t.drain()
            s_loc.inc(t.nop(), 1)
        self.P(f_rep)
        def f_h(v):
            s_loc.wait(v)
            for d in range(8):
                v.tensor_tensor(self.scr[:, 0:n], xsrc[:, d, x0:x0 + n], self.pR1[:, 0:n], ALU.mult)
                v.drain()
                v.tensor_tensor(self.scr[:, self.SCROFF:self.SCROFF + n], self.scr[:, 0:n], self.pR2[:, 0:n], ALU.subtract)
                v.drain()
                v.tensor_scalar(dst[:, d, 0:n], self.scr[:, self.SCROFF:self.SCROFF + n],
                                self.adaln[:, d, ai_c:ai_c + 1], self.adaln[:, d, ai_s:ai_s + 1],
                                ALU.mult, ALU.add)
                v.drain()
            if halo_mask:
                for d in range(8):
                    v.tensor_scalar(dst[:, d, 0:3], dst[:, d, 0:3], self.nw[:, 1:2], None, ALU.mult)
                v.drain()
            s_loc.inc(v.nop(), 1)
        self.V(f_h)

    def build_p1_superslab(self, ss, s_pre):
        T, SS, SSH = self.T, self.SS, self.SSH
        w0 = ss * SS
        s_xs = self.sem("s_xs")
        def f_xd(g):
            s_pre.wait(g)
            ap_x = bass.AP(self.i_x, w0, [[8 * self.TH, 128], [self.TH, 8], [1, SSH]])
            s_xs.inc(g.dma_start(self.xs[:, :, :], ap_x), 16)
        self.G_(f_xd)
        s_g0 = self.sem("s_lng")
        def f_xg(v):
            s_pre.wait(v)
            s_xs.wait(v)
            s_g0.inc(v.nop(), 1)
        self.V(f_xg)
        s_ln = self.ln_apply(0, w0, SSH, s_g0, self.mixed, self.stats, self.statsm, self.hT,
                             halo_mask=(ss == 0), xsrc=self.xs, x0=0)
        s_qkv = self.sem("s_qkv")
        s_cv = self.sem("s_cv")
        s_wqp = [self.sem("s_wqa"), self.sem("s_wqb")]
        q_base, cv_base = s_qkv.val, s_cv.val
        for chn in range(16):
            pp = self.pA if chn % 2 == 0 else self.pB
            wb = self.wqb[chn % 2]
            swq = s_wqp[chn % 2]
            def f_wl(g, chn=chn, wb=wb, swq=swq):
                if chn >= 2:
                    g.wait_ge(s_qkv.h, q_base + chn - 1)
                swq.inc(g.dma_start(wb[:, :, :].rearrange("p a b -> p (a b)"),
                                    self.i_wqkvT[:, :].rearrange("p (a b) -> p a b", b=2048)[:, :, ts(chn, 128)]),
                        16)
            self.G_(f_wl)
            def f_mm(t, chn=chn, pp=pp, wb=wb, swq=swq):
                s_ln.wait(t)
                swq.wait(t)
                if chn >= 2:
                    s_cv.wait(t, cv_base + (chn - 2) * 3 + 1)
                for k in range(8):
                    t.matmul(pp[:, 0:SSH], wb[:, k, :], self.hT[:, k, :],
                             start=(k == 0), stop=(k == 7))
                t.drain()
                s_qkv.inc(t.nop(), 1)
            self.P(f_mm)
            def f_ev(a, chn=chn, pp=pp):
                s_qkv.wait(a)
                a.activation(self.mixed[:, chn, :], pp[:, 0:SSH], ACT.Copy)
                a.drain()
                s_cv.inc(a.nop(), 1)
            self.A(f_ev)
            def f_cv(v, chn=chn):
                s_cv.wait(v)
                cw = self.convw
                v.tensor_scalar(self.cv1[:, 0:SS], self.mixed[:, chn, 0:SS],
                                cw[:, chn, 0:1], None, ALU.mult)
                v.drain()
                v.scalar_tensor_tensor(self.cv2[:, 0:SS], self.mixed[:, chn, 1:1 + SS],
                                       cw[:, chn, 1:2], self.cv1[:, 0:SS], ALU.mult, ALU.add)
                v.drain()
                v.scalar_tensor_tensor(self.cv1[:, 0:SS], self.mixed[:, chn, 2:2 + SS],
                                       cw[:, chn, 2:3], self.cv2[:, 0:SS], ALU.mult, ALU.add)
                v.drain()
                v.scalar_tensor_tensor(self.cv2[:, 0:SS], self.mixed[:, chn, 3:3 + SS],
                                       cw[:, chn, 3:4], self.cv1[:, 0:SS], ALU.mult, ALU.add)
                v.drain()
                s_cv.inc(v.nop(), 1)
            self.V(f_cv)
            def f_si(a, chn=chn):
                s_cv.wait(a)
                a.activation(self.qkvs[:, chn, :], self.cv2[:, 0:SS], ACT.Silu)
                a.drain()
                s_cv.inc(a.nop(), 1)
            self.A(f_si)
        s_ab = self.sem("s_ab")
        def f_ab(t):
            s_ln.wait(t)
            self.s_w.wait(t)
            for k in range(8):
                t.matmul(self.pR1[0:64, 0:SS], self.wabT[:, k, :], self.hT[:, k, 3:3 + SS],
                         start=(k == 0), stop=(k == 7))
            t.drain()
            s_ab.inc(t.nop(), 1)
        self.P(f_ab)
        def f_d0(v):
            s_ab.wait(v)
            v.tensor_scalar(self.sA[:, 0:SS], self.pR1[32:48, 0:SS], self.vecs[:, 0:1], None, ALU.add)
            v.drain()
            s_ab.inc(v.nop(), 1)
        self.V(f_d0)
        def f_d1(a):
            s_ab.wait(a)
            a.activation(self.r_bet[:, :], self.pR1[0:16, 0:SS], ACT.Sigmoid)
            a.drain()
            a.activation(self.sB[:, 0:SS], self.sA[:, 0:SS], ACT.Abs)
            a.drain()
            a.activation(self.sB[:, 0:SS], self.sB[:, 0:SS], ACT.Exp, scale=-1.0)
            a.drain()
            s_ab.inc(a.nop(), 1)
        self.A(f_d1)
        def f_d2(v):
            s_ab.wait(v)
            v.tensor_scalar(self.sB[:, 0:SS], self.sB[:, 0:SS], 1.0, None, ALU.add)
            v.drain()
            s_ab.inc(v.nop(), 1)
        self.V(f_d2)
        def f_d3(a):
            s_ab.wait(a)
            a.activation(self.sB[:, 0:SS], self.sB[:, 0:SS], ACT.Ln)
            a.drain()
            s_ab.inc(a.nop(), 1)
        self.A(f_d3)
        def f_d4(v):
            s_ab.wait(v)
            v.tensor_scalar(self.sC[:, 0:SS], self.sA[:, 0:SS], 0.0, None, ALU.max)
            v.drain()
            v.tensor_tensor(self.sC[:, 0:SS], self.sC[:, 0:SS], self.sB[:, 0:SS], ALU.add)
            v.drain()
            v.tensor_scalar(self.sC[:, 0:SS], self.sC[:, 0:SS], self.negea[:, 0:1], None, ALU.mult)
            v.drain()
            s_ab.inc(v.nop(), 1)
        self.V(f_d4)
        def f_d5(a):
            s_ab.wait(a)
            a.activation(self.sA[:, 0:SS], self.pR1[0:16, 0:SS], ACT.Abs)
            a.drain()
            a.activation(self.sA[:, 0:SS], self.sA[:, 0:SS], ACT.Exp, scale=-1.0)
            a.drain()
            s_ab.inc(a.nop(), 1)
        self.A(f_d5)
        def f_d6(v):
            s_ab.wait(v)
            v.tensor_scalar(self.sA[:, 0:SS], self.sA[:, 0:SS], 1.0, None, ALU.add)
            v.drain()
            s_ab.inc(v.nop(), 1)
        self.V(f_d6)
        def f_d7(a):
            s_ab.wait(a)
            a.activation(self.sA[:, 0:SS], self.sA[:, 0:SS], ACT.Ln)
            a.drain()
            s_ab.inc(a.nop(), 1)
        self.A(f_d7)
        def f_d8(v):
            s_ab.wait(v)
            self.s_msk.wait(v)
            v.tensor_scalar(self.sB[:, 0:SS], self.pR1[0:16, 0:SS], -1.0, 0.0, ALU.mult, ALU.max)
            v.drain()
            v.scalar_tensor_tensor(self.sB[:, 0:SS], self.sB[:, 0:SS], -1.0, self.sA[:, 0:SS],
                                   ALU.mult, ALU.subtract)
            v.drain()
            v.tensor_tensor_scan(self.sA[:, 0:SS], self.scanmask[:, :], self.sC[:, 0:SS],
                                 0.0, ALU.mult, ALU.add)
            v.drain()
            v.tensor_copy(self.r_G[:, :], self.sA[:, 0:SS])
            v.drain()
            v.tensor_tensor(self.r_Gp[:, :], self.sA[:, 0:SS], self.sB[:, 0:SS], ALU.add)
            v.drain()
            s_ab.inc(v.nop(), 1)
        self.V(f_d8)
        def f_d9(a):
            s_ab.wait(a)
            a.activation(self.sC[:, 0:SS], self.sA[:, 0:SS], ACT.Exp)
            a.drain()
            s_ab.inc(a.nop(), 1)
        self.A(f_d9)
        def f_d10(v):
            s_ab.wait(v)
            v.tensor_copy(self.r_lamr[:, :], self.sC[:, 0:SS])
            v.drain()
            v.tensor_tensor(self.r_blam[:, :], self.r_bet[:, :], self.sC[:, 0:SS], ALU.mult)
            v.drain()
            v.tensor_scalar(self.r_lamq[:, :], self.sC[:, 0:SS], 0.125, None, ALU.mult)
            v.drain()
            gend = bass.AP(self.dec, 32 * SS + CH - 1, [[SS, 16], [CH, SS // CH], [0, CH]])
            v.tensor_tensor(self.sA[:, 0:SS].rearrange("p (c u) -> p c u", u=CH), gend,
                            self.r_G[:, :].rearrange("p (c u) -> p c u", u=CH), ALU.subtract)
            v.drain()
            s_ab.inc(v.nop(), 1)
        self.V(f_d10)
        def f_d11(a):
            s_ab.wait(a)
            a.activation(self.r_erev[:, :], self.sA[:, 0:SS], ACT.Exp)
            a.drain()
            s_ab.inc(a.nop(), 1)
        self.A(f_d11)
        s_tr = self.sem("s_tr")
        ntt = SS // 128
        for tt in range(ntt):
            def f_tp1(t, tt=tt):
                s_ab.wait(t)
                self.s_msk.wait(t)
                if s_tr.val:
                    s_tr.wait(t)
                t.transpose(self.pT[:, 0:128], self.dec[:, tt * 128:(tt + 1) * 128],
                            self.eyef[:, :])
                t.drain()
                s_tr.inc(t.nop(), 1)
            self.P(f_tp1)
            def f_te1(a, tt=tt):
                s_tr.wait(a)
                a.activation(self.decT[:, tt, :], self.pT[:, 0:128], ACT.Copy)
                a.drain()
                s_tr.inc(a.nop(), 1)
            self.A(f_te1)
            def f_tp1b(t, tt=tt):
                s_tr.wait(t)
                t.transpose(self.pT[:, 0:96], self.dec2[:, tt * 128:(tt + 1) * 128],
                            self.eyef[0:96, 0:96])
                t.drain()
                s_tr.inc(t.nop(), 1)
            self.P(f_tp1b)
            def f_te1b(a, tt=tt):
                s_tr.wait(a)
                a.activation(self.decT2[:, tt, :], self.pT[:, 0:96], ACT.Copy)
                a.drain()
                s_tr.inc(a.nop(), 1)
            self.A(f_te1b)
        for rr in range(12):
            chn = 4 + rr
            for tt in range(ntt):
                def f_tp2(t, chn=chn, tt=tt):
                    s_cv.wait(t)
                    s_tr.wait(t)
                    t.transpose(self.pTb[:, 0:128], self.qkvs[:, chn, tt * 128:(tt + 1) * 128],
                                self.eyeb[:, :])
                    t.drain()
                    s_tr.inc(t.nop(), 1)
                self.P(f_tp2)
                def f_te2(v, chn=chn, tt=tt, rr=rr):
                    s_tr.wait(v)
                    if rr < 4:
                        dst = self.kTs[:, tt, ts(rr, 128)]
                    else:
                        dst = self.vTs[:, tt, ts(rr - 4, 128)]
                    v.tensor_copy(dst, self.pTb[:, 0:128])
                    v.drain()
                    s_tr.inc(v.nop(), 1)
                self.V(f_te2)
        s_done = self.sem("s_ss")
        def f_fin(v):
            s_tr.wait(v)
            s_cv.wait(v)
            s_ab.wait(v)
            s_done.inc(v.nop(), 1)
        self.V(f_fin)
        return s_done, s_cv


    def declare_scan_ws(self, stk):
        self.Grep = self.sb(stk, "c_Grep", [128, 1024], F32)
        self.Gprep = self.sb(stk, "c_Gprep", [128, 1024], F32)
        self.lamrep = self.sb(stk, "c_lamrep", [128, 2048], F32)
        self.lamCrep = self.sb(stk, "c_lamCrep", [64, 32], F32)
        self.Dw = self.sb(stk, "c_Dw", [128, 1024], F32)
        self.Aw = self.sb(stk, "c_Aw", [128, 1024], BF16)
        self.Bw = self.sb(stk, "c_Bw", [128, 1024], BF16)
        self.QKDT = self.sb(stk, "c_QKDT", [128, 1024], BF16)
        self.Pf = self.sb(stk, "c_Pf", [128, 1024], BF16)
        self.PfT = self.sb(stk, "c_PfT", [128, 1024], BF16)
        self.Pg = self.sb(stk, "c_Pg", [128, 1024], BF16)
        self.PgT = self.sb(stk, "c_PgT", [128, 1024], BF16)
        self.IPb = self.sb(stk, "c_IP", [128, 1024], BF16)
        self.Yc = self.sb(stk, "c_Yc", [128, 1024], BF16)
        self.Yn = self.sb(stk, "c_Yn", [128, 1024], BF16)
        self.Rw = self.sb(stk, "c_Rw", [128, 2048], BF16)
        self.Qt = self.sb(stk, "c_Qt", [64, 1024], F32)
        self.KbT = self.sb(stk, "c_KbT", [128, 1024], BF16)
        self.Uvp = self.sb(stk, "c_Uvp", [128, 1024], BF16)
        self.Ukp = self.sb(stk, "c_Ukp", [128, 1024], BF16)
        self.nUkT = self.sb(stk, "c_nUkT", [64, 1024], F32)
        self.nMT = self.sb(stk, "c_nMT", [64, 1024], BF16)
        self.Et = self.sb(stk, "c_Et", [64, 1024], BF16)
        self.Ssc = self.sb(stk, "c_Ssc", [64, 1024], F32)
        self.Wsc = self.sb(stk, "c_Wsc", [64, 1024], BF16)
        self.Ucp = self.sb(stk, "c_Ucp", [128, 1024], BF16)
        self.opre_st = [self.sb(stk, "c_ost0", [128, 8, CH], BF16),
                        self.sb(stk, "c_ost1", [128, 8, CH], BF16)]
        self.EpT_st = [self.sb(stk, "c_est0", [128, 8, CH], BF16),
                       self.sb(stk, "c_est1", [128, 8, CH], BF16)]
        self.s_osp = [self.sem("s_osp0"), self.sem("s_osp1")]
        self.s_esp = [self.sem("s_esp0"), self.sem("s_esp1")]

    def declare_scan_psum(self, stk, tag):
        self.pX = self.ps(stk, "sp_X" + tag, [128, 1024])
        self.pY = self.ps(stk, "sp_Y" + tag, [128, 1024])
        self.pZ = self.ps(stk, "sp_Z" + tag, [128, 1024])
        self.pV = self.ps(stk, "sp_V" + tag, [64, 1024])

    def declare_p1_psum(self, stk, tag):
        self.pS = self.ps(stk, "p_S" + tag, [1, 512])
        self.pQ = self.ps(stk, "p_Q" + tag, [1, 512])
        self.pR1 = self.ps(stk, "p_R1" + tag, [128, 512])
        self.pR2 = self.ps(stk, "p_R2" + tag, [128, 512])
        self.pA = self.ps(stk, "p_A" + tag, [128, 512])
        self.pB = self.ps(stk, "p_B" + tag, [128, 512])
        self.pT = self.ps(stk, "p_T" + tag, [128, 128])
        self.pTb = self.ps(stk, "p_Tb" + tag, [128, 128], BF16)

    def q_src(self, h, cols):
        j = h // 2
        return self.qkvs[64 * (j % 2):64 * (j % 2) + 64, j // 2, cols]
    def k_src(self, h, cols):
        j = h // 2
        return self.qkvs[64 * (j % 2):64 * (j % 2) + 64, 4 + j // 2, cols]

    @staticmethod
    def hcol(h):
        return (h % 2) * 8 + h // 2

    def build_scan_pair(self, ss, tp, s_p1, s_prev):
        SS = self.SS
        tt = tp
        gc0 = ss * (SS // CH) + 2 * tp
        s_fd = self.sem("s_sfd")
        s_sc = self.sem("s_scc")

        def f_bd(g):
            s_p1.wait(g)
            s_prev.wait(g)
            for half in (0, 1):
                cc = slice(tp * 128 + 64 * half, tp * 128 + 64 * half + 64)
                s_fd.inc(g.dma_start(self.dbc[1024 * half:1024 * half + 1024], self.r_G[:, cc]), 16)
                s_fd.inc(g.dma_start(self.dbc[2048 + 1024 * half:3072 + 1024 * half], self.r_Gp[:, cc]), 16)
                s_fd.inc(g.dma_start(self.dbc[4096 + 1024 * half:5120 + 1024 * half], self.r_lamq[:, cc]), 16)
            lamc_src = bass.AP(self.dec2, 64 * SS + tp * 128 + CH - 1, [[SS, 16], [CH, 2]])
            s_fd.inc(g.dma_start(self.dbc[6144:6176], lamc_src, allow_slow_non_contiguous=True), 16)
            s_fd.wait(g)
            for half in (0, 1):
                sl = slice(64 * half, 64 * half + 64)
                s_fd.inc(g.dma_start(self.Grep[sl, :].rearrange("p (a b) -> p a b", b=64),
                                     bass.AP(self.dbc, 1024 * half, [[0, 64], [64, 16], [1, 64]])), 16)
                s_fd.inc(g.dma_start(self.Gprep[sl, :].rearrange("p (a b) -> p a b", b=64),
                                     bass.AP(self.dbc, 2048 + 1024 * half, [[0, 64], [64, 16], [1, 64]])), 16)
                s_fd.inc(g.dma_start(self.lamrep[:, 1024 * half:1024 * half + 1024].rearrange("p (a b) -> p a b", b=64),
                                     bass.AP(self.dbc, 4096 + 1024 * half, [[0, 128], [64, 16], [1, 64]])), 16)
            s_fd.inc(g.dma_start(self.lamCrep[:, :].rearrange("p (a b) -> p a b", b=2),
                                 bass.AP(self.dbc, 6144, [[0, 64], [2, 16], [1, 2]])), 16)
        self.G_(f_bd)
        if getattr(self, "CUT", None) == 0:
            s_cut = self.sem("s_scd")
            def f_cut(v):
                s_fd.wait(v)
                s_cut.inc(v.nop(), 1)
            self.V(f_cut)
            return s_cut

        def f_kk(t):
            s_p1.wait(t)
            s_prev.wait(t)
            for jpar in range(2):
                for cl in range(2):
                    cc = slice(tp * 128 + 64 * cl, tp * 128 + 64 * cl + 64)
                    for h in range(16):
                        if (h // 2) % 2 != jpar:
                            continue
                        t.matmul(self.pZ[64 * cl:64 * cl + 64, ts(h, 64)],
                                 self.k_src(h, cc), self.k_src(h, cc), start=True, stop=True)
                    t.drain()
            s_sc.inc(t.nop(), 1)
        self.P(f_kk)
        if getattr(self, "CUT", None) == 1:
            s_cut = self.sem("s_scd")
            def f_cut(v):
                s_sc.wait(v)
                s_fd.wait(v)
                s_cut.inc(v.nop(), 1)
            self.V(f_cut)
            return s_cut
        gpcol = self.decT[:, tt, 0:16].unsqueeze(2).broadcast_to([128, 16, 64])
        gcol = self.decT[:, tt, 32:48].unsqueeze(2).broadcast_to([128, 16, 64])
        mA = self.masks[:, 0:64].unsqueeze(1).broadcast_to([128, 16, 64])
        mB = self.masks[:, 64:128].unsqueeze(1).broadcast_to([128, 16, 64])
        mO = self.masks[:, 128:192].unsqueeze(1).broadcast_to([128, 16, 64])
        Dw3 = self.Dw[:, :].rearrange("p (a b) -> p a b", b=64)
        def f_da(v):
            s_fd.wait(v)
            s_sc.wait(v)
            v.scalar_tensor_tensor(Dw3, self.Grep[:, :].rearrange("p (a b) -> p a b", b=64),
                                   -1.0, gpcol, ALU.mult, ALU.add)
            v.drain()
            v.tensor_tensor(Dw3, Dw3, mA, ALU.subtract)
            v.drain()
            s_sc.inc(v.nop(), 1)
        self.V(f_da)
        def f_dae(a):
            s_sc.wait(a)
            a.activation(self.Dw[:, :], self.Dw[:, :], ACT.Exp)
            a.drain()
            s_sc.inc(a.nop(), 1)
        self.A(f_dae)
        def f_aa(v):
            s_sc.wait(v)
            v.tensor_tensor(self.Aw[:, :], self.pZ[:, :], self.Dw[:, :], ALU.mult)
            v.drain()
            v.scalar_tensor_tensor(Dw3, self.Gprep[:, :].rearrange("p (a b) -> p a b", b=64),
                                   1.0, gcol, ALU.mult, ALU.subtract)
            v.drain()
            v.tensor_tensor(Dw3, Dw3, mB, ALU.subtract)
            v.drain()
            s_sc.inc(v.nop(), 1)
        self.V(f_aa)
        def f_dbe(a):
            s_sc.wait(a)
            a.activation(self.Dw[:, :], self.Dw[:, :], ACT.Exp)
            a.drain()
            s_sc.inc(a.nop(), 1)
        self.A(f_dbe)
        def f_bb(v):
            s_sc.wait(v)
            v.tensor_tensor(self.Bw[:, :], self.pZ[:, :], self.Dw[:, :], ALU.mult)
            v.drain()
            v.scalar_tensor_tensor(self.Yc[:, :].rearrange("p (a b) -> p a b", b=64),
                                   self.Bw[:, :].rearrange("p (a b) -> p a b", b=64), -1.0,
                                   self.eyed[:, :].unsqueeze(1).broadcast_to([128, 16, 64]),
                                   ALU.mult, ALU.add)
            v.drain()
            s_sc.inc(v.nop(), 1)
        self.V(f_bb)
        def f_kq(t):
            s_sc.wait(t)
            for jpar in range(2):
                for cl in range(2):
                    cc = slice(tp * 128 + 64 * cl, tp * 128 + 64 * cl + 64)
                    for h in range(16):
                        if (h // 2) % 2 != jpar:
                            continue
                        t.matmul(self.pZ[64 * cl:64 * cl + 64, ts(h, 64)],
                                 self.k_src(h, cc), self.q_src(h, cc), start=True, stop=True)
                    t.drain()
            s_sc.inc(t.nop(), 1)
        self.P(f_kq)
        def f_do(v):
            s_fd.wait(v)
            s_sc.wait(v)
            v.scalar_tensor_tensor(Dw3, self.Grep[:, :].rearrange("p (a b) -> p a b", b=64),
                                   1.0, gcol, ALU.mult, ALU.subtract)
            v.drain()
            v.tensor_tensor(Dw3, Dw3, mO, ALU.subtract)
            v.drain()
            s_sc.inc(v.nop(), 1)
        self.V(f_do)
        def f_doe(a):
            s_sc.wait(a)
            a.activation(self.Dw[:, :], self.Dw[:, :], ACT.Exp)
            a.drain()
            s_sc.inc(a.nop(), 1)
        self.A(f_doe)
        def f_qkdt(v):
            s_sc.wait(v)
            v.tensor_tensor(self.QKDT[:, :], self.pZ[:, :], self.Dw[:, :], ALU.mult)
            v.drain()
            s_sc.inc(v.nop(), 1)
        self.V(f_qkdt)
        if getattr(self, "CUT", None) == 2:
            s_cut = self.sem("s_scd")
            def f_cut(v):
                s_sc.wait(v)
                s_fd.wait(v)
                s_cut.inc(v.nop(), 1)
            self.V(f_cut)
            return s_cut

        eyedb = self.eyed[:, :].unsqueeze(1).broadcast_to([128, 16, 64])
        cF, cT = self.Aw, self.Bw
        for lvl in range(5):
            last = (lvl == 4)
            nF = self.Pf if lvl % 2 == 0 else self.Pg
            nT = self.PfT if lvl % 2 == 0 else self.PgT
            def f_sq(t, cF=cF, cT=cT, last=last):
                s_sc.wait(t)
                for cl in range(2):
                    sl_p = slice(64 * cl, 64 * cl + 64)
                    for h in range(16):
                        t.matmul(self.pX[sl_p, ts(h, 64)], cT[sl_p, ts(h, 64)], cF[sl_p, ts(h, 64)],
                                 start=True, stop=True)
                        if not last:
                            t.matmul(self.pY[sl_p, ts(h, 64)], cF[sl_p, ts(h, 64)], cT[sl_p, ts(h, 64)],
                                     start=True, stop=True)
                    t.drain()
                s_sc.inc(t.nop(), 1)
            self.P(f_sq)
            def f_ev(a, nF=nF, nT=nT, last=last):
                s_sc.wait(a)
                if not last:
                    a.activation(nF[:, :], self.pX[:, :], ACT.Copy)
                    a.drain()
                    a.activation(nT[:, :], self.pY[:, :], ACT.Copy)
                    a.drain()
                s_sc.inc(a.nop(), 1)
            self.A(f_ev)
            def f_ip(v):
                s_sc.wait(v)
                v.scalar_tensor_tensor(self.IPb[:, :].rearrange("p (a b) -> p a b", b=64),
                                       self.pX[:, :].rearrange("p (a b) -> p a b", b=64),
                                       1.0, eyedb, ALU.mult, ALU.add)
                v.drain()
                s_sc.inc(v.nop(), 1)
            self.V(f_ip)
            ycur = self.Yc if lvl % 2 == 0 else self.Yn
            ynew = self.Yn if lvl % 2 == 0 else self.Yc
            def f_yr(t, ycur=ycur):
                s_sc.wait(t)
                for cl in range(2):
                    sl_p = slice(64 * cl, 64 * cl + 64)
                    for h in range(16):
                        t.matmul(self.pZ[sl_p, ts(h, 64)], self.IPb[sl_p, ts(h, 64)],
                                 ycur[sl_p, ts(h, 64)], start=True, stop=True)
                    t.drain()
                s_sc.inc(t.nop(), 1)
            self.P(f_yr)
            def f_ye(a, ynew=ynew):
                s_sc.wait(a)
                a.activation(ynew[:, :], self.pZ[:, :], ACT.Copy)
                a.drain()
                s_sc.inc(a.nop(), 1)
            self.A(f_ye)
            cF, cT = nF, nT
        yfin = self.Yn
        if getattr(self, "CUT", None) == 3:
            s_cut = self.sem("s_scd")
            def f_cut(v):
                s_sc.wait(v)
                s_fd.wait(v)
                s_cut.inc(v.nop(), 1)
            self.V(f_cut)
            return s_cut
        def f_rb(v):
            s_sc.wait(v)
            for h in range(16):
                v.tensor_scalar(self.Rw[:, ts(2 * h, 64)], self.vTs[:, tt, ts(h, 64)],
                                self.decT2[:, tt, h:h + 1], None, ALU.mult)
                v.tensor_scalar(self.Rw[:, ts(2 * h + 1, 64)], self.kTs[:, tt, ts(h // 2, 64)],
                                self.decT[:, tt, 96 + h:97 + h], None, ALU.mult)
                v.tensor_scalar(self.KbT[:, ts(h, 64)], self.kTs[:, tt, ts(h // 2, 64)],
                                self.decT2[:, tt, 32 + h:33 + h], None, ALU.mult)
            v.drain()
            s_sc.inc(v.nop(), 1)
        self.V(f_rb)
        if getattr(self, "CUT", None) == 4:
            s_cut = self.sem("s_scd")
            def f_cut(v):
                s_sc.wait(v)
                s_fd.wait(v)
                s_cut.inc(v.nop(), 1)
            self.V(f_cut)
            return s_cut

        for cl in range(2):
            gc = gc0 + cl
            par, npar = gc % 2, (gc + 1) % 2
            sl_p = slice(64 * cl, 64 * cl + 64)
            ccols = slice(tp * 128 + 64 * cl, tp * 128 + 64 * cl + 64)
            gcols = slice(ss * SS + tp * 128 + 64 * cl, ss * SS + tp * 128 + 64 * cl + 64)
            def f_u2(t, cl=cl, sl_p=sl_p):
                for h in range(16):
                    t.matmul(self.pY[sl_p, ts(h, 64)], yfin[sl_p, ts(h, 64)],
                             self.Rw[sl_p, ts(2 * h + 1, 64)], start=True, stop=True)
                t.drain()
                s_sc.inc(t.nop(), 1)
            def f_u1(t, cl=cl, sl_p=sl_p):
                s_sc.wait(t)
                for h in range(16):
                    t.matmul(self.pX[sl_p, ts(h, 64)], yfin[sl_p, ts(h, 64)],
                             self.Rw[sl_p, ts(2 * h, 64)], start=True, stop=True)
                t.drain()
                s_sc.inc(t.nop(), 1)
            self.P(f_u1)
            self.P(f_u2)
            def f_ut2(t, cl=cl, sl_p=sl_p):
                for h in range(16):
                    t.matmul(self.pZ[:, ts(h, 64)], self.Rw[sl_p, slice(128 * h, 128 * h + 128)],
                             yfin[sl_p, ts(h, 64)], start=True, stop=True)
                t.drain()
                s_sc.inc(t.nop(), 1)
            self.P(f_ut2)
            def f_ue(a, cl=cl, sl_p=sl_p):
                s_sc.wait(a)
                a.activation(self.Uvp[sl_p, :], self.pX[sl_p, :], ACT.Copy)
                a.drain()
                a.activation(self.Ukp[sl_p, :], self.pY[sl_p, :], ACT.Copy)
                a.drain()
                s_sc.inc(a.nop(), 1)
            self.A(f_ue)
            def f_un(v, cl=cl):
                s_sc.wait(v)
                v.tensor_scalar(self.nUkT[:, :], self.pZ[64:128, :], -1.0, None, ALU.mult)
                v.drain()
                s_sc.inc(v.nop(), 1)
            self.V(f_un)
            def f_mr(t, cl=cl, sl_p=sl_p):
                s_sc.wait(t)
                for h in range(16):
                    t.matmul(self.pV[:, ts(h, 64)], self.Ukp[sl_p, ts(h, 64)],
                             self.KbT[sl_p, ts(h, 64)], start=True, stop=True)
                t.drain()
                s_sc.inc(t.nop(), 1)
            self.P(f_mr)
            def f_mre(v, cl=cl):
                s_sc.wait(v)
                v.tensor_scalar(self.nMT[:, :], self.pV[:, :], -1.0, None, ALU.mult)
                v.drain()
                s_sc.inc(v.nop(), 1)
            self.V(f_mre)
            def f_pt(t, cl=cl, sl_p=sl_p):
                s_sc.wait(t)
                for h in range(16):
                    t.matmul(self.pV[:, ts(h, 64)], self.Ukp[sl_p, ts(h, 64)],
                             self.QKDT[sl_p, ts(h, 64)], start=True, stop=True)
                t.drain()
                s_sc.inc(t.nop(), 1)
            self.P(f_pt)
            def f_qt(v, cl=cl, ccols=ccols):
                s_sc.wait(v)
                for h in range(16):
                    qb = 64 * ((h // 2) % 2)
                    v.tensor_tensor(self.Qt[:, ts(h, 64)], self.q_src(h, ccols),
                                    self.lamrep[qb:qb + 64, slice(1024 * cl + 64 * h, 1024 * cl + 64 * h + 64)],
                                    ALU.mult)
                v.drain()
                v.scalar_tensor_tensor(self.Et[:, :], self.pV[:, :], -1.0, self.Qt[:, :],
                                       ALU.mult, ALU.add)
                v.drain()
                s_sc.inc(v.nop(), 1)
            self.V(f_qt)
            def f_ep(t, cl=cl, par=par):
                s_sc.wait(t)
                for h in range(16):
                    wsl = self.Wb[par][:, h, :]
                    t.matmul(self.pV[:, ts(self.hcol(h), 64)], wsl, self.Et[:, ts(h, 64)],
                             start=True, stop=True)
                t.drain()
                s_sc.inc(t.nop(), 1)
            self.P(f_ep)
            def f_epe2(a, par=par):
                s_sc.wait(a)
                self.s_esp[par].wait(a)
                a.activation(self.EpT_st[par][0:64, :, :],
                             self.pV[:, 0:512].rearrange("p (a b) -> p a b", b=64), ACT.Copy)
                a.drain()
                a.activation(self.EpT_st[par][64:128, :, :],
                             self.pV[:, 512:1024].rearrange("p (a b) -> p a b", b=64), ACT.Copy)
                a.drain()
                s_sc.inc(a.nop(), 1)
            self.A(f_epe2)
            def f_epd(g, par=par, g0=gcols.start):
                s_sc.wait(g)
                ap = bass.AP(self.d_EpT, g0, [[8 * self.T, 128], [self.T, 8], [1, CH]])
                self.s_esp[par].inc(g.dma_start(ap, self.EpT_st[par][:, :, :]), 16)
            self.G_(f_epd)
            def f_uc(t, cl=cl, par=par, sl_p=sl_p):
                s_sc.wait(t)
                for h in range(16):
                    t.matmul(self.pV[:, ts(h, 64)], self.nUkT[:, ts(h, 64)],
                             self.Sst[par][:, h, :], start=True, stop=True)
                t.drain()
                s_sc.inc(t.nop(), 1)
            self.P(f_uc)
            def f_uce(v, cl=cl, sl_p=sl_p):
                s_sc.wait(v)
                v.tensor_tensor(self.Ucp[sl_p, :], self.pV[0:64, :], self.Uvp[sl_p, :], ALU.add)
                v.drain()
                s_sc.inc(v.nop(), 1)
            self.V(f_uce)
            def f_ot(t, cl=cl, par=par, sl_p=sl_p):
                s_sc.wait(t)
                for h in range(16):
                    t.matmul(self.pV[:, ts(self.hcol(h), 64)], self.Ucp[sl_p, ts(h, 64)],
                             self.QKDT[sl_p, ts(h, 64)], start=True, stop=True)
                t.drain()
                for h in range(16):
                    t.matmul(self.pZ[0:64, ts(self.hcol(h), 64)],
                             self.Sst[par][:, h, :],
                             self.Qt[:, ts(h, 64)], start=True, stop=True)
                t.drain()
                s_sc.inc(t.nop(), 1)
            self.P(f_ot)
            def f_otc(a):
                s_sc.wait(a)
                a.activation(self.scr[0:64, 0:1024], self.pV[:, :], ACT.Copy)
                a.drain()
                s_sc.inc(a.nop(), 1)
            self.A(f_otc)
            def f_ote(v, par=par):
                s_sc.wait(v)
                self.s_osp[par].wait(v)
                v.tensor_tensor(self.opre_st[par][0:64, :, :],
                                self.scr[0:64, 0:512].rearrange("p (a b) -> p a b", b=64),
                                self.pZ[0:64, 0:512].rearrange("p (a b) -> p a b", b=64), ALU.add)
                v.drain()
                v.tensor_tensor(self.opre_st[par][64:128, :, :],
                                self.scr[0:64, 512:1024].rearrange("p (a b) -> p a b", b=64),
                                self.pZ[0:64, 512:1024].rearrange("p (a b) -> p a b", b=64), ALU.add)
                v.drain()
                s_sc.inc(v.nop(), 1)
            self.V(f_ote)
            def f_otd(g, par=par, g0=gcols.start):
                s_sc.wait(g)
                ap = bass.AP(self.d_opre, g0, [[8 * self.T, 128], [self.T, 8], [1, CH]])
                self.s_osp[par].inc(g.dma_start(ap, self.opre_st[par][:, :, :]), 16)
            self.G_(f_otd)
            lamCsl = self.lamCrep[:, cl::2]
            def f_ssc(v, cl=cl, par=par, lamCsl=lamCsl):
                s_sc.wait(v)
                v.tensor_tensor(self.Ssc[:, :].rearrange("p (a b) -> p a b", b=64),
                                self.Sst[par][:, :, :],
                                lamCsl.unsqueeze(2).broadcast_to([64, 16, 64]), ALU.mult)
                v.drain()
                v.tensor_tensor(self.Wsc[:, :].rearrange("p (a b) -> p a b", b=64),
                                self.Wb[par][:, :, :],
                                lamCsl.unsqueeze(2).broadcast_to([64, 16, 64]), ALU.mult)
                v.drain()
                s_sc.inc(v.nop(), 1)
            self.V(f_ssc)
            def f_st(t, cl=cl, par=par, sl_p=sl_p):
                s_sc.wait(t)
                for h in range(16):
                    t.matmul(self.pV[:, ts(h, 64)], self.KbT[sl_p, ts(h, 64)],
                             self.Ucp[sl_p, ts(h, 64)], start=True, stop=True)
                t.drain()
                s_sc.inc(t.nop(), 1)
            self.P(f_st)
            def f_ste(v, npar=npar):
                s_sc.wait(v)
                v.tensor_tensor(self.Sst[npar][:, :, :].rearrange("p a b -> p (a b)"),
                                self.pV[:, :], self.Ssc[:, :], ALU.add)
                v.drain()
                s_sc.inc(v.nop(), 1)
            self.V(f_ste)
            def f_wm(t, cl=cl, par=par):
                s_sc.wait(t)
                for h in range(16):
                    wsl = self.Wb[par][:, h, :]
                    t.matmul(self.pV[:, ts(h, 64)], self.nMT[:, ts(h, 64)], wsl,
                             start=True, stop=True)
                t.drain()
                s_sc.inc(t.nop(), 1)
            self.P(f_wm)
            def f_wme(v, npar=npar):
                s_sc.wait(v)
                v.tensor_tensor(self.Wb[npar][:, :, :].rearrange("p a b -> p (a b)"),
                                self.pV[:, :], self.Wsc[:, :], ALU.add)
                v.drain()
                s_sc.inc(v.nop(), 1)
            self.V(f_wme)
        s_done = self.sem("s_scd")
        def f_fin(v):
            s_sc.wait(v)
            s_done.inc(v.nop(), 1)
        self.V(f_fin)
        return s_done

    def build_scan_superslab(self, ss, s_p1, s_prev):
        s = s_prev
        for tp in range(self.SS // 128):
            s = self.build_scan_pair(ss, tp, s_p1, s)
        return s


    def declare_epi_ws(self, stk):
        ES = self.ES = 256
        self.Sin = self.sb(stk, "e_Sin", [64, 16, DV], F32)
        self.Sinb = self.sb(stk, "e_Sinb", [128, 16, DV], BF16)
        self.wzT = self.sb(stk, "e_wzT", [128, 8, 1024], BF16)
        self.woutT = self.sb(stk, "e_woutT", [128, 8, 1024], BF16)
        self.xe = [self.sb(stk, "e_xe0", [128, 8, ES], F32),
                   self.sb(stk, "e_xe1", [128, 8, ES], F32)]
        self.opre_sl = [self.sb(stk, "e_op0", [128, 8, ES], BF16),
                        self.sb(stk, "e_op1", [128, 8, ES], BF16)]
        self.EpT_sl = [self.sb(stk, "e_ep0", [128, 8, ES], BF16),
                       self.sb(stk, "e_ep1", [128, 8, ES], BF16)]
        self.hTe = self.sb(stk, "e_hT", [128, 8, ES], BF16)
        self.ofin = self.sb(stk, "e_ofin", [128, 8, ES], F32)
        self.szs = self.sb(stk, "e_szs", [128, 8, ES], BF16)
        self.tf = self.sb(stk, "e_tf", [128, 8, ES], F32)
        self.tsq = self.sb(stk, "e_tsq", [128, ES], F32)
        self.r16 = self.sb(stk, "e_r16", [16, ES], F32)
        self.hgn = self.sb(stk, "e_hgn", [128, 8, ES], BF16)
        self.s_eid = [self.sem("s_eid0"), self.sem("s_eid1")]
        self.s_xod = [self.sem("s_xod0"), self.sem("s_xod1")]

    def declare_epi_psum(self, stk, tag):
        self.peD = self.ps(stk, "pe_D" + tag, [64, 1024])
        self.peZ = self.ps(stk, "pe_Z" + tag, [128, 512])
        self.peG = self.ps(stk, "pe_G" + tag, [16, 512])
        self.peR = self.ps(stk, "pe_R" + tag, [128, 512])
        self.peW = self.ps(stk, "pe_W" + tag, [128, 512])
        self.pR1 = self.ps(stk, "pe_R1" + tag, [128, 512])
        self.pR2 = self.ps(stk, "pe_R2" + tag, [128, 512])

    def build_ag2(self, s_scan):
        s_cc = self.sem("s_cc")
        s_ccw = self.sem("s_ccw")
        self.s_sin = self.sem("s_sin")
        nch = self.T // CH
        fin = nch % 2
        def f_cc(g):
            s_scan.wait(g)
            s_cc.inc(g.dma_start(self.cc_in[:, :], self.Sst[fin][:, :, :].rearrange("p a b -> p (a b)")), 16)
            s_cc.wait(g)
            g.collective_compute(
                "AllGather", mybir.AluOpType.bypass,
                replica_groups=[[i, i + 1] for i in range(0, self.NCORE, 2)],
                ins=[self.cc_in.ap().opt()], outs=[self.cc_out.ap().opt()],
            ).then_inc(s_ccw.h, 1)
            s_ccw.val += 1
            s_ccw.wait(g)
            s_cc.inc(g.dma_start(self.Sin[:, :, :].rearrange("p a b -> p (a b)"), self.cc_out[0:64, :]), 16)
            s_cc.inc(g.dma_start(self.scr[0:64, 0:512], self.cc_out[64:128, 0:512]), 16)
            s_cc.inc(g.dma_start(self.scr[0:64, 512:1024], self.cc_out[64:128, 512:1024]), 16)
        self.G_(f_cc)
        def f_si(v):
            s_cc.wait(v)
            v.tensor_scalar(self.Sin[:, :, :].rearrange("p a b -> p (a b)"),
                            self.Sin[:, :, :].rearrange("p a b -> p (a b)"),
                            self.cvec[0:64, 2:3], None, ALU.mult)
            v.drain()
            v.scalar_tensor_tensor(self.Sin[:, 0:8, :].rearrange("p a b -> p (a b)"),
                                   self.scr[0:64, 0:512], self.cvec[0:64, 3:4],
                                   self.Sin[:, 0:8, :].rearrange("p a b -> p (a b)"),
                                   ALU.mult, ALU.add)
            v.drain()
            v.scalar_tensor_tensor(self.Sin[:, 8:16, :].rearrange("p a b -> p (a b)"),
                                   self.scr[0:64, 512:1024], self.cvec[0:64, 3:4],
                                   self.Sin[:, 8:16, :].rearrange("p a b -> p (a b)"),
                                   ALU.mult, ALU.add)
            v.drain()
            v.tensor_copy(self.Sinb[0:64, :, :].rearrange("p a b -> p (a b)"),
                          self.Sin[:, :, :].rearrange("p a b -> p (a b)"))
            v.drain()
            v.tensor_copy(self.Sinb[64:128, :, :].rearrange("p a b -> p (a b)"),
                          self.Sin[:, :, :].rearrange("p a b -> p (a b)"))
            v.drain()
            self.s_sin.inc(v.nop(), 1)
        self.V(f_si)
        return self.s_sin

    def build_epi_slab(self, sl, s_pre):
        ES = self.ES
        w0 = sl * ES
        par = sl % 2
        s_e = self.sem("s_e")
        def f_ind(g):
            s_pre.wait(g)
            if self.s_xod[par].val:
                self.s_xod[par].wait(g)
            ap_x = bass.AP(self.i_x, 3 + w0, [[8 * self.TH, 128], [self.TH, 8], [1, ES]])
            ap_o = bass.AP(self.d_opre, w0, [[8 * self.T, 128], [self.T, 8], [1, ES]])
            ap_e = bass.AP(self.d_EpT, w0, [[8 * self.T, 128], [self.T, 8], [1, ES]])
            self.s_eid[par].inc(g.dma_start(self.xe[par][:, :, :], ap_x), 16)
            self.s_eid[par].inc(g.dma_start(self.opre_sl[par][:, :, :], ap_o), 16)
            self.s_eid[par].inc(g.dma_start(self.EpT_sl[par][:, :, :], ap_e), 16)
        self.G_(f_ind)
        def f_gate(v):
            s_pre.wait(v)
            self.s_eid[par].wait(v)
            s_e.inc(v.nop(), 1)
        self.V(f_gate)
        self.ln_rep_apply(0, 3 + w0, ES, s_e, self.stats, self.statsm, self.hTe, False,
                          xsrc=self.xe[par], x0=0)
        for d in range(8):
            def f_z(t, d=d):
                s_e.wait(t)
                for kk in range(8):
                    t.matmul(self.peZ[:, 0:ES], self.wzT[:, kk, ts(d, 128)], self.hTe[:, kk, :],
                             start=(kk == 0), stop=(kk == 7))
                t.drain()
                s_e.inc(t.nop(), 1)
            self.P(f_z)
            def f_zs(a, d=d):
                s_e.wait(a)
                a.activation(self.szs[:, d, :], self.peZ[:, 0:ES], ACT.Silu)
                a.drain()
                s_e.inc(a.nop(), 1)
            self.A(f_zs)
        for cc in range(ES // CH):
            lcols = slice(cc * CH, cc * CH + CH)
            def f_do(t, cc=cc, lcols=lcols, par=par):
                s_e.wait(t)
                for hp in range(2):
                    for h in range(hp, 16, 2):
                        base = 64 * (h % 2)
                        t.matmul(self.peD[:, ts(self.hcol(h), 64)],
                                 self.Sinb[base:base + 64, h, :],
                                 self.EpT_sl[par][base:base + 64, h // 2, lcols],
                                 start=True, stop=True)
                    t.drain()
                s_e.inc(t.nop(), 1)
            self.P(f_do)
            def f_oa(v, cc=cc, lcols=lcols, par=par):
                s_e.wait(v)
                v.tensor_tensor(self.ofin[0:64, :, lcols],
                                self.opre_sl[par][0:64, :, lcols],
                                self.peD[:, 0:512].rearrange("p (a b) -> p a b", b=64),
                                ALU.add)
                v.drain()
                v.tensor_tensor(self.ofin[64:128, :, lcols],
                                self.opre_sl[par][64:128, :, lcols],
                                self.peD[:, 512:1024].rearrange("p (a b) -> p a b", b=64),
                                ALU.add)
                v.drain()
                s_e.inc(v.nop(), 1)
            self.V(f_oa)
        def f_tf(v):
            s_e.wait(v)
            v.tensor_tensor(self.tf[:, :, :].rearrange("p a b -> p (a b)"),
                            self.ofin[:, :, :].rearrange("p a b -> p (a b)"),
                            self.szs[:, :, :].rearrange("p a b -> p (a b)"), ALU.mult)
            v.drain()
            s_e.inc(v.nop(), 1)
        self.V(f_tf)
        for d in range(8):
            def f_sq(a, d=d):
                s_e.wait(a)
                a.activation(self.tsq[:, :], self.tf[:, d, :], ACT.Square)
                a.drain()
                s_e.inc(a.nop(), 1)
            self.A(f_sq)
            def f_gs(t, d=d):
                s_e.wait(t)
                t.matmul(self.peG[:, 0:ES], self.gnsel[:, d, :], self.tsq[:, :],
                         start=(d == 0), stop=(d == 7))
                t.drain()
                s_e.inc(t.nop(), 1)
            self.P(f_gs)
        def f_r16(v):
            s_e.wait(v)
            v.tensor_scalar(self.r16[:, :], self.peG[:, 0:ES], 1.0 / 64.0, None, ALU.mult)
            v.drain()
            s_e.inc(v.nop(), 1)
        self.V(f_r16)
        def f_r16a(a):
            s_e.wait(a)
            a.activation(self.r16[:, :], self.r16[:, :], ACT.Ln, bias=self.cvec[0:16, 1:2])
            a.drain()
            a.activation(self.r16[:, :], self.r16[:, :], ACT.Exp, scale=-0.5)
            a.drain()
            s_e.inc(a.nop(), 1)
        self.A(f_r16a)
        for d in range(8):
            def f_rr(t, d=d):
                s_e.wait(t)
                t.matmul(self.peR[:, 0:ES], self.gnselT[:, d, :], self.r16[:, :],
                         start=True, stop=True)
                t.drain()
                s_e.inc(t.nop(), 1)
            self.P(f_rr)
            def f_hg(v, d=d):
                s_e.wait(v)
                v.scalar_tensor_tensor(self.hgn[:, d, :], self.tf[:, d, :], self.nw[:, 0:1],
                                       self.peR[:, 0:ES], ALU.mult, ALU.mult)
                v.drain()
                s_e.inc(v.nop(), 1)
            self.V(f_hg)
        for d in range(8):
            def f_wo(t, d=d):
                s_e.wait(t)
                for kk in range(8):
                    t.matmul(self.peW[:, 0:ES], self.woutT[:, kk, ts(d, 128)], self.hgn[:, kk, :],
                             start=(kk == 0), stop=(kk == 7))
                t.drain()
                s_e.inc(t.nop(), 1)
            self.P(f_wo)
            def f_xu(v, d=d, par=par):
                s_e.wait(v)
                v.scalar_tensor_tensor(self.xe[par][:, d, :], self.peW[:, 0:ES],
                                       self.adaln[:, d, 2:3], self.xe[par][:, d, :],
                                       ALU.mult, ALU.add)
                v.drain()
                s_e.inc(v.nop(), 1)
            self.V(f_xu)
        def f_xod(g, par=par, w0=w0):
            s_e.wait(g)
            ap = bass.AP(self.d_xp, w0, [[8 * self.T, 128], [self.T, 8], [1, ES]])
            self.s_xod[par].inc(g.dma_start(ap, self.xe[par][:, :, :]), 16)
        self.G_(f_xod)
        return s_e

    def quant_emit(self, ch, src, dst, nblk, pre=None, post_sems=()):
        n = nblk * 16
        bm, tq, gq = self.fbm, self.ftq, self.fgq
        aa = gq[:, 0:n]
        acc = gq[:, 512:512 + n]
        r3 = lambda ap: ap.rearrange("p (a b) -> p a b", b=16)
        bcast = lambda ap: ap.unsqueeze(2).broadcast_to([128, nblk, 16])
        def f_q1(v):
            ch.wait(v)
            if pre is not None:
                pre(v)
            v.tensor_reduce(bm[:, 0:nblk], r3(src), mybir.AxisListType.X, ALU.max,
                            apply_absolute_value=True)
            v.drain()
            v.tensor_scalar(bm[:, 64:64 + nblk], bm[:, 0:nblk], 6e-12, None, ALU.max)
            v.drain()
            ch.inc(v.nop(), 1)
        self.V(f_q1)
        def f_q2(a):
            ch.wait(a)
            a.activation(bm[:, 0:nblk], bm[:, 64:64 + nblk], ACT.Ln)
            a.drain()
            a.activation(bm[:, 0:nblk], bm[:, 0:nblk], ACT.Exp, scale=-1.0,
                         bias=self.cvec[:, 5:6])
            a.drain()
            ch.inc(a.nop(), 1)
        self.A(f_q2)
        def f_q3(v):
            ch.wait(v)
            v.tensor_tensor(r3(tq[:, 0:n]), r3(src), bcast(bm[:, 0:nblk]), ALU.mult)
            v.drain()
            v.tensor_scalar(acc, tq[:, 0:n], -1.0, None, ALU.mult)
            v.drain()
            v.tensor_tensor(aa, tq[:, 0:n], acc, ALU.max)
            v.drain()
            v.tensor_scalar(acc, aa, 7.0, None, ALU.is_gt)
            v.drain()
            v.scalar_tensor_tensor(acc, aa, 5.0, acc, ALU.is_gt, ALU.add)
            v.drain()
            v.scalar_tensor_tensor(acc, aa, 3.0, acc, ALU.is_gt, ALU.add)
            v.drain()
            v.scalar_tensor_tensor(acc, aa, 1.0, acc, ALU.is_gt, ALU.add)
            v.drain()
            hi = tq[:, 0:n]
            v.tensor_scalar(hi, aa, 20.0, None, ALU.is_gt)
            v.drain()
            v.tensor_tensor(hi, hi, hi, ALU.add)
            v.drain()
            v.scalar_tensor_tensor(hi, aa, 14.0, hi, ALU.is_gt, ALU.add)
            v.drain()
            v.scalar_tensor_tensor(hi, aa, 10.0, hi, ALU.is_gt, ALU.add)
            v.drain()
            v.scalar_tensor_tensor(acc, hi, 2.0, acc, ALU.mult, ALU.add)
            v.drain()
            v.tensor_scalar(aa, src, 1e13, -1.0, ALU.mult, ALU.max)
            v.drain()
            v.tensor_scalar(aa, aa, 1.0, None, ALU.min)
            v.drain()
            v.tensor_tensor(acc, acc, aa, ALU.mult)
            v.drain()
            v.tensor_scalar(bm[:, 64:64 + nblk], bm[:, 64:64 + nblk], 1.0 / 12.0, None, ALU.mult)
            v.drain()
            v.tensor_tensor(r3(dst), r3(acc), bcast(bm[:, 64:64 + nblk]), ALU.mult)
            v.drain()
            ch.inc(v.nop(), 1)
            for sm, k in post_sems:
                sm.inc(v.nop(), k)
        self.V(f_q3)

    def build_wq(self, stk, s_pre):
        NR, ISHP = self.NR, self.ISHP
        cols = max(NR * 1024, 8 * ISHP)
        wsh = self.sb(stk, "w_sh", [128, cols], F32)
        wqq = self.sb(stk, "w_qq", [128, cols], BF16)
        wts = self.sb(stk, "w_ts", [128, cols], BF16)
        self.fbm = self.sb(stk, "w_bm", [128, 192], F32)
        self.ftq = self.sb(stk, "w_tq", [128, 512], F32)
        self.fgq = self.sb(stk, "w_gq", [128, 1024], F32)
        pWb = self.ps(stk, "p_wT", [128, 128], BF16)
        s_wd = self.sem("s_wd")
        s_wq = self.sem("s_wq")
        s_ag2 = self.sem("s_ag2")
        specs = ((self.i_wg, self.cc_g_in, "g"), (self.i_wu, self.cc_u_in, "u"),
                 (self.i_wd, self.cc_d_in, "d"))
        for wi, (ih, ccin, kind) in enumerate(specs):
            ncols = NR * 1024 if kind != "d" else 8 * ISHP
            def f_in(g, ih=ih, ncols=ncols, wi=wi):
                if wi == 0:
                    s_pre.wait(g)
                else:
                    s_wq.wait(g)
                s_wd.inc(g.dma_start(wsh[:, 0:ncols], ih[:, :]), 16)
                s_wd.wait(g)
                s_wq.inc(g.nop(), 1)
            self.G_(f_in)
            ntile = NR if kind != "d" else 8
            tw = 1024 if kind != "d" else ISHP
            for it in range(ntile):
                for c0 in range(0, tw, 512):
                    wd = min(512, tw - c0)
                    self.quant_emit(s_wq, wsh[:, it * tw + c0: it * tw + c0 + wd],
                                    wqq[:, it * tw + c0: it * tw + c0 + wd], wd // 16)
            if kind != "d":
                tiles = [(r * 1024 + 128 * k, k * ISHP + 128 * r)
                         for k in range(8) for r in range(NR)]
            else:
                tiles = [(a * ISHP + 128 * r, r * 1024 + 128 * a)
                         for a in range(8) for r in range(NR)]
            for (srcc, dstc) in tiles:
                def f_t(t, srcc=srcc):
                    s_wq.wait(t)
                    t.transpose(pWb[:, 0:128], wqq[:, srcc:srcc + 128], self.eyeb[:, :])
                    t.drain()
                    s_wq.inc(t.nop(), 1)
                self.P(f_t)
                def f_te(a, dstc=dstc):
                    s_wq.wait(a)
                    a.activation(wts[:, dstc:dstc + 128], pWb[:, 0:128], ACT.Copy)
                    a.drain()
                    s_wq.inc(a.nop(), 1)
                self.A(f_te)
            if kind != "d":
                dst = bass.AP(ccin, 0, [[ISHP, 128], [128 * ISHP, 8], [1, ISHP]])
                src = wts[:, 0:8 * ISHP].rearrange("p (a b) -> p a b", b=ISHP)
            else:
                dst = bass.AP(ccin, 0, [[1024, 128], [128 * 1024, NR], [1, 1024]])
                src = wts[:, 0:NR * 1024].rearrange("p (a b) -> p a b", b=1024)
            def f_out(g, dst=dst, src=src):
                s_wq.wait(g)
                s_wd.inc(g.dma_start(dst, src), 16)
                s_wd.wait(g)
                s_wq.inc(g.nop(), 1)
            self.G_(f_out)
        def f_ag(g):
            s_wq.wait(g)
            grp = [list(range(self.NCORE))]
            for cin, gout in ((self.cc_g_in, self.g_gT), (self.cc_u_in, self.g_uT),
                              (self.cc_d_in, self.g_dT)):
                g.collective_compute("AllGather", mybir.AluOpType.bypass,
                                     replica_groups=grp, ins=[cin.ap().opt()],
                                     outs=[gout.ap().opt()]).then_inc(s_ag2.h, 1)
                s_ag2.val += 1
        self.G_(f_ag)
        s_wqd = self.sem("s_wqd")
        def f_fence(v):
            s_wq.wait(v)
            s_wqd.inc(v.nop(), 1)
        self.V(f_fence)
        return s_wqd, s_ag2

    def declare_ffn_ws(self, stk):
        FS, NU, ISHP, NJ = self.FS, self.NU, self.ISHP, self.NJ
        IT = self.NCORE * ISHP
        self.hf = self.sb(stk, "f_hf", [128, 8, FS], F32)
        self.fsq = self.sb(stk, "f_sq", [128, 2, FS], F32)
        self.hq = self.sb(stk, "f_hq", [128, 8, FS], BF16)
        self.hfT = self.sb(stk, "f_hfT", [128, 1024], F32)
        self.hqT = self.sb(stk, "f_hqT", [128, 1024], BF16)
        self.h2q = self.sb(stk, "f_h2q", [128, NU, IT], BF16)
        self.h2qT = self.sb(stk, "f_h2qT", [128, NJ, FS], BF16)
        self.h2p = self.sb(stk, "f_h2p", [128, 512], F32)
        self.xf = self.sb(stk, "f_xf", [128, 8, FS], F32)
        self.s_xfd = self.sem("s_xfd")
        self.s_yod = self.sem("s_yod")
        WW = min(512, ISHP)
        self.wgb = [self.sb(stk, "f_wg0", [128, 8, WW], BF16),
                    self.sb(stk, "f_wg1", [128, 8, WW], BF16)]
        self.wub = [self.sb(stk, "f_wu0", [128, 8, WW], BF16),
                    self.sb(stk, "f_wu1", [128, 8, WW], BF16)]
        self.wdb = [self.sb(stk, "f_wd0", [128, NJ, 128], BF16),
                    self.sb(stk, "f_wd1", [128, NJ, 128], BF16)]
        self.fbm = self.sb(stk, "f_bm", [128, 192], F32)
        self.ftq = self.sb(stk, "f_tq", [128, 512], F32)
        self.fgq = self.sb(stk, "f_gq", [128, 1024], F32)

    def declare_ffn_psum(self, stk, tag):
        self.pF1 = self.ps(stk, "pf_1" + tag, [128, 512])
        self.pF2 = self.ps(stk, "pf_2" + tag, [128, 512])
        self.pR1 = self.ps(stk, "pf_R1" + tag, [128, 512])
        self.pR2 = self.ps(stk, "pf_R2" + tag, [128, 512])
        self.pD = self.ps(stk, "pf_D" + tag, [128, 512])
        self.pT = self.ps(stk, "pf_T" + tag, [128, 128])
        self.pTb = self.ps(stk, "pf_Tb" + tag, [128, 128], BF16)
        self.pS = self.pF1
        self.pQ = self.pF2

    def ln_ffn(self, w0, n, s_pre, xsrc, x0):
        s_loc = self.sem("s_ln2")
        def f_gate(v):
            s_pre.wait(v)
            s_loc.inc(v.nop(), 1)
        self.V(f_gate)
        for dp in range(4):
            def f_sq(a, dp=dp):
                s_loc.wait(a)
                a.activation(self.fsq[:, 0, 0:n], xsrc[:, 2 * dp, x0:x0 + n], ACT.Square)
                a.activation(self.fsq[:, 1, 0:n], xsrc[:, 2 * dp + 1, x0:x0 + n], ACT.Square)
                a.drain()
                s_loc.inc(a.nop(), 1)
            self.A(f_sq)
            def f_mm(t, dp=dp):
                s_loc.wait(t)
                for j in range(2):
                    d = 2 * dp + j
                    t.matmul(self.pS[0:1, 0:n], self.onesf[:, 0:1], xsrc[:, d, x0:x0 + n],
                             start=(d == 0), stop=(d == 7))
                for j in range(2):
                    d = 2 * dp + j
                    t.matmul(self.pQ[0:1, 0:n], self.onesf[:, 0:1], self.fsq[:, j, 0:n],
                             start=(d == 0), stop=(d == 7))
                t.drain()
                s_loc.inc(t.nop(), 1)
            self.P(f_mm)
        def f_vst(v):
            s_loc.wait(v)
            v.tensor_scalar(self.scr[0:1, 0:n], self.pS[0:1, 0:n], 1.0 / 1024.0, None, ALU.mult)
            v.drain()
            v.tensor_tensor(self.scr[0:1, 0:n], self.scr[0:1, 0:n], self.scr[0:1, 0:n], ALU.mult)
            v.drain()
            v.scalar_tensor_tensor(self.scr[0:1, self.SCROFF:self.SCROFF + n], self.pQ[0:1, 0:n],
                                   1.0 / 1024.0, self.scr[0:1, 0:n], ALU.mult, ALU.subtract)
            v.drain()
            s_loc.inc(v.nop(), 1)
        self.V(f_vst)
        def f_ast(a):
            s_loc.wait(a)
            a.activation(self.scr[0:1, 0:n], self.scr[0:1, self.SCROFF:self.SCROFF + n],
                         ACT.Ln, bias=self.cvec[0:1, 0:1])
            a.drain()
            a.activation(self.stats[0:1, w0:w0 + n], self.scr[0:1, 0:n], ACT.Exp, scale=-0.5)
            a.drain()
            s_loc.inc(a.nop(), 1)
        self.A(f_ast)
        def f_vst2(v):
            s_loc.wait(v)
            v.scalar_tensor_tensor(self.statsm[32:33, w0:w0 + n], self.pS[0:1, 0:n], 1.0 / 1024.0,
                                   self.stats[0:1, w0:w0 + n], ALU.mult, ALU.mult)
            v.drain()
            s_loc.inc(v.nop(), 1)
        self.V(f_vst2)
        self.ln_rep_apply(1, w0, n, s_loc, self.stats, self.statsm, self.hf, False, xsrc, x0)
        return s_loc

    def build_ffn_slab(self, u4, s_pre, s_ag2):
        FS, NU, NR, ISHP, NJ = self.FS, self.NU, self.NR, self.ISHP, self.NJ
        w0 = u4 * FS
        def f_xfd(g):
            s_pre.wait(g)
            if self.s_yod.val:
                self.s_yod.wait(g)
            ap = bass.AP(self.d_xp, w0, [[8 * self.T, 128], [self.T, 8], [1, FS]])
            self.s_xfd.inc(g.dma_start(self.xf[:, :, :], ap), 16)
        self.G_(f_xfd)
        s_xg = self.sem("s_xg")
        def f_xg(v):
            s_pre.wait(v)
            self.s_xfd.wait(v)
            s_xg.inc(v.nop(), 1)
        self.V(f_xg)
        s_ln = self.ln_ffn(3 + w0, FS, s_xg, self.xf, 0)
        s_f = self.sem("s_f")
        def f_gate(v):
            s_ln.wait(v)
            s_f.inc(v.nop(), 1)
        self.V(f_gate)
        for u in range(NU):
            for d in range(8):
                def f_t(t, u=u, d=d):
                    s_f.wait(t)
                    t.transpose(self.pT[:, 0:128], self.hf[:, d, 128 * u:128 * u + 128],
                                self.eyef[:, :])
                    t.drain()
                    s_f.inc(t.nop(), 1)
                self.P(f_t)
                def f_te(a, d=d):
                    s_f.wait(a)
                    a.activation(self.hfT[:, 128 * d:128 * d + 128], self.pT[:, 0:128], ACT.Copy)
                    a.drain()
                    s_f.inc(a.nop(), 1)
                self.A(f_te)
            self.quant_emit(s_f, self.hfT[:, 0:512], self.hqT[:, 0:512], 32)
            self.quant_emit(s_f, self.hfT[:, 512:1024], self.hqT[:, 512:1024], 32)
            for d in range(8):
                def f_tb(t, d=d):
                    s_f.wait(t)
                    t.transpose(self.pTb[:, 0:128], self.hqT[:, 128 * d:128 * d + 128],
                                self.eyeb[:, :])
                    t.drain()
                    s_f.inc(t.nop(), 1)
                self.P(f_tb)
                def f_tbe(a, u=u, d=d):
                    s_f.wait(a)
                    a.activation(self.hq[:, d, 128 * u:128 * u + 128], self.pTb[:, 0:128], ACT.Copy)
                    a.drain()
                    s_f.inc(a.nop(), 1)
                self.A(f_tbe)
        s_gu = self.sem("s_gu")
        s_wgp = [self.sem("s_wg0"), self.sem("s_wg1")]
        if not hasattr(self, "h_gu"):
            self.h_gu = []
        chunks = [(s, c0) for s in range(self.NCORE) for c0 in range(0, ISHP, 512)]
        for ci, (s, c0) in enumerate(chunks):
            par = len(self.h_gu) % 2
            W = min(512, ISHP - c0)
            free_after = self.h_gu[-2] if len(self.h_gu) >= 2 else None
            def f_wg(g, s=s, c0=c0, W=W, par=par, free_after=free_after):
                s_ag2.wait(g)
                if free_after is not None:
                    g.wait_ge(s_gu.h, free_after)
                off = 1024 * s * ISHP + c0
                ap_g = bass.AP(self.g_gT, off, [[ISHP, 128], [128 * ISHP, 8], [1, W]])
                ap_u = bass.AP(self.g_uT, off, [[ISHP, 128], [128 * ISHP, 8], [1, W]])
                s_wgp[par].inc(g.dma_start(self.wgb[par][:, :, 0:W], ap_g), 16)
                s_wgp[par].inc(g.dma_start(self.wub[par][:, :, 0:W], ap_u), 16)
            self.G_(f_wg)
            for u in range(NU):
                def f_mm(t, par=par, u=u, W=W):
                    s_f.wait(t)
                    s_wgp[par].wait(t)
                    for k in range(8):
                        t.matmul(self.pF1[:, 0:W], self.hq[:, k, 128 * u:128 * u + 128],
                                 self.wgb[par][:, k, 0:W], start=(k == 0), stop=(k == 7))
                    for k in range(8):
                        t.matmul(self.pF2[:, 0:W], self.hq[:, k, 128 * u:128 * u + 128],
                                 self.wub[par][:, k, 0:W], start=(k == 0), stop=(k == 7))
                    t.drain()
                    s_f.inc(t.nop(), 1)
                self.P(f_mm)
                def f_sil(a, W=W):
                    s_f.wait(a)
                    a.activation(self.h2p[:, 0:W], self.pF1[:, 0:W], ACT.Silu)
                    a.drain()
                    s_f.inc(a.nop(), 1)
                self.A(f_sil)
                def pre_mq(v, W=W):
                    v.tensor_tensor(self.h2p[:, 0:W], self.h2p[:, 0:W], self.pF2[:, 0:W], ALU.mult)
                    v.drain()
                self.quant_emit(s_f, self.h2p[:, 0:W],
                                self.h2q[:, u, s * ISHP + c0: s * ISHP + c0 + W], W // 16,
                                pre=pre_mq, post_sems=[(s_gu, 1)])
            self.h_gu.append(s_gu.val)
        for j in range(NJ):
            for u in range(NU):
                def f_ht(t, j=j, u=u):
                    s_f.wait(t)
                    t.transpose(self.pTb[:, 0:128],
                                self.h2q[:, u, 128 * j: 128 * j + 128], self.eyeb[:, :])
                    t.drain()
                    s_f.inc(t.nop(), 1)
                self.P(f_ht)
                def f_hte(a, j=j, u=u):
                    s_f.wait(a)
                    a.activation(self.h2qT[:, j, 128 * u:128 * u + 128],
                                 self.pTb[:, 0:128], ACT.Copy)
                    a.drain()
                    s_f.inc(a.nop(), 1)
                self.A(f_hte)
        s_wdp = [self.sem("s_wdp0"), self.sem("s_wdp1")]
        s_dn = self.sem("s_dn")
        if not hasattr(self, "h_dn"):
            self.h_dn = []
        for d in range(8):
            par = len(self.h_dn) % 2
            free_after = self.h_dn[-2] if len(self.h_dn) >= 2 else None
            def f_wd(g, d=d, par=par, free_after=free_after):
                s_ag2.wait(g)
                if free_after is not None:
                    g.wait_ge(s_dn.h, free_after)
                ap_d = bass.AP(self.g_dT, 128 * d, [[1024, 128], [128 * 1024, NJ], [1, 128]])
                s_wdp[par].inc(g.dma_start(
                    self.wdb[par][:, :, :].rearrange("p a b -> p (a b)"), ap_d), 16)
            self.G_(f_wd)
            def f_dmm(t, d=d, par=par):
                s_f.wait(t)
                s_wdp[par].wait(t)
                for j in range(NJ):
                    t.matmul(self.pD[:, 0:FS], self.wdb[par][:, j, :], self.h2qT[:, j, 0:FS],
                             start=(j == 0), stop=(j == NJ - 1))
                t.drain()
                s_f.inc(t.nop(), 1)
            self.P(f_dmm)
            def f_xu(v, d=d):
                s_f.wait(v)
                v.scalar_tensor_tensor(self.xf[:, d, :], self.pD[:, 0:FS],
                                       self.adaln[:, d, 5:6], self.xf[:, d, :],
                                       ALU.mult, ALU.add)
                v.drain()
                s_f.inc(v.nop(), 1)
                s_dn.inc(v.nop(), 1)
            self.V(f_xu)
            self.h_dn.append(s_dn.val)
        def f_yod(g, w0=w0):
            s_f.wait(g)
            ap = bass.AP(self.o_y, w0, [[8 * self.T, 128], [self.T, 8], [1, FS]])
            self.s_yod.inc(g.dma_start(ap, self.xf[:, :, :]), 16)
        self.G_(f_yod)
        return s_f

    def build_section1(self, sections=1):
        self.declare_global()
        SS, SSH = self.SS, self.SSH
        for name, shape in (("hT", [128, 8 * SSH]), ("mixed", [128, 16 * SSH]),
                            ("qkvs", [128, 16 * SS]), ("dec", [128, SS]), ("dec2", [96, SS]),
                            ("decT", [128, (SS // 128) * 128]), ("decT2", [128, (SS // 128) * 96]),
                            ("stats", [1, self.TH]), ("statsm", [1, self.TH]),
                            ("adaln", [128, 48]), ("kTs", [128, (SS // 128) * KEY]),
                            ("vTs", [128, (SS // 128) * VAL]),
                            ("opre", [128, 8 * self.T]), ("EpT", [128, 8 * self.T]),
                            ("Sst", [64, 16 * DV]), ("Wb", [64, 16 * DK])):
            self.dbg(name, shape)
        with self.nc.Block() as blk:
            self.blk = blk
            with ExitStack() as stkA:
                self.declare_phaseA(stkA)
                if sections >= 2:
                    self.declare_scan_ws(stkA)
                with ExitStack() as stkP0:
                    self.build_p0(stkP0)
                s = self.s_p0
                for ss in range(self.NSS):
                    with ExitStack() as ps1:
                        self.declare_p1_psum(ps1, "_%d" % ss)
                        s, _scv = self.build_p1_superslab(ss, s)
                    if sections >= 2:
                        with ExitStack() as ps2:
                            self.declare_scan_psum(ps2, "_%d" % ss)
                            s = self.build_scan_superslab(ss, s, s)
                self.dump("hT", self.hT[:, :, :].rearrange("p a b -> p (a b)"), s)
                self.dump("qkvs", self.qkvs[:, :, :].rearrange("p a b -> p (a b)"), s)
                self.dump("dec", self.dec[:, :], s)
                self.dump("dec2", self.dec2[:, :], s)
                self.dump("decT", self.decT[:, :, :].rearrange("p a b -> p (a b)"), s)
                self.dump("decT2", self.decT2[:, :, :].rearrange("p a b -> p (a b)"), s)
                self.dump("stats", self.stats[:, :], s)
                self.dump("statsm", self.statsm[32:33, :], s)
                self.dump("adaln", self.adaln[:, :, :].rearrange("p a b -> p (a b)"), s)
                self.dump("kTs", self.kTs[:, :, :].rearrange("p a b -> p (a b)"), s)
                self.dump("vTs", self.vTs[:, :, :].rearrange("p a b -> p (a b)"), s)
                if sections >= 2:
                    if "opre" in self.dbg_bufs or "EpT" in self.dbg_bufs:
                        s_dg = self.sem("s_dmpgate")
                        def f_dg(v):
                            s.wait(v)
                            for sp in self.s_osp + self.s_esp:
                                sp.wait(v)
                            s_dg.inc(v.nop(), 1)
                        self.V(f_dg)
                        self.dump("opre", bass.AP(self.d_opre, 0, [[8 * self.T, 128], [1, 8 * self.T]]), s_dg)
                        self.dump("EpT", bass.AP(self.d_EpT, 0, [[8 * self.T, 128], [1, 8 * self.T]]), s_dg)
                    nch = self.T // CH
                    self.dump("Sst", self.Sst[nch % 2][:, :, :].rearrange("p a b -> p (a b)"), s)
                    self.dump("Wb", self.Wb[nch % 2][:, :, :].rearrange("p a b -> p (a b)"), s)
            s_wqd = s_ag2 = None
            if sections >= 4:
                with ExitStack() as stkW:
                    s_wqd, s_ag2 = self.build_wq(stkW, s)
            if sections >= 3:
                for name, shape in (("ofin", [128, 8 * 256]), ("szs", [128, 8 * 256]),
                                    ("tf", [128, 8 * 256]), ("hgn", [128, 8 * 256]),
                                    ("r16", [16, 256]), ("hTe", [128, 8 * 256])):
                    self.dbg(name, shape)
                with ExitStack() as stkE:
                    self.declare_epi_ws(stkE)
                    self.declare_epi_psum(stkE, "_e")
                    s_wz = self.sem("s_wz")
                    def f_wzd(g):
                        s.wait(g)
                        if s_wqd is not None:
                            s_wqd.wait(g)
                        s_wz.inc(g.dma_start(self.wzT[:, :, :].rearrange("p a b -> p (a b)"),
                                             self.i_wzT[:, :]), 16)
                        s_wz.inc(g.dma_start(self.woutT[:, :, :].rearrange("p a b -> p (a b)"),
                                             self.i_woutT[:, :]), 16)
                    self.G_(f_wzd)
                    s_cc = self.build_ag2(s)
                    s_all = self.sem("s_epi_gate")
                    def f_eg(v):
                        s.wait(v)
                        s_cc.wait(v)
                        self.s_w.wait(v)
                        s_wz.wait(v)
                        for sp in self.s_osp + self.s_esp:
                            sp.wait(v)
                        if s_wqd is not None:
                            s_wqd.wait(v)
                        s_all.inc(v.nop(), 1)
                    self.V(f_eg)
                    s = s_all
                    for sl in range(self.T // 256):
                        s = self.build_epi_slab(sl, s)
                    s_ef = self.sem("s_epi_fin")
                    def f_ef(v):
                        s.wait(v)
                        for sp in self.s_xod:
                            sp.wait(v)
                        s_ef.inc(v.nop(), 1)
                    self.V(f_ef)
                    s = s_ef
            if sections >= 4:
                for name, shape in (("hf", [128, 8 * self.FS]), ("hq", [128, 8 * self.FS]),
                                    ("h2q", [128, self.NU * self.NCORE * self.ISHP]),
                                    ("h2qT", [128, self.NJ * self.FS])):
                    self.dbg(name, shape)
                with ExitStack() as stkF:
                    self.declare_ffn_ws(stkF)
                    self.declare_ffn_psum(stkF, "_f")
                    for u4 in range(self.NFS):
                        s = self.build_ffn_slab(u4, s, s_ag2)
                    for nm in ("hf", "hq", "h2q", "h2qT"):
                        if nm in self.dbg_bufs:
                            self.dump(nm, getattr(self, nm)[:, :, :].rearrange("p a b -> p (a b)"), s)
            if sections == 3 and self.dbg_bufs:
                for nm, tn in (("ofin", "ofin"), ("szs", "szs"), ("tf", "tf"), ("hgn", "hgn"), ("hTe", "hTe")):
                    if nm in self.dbg_bufs:
                        self.dump(nm, getattr(self, tn)[:, :, :].rearrange("p a b -> p (a b)"), s)
                if "r16" in self.dbg_bufs:
                    self.dump("r16", self.r16[:, :], s)
            if sections >= 4:
                def f_yfin(g):
                    s.wait(g)
                    self.s_yod.wait(g)
                self.G_(f_yfin)
            elif sections == 3:
                s_y = self.sem("s_y")
                def f_y3(g):
                    s.wait(g)
                    src = bass.AP(self.d_xp, 0, [[8 * self.T, 128], [1, 8 * self.T]])
                    s_y.inc(g.dma_start(self.o_y[:, :], src), 16)
                    s_y.wait(g)
                self.G_(f_y3)
        self.gstack.close()
        return self.nc


def prep_inputs(core, d, T, ncores=8):
    b, half = core // 2, core % 2
    x = np.asarray(d["L0_attn_hidden_states"][b], np.float32)
    t0 = half * T
    if t0 == 0:
        xs = np.concatenate([np.zeros((3, 1024), np.float32), x[0:T]], 0).T
    else:
        xs = x[t0 - 3: t0 + T].T
    TH = T + 3
    out = {}
    out["x"] = np.ascontiguousarray(xs.reshape(8, 128, TH).transpose(1, 0, 2)).reshape(128, 8 * TH)
    def packT(w):
        wT = np.ascontiguousarray(np.asarray(w, np.float32).T)
        n = wT.shape[1]
        return np.ascontiguousarray(wT.reshape(8, 128, n).transpose(1, 0, 2)).reshape(128, 8 * n)
    out["wqkvT"] = packT(d["L0_attn_W_qkv"])
    wab = np.zeros((64, 1024), np.float32)
    wab[0:16] = np.asarray(d["L0_attn_W_b"])
    wab[32:48] = np.asarray(d["L0_attn_W_a"])
    out["wabT"] = packT(wab)
    out["wzT"] = packT(d["L0_attn_W_z"])
    out["woutT"] = packT(d["L0_attn_W_out"])
    cw = np.asarray(d["L0_attn_conv_w"], np.float32)[:, 0, :]
    out["convw"] = np.ascontiguousarray(cw.reshape(16, 128, 4).transpose(1, 0, 2)).reshape(128, 64)
    wm = np.asarray(d["L0_wmod"], np.float32)
    out["wmod"] = np.ascontiguousarray(wm.reshape(8, 128, 6144).transpose(1, 0, 2)).reshape(128, 8 * 6144)
    out["crow"] = np.ascontiguousarray(np.asarray(d["L0_c"], np.float32)[b].reshape(8, 128).T)
    vec = np.zeros((16, 4), np.float32)
    vec[:, 0] = np.asarray(d["L0_attn_dt_bias"])
    vec[:, 1] = np.asarray(d["L0_attn_A_log"])
    out["vec"] = vec
    nwcol = np.tile(np.asarray(d["L0_attn_norm_weight"], np.float32), 2).reshape(128, 1)
    out["nw"] = np.concatenate([nwcol, np.full((128, 1), float(half), np.float32)], 1)
    r = np.arange(64)
    mA = (r[:, None] > r[None, :])
    mB = (r[:, None] < r[None, :])
    mI = (r[:, None] <= r[None, :])
    bigA = (1e30 * (1 - mA)).astype(np.float32)
    bigB = (1e30 * (1 - mB)).astype(np.float32)
    bigO = (1e30 * (1 - mI) + np.log(8.0)).astype(np.float32)
    m1 = np.concatenate([bigA, bigB, bigO], 1)
    out["masks"] = np.ascontiguousarray(np.concatenate([m1, m1], 0))
    out["eyed"] = np.ascontiguousarray(np.concatenate([np.eye(64), np.eye(64)], 0)).astype(ml_dtypes.bfloat16)
    cv = np.zeros((128, 8), np.float32); cv[:, 0] = 1e-5; cv[:, 1] = 1e-6
    cv[:, 2] = 1.0 if half == 1 else 0.0
    cv[:, 4] = 24.0
    cv[:, 5] = np.log(24.0)
    out["cvec"] = cv
    ISH = 2816 // ncores
    NR = (ISH + 127) // 128
    ISHP = NR * 128
    wg = np.asarray(d["L0_ffn_w_gate"], np.float32)
    wu = np.asarray(d["L0_ffn_w_up"], np.float32)
    wd_ = np.asarray(d["L0_ffn_w_down"], np.float32)
    gsh = np.zeros((ISHP, 1024), np.float32); gsh[0:ISH] = wg[ISH * core: ISH * (core + 1)]
    out["wg_sh"] = np.ascontiguousarray(gsh.reshape(NR, 128, 1024).transpose(1, 0, 2)).reshape(128, NR * 1024)
    ush = np.zeros((ISHP, 1024), np.float32); ush[0:ISH] = wu[ISH * core: ISH * (core + 1)]
    out["wu_sh"] = np.ascontiguousarray(ush.reshape(NR, 128, 1024).transpose(1, 0, 2)).reshape(128, NR * 1024)
    dsh = np.zeros((1024, ISHP), np.float32); dsh[:, 0:ISH] = wd_[:, ISH * core: ISH * (core + 1)]
    out["wd_sh"] = np.ascontiguousarray(dsh.reshape(8, 128, ISHP).transpose(1, 0, 2)).reshape(128, 8 * ISHP)
    gnsel = np.zeros((128, 8, 16), np.float32)
    for h in range(16):
        gnsel[64 * (h % 2):64 * (h % 2) + 64, h // 2, h] = 1.0
    out["gnsel"] = np.ascontiguousarray(gnsel.reshape(128, 128))
    out["gnselT"] = np.ascontiguousarray(gnsel.transpose(2, 1, 0).reshape(16, 1024))
    out["eyeb"] = np.eye(128).astype(ml_dtypes.bfloat16)
    out["eyef"] = np.eye(128, dtype=np.float32)
    out["ones"] = np.ones((128, 129), np.float32)
    return out


def kernel(**inputs):
    T = 4096
    kb = KB(T=T, SS=256, ncores=8)
    nc = kb.build_section1(sections=4)
    in_maps = [prep_inputs(c, inputs, T, ncores=8) for c in range(8)]
    from concourse.bass_utils import run_bass_kernel_spmd
    res = run_bass_kernel_spmd(nc, in_maps, core_ids=list(range(8)))
    out = np.zeros((4, 8192, 1024), np.float32)
    for c in range(8):
        b, half = c // 2, c % 2
        y = np.asarray(res.results[c]["y"]).reshape(128, 8, T)
        out[b, half * T:(half + 1) * T] = y.transpose(2, 1, 0).reshape(T, 1024)
    return out


# version 04
# speedup vs baseline: 1.1851x; 1.1851x over previous
import numpy as np
import ml_dtypes
from contextlib import ExitStack

import concourse.bass as bass
import concourse.mybir as mybir

F32 = mybir.dt.float32
BF16 = mybir.dt.bfloat16
ALU = mybir.AluOpType
ACT = mybir.ActivationFunctionType

NKH, NVH, DK, DV = 8, 16, 64, 64
KEY, VAL, CDIM, KSZ = 512, 1024, 2048, 4
CH = 64


def ts(i, n):
    return slice(i * n, (i + 1) * n)


def register_custom_ops():
    return


class Sem:
    def __init__(self, nc, stack, name):
        self.h = stack.enter_context(nc.semaphore(name))
        self.val = 0
    def inc(self, instr, n=1):
        self.val += n
        instr.then_inc(self.h, n)
        return instr
    def wait(self, eng, val=None):
        eng.wait_ge(self.h, self.val if val is None else val)


class KB:
    def __init__(self, T=2048, SS=256, debug=(), ncores=8):
        register_custom_ops()
        assert T % SS == 0 and SS % CH == 0 and SS % 128 == 0
        self.T, self.SS = T, SS
        self.TH = T + 3
        self.SSH = SS + 3
        self.NSS = T // SS
        self.NCH = T // CH
        self.NCORE = ncores
        self.ISH = 2816 // ncores
        self.NR = (self.ISH + 127) // 128
        self.ISHP = self.NR * 128
        self.NJ = (ncores * self.ISHP) // 128
        self.FS = min(512, T)
        self.NFS = T // self.FS
        self.NU = self.FS // 128
        self.SCROFF = 544
        self.debug = set(debug)
        self.nc = bass.Bass("TRN2", target_bir_lowering=False)
        self.gstack = ExitStack()
        self.dbg_bufs = {}

    def inp(self, name, shape, dtype=F32):
        return self.nc.dram_tensor(name, shape, dtype, kind="ExternalInput")
    def outp(self, name, shape, dtype=F32):
        return self.nc.dram_tensor(name, shape, dtype, kind="ExternalOutput")
    def sb(self, stack, name, shape, dtype):
        return stack.enter_context(self.nc.sbuf_tensor(name, shape, dtype))
    def ps(self, stack, name, shape, dtype=F32):
        return stack.enter_context(self.nc.psum_tensor(name, shape, dtype))
    def sem(self, name):
        if not hasattr(self, "_semcache"):
            self._semcache = {}
        if name not in self._semcache:
            self._semcache[name] = Sem(self.nc, self.gstack, name)
        return self._semcache[name]
    def V(self, fn): self.blk.vector(fn)
    def A(self, fn): self.blk.scalar(fn)
    def P(self, fn): self.blk.tensor(fn)
    def G_(self, fn): self.blk.gpsimd(fn)
    def S_(self, fn): self.blk.sync(fn)

    def dbg(self, name, shape, dtype=F32):
        if name in self.debug:
            self.dbg_bufs[name] = self.outp("dbg_" + name, shape, dtype)

    def dump(self, name, src_ap, s_after):
        if name not in self.dbg_bufs:
            return
        h = self.dbg_bufs[name]
        s_d = self.sem("s_dbg_" + name)
        def f(g):
            s_after.wait(g)
            s_d.inc(g.dma_start(h[:, :], src_ap), 16)
            s_d.wait(g)
        self.G_(f)

    def declare_global(self):
        T, TH, SS, SSH = self.T, self.TH, self.SS, self.SSH
        gs = self.gstack
        self.i_x = self.inp("x", [128, 8 * TH])
        self.i_wqkvT = self.inp("wqkvT", [128, 8 * 2048])
        self.i_wabT = self.inp("wabT", [128, 8 * 64])
        self.i_wzT = self.inp("wzT", [128, 8 * 1024])
        self.i_woutT = self.inp("woutT", [128, 8 * 1024])
        self.i_convw = self.inp("convw", [128, 16 * 4])
        self.i_wmod = self.inp("wmod", [128, 8 * 6144])
        self.i_crow = self.inp("crow", [128, 8])
        self.i_vec = self.inp("vec", [16, 4])
        self.i_nw = self.inp("nw", [128, 2])
        self.i_masks = self.inp("masks", [128, 3 * 64])
        self.i_eyed = self.inp("eyed", [128, 64], BF16)
        self.i_eyeb = self.inp("eyeb", [128, 128], BF16)
        self.i_eyef = self.inp("eyef", [128, 128])
        self.i_ones = self.inp("ones", [128, 129])
        self.i_cvec = self.inp("cvec", [128, 8])
        self.i_gnsel = self.inp("gnsel", [128, 8 * 16])
        self.i_gnselT = self.inp("gnselT", [16, 8 * 128])
        self.i_wg = self.inp("wg_sh", [128, self.NR * 1024])
        self.i_wu = self.inp("wu_sh", [128, self.NR * 1024])
        self.i_wd = self.inp("wd_sh", [128, 8 * self.ISHP])
        self.o_y = self.outp("y", [128, 8 * T])
        self.dbc = self.nc.dram_tensor("dbc", [8192], F32)
        self.cc_in = self.nc.dram_tensor("cc_in", [64, 1024], F32)
        self.cc_out = self.nc.dram_tensor("cc_out", [128, 1024], F32)
        self.d_xp = self.nc.dram_tensor("d_xp", [128, 8 * T], F32)
        self.d_opre = self.nc.dram_tensor("d_opre", [128, 8 * T], BF16)
        self.d_EpT = self.nc.dram_tensor("d_EpT", [128, 8 * T], BF16)
        PB = self.ISHP
        self.cc_g_in = self.nc.dram_tensor("cc_g_in", [1024, PB], BF16)
        self.cc_u_in = self.nc.dram_tensor("cc_u_in", [1024, PB], BF16)
        self.cc_d_in = self.nc.dram_tensor("cc_d_in", [PB, 1024], BF16)
        self.g_gT = self.nc.dram_tensor("g_gT", [self.NCORE * 1024, PB], BF16)
        self.g_uT = self.nc.dram_tensor("g_uT", [self.NCORE * 1024, PB], BF16)
        self.g_dT = self.nc.dram_tensor("g_dT", [self.NCORE * PB, 1024], BF16)
        self.Sst = [self.sb(gs, "g_S0", [64, 16, DV], F32),
                    self.sb(gs, "g_S1", [64, 16, DV], F32)]
        self.Wb = [self.sb(gs, "g_W0", [64, 16, DK], BF16),
                   self.sb(gs, "g_W1", [64, 16, DK], BF16)]
        self.statpack = self.sb(gs, "g_stats", [33, TH], F32)
        self.stats = self.statpack
        self.statsm = self.statpack
        self.convw = self.sb(gs, "g_convw", [128, 16, 4], F32)
        self.crow = self.sb(gs, "g_crow", [128, 8], F32)
        self.vecs = self.sb(gs, "g_vecs", [16, 4], F32)
        self.nw = self.sb(gs, "g_nw", [128, 2], F32)
        self.masks = self.sb(gs, "g_masks", [128, 3 * 64], F32)
        self.eyed = self.sb(gs, "g_eyed", [128, 64], BF16)
        self.eyeb = self.sb(gs, "g_eyeb", [128, 128], BF16)
        self.eyef = self.sb(gs, "g_eyef", [128, 128], F32)
        self.onesf = self.sb(gs, "g_onesf", [128, 129], F32)
        self.cvec = self.sb(gs, "g_cvec", [128, 8], F32)
        self.gnsel = self.sb(gs, "g_gnsel", [128, 8, 16], F32)
        self.gnselT = self.sb(gs, "g_gnselT", [16, 8, 128], F32)
        self.adaln = self.sb(gs, "g_adaln", [128, 8, 6], F32)
        self.negea = self.sb(gs, "g_negea", [16, 1], F32)
        self.scanmask = self.sb(gs, "g_scanmask", [16, SS], F32)
        self.scr = self.sb(gs, "g_scr", [128, 1088], F32)
        self.scrb = self.sb(gs, "g_scrb", [128, 576], BF16)

    def declare_phaseA(self, stk):
        SS, SSH = self.SS, self.SSH
        self.xs = self.sb(stk, "a_xs", [128, 8, SSH], F32)
        self.wqb = [self.sb(stk, "a_wq0", [128, 8, 128], BF16),
                    self.sb(stk, "a_wq1", [128, 8, 128], BF16)]
        self.wabT = self.sb(stk, "a_wabT", [128, 8, 64], BF16)
        self.hT = self.sb(stk, "a_hT", [128, 8, SSH], BF16)
        self.mixed = self.sb(stk, "a_mixed", [128, 16, SSH], F32)
        self.cv1 = self.sb(stk, "a_cv1", [128, SS], F32)
        self.cv2 = self.sb(stk, "a_cv2", [128, SS], F32)
        self.qkvs = self.sb(stk, "a_qkvs", [128, 16, SS], BF16)
        self.kTs = self.sb(stk, "a_kTs", [128, SS // 128, KEY], BF16)
        self.vTs = self.sb(stk, "a_vTs", [128, SS // 128, VAL], BF16)
        self.dec = self.sb(stk, "a_dec", [128, SS], F32)
        self.dec2 = self.sb(stk, "a_dec2", [96, SS], F32)
        self.decT = self.sb(stk, "a_decT", [128, SS // 128, 128], F32)
        self.decT2 = self.sb(stk, "a_decT2", [128, SS // 128, 96], F32)
        self.sA = self.sb(stk, "a_sA", [16, SS], F32)
        self.sB = self.sb(stk, "a_sB", [16, SS], F32)
        self.sC = self.sb(stk, "a_sC", [16, SS], F32)
        self.r_Gp = self.dec[0:16, :]
        self.r_G = self.dec[32:48, :]
        self.r_lamq = self.dec[64:80, :]
        self.r_blam = self.dec[96:112, :]
        self.r_bet = self.dec2[0:16, :]
        self.r_erev = self.dec2[32:48, :]
        self.r_lamr = self.dec2[64:80, :]

    def build_p0(self, stk):
        s_in = self.s_in = self.sem("s_in")
        s_w = self.s_w = self.sem("s_w")
        s_msk = self.s_msk = self.sem("s_msk")
        s_p0 = self.s_p0 = self.sem("s_p0")
        wmbuf = self.sb(stk, "p0_wmod", [128, 6144], BF16)
        modacc = self.sb(stk, "p0_modacc", [128, 48], F32)
        self.pR1 = self.ps(stk, "p_p0mod", [128, 512])

        def f_sync(s):
            for dst, src in ((self.convw, self.i_convw),
                             (self.vecs, self.i_vec), (self.nw, self.i_nw),
                             (self.masks, self.i_masks), (self.eyed, self.i_eyed), (self.eyeb, self.i_eyeb),
                             (self.eyef, self.i_eyef), (self.onesf, self.i_ones),
                             (self.cvec, self.i_cvec), (self.gnsel, self.i_gnsel), (self.gnselT, self.i_gnselT),
                             (self.crow, self.i_crow)):
                d = dst[:, :, :].rearrange("p a b -> p (a b)") if len(dst.shape) == 3 else dst[:, :]
                s_in.inc(s.dma_start(d, src[:, :]), 16)
        self.S_(f_sync)

        def f_g(g):
            s_w.inc(g.dma_start(self.wabT[:, :, :].rearrange("p a b -> p (a b)"),
                                self.i_wabT[:, :]), 16)
            g.memset(self.scanmask[:, :], 1.0)
            g.drain()
            g.memset(bass.AP(self.scanmask, 0, [[self.SS, 16], [CH, self.SS // CH]]), 0.0)
            g.drain()
            g.memset(self.dec[:, :], 0.0)
            g.drain()
            g.memset(self.dec2[:, :], 0.0)
            g.drain()
            g.memset(self.Sst[0][:, :, :].rearrange("p a b -> p (a b)"), 0.0)
            g.drain()
            s_msk.inc(g.nop(), 1)
        self.G_(f_g)

        def f_wi(v):
            s_in.wait(v)
            for j in range(16):
                v.tensor_copy(self.Wb[0][:, j, :], self.eyed[0:64, :])
            v.drain()
            s_msk.inc(v.nop(), 1)
        self.V(f_wi)
        def f_a(a):
            s_in.wait(a)
            a.activation(self.scr[:, 0:8], self.crow[:, :], ACT.Silu)
            a.drain()
            s_p0.inc(a.nop(), 1)
        self.A(f_a)
        def f_v(v):
            s_p0.wait(v)
            v.tensor_copy(self.scrb[:, 0:8], self.scr[:, 0:8])
            v.drain()
            s_p0.inc(v.nop(), 1)
        self.V(f_v)

        s_wmd = self.sem("s_wmd")
        s_wmc = self.sem("s_wmc")
        for k in range(8):
            def f_gk(g, k=k):
                if k > 0:
                    g.wait_ge(s_wmc.h, 2 * k - 1)
                s_wmd.inc(g.dma_start(wmbuf[:, :], self.i_wmod[:, ts(k, 6144)]), 16)
            self.G_(f_gk)
            def f_tk(t, k=k):
                s_wmd.wait(t)
                s_p0.wait(t, 2)
                if k > 0:
                    t.wait_ge(s_wmc.h, 2 * k)
                for j in range(48):
                    t.matmul(self.pR1[:, j:j + 1], wmbuf[:, ts(j, 128)], self.scrb[:, k:k + 1],
                             start=True, stop=True)
                t.drain()
                s_wmc.inc(t.nop(), 1)
            self.P(f_tk)
            def f_vk(v, k=k):
                s_wmc.wait(v)
                if k == 0:
                    v.tensor_copy(modacc[:, :], self.pR1[:, 0:48])
                else:
                    v.tensor_tensor(modacc[:, :], modacc[:, :], self.pR1[:, 0:48], ALU.add)
                v.drain()
                s_wmc.inc(v.nop(), 1)
            self.V(f_vk)
        s_wm = s_wmc
        def f_v2(v):
            s_wm.wait(v)
            v.tensor_copy(self.adaln[:, :, :].transpose([0, 2, 1]), modacc[:, :])
            v.drain()
            v.tensor_scalar(self.adaln[:, :, 1], self.adaln[:, :, 1], 1.0, None, ALU.add)
            v.tensor_scalar(self.adaln[:, :, 4], self.adaln[:, :, 4], 1.0, None, ALU.add)
            v.drain()
            s_p0.inc(v.nop(), 1)
        self.V(f_v2)
        def f_a2(a):
            s_p0.wait(a)
            a.activation(self.scr[:, 0:8], self.adaln[:, :, 2], ACT.Abs)
            a.activation(self.scr[:, 8:16], self.adaln[:, :, 5], ACT.Abs)
            a.activation(self.scr[:, 16:24], self.adaln[:, :, 2], ACT.Sign)
            a.activation(self.scr[:, 24:32], self.adaln[:, :, 5], ACT.Sign)
            a.activation(self.negea[:, 0:1], self.vecs[:, 1:2], ACT.Exp)
            a.drain()
            s_p0.inc(a.nop(), 1)
        self.A(f_a2)
        def f_v3(v):
            s_p0.wait(v)
            v.tensor_scalar(self.scr[:, 0:16], self.scr[:, 0:16], 0.25, None, ALU.max)
            v.tensor_scalar(self.negea[:, 0:1], self.negea[:, 0:1], -1.0, None, ALU.mult)
            v.drain()
            v.tensor_tensor(self.adaln[:, :, 2], self.scr[:, 0:8], self.scr[:, 16:24], ALU.mult)
            v.tensor_tensor(self.adaln[:, :, 5], self.scr[:, 8:16], self.scr[:, 24:32], ALU.mult)
            v.drain()
            s_p0.inc(v.nop(), 1)
        self.V(f_v3)

    def ln_apply(self, which, w0, n, s_pre, sq_scratch, stats_r, stats_m, dst, halo_mask,
                 xsrc, x0):
        ai_s, ai_c = (0, 1) if which == 0 else (3, 4)
        s_loc = self.sem("s_ln%d" % which)
        def f_sq(a):
            s_pre.wait(a)
            for d in range(8):
                a.activation(sq_scratch[:, d, 0:n], xsrc[:, d, x0:x0 + n], ACT.Square)
            a.drain()
            s_loc.inc(a.nop(), 1)
        self.A(f_sq)
        def f_mm(t):
            s_loc.wait(t)
            s_pre.wait(t)
            for d in range(8):
                t.matmul(self.pS[0:1, 0:n], self.onesf[:, 0:1], xsrc[:, d, x0:x0 + n],
                         start=(d == 0), stop=(d == 7))
            for d in range(8):
                t.matmul(self.pQ[0:1, 0:n], self.onesf[:, 0:1], sq_scratch[:, d, 0:n],
                         start=(d == 0), stop=(d == 7))
            t.drain()
            s_loc.inc(t.nop(), 1)
        self.P(f_mm)
        def f_vst(v):
            s_loc.wait(v)
            v.tensor_scalar(self.scr[0:1, 0:n], self.pS[0:1, 0:n], 1.0 / 1024.0, None, ALU.mult)
            v.drain()
            v.tensor_tensor(self.scr[0:1, 0:n], self.scr[0:1, 0:n], self.scr[0:1, 0:n], ALU.mult)
            v.drain()
            v.scalar_tensor_tensor(self.scr[0:1, self.SCROFF:self.SCROFF + n], self.pQ[0:1, 0:n], 1.0 / 1024.0,
                                   self.scr[0:1, 0:n], ALU.mult, ALU.subtract)
            v.drain()
            s_loc.inc(v.nop(), 1)
        self.V(f_vst)
        def f_ast(a):
            s_loc.wait(a)
            a.activation(self.scr[0:1, 0:n], self.scr[0:1, self.SCROFF:self.SCROFF + n], ACT.Ln, bias=self.cvec[0:1, 0:1])
            a.drain()
            a.activation(stats_r[0:1, w0:w0 + n], self.scr[0:1, 0:n], ACT.Exp, scale=-0.5)
            a.drain()
            s_loc.inc(a.nop(), 1)
        self.A(f_ast)
        def f_vst2(v):
            s_loc.wait(v)
            v.scalar_tensor_tensor(stats_m[32:33, w0:w0 + n], self.pS[0:1, 0:n], 1.0 / 1024.0,
                                   stats_r[0:1, w0:w0 + n], ALU.mult, ALU.mult)
            v.drain()
            s_loc.inc(v.nop(), 1)
        self.V(f_vst2)
        self.ln_rep_apply(which, w0, n, s_loc, stats_r, stats_m, dst, halo_mask, xsrc, x0)
        return s_loc

    def ln_rep_apply(self, which, w0, n, s_loc, stats_r, stats_m, dst, halo_mask, xsrc, x0):
        ai_s, ai_c = (0, 1) if which == 0 else (3, 4)
        def f_rep(t):
            s_loc.wait(t)
            t.matmul(self.pR1[:, 0:n], self.onesf[0:1, 0:128], stats_r[0:1, w0:w0 + n],
                     start=True, stop=True)
            t.drain()
            t.matmul(self.pR2[:, 0:n], self.onesf[32:33, 0:128], stats_m[32:33, w0:w0 + n],
                     start=True, stop=True)
            t.drain()
            s_loc.inc(t.nop(), 1)
        self.P(f_rep)
        def f_h(v):
            s_loc.wait(v)
            for d in range(8):
                v.tensor_tensor(self.scr[:, 0:n], xsrc[:, d, x0:x0 + n], self.pR1[:, 0:n], ALU.mult)
                v.drain()
                v.tensor_tensor(self.scr[:, self.SCROFF:self.SCROFF + n], self.scr[:, 0:n], self.pR2[:, 0:n], ALU.subtract)
                v.drain()
                v.tensor_scalar(dst[:, d, 0:n], self.scr[:, self.SCROFF:self.SCROFF + n],
                                self.adaln[:, d, ai_c:ai_c + 1], self.adaln[:, d, ai_s:ai_s + 1],
                                ALU.mult, ALU.add)
                v.drain()
            if halo_mask:
                for d in range(8):
                    v.tensor_scalar(dst[:, d, 0:3], dst[:, d, 0:3], self.nw[:, 1:2], None, ALU.mult)
                v.drain()
            s_loc.inc(v.nop(), 1)
        self.V(f_h)

    def build_p1_superslab(self, ss, s_pre):
        T, SS, SSH = self.T, self.SS, self.SSH
        w0 = ss * SS
        s_xs = self.sem("s_xs")
        def f_xd(g):
            s_pre.wait(g)
            ap_x = bass.AP(self.i_x, w0, [[8 * self.TH, 128], [self.TH, 8], [1, SSH]])
            s_xs.inc(g.dma_start(self.xs[:, :, :], ap_x), 16)
        self.G_(f_xd)
        s_g0 = self.sem("s_lng")
        def f_xg(v):
            s_pre.wait(v)
            s_xs.wait(v)
            s_g0.inc(v.nop(), 1)
        self.V(f_xg)
        s_ln = self.ln_apply(0, w0, SSH, s_g0, self.mixed, self.stats, self.statsm, self.hT,
                             halo_mask=(ss == 0), xsrc=self.xs, x0=0)
        s_qkv = self.sem("s_qkv")
        s_cv = self.sem("s_cv")
        s_wqp = [self.sem("s_wqa"), self.sem("s_wqb")]
        q_base, cv_base = s_qkv.val, s_cv.val
        for chn in range(16):
            pp = self.pA if chn % 2 == 0 else self.pB
            wb = self.wqb[chn % 2]
            swq = s_wqp[chn % 2]
            def f_wl(g, chn=chn, wb=wb, swq=swq):
                if chn >= 2:
                    g.wait_ge(s_qkv.h, q_base + chn - 1)
                swq.inc(g.dma_start(wb[:, :, :].rearrange("p a b -> p (a b)"),
                                    self.i_wqkvT[:, :].rearrange("p (a b) -> p a b", b=2048)[:, :, ts(chn, 128)]),
                        16)
            self.G_(f_wl)
            def f_mm(t, chn=chn, pp=pp, wb=wb, swq=swq):
                s_ln.wait(t)
                swq.wait(t)
                if chn >= 2:
                    s_cv.wait(t, cv_base + (chn - 2) * 2 + 1)
                for k in range(8):
                    t.matmul(pp[:, 0:SSH], wb[:, k, :], self.hT[:, k, :],
                             start=(k == 0), stop=(k == 7))
                t.drain()
                s_qkv.inc(t.nop(), 1)
            self.P(f_mm)
            def f_cv(v, chn=chn, pp=pp):
                s_qkv.wait(v)
                s_cv.wait(v)
                cw = self.convw
                v.tensor_scalar(self.cv1[:, 0:SS], pp[:, 0:SS],
                                cw[:, chn, 0:1], None, ALU.mult)
                v.drain()
                v.scalar_tensor_tensor(self.cv2[:, 0:SS], pp[:, 1:1 + SS],
                                       cw[:, chn, 1:2], self.cv1[:, 0:SS], ALU.mult, ALU.add)
                v.drain()
                v.scalar_tensor_tensor(self.cv1[:, 0:SS], pp[:, 2:2 + SS],
                                       cw[:, chn, 2:3], self.cv2[:, 0:SS], ALU.mult, ALU.add)
                v.drain()
                v.scalar_tensor_tensor(self.cv2[:, 0:SS], pp[:, 3:3 + SS],
                                       cw[:, chn, 3:4], self.cv1[:, 0:SS], ALU.mult, ALU.add)
                v.drain()
                s_cv.inc(v.nop(), 1)
            self.V(f_cv)
            def f_si(a, chn=chn):
                s_cv.wait(a)
                a.activation(self.qkvs[:, chn, :], self.cv2[:, 0:SS], ACT.Silu)
                a.drain()
                s_cv.inc(a.nop(), 1)
            self.A(f_si)
        s_ab = self.sem("s_ab")
        def f_ab(t):
            s_ln.wait(t)
            self.s_w.wait(t)
            for k in range(8):
                t.matmul(self.pR1[0:64, 0:SS], self.wabT[:, k, :], self.hT[:, k, 3:3 + SS],
                         start=(k == 0), stop=(k == 7))
            t.drain()
            s_ab.inc(t.nop(), 1)
        self.P(f_ab)
        def f_d0(v):
            s_ab.wait(v)
            v.tensor_scalar(self.sA[:, 0:SS], self.pR1[32:48, 0:SS], self.vecs[:, 0:1], None, ALU.add)
            v.drain()
            s_ab.inc(v.nop(), 1)
        self.V(f_d0)
        def f_d1(a):
            s_ab.wait(a)
            a.activation(self.r_bet[:, :], self.pR1[0:16, 0:SS], ACT.Sigmoid)
            a.drain()
            a.activation(self.sB[:, 0:SS], self.sA[:, 0:SS], ACT.Abs)
            a.drain()
            a.activation(self.sB[:, 0:SS], self.sB[:, 0:SS], ACT.Exp, scale=-1.0)
            a.drain()
            s_ab.inc(a.nop(), 1)
        self.A(f_d1)
        def f_d2(v):
            s_ab.wait(v)
            v.tensor_scalar(self.sB[:, 0:SS], self.sB[:, 0:SS], 1.0, None, ALU.add)
            v.drain()
            s_ab.inc(v.nop(), 1)
        self.V(f_d2)
        def f_d3(a):
            s_ab.wait(a)
            a.activation(self.sB[:, 0:SS], self.sB[:, 0:SS], ACT.Ln)
            a.drain()
            s_ab.inc(a.nop(), 1)
        self.A(f_d3)
        def f_d4(v):
            s_ab.wait(v)
            v.tensor_scalar(self.sC[:, 0:SS], self.sA[:, 0:SS], 0.0, None, ALU.max)
            v.drain()
            v.tensor_tensor(self.sC[:, 0:SS], self.sC[:, 0:SS], self.sB[:, 0:SS], ALU.add)
            v.drain()
            v.tensor_scalar(self.sC[:, 0:SS], self.sC[:, 0:SS], self.negea[:, 0:1], None, ALU.mult)
            v.drain()
            s_ab.inc(v.nop(), 1)
        self.V(f_d4)
        def f_d5(a):
            s_ab.wait(a)
            a.activation(self.sA[:, 0:SS], self.pR1[0:16, 0:SS], ACT.Abs)
            a.drain()
            a.activation(self.sA[:, 0:SS], self.sA[:, 0:SS], ACT.Exp, scale=-1.0)
            a.drain()
            s_ab.inc(a.nop(), 1)
        self.A(f_d5)
        def f_d6(v):
            s_ab.wait(v)
            v.tensor_scalar(self.sA[:, 0:SS], self.sA[:, 0:SS], 1.0, None, ALU.add)
            v.drain()
            s_ab.inc(v.nop(), 1)
        self.V(f_d6)
        def f_d7(a):
            s_ab.wait(a)
            a.activation(self.sA[:, 0:SS], self.sA[:, 0:SS], ACT.Ln)
            a.drain()
            s_ab.inc(a.nop(), 1)
        self.A(f_d7)
        def f_d8(v):
            s_ab.wait(v)
            self.s_msk.wait(v)
            v.tensor_scalar(self.sB[:, 0:SS], self.pR1[0:16, 0:SS], -1.0, 0.0, ALU.mult, ALU.max)
            v.drain()
            v.scalar_tensor_tensor(self.sB[:, 0:SS], self.sB[:, 0:SS], -1.0, self.sA[:, 0:SS],
                                   ALU.mult, ALU.subtract)
            v.drain()
            v.tensor_tensor_scan(self.sA[:, 0:SS], self.scanmask[:, :], self.sC[:, 0:SS],
                                 0.0, ALU.mult, ALU.add)
            v.drain()
            v.tensor_copy(self.r_G[:, :], self.sA[:, 0:SS])
            v.drain()
            v.tensor_tensor(self.r_Gp[:, :], self.sA[:, 0:SS], self.sB[:, 0:SS], ALU.add)
            v.drain()
            s_ab.inc(v.nop(), 1)
        self.V(f_d8)
        def f_d9(a):
            s_ab.wait(a)
            a.activation(self.sC[:, 0:SS], self.sA[:, 0:SS], ACT.Exp)
            a.drain()
            s_ab.inc(a.nop(), 1)
        self.A(f_d9)
        def f_d10(v):
            s_ab.wait(v)
            v.tensor_copy(self.r_lamr[:, :], self.sC[:, 0:SS])
            v.drain()
            v.tensor_tensor(self.r_blam[:, :], self.r_bet[:, :], self.sC[:, 0:SS], ALU.mult)
            v.drain()
            v.tensor_scalar(self.r_lamq[:, :], self.sC[:, 0:SS], 0.125, None, ALU.mult)
            v.drain()
            gend = bass.AP(self.dec, 32 * SS + CH - 1, [[SS, 16], [CH, SS // CH], [0, CH]])
            v.tensor_tensor(self.sA[:, 0:SS].rearrange("p (c u) -> p c u", u=CH), gend,
                            self.r_G[:, :].rearrange("p (c u) -> p c u", u=CH), ALU.subtract)
            v.drain()
            s_ab.inc(v.nop(), 1)
        self.V(f_d10)
        def f_d11(a):
            s_ab.wait(a)
            a.activation(self.r_erev[:, :], self.sA[:, 0:SS], ACT.Exp)
            a.drain()
            s_ab.inc(a.nop(), 1)
        self.A(f_d11)
        s_tr = self.sem("s_tr")
        ntt = SS // 128
        for tt in range(ntt):
            def f_tp1(t, tt=tt):
                s_ab.wait(t)
                self.s_msk.wait(t)
                if s_tr.val:
                    s_tr.wait(t)
                t.transpose(self.pT[:, 0:128], self.dec[:, tt * 128:(tt + 1) * 128],
                            self.eyef[:, :])
                t.drain()
                s_tr.inc(t.nop(), 1)
            self.P(f_tp1)
            def f_te1(a, tt=tt):
                s_tr.wait(a)
                a.activation(self.decT[:, tt, :], self.pT[:, 0:128], ACT.Copy)
                a.drain()
                s_tr.inc(a.nop(), 1)
            self.A(f_te1)
            def f_tp1b(t, tt=tt):
                s_tr.wait(t)
                t.transpose(self.pT[:, 0:96], self.dec2[:, tt * 128:(tt + 1) * 128],
                            self.eyef[0:96, 0:96])
                t.drain()
                s_tr.inc(t.nop(), 1)
            self.P(f_tp1b)
            def f_te1b(a, tt=tt):
                s_tr.wait(a)
                a.activation(self.decT2[:, tt, :], self.pT[:, 0:96], ACT.Copy)
                a.drain()
                s_tr.inc(a.nop(), 1)
            self.A(f_te1b)
        for tt in range(ntt):
            for rr0, nrr in ((0, 4), (4, 4), (8, 4)):
                def f_tp2(t, rr0=rr0, nrr=nrr, tt=tt):
                    s_cv.wait(t)
                    s_tr.wait(t)
                    for i in range(nrr):
                        t.transpose(self.pTb[:, 128 * i:128 * i + 128],
                                    self.qkvs[:, 4 + rr0 + i, tt * 128:(tt + 1) * 128],
                                    self.eyeb[:, :])
                    t.drain()
                    s_tr.inc(t.nop(), 1)
                self.P(f_tp2)
                def f_te2(v, rr0=rr0, nrr=nrr, tt=tt):
                    s_tr.wait(v)
                    if rr0 == 0:
                        dst = self.kTs[:, tt, 0:512]
                    elif rr0 == 4:
                        dst = self.vTs[:, tt, 0:512]
                    else:
                        dst = self.vTs[:, tt, 512:1024]
                    v.tensor_copy(dst, self.pTb[:, 0:128 * nrr])
                    v.drain()
                    s_tr.inc(v.nop(), 1)
                self.V(f_te2)
        s_done = self.sem("s_ss")
        def f_fin(v):
            s_tr.wait(v)
            s_cv.wait(v)
            s_ab.wait(v)
            s_done.inc(v.nop(), 1)
        self.V(f_fin)
        return s_done, s_cv


    def declare_scan_ws(self, stk):
        self.Grep = self.sb(stk, "c_Grep", [128, 1024], F32)
        self.Gprep = self.sb(stk, "c_Gprep", [128, 1024], F32)
        self.lamrep = self.sb(stk, "c_lamrep", [128, 2048], F32)
        self.lamCrep = self.sb(stk, "c_lamCrep", [64, 32], F32)
        self.Dw = self.sb(stk, "c_Dw", [128, 1024], F32)
        self.Aw = self.sb(stk, "c_Aw", [128, 1024], BF16)
        self.Bw = self.sb(stk, "c_Bw", [128, 1024], BF16)
        self.QKDT = self.sb(stk, "c_QKDT", [128, 1024], BF16)
        self.Pf = self.sb(stk, "c_Pf", [128, 1024], BF16)
        self.PfT = self.sb(stk, "c_PfT", [128, 1024], BF16)
        self.Pg = self.sb(stk, "c_Pg", [128, 1024], BF16)
        self.PgT = self.sb(stk, "c_PgT", [128, 1024], BF16)
        self.IPb = self.sb(stk, "c_IP", [128, 1024], BF16)
        self.Yc = self.sb(stk, "c_Yc", [128, 1024], BF16)
        self.Yn = self.sb(stk, "c_Yn", [128, 1024], BF16)
        self.Rw = self.sb(stk, "c_Rw", [128, 2048], BF16)
        self.Qt = self.sb(stk, "c_Qt", [64, 1024], F32)
        self.KbT = self.sb(stk, "c_KbT", [128, 1024], BF16)
        self.Uvp = self.sb(stk, "c_Uvp", [128, 1024], BF16)
        self.Ukp = self.sb(stk, "c_Ukp", [128, 1024], BF16)
        self.nUkT = self.sb(stk, "c_nUkT", [64, 1024], F32)
        self.nMT = self.sb(stk, "c_nMT", [64, 1024], BF16)
        self.Et = self.sb(stk, "c_Et", [64, 1024], BF16)
        self.Ssc = self.sb(stk, "c_Ssc", [64, 1024], F32)
        self.Wsc = self.sb(stk, "c_Wsc", [64, 1024], BF16)
        self.Ucp = self.sb(stk, "c_Ucp", [128, 1024], BF16)
        self.opre_st = [self.sb(stk, "c_ost0", [128, 8, CH], BF16),
                        self.sb(stk, "c_ost1", [128, 8, CH], BF16)]
        self.EpT_st = [self.sb(stk, "c_est0", [128, 8, CH], BF16),
                       self.sb(stk, "c_est1", [128, 8, CH], BF16)]
        self.s_osp = [self.sem("s_osp0"), self.sem("s_osp1")]
        self.s_esp = [self.sem("s_esp0"), self.sem("s_esp1")]

    def declare_scan_psum(self, stk, tag):
        self.pX = self.ps(stk, "sp_X" + tag, [128, 1024])
        self.pY = self.ps(stk, "sp_Y" + tag, [128, 1024])
        self.pZ = self.ps(stk, "sp_Z" + tag, [128, 1024])
        self.pV = self.ps(stk, "sp_V" + tag, [64, 1024])

    def declare_p1_psum(self, stk, tag):
        self.pS = self.ps(stk, "p_S" + tag, [1, 512])
        self.pQ = self.ps(stk, "p_Q" + tag, [1, 512])
        self.pR1 = self.ps(stk, "p_R1" + tag, [128, 512])
        self.pR2 = self.ps(stk, "p_R2" + tag, [128, 512])
        self.pA = self.ps(stk, "p_A" + tag, [128, 512])
        self.pB = self.ps(stk, "p_B" + tag, [128, 512])
        self.pT = self.ps(stk, "p_T" + tag, [128, 128])
        self.pTb = self.ps(stk, "p_Tb" + tag, [128, 512], BF16)

    def q_src(self, h, cols):
        j = h // 2
        return self.qkvs[64 * (j % 2):64 * (j % 2) + 64, j // 2, cols]
    def k_src(self, h, cols):
        j = h // 2
        return self.qkvs[64 * (j % 2):64 * (j % 2) + 64, 4 + j // 2, cols]

    @staticmethod
    def hcol(h):
        return (h % 2) * 8 + h // 2

    def build_scan_pair(self, ss, tp, s_p1, s_prev):
        SS = self.SS
        tt = tp
        gc0 = ss * (SS // CH) + 2 * tp
        s_fd = self.sem("s_sfd")
        s_sc = self.sem("s_scc")

        def f_bd(g):
            s_p1.wait(g)
            s_prev.wait(g)
            for half in (0, 1):
                cc = slice(tp * 128 + 64 * half, tp * 128 + 64 * half + 64)
                s_fd.inc(g.dma_start(self.dbc[1024 * half:1024 * half + 1024], self.r_G[:, cc]), 16)
                s_fd.inc(g.dma_start(self.dbc[2048 + 1024 * half:3072 + 1024 * half], self.r_Gp[:, cc]), 16)
                s_fd.inc(g.dma_start(self.dbc[4096 + 1024 * half:5120 + 1024 * half], self.r_lamq[:, cc]), 16)
            lamc_src = bass.AP(self.dec2, 64 * SS + tp * 128 + CH - 1, [[SS, 16], [CH, 2]])
            s_fd.inc(g.dma_start(self.dbc[6144:6176], lamc_src, allow_slow_non_contiguous=True), 16)
            s_fd.wait(g)
            for half in (0, 1):
                sl = slice(64 * half, 64 * half + 64)
                s_fd.inc(g.dma_start(self.Grep[sl, :].rearrange("p (a b) -> p a b", b=64),
                                     bass.AP(self.dbc, 1024 * half, [[0, 64], [64, 16], [1, 64]])), 16)
                s_fd.inc(g.dma_start(self.Gprep[sl, :].rearrange("p (a b) -> p a b", b=64),
                                     bass.AP(self.dbc, 2048 + 1024 * half, [[0, 64], [64, 16], [1, 64]])), 16)
                s_fd.inc(g.dma_start(self.lamrep[:, 1024 * half:1024 * half + 1024].rearrange("p (a b) -> p a b", b=64),
                                     bass.AP(self.dbc, 4096 + 1024 * half, [[0, 128], [64, 16], [1, 64]])), 16)
            s_fd.inc(g.dma_start(self.lamCrep[:, :].rearrange("p (a b) -> p a b", b=2),
                                 bass.AP(self.dbc, 6144, [[0, 64], [2, 16], [1, 2]])), 16)
        self.G_(f_bd)
        if getattr(self, "CUT", None) == 0:
            s_cut = self.sem("s_scd")
            def f_cut(v):
                s_fd.wait(v)
                s_cut.inc(v.nop(), 1)
            self.V(f_cut)
            return s_cut

        def f_kk(t):
            s_p1.wait(t)
            s_prev.wait(t)
            for jpar in range(2):
                for cl in range(2):
                    cc = slice(tp * 128 + 64 * cl, tp * 128 + 64 * cl + 64)
                    for h in range(16):
                        if (h // 2) % 2 != jpar:
                            continue
                        t.matmul(self.pZ[64 * cl:64 * cl + 64, ts(h, 64)],
                                 self.k_src(h, cc), self.k_src(h, cc), start=True, stop=True)
                    t.drain()
            s_sc.inc(t.nop(), 1)
        self.P(f_kk)
        if getattr(self, "CUT", None) == 1:
            s_cut = self.sem("s_scd")
            def f_cut(v):
                s_sc.wait(v)
                s_fd.wait(v)
                s_cut.inc(v.nop(), 1)
            self.V(f_cut)
            return s_cut
        gpcol = self.decT[:, tt, 0:16].unsqueeze(2).broadcast_to([128, 16, 64])
        gcol = self.decT[:, tt, 32:48].unsqueeze(2).broadcast_to([128, 16, 64])
        mA = self.masks[:, 0:64].unsqueeze(1).broadcast_to([128, 16, 64])
        mB = self.masks[:, 64:128].unsqueeze(1).broadcast_to([128, 16, 64])
        mO = self.masks[:, 128:192].unsqueeze(1).broadcast_to([128, 16, 64])
        Dw3 = self.Dw[:, :].rearrange("p (a b) -> p a b", b=64)
        def f_da(v):
            s_fd.wait(v)
            s_sc.wait(v)
            v.scalar_tensor_tensor(Dw3, self.Grep[:, :].rearrange("p (a b) -> p a b", b=64),
                                   -1.0, gpcol, ALU.mult, ALU.add)
            v.drain()
            v.tensor_tensor(Dw3, Dw3, mA, ALU.subtract)
            v.drain()
            s_sc.inc(v.nop(), 1)
        self.V(f_da)
        def f_dae(a):
            s_sc.wait(a)
            a.activation(self.Dw[:, :], self.Dw[:, :], ACT.Exp)
            a.drain()
            s_sc.inc(a.nop(), 1)
        self.A(f_dae)
        def f_aa(v):
            s_sc.wait(v)
            v.tensor_tensor(self.Aw[:, :], self.pZ[:, :], self.Dw[:, :], ALU.mult)
            v.drain()
            v.scalar_tensor_tensor(Dw3, self.Gprep[:, :].rearrange("p (a b) -> p a b", b=64),
                                   1.0, gcol, ALU.mult, ALU.subtract)
            v.drain()
            v.tensor_tensor(Dw3, Dw3, mB, ALU.subtract)
            v.drain()
            s_sc.inc(v.nop(), 1)
        self.V(f_aa)
        def f_dbe(a):
            s_sc.wait(a)
            a.activation(self.Dw[:, :], self.Dw[:, :], ACT.Exp)
            a.drain()
            s_sc.inc(a.nop(), 1)
        self.A(f_dbe)
        def f_bb(v):
            s_sc.wait(v)
            v.tensor_tensor(self.Bw[:, :], self.pZ[:, :], self.Dw[:, :], ALU.mult)
            v.drain()
            v.scalar_tensor_tensor(self.Yc[:, :].rearrange("p (a b) -> p a b", b=64),
                                   self.Bw[:, :].rearrange("p (a b) -> p a b", b=64), -1.0,
                                   self.eyed[:, :].unsqueeze(1).broadcast_to([128, 16, 64]),
                                   ALU.mult, ALU.add)
            v.drain()
            s_sc.inc(v.nop(), 1)
        self.V(f_bb)
        def f_kq(t):
            s_sc.wait(t)
            for jpar in range(2):
                for cl in range(2):
                    cc = slice(tp * 128 + 64 * cl, tp * 128 + 64 * cl + 64)
                    for h in range(16):
                        if (h // 2) % 2 != jpar:
                            continue
                        t.matmul(self.pZ[64 * cl:64 * cl + 64, ts(h, 64)],
                                 self.k_src(h, cc), self.q_src(h, cc), start=True, stop=True)
                    t.drain()
            s_sc.inc(t.nop(), 1)
        self.P(f_kq)
        def f_do(v):
            s_fd.wait(v)
            s_sc.wait(v)
            v.scalar_tensor_tensor(Dw3, self.Grep[:, :].rearrange("p (a b) -> p a b", b=64),
                                   1.0, gcol, ALU.mult, ALU.subtract)
            v.drain()
            v.tensor_tensor(Dw3, Dw3, mO, ALU.subtract)
            v.drain()
            s_sc.inc(v.nop(), 1)
        self.V(f_do)
        def f_doe(a):
            s_sc.wait(a)
            a.activation(self.Dw[:, :], self.Dw[:, :], ACT.Exp)
            a.drain()
            s_sc.inc(a.nop(), 1)
        self.A(f_doe)
        def f_qkdt(v):
            s_sc.wait(v)
            v.tensor_tensor(self.QKDT[:, :], self.pZ[:, :], self.Dw[:, :], ALU.mult)
            v.drain()
            s_sc.inc(v.nop(), 1)
        self.V(f_qkdt)
        if getattr(self, "CUT", None) == 2:
            s_cut = self.sem("s_scd")
            def f_cut(v):
                s_sc.wait(v)
                s_fd.wait(v)
                s_cut.inc(v.nop(), 1)
            self.V(f_cut)
            return s_cut

        eyedb = self.eyed[:, :].unsqueeze(1).broadcast_to([128, 16, 64])
        cF, cT = self.Aw, self.Bw
        for lvl in range(5):
            last = (lvl == 4)
            nF = self.Pf if lvl % 2 == 0 else self.Pg
            nT = self.PfT if lvl % 2 == 0 else self.PgT
            def f_sq(t, cF=cF, cT=cT, last=last):
                s_sc.wait(t)
                for cl in range(2):
                    sl_p = slice(64 * cl, 64 * cl + 64)
                    for h in range(16):
                        t.matmul(self.pX[sl_p, ts(h, 64)], cT[sl_p, ts(h, 64)], cF[sl_p, ts(h, 64)],
                                 start=True, stop=True)
                        if not last:
                            t.matmul(self.pY[sl_p, ts(h, 64)], cF[sl_p, ts(h, 64)], cT[sl_p, ts(h, 64)],
                                     start=True, stop=True)
                    t.drain()
                s_sc.inc(t.nop(), 1)
            self.P(f_sq)
            def f_ev(a, nF=nF, nT=nT, last=last):
                s_sc.wait(a)
                if not last:
                    a.activation(nF[:, :], self.pX[:, :], ACT.Copy)
                    a.drain()
                    a.activation(nT[:, :], self.pY[:, :], ACT.Copy)
                    a.drain()
                s_sc.inc(a.nop(), 1)
            self.A(f_ev)
            def f_ip(v):
                s_sc.wait(v)
                v.scalar_tensor_tensor(self.IPb[:, :].rearrange("p (a b) -> p a b", b=64),
                                       self.pX[:, :].rearrange("p (a b) -> p a b", b=64),
                                       1.0, eyedb, ALU.mult, ALU.add)
                v.drain()
                s_sc.inc(v.nop(), 1)
            self.V(f_ip)
            ycur = self.Yc if lvl % 2 == 0 else self.Yn
            ynew = self.Yn if lvl % 2 == 0 else self.Yc
            def f_yr(t, ycur=ycur):
                s_sc.wait(t)
                for cl in range(2):
                    sl_p = slice(64 * cl, 64 * cl + 64)
                    for h in range(16):
                        t.matmul(self.pZ[sl_p, ts(h, 64)], self.IPb[sl_p, ts(h, 64)],
                                 ycur[sl_p, ts(h, 64)], start=True, stop=True)
                    t.drain()
                s_sc.inc(t.nop(), 1)
            self.P(f_yr)
            def f_ye(a, ynew=ynew):
                s_sc.wait(a)
                a.activation(ynew[:, :], self.pZ[:, :], ACT.Copy)
                a.drain()
                s_sc.inc(a.nop(), 1)
            self.A(f_ye)
            cF, cT = nF, nT
        yfin = self.Yn
        if getattr(self, "CUT", None) == 3:
            s_cut = self.sem("s_scd")
            def f_cut(v):
                s_sc.wait(v)
                s_fd.wait(v)
                s_cut.inc(v.nop(), 1)
            self.V(f_cut)
            return s_cut
        def f_rb(v):
            s_sc.wait(v)
            for h in range(16):
                v.tensor_scalar(self.Rw[:, ts(2 * h, 64)], self.vTs[:, tt, ts(h, 64)],
                                self.decT2[:, tt, h:h + 1], None, ALU.mult)
                v.tensor_scalar(self.Rw[:, ts(2 * h + 1, 64)], self.kTs[:, tt, ts(h // 2, 64)],
                                self.decT[:, tt, 96 + h:97 + h], None, ALU.mult)
                v.tensor_scalar(self.KbT[:, ts(h, 64)], self.kTs[:, tt, ts(h // 2, 64)],
                                self.decT2[:, tt, 32 + h:33 + h], None, ALU.mult)
            v.drain()
            s_sc.inc(v.nop(), 1)
        self.V(f_rb)
        if getattr(self, "CUT", None) == 4:
            s_cut = self.sem("s_scd")
            def f_cut(v):
                s_sc.wait(v)
                s_fd.wait(v)
                s_cut.inc(v.nop(), 1)
            self.V(f_cut)
            return s_cut

        for cl in range(2):
            gc = gc0 + cl
            par, npar = gc % 2, (gc + 1) % 2
            sl_p = slice(64 * cl, 64 * cl + 64)
            ccols = slice(tp * 128 + 64 * cl, tp * 128 + 64 * cl + 64)
            gcols = slice(ss * SS + tp * 128 + 64 * cl, ss * SS + tp * 128 + 64 * cl + 64)
            def f_u2(t, cl=cl, sl_p=sl_p):
                for h in range(16):
                    t.matmul(self.pY[sl_p, ts(h, 64)], yfin[sl_p, ts(h, 64)],
                             self.Rw[sl_p, ts(2 * h + 1, 64)], start=True, stop=True)
                t.drain()
                s_sc.inc(t.nop(), 1)
            def f_u1(t, cl=cl, sl_p=sl_p):
                s_sc.wait(t)
                for h in range(16):
                    t.matmul(self.pX[sl_p, ts(h, 64)], yfin[sl_p, ts(h, 64)],
                             self.Rw[sl_p, ts(2 * h, 64)], start=True, stop=True)
                t.drain()
                s_sc.inc(t.nop(), 1)
            self.P(f_u1)
            self.P(f_u2)
            def f_ut2(t, cl=cl, sl_p=sl_p):
                for h in range(16):
                    t.matmul(self.pZ[:, ts(h, 64)], self.Rw[sl_p, slice(128 * h, 128 * h + 128)],
                             yfin[sl_p, ts(h, 64)], start=True, stop=True)
                t.drain()
                s_sc.inc(t.nop(), 1)
            self.P(f_ut2)
            def f_ue(a, cl=cl, sl_p=sl_p):
                s_sc.wait(a)
                a.activation(self.Uvp[sl_p, :], self.pX[sl_p, :], ACT.Copy)
                a.drain()
                a.activation(self.Ukp[sl_p, :], self.pY[sl_p, :], ACT.Copy)
                a.drain()
                s_sc.inc(a.nop(), 1)
            self.A(f_ue)
            def f_un(v, cl=cl):
                s_sc.wait(v)
                v.tensor_scalar(self.nUkT[:, :], self.pZ[64:128, :], -1.0, None, ALU.mult)
                v.drain()
                s_sc.inc(v.nop(), 1)
            self.V(f_un)
            def f_mr(t, cl=cl, sl_p=sl_p):
                s_sc.wait(t)
                for h in range(16):
                    t.matmul(self.pV[:, ts(h, 64)], self.Ukp[sl_p, ts(h, 64)],
                             self.KbT[sl_p, ts(h, 64)], start=True, stop=True)
                t.drain()
                s_sc.inc(t.nop(), 1)
            self.P(f_mr)
            def f_mre(v, cl=cl):
                s_sc.wait(v)
                v.tensor_scalar(self.nMT[:, :], self.pV[:, :], -1.0, None, ALU.mult)
                v.drain()
                s_sc.inc(v.nop(), 1)
            self.V(f_mre)
            def f_pt(t, cl=cl, sl_p=sl_p):
                s_sc.wait(t)
                for h in range(16):
                    t.matmul(self.pV[:, ts(h, 64)], self.Ukp[sl_p, ts(h, 64)],
                             self.QKDT[sl_p, ts(h, 64)], start=True, stop=True)
                t.drain()
                s_sc.inc(t.nop(), 1)
            self.P(f_pt)
            def f_qt(v, cl=cl, ccols=ccols):
                s_sc.wait(v)
                for h in range(16):
                    qb = 64 * ((h // 2) % 2)
                    v.tensor_tensor(self.Qt[:, ts(h, 64)], self.q_src(h, ccols),
                                    self.lamrep[qb:qb + 64, slice(1024 * cl + 64 * h, 1024 * cl + 64 * h + 64)],
                                    ALU.mult)
                v.drain()
                v.scalar_tensor_tensor(self.Et[:, :], self.pV[:, :], -1.0, self.Qt[:, :],
                                       ALU.mult, ALU.add)
                v.drain()
                s_sc.inc(v.nop(), 1)
            self.V(f_qt)
            def f_ep(t, cl=cl, par=par):
                s_sc.wait(t)
                for h in range(16):
                    wsl = self.Wb[par][:, h, :]
                    t.matmul(self.pV[:, ts(self.hcol(h), 64)], wsl, self.Et[:, ts(h, 64)],
                             start=True, stop=True)
                t.drain()
                s_sc.inc(t.nop(), 1)
            self.P(f_ep)
            def f_epe2(a, par=par):
                s_sc.wait(a)
                self.s_esp[par].wait(a)
                a.activation(self.EpT_st[par][0:64, :, :],
                             self.pV[:, 0:512].rearrange("p (a b) -> p a b", b=64), ACT.Copy)
                a.drain()
                a.activation(self.EpT_st[par][64:128, :, :],
                             self.pV[:, 512:1024].rearrange("p (a b) -> p a b", b=64), ACT.Copy)
                a.drain()
                s_sc.inc(a.nop(), 1)
            self.A(f_epe2)
            def f_epd(g, par=par, g0=gcols.start):
                s_sc.wait(g)
                ap = bass.AP(self.d_EpT, g0, [[8 * self.T, 128], [self.T, 8], [1, CH]])
                self.s_esp[par].inc(g.dma_start(ap, self.EpT_st[par][:, :, :]), 16)
            self.G_(f_epd)
            def f_uc(t, cl=cl, par=par, sl_p=sl_p):
                s_sc.wait(t)
                for h in range(16):
                    t.matmul(self.pV[:, ts(h, 64)], self.nUkT[:, ts(h, 64)],
                             self.Sst[par][:, h, :], start=True, stop=True)
                t.drain()
                s_sc.inc(t.nop(), 1)
            self.P(f_uc)
            def f_uce(v, cl=cl, sl_p=sl_p):
                s_sc.wait(v)
                v.tensor_tensor(self.Ucp[sl_p, :], self.pV[0:64, :], self.Uvp[sl_p, :], ALU.add)
                v.drain()
                s_sc.inc(v.nop(), 1)
            self.V(f_uce)
            def f_ot(t, cl=cl, par=par, sl_p=sl_p):
                s_sc.wait(t)
                for h in range(16):
                    t.matmul(self.pV[:, ts(self.hcol(h), 64)], self.Ucp[sl_p, ts(h, 64)],
                             self.QKDT[sl_p, ts(h, 64)], start=True, stop=True)
                t.drain()
                for h in range(16):
                    t.matmul(self.pZ[0:64, ts(self.hcol(h), 64)],
                             self.Sst[par][:, h, :],
                             self.Qt[:, ts(h, 64)], start=True, stop=True)
                t.drain()
                s_sc.inc(t.nop(), 1)
            self.P(f_ot)
            def f_otc(a):
                s_sc.wait(a)
                a.activation(self.scr[0:64, 0:1024], self.pV[:, :], ACT.Copy)
                a.drain()
                s_sc.inc(a.nop(), 1)
            self.A(f_otc)
            def f_ote(v, par=par):
                s_sc.wait(v)
                self.s_osp[par].wait(v)
                v.tensor_tensor(self.opre_st[par][0:64, :, :],
                                self.scr[0:64, 0:512].rearrange("p (a b) -> p a b", b=64),
                                self.pZ[0:64, 0:512].rearrange("p (a b) -> p a b", b=64), ALU.add)
                v.drain()
                v.tensor_tensor(self.opre_st[par][64:128, :, :],
                                self.scr[0:64, 512:1024].rearrange("p (a b) -> p a b", b=64),
                                self.pZ[0:64, 512:1024].rearrange("p (a b) -> p a b", b=64), ALU.add)
                v.drain()
                s_sc.inc(v.nop(), 1)
            self.V(f_ote)
            def f_otd(g, par=par, g0=gcols.start):
                s_sc.wait(g)
                ap = bass.AP(self.d_opre, g0, [[8 * self.T, 128], [self.T, 8], [1, CH]])
                self.s_osp[par].inc(g.dma_start(ap, self.opre_st[par][:, :, :]), 16)
            self.G_(f_otd)
            lamCsl = self.lamCrep[:, cl::2]
            def f_ssc(v, cl=cl, par=par, lamCsl=lamCsl):
                s_sc.wait(v)
                v.tensor_tensor(self.Ssc[:, :].rearrange("p (a b) -> p a b", b=64),
                                self.Sst[par][:, :, :],
                                lamCsl.unsqueeze(2).broadcast_to([64, 16, 64]), ALU.mult)
                v.drain()
                v.tensor_tensor(self.Wsc[:, :].rearrange("p (a b) -> p a b", b=64),
                                self.Wb[par][:, :, :],
                                lamCsl.unsqueeze(2).broadcast_to([64, 16, 64]), ALU.mult)
                v.drain()
                s_sc.inc(v.nop(), 1)
            self.V(f_ssc)
            def f_st(t, cl=cl, par=par, sl_p=sl_p):
                s_sc.wait(t)
                for h in range(16):
                    t.matmul(self.pV[:, ts(h, 64)], self.KbT[sl_p, ts(h, 64)],
                             self.Ucp[sl_p, ts(h, 64)], start=True, stop=True)
                t.drain()
                s_sc.inc(t.nop(), 1)
            self.P(f_st)
            def f_ste(v, npar=npar):
                s_sc.wait(v)
                v.tensor_tensor(self.Sst[npar][:, :, :].rearrange("p a b -> p (a b)"),
                                self.pV[:, :], self.Ssc[:, :], ALU.add)
                v.drain()
                s_sc.inc(v.nop(), 1)
            self.V(f_ste)
            def f_wm(t, cl=cl, par=par):
                s_sc.wait(t)
                for h in range(16):
                    wsl = self.Wb[par][:, h, :]
                    t.matmul(self.pV[:, ts(h, 64)], self.nMT[:, ts(h, 64)], wsl,
                             start=True, stop=True)
                t.drain()
                s_sc.inc(t.nop(), 1)
            self.P(f_wm)
            def f_wme(v, npar=npar):
                s_sc.wait(v)
                v.tensor_tensor(self.Wb[npar][:, :, :].rearrange("p a b -> p (a b)"),
                                self.pV[:, :], self.Wsc[:, :], ALU.add)
                v.drain()
                s_sc.inc(v.nop(), 1)
            self.V(f_wme)
        s_done = self.sem("s_scd")
        def f_fin(v):
            s_sc.wait(v)
            s_done.inc(v.nop(), 1)
        self.V(f_fin)
        return s_done

    def build_scan_superslab(self, ss, s_p1, s_prev):
        s = s_prev
        for tp in range(self.SS // 128):
            s = self.build_scan_pair(ss, tp, s_p1, s)
        return s


    def declare_epi_ws(self, stk):
        ES = self.ES = 256
        self.Sin = self.sb(stk, "e_Sin", [64, 16, DV], F32)
        self.Sinb = self.sb(stk, "e_Sinb", [128, 16, DV], BF16)
        self.wzT = self.sb(stk, "e_wzT", [128, 8, 1024], BF16)
        self.woutT = self.sb(stk, "e_woutT", [128, 8, 1024], BF16)
        self.xe = [self.sb(stk, "e_xe0", [128, 8, ES], F32),
                   self.sb(stk, "e_xe1", [128, 8, ES], F32)]
        self.opre_sl = [self.sb(stk, "e_op0", [128, 8, ES], BF16),
                        self.sb(stk, "e_op1", [128, 8, ES], BF16)]
        self.EpT_sl = [self.sb(stk, "e_ep0", [128, 8, ES], BF16),
                       self.sb(stk, "e_ep1", [128, 8, ES], BF16)]
        self.hTe = self.sb(stk, "e_hT", [128, 8, ES], BF16)
        self.ofin = self.sb(stk, "e_ofin", [128, 8, ES], F32)
        self.szs = self.sb(stk, "e_szs", [128, 8, ES], BF16)
        self.tf = self.sb(stk, "e_tf", [128, 8, ES], F32)
        self.tsq = self.sb(stk, "e_tsq", [128, 8, ES], F32)
        self.r16 = self.sb(stk, "e_r16", [16, ES], F32)
        self.hgn = self.sb(stk, "e_hgn", [128, 8, ES], BF16)
        self.s_eid = [self.sem("s_eid0"), self.sem("s_eid1")]
        self.s_xod = [self.sem("s_xod0"), self.sem("s_xod1")]

    def declare_epi_psum(self, stk, tag):
        self.peD = self.ps(stk, "pe_D" + tag, [64, 1024])
        self.peZ = self.ps(stk, "pe_Z" + tag, [128, 512])
        self.peG = self.ps(stk, "pe_G" + tag, [16, 512])
        self.peR = self.ps(stk, "pe_R" + tag, [128, 512])
        self.peW = self.ps(stk, "pe_W" + tag, [128, 512])
        self.pR1 = self.ps(stk, "pe_R1" + tag, [128, 512])
        self.pR2 = self.ps(stk, "pe_R2" + tag, [128, 512])

    def build_ag2(self, s_scan):
        s_cc = self.sem("s_cc")
        s_ccw = self.sem("s_ccw")
        self.s_sin = self.sem("s_sin")
        nch = self.T // CH
        fin = nch % 2
        def f_cc(g):
            s_scan.wait(g)
            s_cc.inc(g.dma_start(self.cc_in[:, :], self.Sst[fin][:, :, :].rearrange("p a b -> p (a b)")), 16)
            s_cc.wait(g)
            g.collective_compute(
                "AllGather", mybir.AluOpType.bypass,
                replica_groups=[[i, i + 1] for i in range(0, self.NCORE, 2)],
                ins=[self.cc_in.ap().opt()], outs=[self.cc_out.ap().opt()],
            ).then_inc(s_ccw.h, 1)
            s_ccw.val += 1
            s_ccw.wait(g)
            s_cc.inc(g.dma_start(self.Sin[:, :, :].rearrange("p a b -> p (a b)"), self.cc_out[0:64, :]), 16)
            s_cc.inc(g.dma_start(self.scr[0:64, 0:512], self.cc_out[64:128, 0:512]), 16)
            s_cc.inc(g.dma_start(self.scr[0:64, 512:1024], self.cc_out[64:128, 512:1024]), 16)
        self.G_(f_cc)
        def f_si(v):
            s_cc.wait(v)
            v.tensor_scalar(self.Sin[:, :, :].rearrange("p a b -> p (a b)"),
                            self.Sin[:, :, :].rearrange("p a b -> p (a b)"),
                            self.cvec[0:64, 2:3], None, ALU.mult)
            v.drain()
            v.scalar_tensor_tensor(self.Sin[:, 0:8, :].rearrange("p a b -> p (a b)"),
                                   self.scr[0:64, 0:512], self.cvec[0:64, 3:4],
                                   self.Sin[:, 0:8, :].rearrange("p a b -> p (a b)"),
                                   ALU.mult, ALU.add)
            v.drain()
            v.scalar_tensor_tensor(self.Sin[:, 8:16, :].rearrange("p a b -> p (a b)"),
                                   self.scr[0:64, 512:1024], self.cvec[0:64, 3:4],
                                   self.Sin[:, 8:16, :].rearrange("p a b -> p (a b)"),
                                   ALU.mult, ALU.add)
            v.drain()
            v.tensor_copy(self.Sinb[0:64, :, :].rearrange("p a b -> p (a b)"),
                          self.Sin[:, :, :].rearrange("p a b -> p (a b)"))
            v.drain()
            v.tensor_copy(self.Sinb[64:128, :, :].rearrange("p a b -> p (a b)"),
                          self.Sin[:, :, :].rearrange("p a b -> p (a b)"))
            v.drain()
            self.s_sin.inc(v.nop(), 1)
        self.V(f_si)
        return self.s_sin

    def build_epi_slab(self, sl, s_pre):
        ES = self.ES
        w0 = sl * ES
        par = sl % 2
        s_e = self.sem("s_e")
        def f_ind(g):
            s_pre.wait(g)
            if self.s_xod[par].val:
                self.s_xod[par].wait(g)
            ap_x = bass.AP(self.i_x, 3 + w0, [[8 * self.TH, 128], [self.TH, 8], [1, ES]])
            ap_o = bass.AP(self.d_opre, w0, [[8 * self.T, 128], [self.T, 8], [1, ES]])
            ap_e = bass.AP(self.d_EpT, w0, [[8 * self.T, 128], [self.T, 8], [1, ES]])
            self.s_eid[par].inc(g.dma_start(self.xe[par][:, :, :], ap_x), 16)
            self.s_eid[par].inc(g.dma_start(self.opre_sl[par][:, :, :], ap_o), 16)
            self.s_eid[par].inc(g.dma_start(self.EpT_sl[par][:, :, :], ap_e), 16)
        self.G_(f_ind)
        def f_gate(v):
            s_pre.wait(v)
            self.s_eid[par].wait(v)
            s_e.inc(v.nop(), 1)
        self.V(f_gate)
        self.ln_rep_apply(0, 3 + w0, ES, s_e, self.stats, self.statsm, self.hTe, False,
                          xsrc=self.xe[par], x0=0)
        for dp in range(4):
            def f_z(t, dp=dp):
                s_e.wait(t)
                for j in range(2):
                    d = 2 * dp + j
                    for kk in range(8):
                        t.matmul(self.peZ[:, 256 * j:256 * j + ES], self.wzT[:, kk, ts(d, 128)],
                                 self.hTe[:, kk, :], start=(kk == 0), stop=(kk == 7))
                t.drain()
                s_e.inc(t.nop(), 1)
            self.P(f_z)
            def f_zs(a, dp=dp):
                s_e.wait(a)
                a.activation(self.szs[:, 2 * dp:2 * dp + 2, :],
                             self.peZ[:, 0:512].rearrange("p (a b) -> p a b", b=ES), ACT.Silu)
                a.drain()
                s_e.inc(a.nop(), 1)
            self.A(f_zs)
        for cc in range(ES // CH):
            lcols = slice(cc * CH, cc * CH + CH)
            def f_do(t, cc=cc, lcols=lcols, par=par):
                s_e.wait(t)
                for hp in range(2):
                    for h in range(hp, 16, 2):
                        base = 64 * (h % 2)
                        t.matmul(self.peD[:, ts(self.hcol(h), 64)],
                                 self.Sinb[base:base + 64, h, :],
                                 self.EpT_sl[par][base:base + 64, h // 2, lcols],
                                 start=True, stop=True)
                    t.drain()
                s_e.inc(t.nop(), 1)
            self.P(f_do)
            def f_oa(v, cc=cc, lcols=lcols, par=par):
                s_e.wait(v)
                v.tensor_tensor(self.ofin[0:64, :, lcols],
                                self.opre_sl[par][0:64, :, lcols],
                                self.peD[:, 0:512].rearrange("p (a b) -> p a b", b=64),
                                ALU.add)
                v.drain()
                v.tensor_tensor(self.ofin[64:128, :, lcols],
                                self.opre_sl[par][64:128, :, lcols],
                                self.peD[:, 512:1024].rearrange("p (a b) -> p a b", b=64),
                                ALU.add)
                v.drain()
                s_e.inc(v.nop(), 1)
            self.V(f_oa)
        def f_tf(v):
            s_e.wait(v)
            v.tensor_tensor(self.tf[:, :, :].rearrange("p a b -> p (a b)"),
                            self.ofin[:, :, :].rearrange("p a b -> p (a b)"),
                            self.szs[:, :, :].rearrange("p a b -> p (a b)"), ALU.mult)
            v.drain()
            s_e.inc(v.nop(), 1)
        self.V(f_tf)
        def f_sq(a):
            s_e.wait(a)
            for d in range(8):
                a.activation(self.tsq[:, d, :], self.tf[:, d, :], ACT.Square)
            a.drain()
            s_e.inc(a.nop(), 1)
        self.A(f_sq)
        def f_gs(t):
            s_e.wait(t)
            for d in range(8):
                t.matmul(self.peG[:, 0:ES], self.gnsel[:, d, :], self.tsq[:, d, :],
                         start=(d == 0), stop=(d == 7))
            t.drain()
            s_e.inc(t.nop(), 1)
        self.P(f_gs)
        def f_r16(v):
            s_e.wait(v)
            v.tensor_scalar(self.r16[:, :], self.peG[:, 0:ES], 1.0 / 64.0, None, ALU.mult)
            v.drain()
            s_e.inc(v.nop(), 1)
        self.V(f_r16)
        def f_r16a(a):
            s_e.wait(a)
            a.activation(self.r16[:, :], self.r16[:, :], ACT.Ln, bias=self.cvec[0:16, 1:2])
            a.drain()
            a.activation(self.r16[:, :], self.r16[:, :], ACT.Exp, scale=-0.5)
            a.drain()
            s_e.inc(a.nop(), 1)
        self.A(f_r16a)
        for dp in range(4):
            def f_rr(t, dp=dp):
                s_e.wait(t)
                for j in range(2):
                    t.matmul(self.peR[:, 256 * j:256 * j + ES], self.gnselT[:, 2 * dp + j, :],
                             self.r16[:, :], start=True, stop=True)
                t.drain()
                s_e.inc(t.nop(), 1)
            self.P(f_rr)
            def f_hg(v, dp=dp):
                s_e.wait(v)
                for j in range(2):
                    d = 2 * dp + j
                    v.scalar_tensor_tensor(self.hgn[:, d, :], self.tf[:, d, :], self.nw[:, 0:1],
                                           self.peR[:, 256 * j:256 * j + ES], ALU.mult, ALU.mult)
                v.drain()
                s_e.inc(v.nop(), 1)
            self.V(f_hg)
        for dp in range(4):
            def f_wo(t, dp=dp):
                s_e.wait(t)
                for j in range(2):
                    d = 2 * dp + j
                    for kk in range(8):
                        t.matmul(self.peW[:, 256 * j:256 * j + ES], self.woutT[:, kk, ts(d, 128)],
                                 self.hgn[:, kk, :], start=(kk == 0), stop=(kk == 7))
                t.drain()
                s_e.inc(t.nop(), 1)
            self.P(f_wo)
            def f_xu(v, dp=dp, par=par):
                s_e.wait(v)
                for j in range(2):
                    d = 2 * dp + j
                    v.scalar_tensor_tensor(self.xe[par][:, d, :], self.peW[:, 256 * j:256 * j + ES],
                                           self.adaln[:, d, 2:3], self.xe[par][:, d, :],
                                           ALU.mult, ALU.add)
                v.drain()
                s_e.inc(v.nop(), 1)
            self.V(f_xu)
        def f_xod(g, par=par, w0=w0):
            s_e.wait(g)
            ap = bass.AP(self.d_xp, w0, [[8 * self.T, 128], [self.T, 8], [1, ES]])
            self.s_xod[par].inc(g.dma_start(ap, self.xe[par][:, :, :]), 16)
        self.G_(f_xod)
        return s_e

    def quant_emit(self, ch, src, dst, nblk, pre=None, post_sems=()):
        n = nblk * 16
        bm, tq, gq = self.fbm, self.ftq, self.fgq
        aa = gq[:, 0:n]
        acc = gq[:, 512:512 + n]
        r3 = lambda ap: ap.rearrange("p (a b) -> p a b", b=16)
        bcast = lambda ap: ap.unsqueeze(2).broadcast_to([128, nblk, 16])
        def f_q1(v):
            ch.wait(v)
            if pre is not None:
                pre(v)
            v.tensor_reduce(bm[:, 0:nblk], r3(src), mybir.AxisListType.X, ALU.max,
                            apply_absolute_value=True)
            v.drain()
            v.tensor_scalar(bm[:, 64:64 + nblk], bm[:, 0:nblk], 6e-12, None, ALU.max)
            v.drain()
            ch.inc(v.nop(), 1)
        self.V(f_q1)
        def f_q2(a):
            ch.wait(a)
            a.activation(bm[:, 0:nblk], bm[:, 64:64 + nblk], ACT.Ln)
            a.drain()
            a.activation(bm[:, 0:nblk], bm[:, 0:nblk], ACT.Exp, scale=-1.0,
                         bias=self.cvec[:, 5:6])
            a.drain()
            ch.inc(a.nop(), 1)
        self.A(f_q2)
        def f_q3(v):
            ch.wait(v)
            v.tensor_tensor(r3(tq[:, 0:n]), r3(src), bcast(bm[:, 0:nblk]), ALU.mult)
            v.drain()
            v.tensor_scalar(acc, tq[:, 0:n], -1.0, None, ALU.mult)
            v.drain()
            v.tensor_tensor(aa, tq[:, 0:n], acc, ALU.max)
            v.drain()
            v.tensor_scalar(acc, aa, 7.0, None, ALU.is_gt)
            v.drain()
            v.scalar_tensor_tensor(acc, aa, 5.0, acc, ALU.is_gt, ALU.add)
            v.drain()
            v.scalar_tensor_tensor(acc, aa, 3.0, acc, ALU.is_gt, ALU.add)
            v.drain()
            v.scalar_tensor_tensor(acc, aa, 1.0, acc, ALU.is_gt, ALU.add)
            v.drain()
            hi = tq[:, 0:n]
            v.tensor_scalar(hi, aa, 20.0, None, ALU.is_gt)
            v.drain()
            v.tensor_tensor(hi, hi, hi, ALU.add)
            v.drain()
            v.scalar_tensor_tensor(hi, aa, 14.0, hi, ALU.is_gt, ALU.add)
            v.drain()
            v.scalar_tensor_tensor(hi, aa, 10.0, hi, ALU.is_gt, ALU.add)
            v.drain()
            v.scalar_tensor_tensor(acc, hi, 2.0, acc, ALU.mult, ALU.add)
            v.drain()
            v.tensor_scalar(aa, src, 1e13, -1.0, ALU.mult, ALU.max)
            v.drain()
            v.tensor_scalar(aa, aa, 1.0, None, ALU.min)
            v.drain()
            v.tensor_tensor(acc, acc, aa, ALU.mult)
            v.drain()
            v.tensor_scalar(bm[:, 64:64 + nblk], bm[:, 64:64 + nblk], 1.0 / 12.0, None, ALU.mult)
            v.drain()
            v.tensor_tensor(r3(dst), r3(acc), bcast(bm[:, 64:64 + nblk]), ALU.mult)
            v.drain()
            ch.inc(v.nop(), 1)
            for sm, k in post_sems:
                sm.inc(v.nop(), k)
        self.V(f_q3)

    def build_wq(self, stk, s_pre):
        NR, ISHP = self.NR, self.ISHP
        cols = max(NR * 1024, 8 * ISHP)
        wsh = self.sb(stk, "w_sh", [128, cols], F32)
        wqq = self.sb(stk, "w_qq", [128, cols], BF16)
        wts = self.sb(stk, "w_ts", [128, cols], BF16)
        self.fbm = self.sb(stk, "w_bm", [128, 192], F32)
        self.ftq = self.sb(stk, "w_tq", [128, 512], F32)
        self.fgq = self.sb(stk, "w_gq", [128, 1024], F32)
        pWb = self.ps(stk, "p_wT", [128, 1024], BF16)
        s_wd = self.sem("s_wd")
        s_wq = self.sem("s_wq")
        s_ag2 = self.sem("s_ag2")
        specs = ((self.i_wg, self.cc_g_in, "g"), (self.i_wu, self.cc_u_in, "u"),
                 (self.i_wd, self.cc_d_in, "d"))
        for wi, (ih, ccin, kind) in enumerate(specs):
            ncols = NR * 1024 if kind != "d" else 8 * ISHP
            def f_in(g, ih=ih, ncols=ncols, wi=wi):
                if wi == 0:
                    s_pre.wait(g)
                else:
                    s_wq.wait(g)
                s_wd.inc(g.dma_start(wsh[:, 0:ncols], ih[:, :]), 16)
                s_wd.wait(g)
                s_wq.inc(g.nop(), 1)
            self.G_(f_in)
            ntile = NR if kind != "d" else 8
            tw = 1024 if kind != "d" else ISHP
            for it in range(ntile):
                for c0 in range(0, tw, 512):
                    wd = min(512, tw - c0)
                    self.quant_emit(s_wq, wsh[:, it * tw + c0: it * tw + c0 + wd],
                                    wqq[:, it * tw + c0: it * tw + c0 + wd], wd // 16)
            for k in range(8):
                for r0 in range(0, NR, 8):
                    nr = min(8, NR - r0)
                    if kind != "d":
                        srcs = [(r0 + i) * 1024 + 128 * k for i in range(nr)]
                        dst = wts[:, k * ISHP + 128 * r0: k * ISHP + 128 * (r0 + nr)]
                        dst = dst.rearrange("p (a b) -> p a b", b=128)
                    else:
                        srcs = [k * ISHP + 128 * (r0 + i) for i in range(nr)]
                        dst = wts[:, 0:NR * 1024].rearrange("p (a b) -> p a b", b=1024)[
                            :, r0:r0 + nr, 128 * k:128 * k + 128]
                    def f_t(t, srcs=srcs):
                        s_wq.wait(t)
                        for i, sc in enumerate(srcs):
                            t.transpose(pWb[:, 128 * i:128 * i + 128], wqq[:, sc:sc + 128],
                                        self.eyeb[:, :])
                        t.drain()
                        s_wq.inc(t.nop(), 1)
                    self.P(f_t)
                    def f_te(a, dst=dst, nr=nr):
                        s_wq.wait(a)
                        a.activation(dst, pWb[:, 0:128 * nr].rearrange("p (a b) -> p a b", b=128),
                                     ACT.Copy)
                        a.drain()
                        s_wq.inc(a.nop(), 1)
                    self.A(f_te)
            if kind != "d":
                dst = bass.AP(ccin, 0, [[ISHP, 128], [128 * ISHP, 8], [1, ISHP]])
                src = wts[:, 0:8 * ISHP].rearrange("p (a b) -> p a b", b=ISHP)
            else:
                dst = bass.AP(ccin, 0, [[1024, 128], [128 * 1024, NR], [1, 1024]])
                src = wts[:, 0:NR * 1024].rearrange("p (a b) -> p a b", b=1024)
            def f_out(g, dst=dst, src=src):
                s_wq.wait(g)
                s_wd.inc(g.dma_start(dst, src), 16)
                s_wd.wait(g)
                s_wq.inc(g.nop(), 1)
            self.G_(f_out)
        def f_ag(g):
            s_wq.wait(g)
            grp = [list(range(self.NCORE))]
            for cin, gout in ((self.cc_g_in, self.g_gT), (self.cc_u_in, self.g_uT),
                              (self.cc_d_in, self.g_dT)):
                g.collective_compute("AllGather", mybir.AluOpType.bypass,
                                     replica_groups=grp, ins=[cin.ap().opt()],
                                     outs=[gout.ap().opt()]).then_inc(s_ag2.h, 1)
                s_ag2.val += 1
        self.G_(f_ag)
        s_wqd = self.sem("s_wqd")
        def f_fence(v):
            s_wq.wait(v)
            s_wqd.inc(v.nop(), 1)
        self.V(f_fence)
        return s_wqd, s_ag2

    def declare_ffn_ws(self, stk):
        FS, NU, ISHP, NJ = self.FS, self.NU, self.ISHP, self.NJ
        IT = self.NCORE * ISHP
        self.hf = self.sb(stk, "f_hf", [128, 8, FS], F32)
        self.fsq = self.sb(stk, "f_sq", [128, 2, FS], F32)
        self.hq = self.sb(stk, "f_hq", [128, 8, FS], BF16)
        self.hfT = self.sb(stk, "f_hfT", [128, 1024], F32)
        self.hqT = self.sb(stk, "f_hqT", [128, 1024], BF16)
        self.h2q = self.sb(stk, "f_h2q", [128, NU, IT], BF16)
        self.h2qT = self.sb(stk, "f_h2qT", [128, NJ, FS], BF16)
        self.h2p = self.sb(stk, "f_h2p", [128, 512], F32)
        self.xf = self.sb(stk, "f_xf", [128, 8, FS], F32)
        self.s_xfd = self.sem("s_xfd")
        self.s_yod = self.sem("s_yod")
        WW = min(512, ISHP)
        self.wgb = [self.sb(stk, "f_wg0", [128, 8, WW], BF16),
                    self.sb(stk, "f_wg1", [128, 8, WW], BF16)]
        self.wub = [self.sb(stk, "f_wu0", [128, 8, WW], BF16),
                    self.sb(stk, "f_wu1", [128, 8, WW], BF16)]
        self.wdb = [self.sb(stk, "f_wd0", [128, NJ, 128], BF16),
                    self.sb(stk, "f_wd1", [128, NJ, 128], BF16)]
        self.fbm = self.sb(stk, "f_bm", [128, 192], F32)
        self.ftq = self.sb(stk, "f_tq", [128, 512], F32)
        self.fgq = self.sb(stk, "f_gq", [128, 1024], F32)

    def declare_ffn_psum(self, stk, tag):
        self.pF1 = self.ps(stk, "pf_1" + tag, [128, 512])
        self.pF2 = self.ps(stk, "pf_2" + tag, [128, 512])
        self.pR1 = self.ps(stk, "pf_R1" + tag, [128, 512])
        self.pR2 = self.ps(stk, "pf_R2" + tag, [128, 512])
        self.pD = self.ps(stk, "pf_D" + tag, [128, 512])
        self.pTqf = self.ps(stk, "pf_Tqf" + tag, [128, 512])
        self.pTq = self.ps(stk, "pf_Tq" + tag, [128, 1024], BF16)
        self.pS = self.pF1
        self.pQ = self.pF2

    def ln_ffn(self, w0, n, s_pre, xsrc, x0):
        s_loc = self.sem("s_ln2")
        def f_gate(v):
            s_pre.wait(v)
            s_loc.inc(v.nop(), 1)
        self.V(f_gate)
        for dp in range(4):
            def f_sq(a, dp=dp):
                s_loc.wait(a)
                a.activation(self.fsq[:, 0, 0:n], xsrc[:, 2 * dp, x0:x0 + n], ACT.Square)
                a.activation(self.fsq[:, 1, 0:n], xsrc[:, 2 * dp + 1, x0:x0 + n], ACT.Square)
                a.drain()
                s_loc.inc(a.nop(), 1)
            self.A(f_sq)
            def f_mm(t, dp=dp):
                s_loc.wait(t)
                for j in range(2):
                    d = 2 * dp + j
                    t.matmul(self.pS[0:1, 0:n], self.onesf[:, 0:1], xsrc[:, d, x0:x0 + n],
                             start=(d == 0), stop=(d == 7))
                for j in range(2):
                    d = 2 * dp + j
                    t.matmul(self.pQ[0:1, 0:n], self.onesf[:, 0:1], self.fsq[:, j, 0:n],
                             start=(d == 0), stop=(d == 7))
                t.drain()
                s_loc.inc(t.nop(), 1)
            self.P(f_mm)
        def f_vst(v):
            s_loc.wait(v)
            v.tensor_scalar(self.scr[0:1, 0:n], self.pS[0:1, 0:n], 1.0 / 1024.0, None, ALU.mult)
            v.drain()
            v.tensor_tensor(self.scr[0:1, 0:n], self.scr[0:1, 0:n], self.scr[0:1, 0:n], ALU.mult)
            v.drain()
            v.scalar_tensor_tensor(self.scr[0:1, self.SCROFF:self.SCROFF + n], self.pQ[0:1, 0:n],
                                   1.0 / 1024.0, self.scr[0:1, 0:n], ALU.mult, ALU.subtract)
            v.drain()
            s_loc.inc(v.nop(), 1)
        self.V(f_vst)
        def f_ast(a):
            s_loc.wait(a)
            a.activation(self.scr[0:1, 0:n], self.scr[0:1, self.SCROFF:self.SCROFF + n],
                         ACT.Ln, bias=self.cvec[0:1, 0:1])
            a.drain()
            a.activation(self.stats[0:1, w0:w0 + n], self.scr[0:1, 0:n], ACT.Exp, scale=-0.5)
            a.drain()
            s_loc.inc(a.nop(), 1)
        self.A(f_ast)
        def f_vst2(v):
            s_loc.wait(v)
            v.scalar_tensor_tensor(self.statsm[32:33, w0:w0 + n], self.pS[0:1, 0:n], 1.0 / 1024.0,
                                   self.stats[0:1, w0:w0 + n], ALU.mult, ALU.mult)
            v.drain()
            s_loc.inc(v.nop(), 1)
        self.V(f_vst2)
        self.ln_rep_apply(1, w0, n, s_loc, self.stats, self.statsm, self.hf, False, xsrc, x0)
        return s_loc

    def build_ffn_slab(self, u4, s_pre, s_ag2):
        FS, NU, NR, ISHP, NJ = self.FS, self.NU, self.NR, self.ISHP, self.NJ
        w0 = u4 * FS
        def f_xfd(g):
            s_pre.wait(g)
            if self.s_yod.val:
                self.s_yod.wait(g)
            ap = bass.AP(self.d_xp, w0, [[8 * self.T, 128], [self.T, 8], [1, FS]])
            self.s_xfd.inc(g.dma_start(self.xf[:, :, :], ap), 16)
        self.G_(f_xfd)
        s_xg = self.sem("s_xg")
        def f_xg(v):
            s_pre.wait(v)
            self.s_xfd.wait(v)
            s_xg.inc(v.nop(), 1)
        self.V(f_xg)
        s_ln = self.ln_ffn(3 + w0, FS, s_xg, self.xf, 0)
        s_f = self.sem("s_f")
        def f_gate(v):
            s_ln.wait(v)
            s_f.inc(v.nop(), 1)
        self.V(f_gate)
        for u in range(NU):
            for d0 in (0, 4):
                def f_t(t, u=u, d0=d0):
                    s_f.wait(t)
                    for i in range(4):
                        t.transpose(self.pTqf[:, 128 * i:128 * i + 128],
                                    self.hf[:, d0 + i, 128 * u:128 * u + 128], self.eyef[:, :])
                    t.drain()
                    s_f.inc(t.nop(), 1)
                self.P(f_t)
                def f_te(a, d0=d0):
                    s_f.wait(a)
                    a.activation(self.hfT[:, 512 * (d0 // 4):512 * (d0 // 4) + 512],
                                 self.pTqf[:, 0:512], ACT.Copy)
                    a.drain()
                    s_f.inc(a.nop(), 1)
                self.A(f_te)
            self.quant_emit(s_f, self.hfT[:, 0:512], self.hqT[:, 0:512], 32)
            self.quant_emit(s_f, self.hfT[:, 512:1024], self.hqT[:, 512:1024], 32)
            def f_tb(t, u=u):
                s_f.wait(t)
                for d in range(8):
                    t.transpose(self.pTq[:, 128 * d:128 * d + 128],
                                self.hqT[:, 128 * d:128 * d + 128], self.eyeb[:, :])
                t.drain()
                s_f.inc(t.nop(), 1)
            self.P(f_tb)
            def f_tbe(a, u=u):
                s_f.wait(a)
                a.activation(self.hq[:, :, 128 * u:128 * u + 128],
                             self.pTq[:, :].rearrange("p (a b) -> p a b", b=128), ACT.Copy)
                a.drain()
                s_f.inc(a.nop(), 1)
            self.A(f_tbe)
        s_gu = self.sem("s_gu")
        s_wgp = [self.sem("s_wg0"), self.sem("s_wg1")]
        if not hasattr(self, "h_gu"):
            self.h_gu = []
        chunks = [(s, c0) for s in range(self.NCORE) for c0 in range(0, ISHP, 512)]
        for ci, (s, c0) in enumerate(chunks):
            par = len(self.h_gu) % 2
            W = min(512, ISHP - c0)
            free_after = self.h_gu[-2] if len(self.h_gu) >= 2 else None
            def f_wg(g, s=s, c0=c0, W=W, par=par, free_after=free_after):
                s_ag2.wait(g)
                if free_after is not None:
                    g.wait_ge(s_gu.h, free_after)
                off = 1024 * s * ISHP + c0
                ap_g = bass.AP(self.g_gT, off, [[ISHP, 128], [128 * ISHP, 8], [1, W]])
                ap_u = bass.AP(self.g_uT, off, [[ISHP, 128], [128 * ISHP, 8], [1, W]])
                s_wgp[par].inc(g.dma_start(self.wgb[par][:, :, 0:W], ap_g), 16)
                s_wgp[par].inc(g.dma_start(self.wub[par][:, :, 0:W], ap_u), 16)
            self.G_(f_wg)
            for u in range(NU):
                def f_mm(t, par=par, u=u, W=W):
                    s_f.wait(t)
                    s_wgp[par].wait(t)
                    for k in range(8):
                        t.matmul(self.pF1[:, 0:W], self.hq[:, k, 128 * u:128 * u + 128],
                                 self.wgb[par][:, k, 0:W], start=(k == 0), stop=(k == 7))
                    for k in range(8):
                        t.matmul(self.pF2[:, 0:W], self.hq[:, k, 128 * u:128 * u + 128],
                                 self.wub[par][:, k, 0:W], start=(k == 0), stop=(k == 7))
                    t.drain()
                    s_f.inc(t.nop(), 1)
                self.P(f_mm)
                def f_sil(a, W=W):
                    s_f.wait(a)
                    a.activation(self.h2p[:, 0:W], self.pF1[:, 0:W], ACT.Silu)
                    a.drain()
                    s_f.inc(a.nop(), 1)
                self.A(f_sil)
                def pre_mq(v, W=W):
                    v.tensor_tensor(self.h2p[:, 0:W], self.h2p[:, 0:W], self.pF2[:, 0:W], ALU.mult)
                    v.drain()
                self.quant_emit(s_f, self.h2p[:, 0:W],
                                self.h2q[:, u, s * ISHP + c0: s * ISHP + c0 + W], W // 16,
                                pre=pre_mq, post_sems=[(s_gu, 1)])
            self.h_gu.append(s_gu.val)
        for u in range(NU):
            for j0 in range(0, NJ, 8):
                nj = min(8, NJ - j0)
                def f_ht(t, u=u, j0=j0, nj=nj):
                    s_f.wait(t)
                    for i in range(nj):
                        t.transpose(self.pTq[:, 128 * i:128 * i + 128],
                                    self.h2q[:, u, 128 * (j0 + i): 128 * (j0 + i) + 128],
                                    self.eyeb[:, :])
                    t.drain()
                    s_f.inc(t.nop(), 1)
                self.P(f_ht)
                def f_hte(a, u=u, j0=j0, nj=nj):
                    s_f.wait(a)
                    a.activation(self.h2qT[:, j0:j0 + nj, 128 * u:128 * u + 128],
                                 self.pTq[:, 0:128 * nj].rearrange("p (a b) -> p a b", b=128),
                                 ACT.Copy)
                    a.drain()
                    s_f.inc(a.nop(), 1)
                self.A(f_hte)
        s_wdp = [self.sem("s_wdp0"), self.sem("s_wdp1")]
        s_dn = self.sem("s_dn")
        if not hasattr(self, "h_dn"):
            self.h_dn = []
        for d in range(8):
            par = len(self.h_dn) % 2
            free_after = self.h_dn[-2] if len(self.h_dn) >= 2 else None
            def f_wd(g, d=d, par=par, free_after=free_after):
                s_ag2.wait(g)
                if free_after is not None:
                    g.wait_ge(s_dn.h, free_after)
                ap_d = bass.AP(self.g_dT, 128 * d, [[1024, 128], [128 * 1024, NJ], [1, 128]])
                s_wdp[par].inc(g.dma_start(
                    self.wdb[par][:, :, :].rearrange("p a b -> p (a b)"), ap_d), 16)
            self.G_(f_wd)
            def f_dmm(t, d=d, par=par):
                s_f.wait(t)
                s_wdp[par].wait(t)
                for j in range(NJ):
                    t.matmul(self.pD[:, 0:FS], self.wdb[par][:, j, :], self.h2qT[:, j, 0:FS],
                             start=(j == 0), stop=(j == NJ - 1))
                t.drain()
                s_f.inc(t.nop(), 1)
            self.P(f_dmm)
            def f_xu(v, d=d):
                s_f.wait(v)
                v.scalar_tensor_tensor(self.xf[:, d, :], self.pD[:, 0:FS],
                                       self.adaln[:, d, 5:6], self.xf[:, d, :],
                                       ALU.mult, ALU.add)
                v.drain()
                s_f.inc(v.nop(), 1)
                s_dn.inc(v.nop(), 1)
            self.V(f_xu)
            self.h_dn.append(s_dn.val)
        def f_yod(g, w0=w0):
            s_f.wait(g)
            ap = bass.AP(self.o_y, w0, [[8 * self.T, 128], [self.T, 8], [1, FS]])
            self.s_yod.inc(g.dma_start(ap, self.xf[:, :, :]), 16)
        self.G_(f_yod)
        return s_f

    def build_section1(self, sections=1):
        self.declare_global()
        SS, SSH = self.SS, self.SSH
        for name, shape in (("hT", [128, 8 * SSH]), ("mixed", [128, 16 * SSH]),
                            ("qkvs", [128, 16 * SS]), ("dec", [128, SS]), ("dec2", [96, SS]),
                            ("decT", [128, (SS // 128) * 128]), ("decT2", [128, (SS // 128) * 96]),
                            ("stats", [1, self.TH]), ("statsm", [1, self.TH]),
                            ("adaln", [128, 48]), ("kTs", [128, (SS // 128) * KEY]),
                            ("vTs", [128, (SS // 128) * VAL]),
                            ("opre", [128, 8 * self.T]), ("EpT", [128, 8 * self.T]),
                            ("Sst", [64, 16 * DV]), ("Wb", [64, 16 * DK])):
            self.dbg(name, shape)
        with self.nc.Block() as blk:
            self.blk = blk
            with ExitStack() as stkA:
                self.declare_phaseA(stkA)
                if sections >= 2:
                    self.declare_scan_ws(stkA)
                with ExitStack() as stkP0:
                    self.build_p0(stkP0)
                s = self.s_p0
                for ss in range(self.NSS):
                    with ExitStack() as ps1:
                        self.declare_p1_psum(ps1, "_%d" % ss)
                        s, _scv = self.build_p1_superslab(ss, s)
                    if sections >= 2:
                        with ExitStack() as ps2:
                            self.declare_scan_psum(ps2, "_%d" % ss)
                            s = self.build_scan_superslab(ss, s, s)
                self.dump("hT", self.hT[:, :, :].rearrange("p a b -> p (a b)"), s)
                self.dump("qkvs", self.qkvs[:, :, :].rearrange("p a b -> p (a b)"), s)
                self.dump("dec", self.dec[:, :], s)
                self.dump("dec2", self.dec2[:, :], s)
                self.dump("decT", self.decT[:, :, :].rearrange("p a b -> p (a b)"), s)
                self.dump("decT2", self.decT2[:, :, :].rearrange("p a b -> p (a b)"), s)
                self.dump("stats", self.stats[:, :], s)
                self.dump("statsm", self.statsm[32:33, :], s)
                self.dump("adaln", self.adaln[:, :, :].rearrange("p a b -> p (a b)"), s)
                self.dump("kTs", self.kTs[:, :, :].rearrange("p a b -> p (a b)"), s)
                self.dump("vTs", self.vTs[:, :, :].rearrange("p a b -> p (a b)"), s)
                if sections >= 2:
                    if "opre" in self.dbg_bufs or "EpT" in self.dbg_bufs:
                        s_dg = self.sem("s_dmpgate")
                        def f_dg(v):
                            s.wait(v)
                            for sp in self.s_osp + self.s_esp:
                                sp.wait(v)
                            s_dg.inc(v.nop(), 1)
                        self.V(f_dg)
                        self.dump("opre", bass.AP(self.d_opre, 0, [[8 * self.T, 128], [1, 8 * self.T]]), s_dg)
                        self.dump("EpT", bass.AP(self.d_EpT, 0, [[8 * self.T, 128], [1, 8 * self.T]]), s_dg)
                    nch = self.T // CH
                    self.dump("Sst", self.Sst[nch % 2][:, :, :].rearrange("p a b -> p (a b)"), s)
                    self.dump("Wb", self.Wb[nch % 2][:, :, :].rearrange("p a b -> p (a b)"), s)
            s_wqd = s_ag2 = None
            if sections >= 4:
                with ExitStack() as stkW:
                    s_wqd, s_ag2 = self.build_wq(stkW, s)
            if sections >= 3:
                for name, shape in (("ofin", [128, 8 * 256]), ("szs", [128, 8 * 256]),
                                    ("tf", [128, 8 * 256]), ("hgn", [128, 8 * 256]),
                                    ("r16", [16, 256]), ("hTe", [128, 8 * 256])):
                    self.dbg(name, shape)
                with ExitStack() as stkE:
                    self.declare_epi_ws(stkE)
                    self.declare_epi_psum(stkE, "_e")
                    s_wz = self.sem("s_wz")
                    def f_wzd(g):
                        s.wait(g)
                        if s_wqd is not None:
                            s_wqd.wait(g)
                        s_wz.inc(g.dma_start(self.wzT[:, :, :].rearrange("p a b -> p (a b)"),
                                             self.i_wzT[:, :]), 16)
                        s_wz.inc(g.dma_start(self.woutT[:, :, :].rearrange("p a b -> p (a b)"),
                                             self.i_woutT[:, :]), 16)
                    self.G_(f_wzd)
                    s_cc = self.build_ag2(s)
                    s_all = self.sem("s_epi_gate")
                    def f_eg(v):
                        s.wait(v)
                        s_cc.wait(v)
                        self.s_w.wait(v)
                        s_wz.wait(v)
                        for sp in self.s_osp + self.s_esp:
                            sp.wait(v)
                        if s_wqd is not None:
                            s_wqd.wait(v)
                        s_all.inc(v.nop(), 1)
                    self.V(f_eg)
                    s = s_all
                    for sl in range(self.T // 256):
                        s = self.build_epi_slab(sl, s)
                    s_ef = self.sem("s_epi_fin")
                    def f_ef(v):
                        s.wait(v)
                        for sp in self.s_xod:
                            sp.wait(v)
                        s_ef.inc(v.nop(), 1)
                    self.V(f_ef)
                    s = s_ef
            if sections >= 4:
                for name, shape in (("hf", [128, 8 * self.FS]), ("hq", [128, 8 * self.FS]),
                                    ("h2q", [128, self.NU * self.NCORE * self.ISHP]),
                                    ("h2qT", [128, self.NJ * self.FS])):
                    self.dbg(name, shape)
                with ExitStack() as stkF:
                    self.declare_ffn_ws(stkF)
                    self.declare_ffn_psum(stkF, "_f")
                    for u4 in range(self.NFS):
                        s = self.build_ffn_slab(u4, s, s_ag2)
                    for nm in ("hf", "hq", "h2q", "h2qT"):
                        if nm in self.dbg_bufs:
                            self.dump(nm, getattr(self, nm)[:, :, :].rearrange("p a b -> p (a b)"), s)
            if sections == 3 and self.dbg_bufs:
                for nm, tn in (("ofin", "ofin"), ("szs", "szs"), ("tf", "tf"), ("hgn", "hgn"), ("hTe", "hTe")):
                    if nm in self.dbg_bufs:
                        self.dump(nm, getattr(self, tn)[:, :, :].rearrange("p a b -> p (a b)"), s)
                if "r16" in self.dbg_bufs:
                    self.dump("r16", self.r16[:, :], s)
            if sections >= 4:
                def f_yfin(g):
                    s.wait(g)
                    self.s_yod.wait(g)
                self.G_(f_yfin)
            elif sections == 3:
                s_y = self.sem("s_y")
                def f_y3(g):
                    s.wait(g)
                    src = bass.AP(self.d_xp, 0, [[8 * self.T, 128], [1, 8 * self.T]])
                    s_y.inc(g.dma_start(self.o_y[:, :], src), 16)
                    s_y.wait(g)
                self.G_(f_y3)
        self.gstack.close()
        return self.nc


def prep_inputs(core, d, T, ncores=8):
    b, half = core // 2, core % 2
    x = np.asarray(d["L0_attn_hidden_states"][b], np.float32)
    t0 = half * T
    if t0 == 0:
        xs = np.concatenate([np.zeros((3, 1024), np.float32), x[0:T]], 0).T
    else:
        xs = x[t0 - 3: t0 + T].T
    TH = T + 3
    out = {}
    out["x"] = np.ascontiguousarray(xs.reshape(8, 128, TH).transpose(1, 0, 2)).reshape(128, 8 * TH)
    def packT(w):
        wT = np.ascontiguousarray(np.asarray(w, np.float32).T)
        n = wT.shape[1]
        return np.ascontiguousarray(wT.reshape(8, 128, n).transpose(1, 0, 2)).reshape(128, 8 * n)
    out["wqkvT"] = packT(d["L0_attn_W_qkv"])
    wab = np.zeros((64, 1024), np.float32)
    wab[0:16] = np.asarray(d["L0_attn_W_b"])
    wab[32:48] = np.asarray(d["L0_attn_W_a"])
    out["wabT"] = packT(wab)
    out["wzT"] = packT(d["L0_attn_W_z"])
    out["woutT"] = packT(d["L0_attn_W_out"])
    cw = np.asarray(d["L0_attn_conv_w"], np.float32)[:, 0, :]
    out["convw"] = np.ascontiguousarray(cw.reshape(16, 128, 4).transpose(1, 0, 2)).reshape(128, 64)
    wm = np.asarray(d["L0_wmod"], np.float32)
    out["wmod"] = np.ascontiguousarray(wm.reshape(8, 128, 6144).transpose(1, 0, 2)).reshape(128, 8 * 6144)
    out["crow"] = np.ascontiguousarray(np.asarray(d["L0_c"], np.float32)[b].reshape(8, 128).T)
    vec = np.zeros((16, 4), np.float32)
    vec[:, 0] = np.asarray(d["L0_attn_dt_bias"])
    vec[:, 1] = np.asarray(d["L0_attn_A_log"])
    out["vec"] = vec
    nwcol = np.tile(np.asarray(d["L0_attn_norm_weight"], np.float32), 2).reshape(128, 1)
    out["nw"] = np.concatenate([nwcol, np.full((128, 1), float(half), np.float32)], 1)
    r = np.arange(64)
    mA = (r[:, None] > r[None, :])
    mB = (r[:, None] < r[None, :])
    mI = (r[:, None] <= r[None, :])
    bigA = (1e30 * (1 - mA)).astype(np.float32)
    bigB = (1e30 * (1 - mB)).astype(np.float32)
    bigO = (1e30 * (1 - mI) + np.log(8.0)).astype(np.float32)
    m1 = np.concatenate([bigA, bigB, bigO], 1)
    out["masks"] = np.ascontiguousarray(np.concatenate([m1, m1], 0))
    out["eyed"] = np.ascontiguousarray(np.concatenate([np.eye(64), np.eye(64)], 0)).astype(ml_dtypes.bfloat16)
    cv = np.zeros((128, 8), np.float32); cv[:, 0] = 1e-5; cv[:, 1] = 1e-6
    cv[:, 2] = 1.0 if half == 1 else 0.0
    cv[:, 4] = 24.0
    cv[:, 5] = np.log(24.0)
    out["cvec"] = cv
    ISH = 2816 // ncores
    NR = (ISH + 127) // 128
    ISHP = NR * 128
    wg = np.asarray(d["L0_ffn_w_gate"], np.float32)
    wu = np.asarray(d["L0_ffn_w_up"], np.float32)
    wd_ = np.asarray(d["L0_ffn_w_down"], np.float32)
    gsh = np.zeros((ISHP, 1024), np.float32); gsh[0:ISH] = wg[ISH * core: ISH * (core + 1)]
    out["wg_sh"] = np.ascontiguousarray(gsh.reshape(NR, 128, 1024).transpose(1, 0, 2)).reshape(128, NR * 1024)
    ush = np.zeros((ISHP, 1024), np.float32); ush[0:ISH] = wu[ISH * core: ISH * (core + 1)]
    out["wu_sh"] = np.ascontiguousarray(ush.reshape(NR, 128, 1024).transpose(1, 0, 2)).reshape(128, NR * 1024)
    dsh = np.zeros((1024, ISHP), np.float32); dsh[:, 0:ISH] = wd_[:, ISH * core: ISH * (core + 1)]
    out["wd_sh"] = np.ascontiguousarray(dsh.reshape(8, 128, ISHP).transpose(1, 0, 2)).reshape(128, 8 * ISHP)
    gnsel = np.zeros((128, 8, 16), np.float32)
    for h in range(16):
        gnsel[64 * (h % 2):64 * (h % 2) + 64, h // 2, h] = 1.0
    out["gnsel"] = np.ascontiguousarray(gnsel.reshape(128, 128))
    out["gnselT"] = np.ascontiguousarray(gnsel.transpose(2, 1, 0).reshape(16, 1024))
    out["eyeb"] = np.eye(128).astype(ml_dtypes.bfloat16)
    out["eyef"] = np.eye(128, dtype=np.float32)
    out["ones"] = np.ones((128, 129), np.float32)
    return out


def kernel(**inputs):
    T = 4096
    kb = KB(T=T, SS=256, ncores=8)
    nc = kb.build_section1(sections=4)
    in_maps = [prep_inputs(c, inputs, T, ncores=8) for c in range(8)]
    from concourse.bass_utils import run_bass_kernel_spmd
    res = run_bass_kernel_spmd(nc, in_maps, core_ids=list(range(8)))
    out = np.zeros((4, 8192, 1024), np.float32)
    for c in range(8):
        b, half = c // 2, c % 2
        y = np.asarray(res.results[c]["y"]).reshape(128, 8, T)
        out[b, half * T:(half + 1) * T] = y.transpose(2, 1, 0).reshape(T, 1024)
    return out


# version 05
# speedup vs baseline: 1.1981x; 1.0110x over previous
import numpy as np
import ml_dtypes
from contextlib import ExitStack

import concourse.bass as bass
import concourse.mybir as mybir

F32 = mybir.dt.float32
BF16 = mybir.dt.bfloat16
ALU = mybir.AluOpType
ACT = mybir.ActivationFunctionType

NKH, NVH, DK, DV = 8, 16, 64, 64
KEY, VAL, CDIM, KSZ = 512, 1024, 2048, 4
CH = 64


def ts(i, n):
    return slice(i * n, (i + 1) * n)


def register_custom_ops():
    return


class Sem:
    def __init__(self, nc, stack, name):
        self.h = stack.enter_context(nc.semaphore(name))
        self.val = 0
    def inc(self, instr, n=1):
        self.val += n
        instr.then_inc(self.h, n)
        return instr
    def wait(self, eng, val=None):
        eng.wait_ge(self.h, self.val if val is None else val)


class KB:
    def __init__(self, T=2048, SS=256, debug=(), ncores=8):
        register_custom_ops()
        assert T % SS == 0 and SS % CH == 0 and SS % 128 == 0
        self.T, self.SS = T, SS
        self.TH = T + 3
        self.SSH = SS + 3
        self.NSS = T // SS
        self.NCH = T // CH
        self.NCORE = ncores
        self.ISH = 2816 // ncores
        self.NR = (self.ISH + 127) // 128
        self.ISHP = self.NR * 128
        self.NJ = (ncores * self.ISHP) // 128
        self.FS = min(512, T)
        self.NFS = T // self.FS
        self.NU = self.FS // 128
        self.SCROFF = 544
        self.debug = set(debug)
        self.nc = bass.Bass("TRN2", target_bir_lowering=False)
        self.gstack = ExitStack()
        self.dbg_bufs = {}

    def inp(self, name, shape, dtype=F32):
        return self.nc.dram_tensor(name, shape, dtype, kind="ExternalInput")
    def outp(self, name, shape, dtype=F32):
        return self.nc.dram_tensor(name, shape, dtype, kind="ExternalOutput")
    def sb(self, stack, name, shape, dtype):
        return stack.enter_context(self.nc.sbuf_tensor(name, shape, dtype))
    def ps(self, stack, name, shape, dtype=F32):
        return stack.enter_context(self.nc.psum_tensor(name, shape, dtype))
    def sem(self, name):
        if not hasattr(self, "_semcache"):
            self._semcache = {}
        if name not in self._semcache:
            self._semcache[name] = Sem(self.nc, self.gstack, name)
        return self._semcache[name]
    def V(self, fn): self.blk.vector(fn)
    def A(self, fn): self.blk.scalar(fn)
    def P(self, fn): self.blk.tensor(fn)
    def G_(self, fn): self.blk.gpsimd(fn)
    def S_(self, fn): self.blk.sync(fn)

    def dbg(self, name, shape, dtype=F32):
        if name in self.debug:
            self.dbg_bufs[name] = self.outp("dbg_" + name, shape, dtype)

    def dump(self, name, src_ap, s_after):
        if name not in self.dbg_bufs:
            return
        h = self.dbg_bufs[name]
        s_d = self.sem("s_dbg_" + name)
        def f(g):
            s_after.wait(g)
            s_d.inc(g.dma_start(h[:, :], src_ap), 16)
            s_d.wait(g)
        self.G_(f)

    def declare_global(self):
        T, TH, SS, SSH = self.T, self.TH, self.SS, self.SSH
        gs = self.gstack
        self.i_x = self.inp("x", [128, 8 * TH])
        self.i_wqkvT = self.inp("wqkvT", [128, 8 * 2048])
        self.i_wabT = self.inp("wabT", [128, 8 * 64])
        self.i_wzT = self.inp("wzT", [128, 8 * 1024])
        self.i_woutT = self.inp("woutT", [128, 8 * 1024])
        self.i_convw = self.inp("convw", [128, 16 * 4])
        self.i_wmod = self.inp("wmod", [128, 8 * 6144])
        self.i_crow = self.inp("crow", [128, 8])
        self.i_vec = self.inp("vec", [16, 4])
        self.i_nw = self.inp("nw", [128, 2])
        self.i_masks = self.inp("masks", [128, 3 * 64])
        self.i_eyed = self.inp("eyed", [128, 64], BF16)
        self.i_eyeb = self.inp("eyeb", [128, 128], BF16)
        self.i_eyef = self.inp("eyef", [128, 128])
        self.i_ones = self.inp("ones", [128, 129])
        self.i_cvec = self.inp("cvec", [128, 8])
        self.i_gnsel = self.inp("gnsel", [128, 8 * 16])
        self.i_gnselT = self.inp("gnselT", [16, 8 * 128])
        self.i_wg = self.inp("wg_sh", [128, self.NR * 1024])
        self.i_wu = self.inp("wu_sh", [128, self.NR * 1024])
        self.i_wd = self.inp("wd_sh", [128, 8 * self.ISHP])
        self.o_y = self.outp("y", [128, 8 * T])
        self.dbc = self.nc.dram_tensor("dbc", [8192], F32)
        self.cc_in = self.nc.dram_tensor("cc_in", [64, 1024], F32)
        self.cc_out = self.nc.dram_tensor("cc_out", [128, 1024], F32)
        self.d_xp = self.nc.dram_tensor("d_xp", [128, 8 * T], F32)
        self.d_opre = self.nc.dram_tensor("d_opre", [128, 8 * T], BF16)
        self.d_EpT = self.nc.dram_tensor("d_EpT", [128, 8 * T], BF16)
        PB = self.ISHP
        self.cc_g_in = self.nc.dram_tensor("cc_g_in", [1024, PB], BF16)
        self.cc_u_in = self.nc.dram_tensor("cc_u_in", [1024, PB], BF16)
        self.cc_d_in = self.nc.dram_tensor("cc_d_in", [PB, 1024], BF16)
        self.g_gT = self.nc.dram_tensor("g_gT", [self.NCORE * 1024, PB], BF16)
        self.g_uT = self.nc.dram_tensor("g_uT", [self.NCORE * 1024, PB], BF16)
        self.g_dT = self.nc.dram_tensor("g_dT", [self.NCORE * PB, 1024], BF16)
        self.Sst = [self.sb(gs, "g_S0", [64, 16, DV], F32),
                    self.sb(gs, "g_S1", [64, 16, DV], F32)]
        self.Wb = [self.sb(gs, "g_W0", [64, 16, DK], BF16),
                   self.sb(gs, "g_W1", [64, 16, DK], BF16)]
        self.statpack = self.sb(gs, "g_stats", [33, TH], F32)
        self.stats = self.statpack
        self.statsm = self.statpack
        self.convw = self.sb(gs, "g_convw", [128, 16, 4], F32)
        self.crow = self.sb(gs, "g_crow", [128, 8], F32)
        self.vecs = self.sb(gs, "g_vecs", [16, 4], F32)
        self.nw = self.sb(gs, "g_nw", [128, 2], F32)
        self.masks = self.sb(gs, "g_masks", [128, 3 * 64], F32)
        self.eyed = self.sb(gs, "g_eyed", [128, 64], BF16)
        self.eyeb = self.sb(gs, "g_eyeb", [128, 128], BF16)
        self.eyef = self.sb(gs, "g_eyef", [128, 128], F32)
        self.onesf = self.sb(gs, "g_onesf", [128, 129], F32)
        self.cvec = self.sb(gs, "g_cvec", [128, 8], F32)
        self.gnsel = self.sb(gs, "g_gnsel", [128, 8, 16], F32)
        self.gnselT = self.sb(gs, "g_gnselT", [16, 8, 128], F32)
        self.adaln = self.sb(gs, "g_adaln", [128, 8, 6], F32)
        self.negea = self.sb(gs, "g_negea", [16, 1], F32)
        self.scanmask = self.sb(gs, "g_scanmask", [16, SS], F32)
        self.scr = self.sb(gs, "g_scr", [128, 1088], F32)
        self.scrb = self.sb(gs, "g_scrb", [128, 576], BF16)

    def declare_phaseA(self, stk):
        SS, SSH = self.SS, self.SSH
        self.xs = self.sb(stk, "a_xs", [128, 8, SSH], F32)
        self.wqb = [self.sb(stk, "a_wq0", [128, 8, 128], BF16),
                    self.sb(stk, "a_wq1", [128, 8, 128], BF16)]
        self.wabT = self.sb(stk, "a_wabT", [128, 8, 64], BF16)
        self.hT = self.sb(stk, "a_hT", [128, 8, SSH], BF16)
        self.mixed = self.sb(stk, "a_mixed", [128, 16, SSH], F32)
        self.cv1 = self.sb(stk, "a_cv1", [128, SS], F32)
        self.cv2 = self.sb(stk, "a_cv2", [128, SS], F32)
        self.qkvs = self.sb(stk, "a_qkvs", [128, 16, SS], BF16)
        self.kTs = self.sb(stk, "a_kTs", [128, SS // 128, KEY], BF16)
        self.vTs = self.sb(stk, "a_vTs", [128, SS // 128, VAL], BF16)
        self.dec = self.sb(stk, "a_dec", [128, SS], F32)
        self.dec2 = self.sb(stk, "a_dec2", [96, SS], F32)
        self.decT = self.sb(stk, "a_decT", [128, SS // 128, 128], F32)
        self.decT2 = self.sb(stk, "a_decT2", [128, SS // 128, 96], F32)
        self.sA = self.sb(stk, "a_sA", [16, SS], F32)
        self.sB = self.sb(stk, "a_sB", [16, SS], F32)
        self.sC = self.sb(stk, "a_sC", [16, SS], F32)
        self.r_Gp = self.dec[0:16, :]
        self.r_G = self.dec[32:48, :]
        self.r_lamq = self.dec[64:80, :]
        self.r_blam = self.dec[96:112, :]
        self.r_bet = self.dec2[0:16, :]
        self.r_erev = self.dec2[32:48, :]
        self.r_lamr = self.dec2[64:80, :]

    def build_p0(self, stk):
        s_in = self.s_in = self.sem("s_in")
        s_w = self.s_w = self.sem("s_w")
        s_msk = self.s_msk = self.sem("s_msk")
        s_p0 = self.s_p0 = self.sem("s_p0")
        wmbuf = self.sb(stk, "p0_wmod", [128, 6144], BF16)
        modacc = self.sb(stk, "p0_modacc", [128, 48], F32)
        self.pR1 = self.ps(stk, "p_p0mod", [128, 512])

        def f_sync(s):
            for dst, src in ((self.convw, self.i_convw),
                             (self.vecs, self.i_vec), (self.nw, self.i_nw),
                             (self.masks, self.i_masks), (self.eyed, self.i_eyed), (self.eyeb, self.i_eyeb),
                             (self.eyef, self.i_eyef), (self.onesf, self.i_ones),
                             (self.cvec, self.i_cvec), (self.gnsel, self.i_gnsel), (self.gnselT, self.i_gnselT),
                             (self.crow, self.i_crow)):
                d = dst[:, :, :].rearrange("p a b -> p (a b)") if len(dst.shape) == 3 else dst[:, :]
                s_in.inc(s.dma_start(d, src[:, :]), 16)
        self.S_(f_sync)

        def f_g(g):
            s_w.inc(g.dma_start(self.wabT[:, :, :].rearrange("p a b -> p (a b)"),
                                self.i_wabT[:, :]), 16)
            g.memset(self.scanmask[:, :], 1.0)
            g.drain()
            g.memset(bass.AP(self.scanmask, 0, [[self.SS, 16], [CH, self.SS // CH]]), 0.0)
            g.drain()
            g.memset(self.dec[:, :], 0.0)
            g.drain()
            g.memset(self.dec2[:, :], 0.0)
            g.drain()
            g.memset(self.Sst[0][:, :, :].rearrange("p a b -> p (a b)"), 0.0)
            g.drain()
            s_msk.inc(g.nop(), 1)
        self.G_(f_g)

        def f_wi(v):
            s_in.wait(v)
            for j in range(16):
                v.tensor_copy(self.Wb[0][:, j, :], self.eyed[0:64, :])
            v.drain()
            s_msk.inc(v.nop(), 1)
        self.V(f_wi)
        def f_a(a):
            s_in.wait(a)
            a.activation(self.scr[:, 0:8], self.crow[:, :], ACT.Silu)
            a.drain()
            s_p0.inc(a.nop(), 1)
        self.A(f_a)
        def f_v(v):
            s_p0.wait(v)
            v.tensor_copy(self.scrb[:, 0:8], self.scr[:, 0:8])
            v.drain()
            s_p0.inc(v.nop(), 1)
        self.V(f_v)

        s_wmd = self.sem("s_wmd")
        s_wmc = self.sem("s_wmc")
        for k in range(8):
            def f_gk(g, k=k):
                if k > 0:
                    g.wait_ge(s_wmc.h, 2 * k - 1)
                s_wmd.inc(g.dma_start(wmbuf[:, :], self.i_wmod[:, ts(k, 6144)]), 16)
            self.G_(f_gk)
            def f_tk(t, k=k):
                s_wmd.wait(t)
                s_p0.wait(t, 2)
                if k > 0:
                    t.wait_ge(s_wmc.h, 2 * k)
                for j in range(48):
                    t.matmul(self.pR1[:, j:j + 1], wmbuf[:, ts(j, 128)], self.scrb[:, k:k + 1],
                             start=True, stop=True)
                t.drain()
                s_wmc.inc(t.nop(), 1)
            self.P(f_tk)
            def f_vk(v, k=k):
                s_wmc.wait(v)
                if k == 0:
                    v.tensor_copy(modacc[:, :], self.pR1[:, 0:48])
                else:
                    v.tensor_tensor(modacc[:, :], modacc[:, :], self.pR1[:, 0:48], ALU.add)
                v.drain()
                s_wmc.inc(v.nop(), 1)
            self.V(f_vk)
        s_wm = s_wmc
        def f_v2(v):
            s_wm.wait(v)
            v.tensor_copy(self.adaln[:, :, :].transpose([0, 2, 1]), modacc[:, :])
            v.drain()
            v.tensor_scalar(self.adaln[:, :, 1], self.adaln[:, :, 1], 1.0, None, ALU.add)
            v.tensor_scalar(self.adaln[:, :, 4], self.adaln[:, :, 4], 1.0, None, ALU.add)
            v.drain()
            s_p0.inc(v.nop(), 1)
        self.V(f_v2)
        def f_a2(a):
            s_p0.wait(a)
            a.activation(self.scr[:, 0:8], self.adaln[:, :, 2], ACT.Abs)
            a.activation(self.scr[:, 8:16], self.adaln[:, :, 5], ACT.Abs)
            a.activation(self.scr[:, 16:24], self.adaln[:, :, 2], ACT.Sign)
            a.activation(self.scr[:, 24:32], self.adaln[:, :, 5], ACT.Sign)
            a.activation(self.negea[:, 0:1], self.vecs[:, 1:2], ACT.Exp)
            a.drain()
            s_p0.inc(a.nop(), 1)
        self.A(f_a2)
        def f_v3(v):
            s_p0.wait(v)
            v.tensor_scalar(self.scr[:, 0:16], self.scr[:, 0:16], 0.25, None, ALU.max)
            v.tensor_scalar(self.negea[:, 0:1], self.negea[:, 0:1], -1.0, None, ALU.mult)
            v.drain()
            v.tensor_tensor(self.adaln[:, :, 2], self.scr[:, 0:8], self.scr[:, 16:24], ALU.mult)
            v.tensor_tensor(self.adaln[:, :, 5], self.scr[:, 8:16], self.scr[:, 24:32], ALU.mult)
            v.drain()
            s_p0.inc(v.nop(), 1)
        self.V(f_v3)

    def ln_apply(self, which, w0, n, s_pre, sq_scratch, stats_r, stats_m, dst, halo_mask,
                 xsrc, x0):
        ai_s, ai_c = (0, 1) if which == 0 else (3, 4)
        s_loc = self.sem("s_ln%d" % which)
        def f_sq(a):
            s_pre.wait(a)
            for d in range(8):
                a.activation(sq_scratch[:, d, 0:n], xsrc[:, d, x0:x0 + n], ACT.Square)
            a.drain()
            s_loc.inc(a.nop(), 1)
        self.A(f_sq)
        def f_mm(t):
            s_loc.wait(t)
            s_pre.wait(t)
            for d in range(8):
                t.matmul(self.pS[0:1, 0:n], self.onesf[:, 0:1], xsrc[:, d, x0:x0 + n],
                         start=(d == 0), stop=(d == 7))
            for d in range(8):
                t.matmul(self.pQ[0:1, 0:n], self.onesf[:, 0:1], sq_scratch[:, d, 0:n],
                         start=(d == 0), stop=(d == 7))
            t.drain()
            s_loc.inc(t.nop(), 1)
        self.P(f_mm)
        def f_vst(v):
            s_loc.wait(v)
            v.tensor_scalar(self.scr[0:1, 0:n], self.pS[0:1, 0:n], 1.0 / 1024.0, None, ALU.mult)
            v.drain()
            v.tensor_tensor(self.scr[0:1, 0:n], self.scr[0:1, 0:n], self.scr[0:1, 0:n], ALU.mult)
            v.drain()
            v.scalar_tensor_tensor(self.scr[0:1, self.SCROFF:self.SCROFF + n], self.pQ[0:1, 0:n], 1.0 / 1024.0,
                                   self.scr[0:1, 0:n], ALU.mult, ALU.subtract)
            v.drain()
            s_loc.inc(v.nop(), 1)
        self.V(f_vst)
        def f_ast(a):
            s_loc.wait(a)
            a.activation(self.scr[0:1, 0:n], self.scr[0:1, self.SCROFF:self.SCROFF + n], ACT.Ln, bias=self.cvec[0:1, 0:1])
            a.drain()
            a.activation(stats_r[0:1, w0:w0 + n], self.scr[0:1, 0:n], ACT.Exp, scale=-0.5)
            a.drain()
            s_loc.inc(a.nop(), 1)
        self.A(f_ast)
        def f_vst2(v):
            s_loc.wait(v)
            v.scalar_tensor_tensor(stats_m[32:33, w0:w0 + n], self.pS[0:1, 0:n], 1.0 / 1024.0,
                                   stats_r[0:1, w0:w0 + n], ALU.mult, ALU.mult)
            v.drain()
            s_loc.inc(v.nop(), 1)
        self.V(f_vst2)
        self.ln_rep_apply(which, w0, n, s_loc, stats_r, stats_m, dst, halo_mask, xsrc, x0)
        return s_loc

    def ln_rep_apply(self, which, w0, n, s_loc, stats_r, stats_m, dst, halo_mask, xsrc, x0):
        ai_s, ai_c = (0, 1) if which == 0 else (3, 4)
        def f_rep(t):
            s_loc.wait(t)
            t.matmul(self.pR1[:, 0:n], self.onesf[0:1, 0:128], stats_r[0:1, w0:w0 + n],
                     start=True, stop=True)
            t.drain()
            t.matmul(self.pR2[:, 0:n], self.onesf[32:33, 0:128], stats_m[32:33, w0:w0 + n],
                     start=True, stop=True)
            t.drain()
            s_loc.inc(t.nop(), 1)
        self.P(f_rep)
        def f_h(v):
            s_loc.wait(v)
            for d in range(8):
                v.tensor_tensor(self.scr[:, 0:n], xsrc[:, d, x0:x0 + n], self.pR1[:, 0:n], ALU.mult)
                v.drain()
                v.tensor_tensor(self.scr[:, self.SCROFF:self.SCROFF + n], self.scr[:, 0:n], self.pR2[:, 0:n], ALU.subtract)
                v.drain()
                v.tensor_scalar(dst[:, d, 0:n], self.scr[:, self.SCROFF:self.SCROFF + n],
                                self.adaln[:, d, ai_c:ai_c + 1], self.adaln[:, d, ai_s:ai_s + 1],
                                ALU.mult, ALU.add)
                v.drain()
            if halo_mask:
                for d in range(8):
                    v.tensor_scalar(dst[:, d, 0:3], dst[:, d, 0:3], self.nw[:, 1:2], None, ALU.mult)
                v.drain()
            s_loc.inc(v.nop(), 1)
        self.V(f_h)

    def build_p1_superslab(self, ss, s_pre):
        T, SS, SSH = self.T, self.SS, self.SSH
        w0 = ss * SS
        s_xs = self.sem("s_xs")
        def f_xd(g):
            s_pre.wait(g)
            ap_x = bass.AP(self.i_x, w0, [[8 * self.TH, 128], [self.TH, 8], [1, SSH]])
            s_xs.inc(g.dma_start(self.xs[:, :, :], ap_x), 16)
        self.G_(f_xd)
        s_g0 = self.sem("s_lng")
        def f_xg(v):
            s_pre.wait(v)
            s_xs.wait(v)
            s_g0.inc(v.nop(), 1)
        self.V(f_xg)
        s_ln = self.ln_apply(0, w0, SSH, s_g0, self.mixed, self.stats, self.statsm, self.hT,
                             halo_mask=(ss == 0), xsrc=self.xs, x0=0)
        s_qkv = self.sem("s_qkv")
        s_cv = self.sem("s_cv")
        s_wqp = [self.sem("s_wqa"), self.sem("s_wqb")]
        q_base, cv_base = s_qkv.val, s_cv.val
        for chn in range(16):
            pp = self.pA if chn % 2 == 0 else self.pB
            wb = self.wqb[chn % 2]
            swq = s_wqp[chn % 2]
            def f_wl(g, chn=chn, wb=wb, swq=swq):
                if chn >= 2:
                    g.wait_ge(s_qkv.h, q_base + chn - 1)
                swq.inc(g.dma_start(wb[:, :, :].rearrange("p a b -> p (a b)"),
                                    self.i_wqkvT[:, :].rearrange("p (a b) -> p a b", b=2048)[:, :, ts(chn, 128)]),
                        16)
            self.G_(f_wl)
            def f_mm(t, chn=chn, pp=pp, wb=wb, swq=swq):
                s_ln.wait(t)
                swq.wait(t)
                if chn >= 2:
                    s_cv.wait(t, cv_base + (chn - 2) * 2 + 1)
                for k in range(8):
                    t.matmul(pp[:, 0:SSH], wb[:, k, :], self.hT[:, k, :],
                             start=(k == 0), stop=(k == 7))
                t.drain()
                s_qkv.inc(t.nop(), 1)
            self.P(f_mm)
            def f_cv(v, chn=chn, pp=pp):
                s_qkv.wait(v)
                s_cv.wait(v)
                cw = self.convw
                v.tensor_scalar(self.cv1[:, 0:SS], pp[:, 0:SS],
                                cw[:, chn, 0:1], None, ALU.mult)
                v.drain()
                v.scalar_tensor_tensor(self.cv2[:, 0:SS], pp[:, 1:1 + SS],
                                       cw[:, chn, 1:2], self.cv1[:, 0:SS], ALU.mult, ALU.add)
                v.drain()
                v.scalar_tensor_tensor(self.cv1[:, 0:SS], pp[:, 2:2 + SS],
                                       cw[:, chn, 2:3], self.cv2[:, 0:SS], ALU.mult, ALU.add)
                v.drain()
                v.scalar_tensor_tensor(self.cv2[:, 0:SS], pp[:, 3:3 + SS],
                                       cw[:, chn, 3:4], self.cv1[:, 0:SS], ALU.mult, ALU.add)
                v.drain()
                s_cv.inc(v.nop(), 1)
            self.V(f_cv)
            def f_si(a, chn=chn):
                s_cv.wait(a)
                a.activation(self.qkvs[:, chn, :], self.cv2[:, 0:SS], ACT.Silu)
                a.drain()
                s_cv.inc(a.nop(), 1)
            self.A(f_si)
        s_ab = self.sem("s_ab")
        def f_ab(t):
            s_ln.wait(t)
            self.s_w.wait(t)
            for k in range(8):
                t.matmul(self.pR1[0:64, 0:SS], self.wabT[:, k, :], self.hT[:, k, 3:3 + SS],
                         start=(k == 0), stop=(k == 7))
            t.drain()
            s_ab.inc(t.nop(), 1)
        self.P(f_ab)
        def f_d0(v):
            s_ab.wait(v)
            v.tensor_scalar(self.sA[:, 0:SS], self.pR1[32:48, 0:SS], self.vecs[:, 0:1], None, ALU.add)
            v.drain()
            s_ab.inc(v.nop(), 1)
        self.V(f_d0)
        def f_d1(a):
            s_ab.wait(a)
            a.activation(self.r_bet[:, :], self.pR1[0:16, 0:SS], ACT.Sigmoid)
            a.drain()
            a.activation(self.sB[:, 0:SS], self.sA[:, 0:SS], ACT.Abs)
            a.drain()
            a.activation(self.sB[:, 0:SS], self.sB[:, 0:SS], ACT.Exp, scale=-1.0)
            a.drain()
            s_ab.inc(a.nop(), 1)
        self.A(f_d1)
        def f_d2(v):
            s_ab.wait(v)
            v.tensor_scalar(self.sB[:, 0:SS], self.sB[:, 0:SS], 1.0, None, ALU.add)
            v.drain()
            s_ab.inc(v.nop(), 1)
        self.V(f_d2)
        def f_d3(a):
            s_ab.wait(a)
            a.activation(self.sB[:, 0:SS], self.sB[:, 0:SS], ACT.Ln)
            a.drain()
            s_ab.inc(a.nop(), 1)
        self.A(f_d3)
        def f_d4(v):
            s_ab.wait(v)
            v.tensor_scalar(self.sC[:, 0:SS], self.sA[:, 0:SS], 0.0, None, ALU.max)
            v.drain()
            v.tensor_tensor(self.sC[:, 0:SS], self.sC[:, 0:SS], self.sB[:, 0:SS], ALU.add)
            v.drain()
            v.tensor_scalar(self.sC[:, 0:SS], self.sC[:, 0:SS], self.negea[:, 0:1], None, ALU.mult)
            v.drain()
            s_ab.inc(v.nop(), 1)
        self.V(f_d4)
        def f_d5(a):
            s_ab.wait(a)
            a.activation(self.sA[:, 0:SS], self.pR1[0:16, 0:SS], ACT.Abs)
            a.drain()
            a.activation(self.sA[:, 0:SS], self.sA[:, 0:SS], ACT.Exp, scale=-1.0)
            a.drain()
            s_ab.inc(a.nop(), 1)
        self.A(f_d5)
        def f_d6(v):
            s_ab.wait(v)
            v.tensor_scalar(self.sA[:, 0:SS], self.sA[:, 0:SS], 1.0, None, ALU.add)
            v.drain()
            s_ab.inc(v.nop(), 1)
        self.V(f_d6)
        def f_d7(a):
            s_ab.wait(a)
            a.activation(self.sA[:, 0:SS], self.sA[:, 0:SS], ACT.Ln)
            a.drain()
            s_ab.inc(a.nop(), 1)
        self.A(f_d7)
        def f_d8(v):
            s_ab.wait(v)
            self.s_msk.wait(v)
            v.tensor_scalar(self.sB[:, 0:SS], self.pR1[0:16, 0:SS], -1.0, 0.0, ALU.mult, ALU.max)
            v.drain()
            v.scalar_tensor_tensor(self.sB[:, 0:SS], self.sB[:, 0:SS], -1.0, self.sA[:, 0:SS],
                                   ALU.mult, ALU.subtract)
            v.drain()
            v.tensor_tensor_scan(self.sA[:, 0:SS], self.scanmask[:, :], self.sC[:, 0:SS],
                                 0.0, ALU.mult, ALU.add)
            v.drain()
            v.tensor_copy(self.r_G[:, :], self.sA[:, 0:SS])
            v.drain()
            v.tensor_tensor(self.r_Gp[:, :], self.sA[:, 0:SS], self.sB[:, 0:SS], ALU.add)
            v.drain()
            s_ab.inc(v.nop(), 1)
        self.V(f_d8)
        def f_d9(a):
            s_ab.wait(a)
            a.activation(self.sC[:, 0:SS], self.sA[:, 0:SS], ACT.Exp)
            a.drain()
            s_ab.inc(a.nop(), 1)
        self.A(f_d9)
        def f_d10(v):
            s_ab.wait(v)
            v.tensor_copy(self.r_lamr[:, :], self.sC[:, 0:SS])
            v.drain()
            v.tensor_tensor(self.r_blam[:, :], self.r_bet[:, :], self.sC[:, 0:SS], ALU.mult)
            v.drain()
            v.tensor_scalar(self.r_lamq[:, :], self.sC[:, 0:SS], 0.125, None, ALU.mult)
            v.drain()
            gend = bass.AP(self.dec, 32 * SS + CH - 1, [[SS, 16], [CH, SS // CH], [0, CH]])
            v.tensor_tensor(self.sA[:, 0:SS].rearrange("p (c u) -> p c u", u=CH), gend,
                            self.r_G[:, :].rearrange("p (c u) -> p c u", u=CH), ALU.subtract)
            v.drain()
            s_ab.inc(v.nop(), 1)
        self.V(f_d10)
        def f_d11(a):
            s_ab.wait(a)
            a.activation(self.r_erev[:, :], self.sA[:, 0:SS], ACT.Exp)
            a.drain()
            s_ab.inc(a.nop(), 1)
        self.A(f_d11)
        s_tr = self.sem("s_tr")
        ntt = SS // 128
        for tt in range(ntt):
            def f_tp1(t, tt=tt):
                s_ab.wait(t)
                self.s_msk.wait(t)
                if s_tr.val:
                    s_tr.wait(t)
                t.transpose(self.pT[:, 0:128], self.dec[:, tt * 128:(tt + 1) * 128],
                            self.eyef[:, :])
                t.drain()
                s_tr.inc(t.nop(), 1)
            self.P(f_tp1)
            def f_te1(a, tt=tt):
                s_tr.wait(a)
                a.activation(self.decT[:, tt, :], self.pT[:, 0:128], ACT.Copy)
                a.drain()
                s_tr.inc(a.nop(), 1)
            self.A(f_te1)
            def f_tp1b(t, tt=tt):
                s_tr.wait(t)
                t.transpose(self.pT[:, 0:96], self.dec2[:, tt * 128:(tt + 1) * 128],
                            self.eyef[0:96, 0:96])
                t.drain()
                s_tr.inc(t.nop(), 1)
            self.P(f_tp1b)
            def f_te1b(a, tt=tt):
                s_tr.wait(a)
                a.activation(self.decT2[:, tt, :], self.pT[:, 0:96], ACT.Copy)
                a.drain()
                s_tr.inc(a.nop(), 1)
            self.A(f_te1b)
        for tt in range(ntt):
            for rr0, nrr in ((0, 4), (4, 4), (8, 4)):
                def f_tp2(t, rr0=rr0, nrr=nrr, tt=tt):
                    s_cv.wait(t)
                    s_tr.wait(t)
                    for i in range(nrr):
                        t.transpose(self.pTb[:, 128 * i:128 * i + 128],
                                    self.qkvs[:, 4 + rr0 + i, tt * 128:(tt + 1) * 128],
                                    self.eyeb[:, :])
                    t.drain()
                    s_tr.inc(t.nop(), 1)
                self.P(f_tp2)
                def f_te2(v, rr0=rr0, nrr=nrr, tt=tt):
                    s_tr.wait(v)
                    if rr0 == 0:
                        dst = self.kTs[:, tt, 0:512]
                    elif rr0 == 4:
                        dst = self.vTs[:, tt, 0:512]
                    else:
                        dst = self.vTs[:, tt, 512:1024]
                    v.tensor_copy(dst, self.pTb[:, 0:128 * nrr])
                    v.drain()
                    s_tr.inc(v.nop(), 1)
                self.V(f_te2)
        s_done = self.sem("s_ss")
        def f_fin(v):
            s_tr.wait(v)
            s_cv.wait(v)
            s_ab.wait(v)
            s_done.inc(v.nop(), 1)
        self.V(f_fin)
        return s_done, s_cv


    def declare_scan_ws(self, stk):
        self.Grep = self.sb(stk, "c_Grep", [128, 1024], F32)
        self.Gprep = self.sb(stk, "c_Gprep", [128, 1024], F32)
        self.lamrep = self.sb(stk, "c_lamrep", [128, 2048], F32)
        self.lamCrep = self.sb(stk, "c_lamCrep", [64, 32], F32)
        self.Dw = self.sb(stk, "c_Dw", [128, 1024], F32)
        self.Aw = self.sb(stk, "c_Aw", [128, 1024], BF16)
        self.Bw = self.sb(stk, "c_Bw", [128, 1024], BF16)
        self.QKDT = self.sb(stk, "c_QKDT", [128, 1024], BF16)
        self.Pf = self.sb(stk, "c_Pf", [128, 1024], BF16)
        self.PfT = self.sb(stk, "c_PfT", [128, 1024], BF16)
        self.Pg = self.sb(stk, "c_Pg", [128, 1024], BF16)
        self.PgT = self.sb(stk, "c_PgT", [128, 1024], BF16)
        self.IPb = self.sb(stk, "c_IP", [128, 1024], BF16)
        self.Yc = self.sb(stk, "c_Yc", [128, 1024], BF16)
        self.Yn = self.sb(stk, "c_Yn", [128, 1024], BF16)
        self.Rw = self.sb(stk, "c_Rw", [128, 2048], BF16)
        self.Qt = self.sb(stk, "c_Qt", [64, 1024], F32)
        self.KbT = self.sb(stk, "c_KbT", [128, 1024], BF16)
        self.Uvp = self.sb(stk, "c_Uvp", [128, 1024], BF16)
        self.Ukp = self.sb(stk, "c_Ukp", [128, 1024], BF16)
        self.nUkT = self.sb(stk, "c_nUkT", [64, 1024], F32)
        self.nMT = self.sb(stk, "c_nMT", [64, 1024], BF16)
        self.Et = self.sb(stk, "c_Et", [64, 1024], BF16)
        self.Ssc = self.sb(stk, "c_Ssc", [64, 1024], F32)
        self.Wsc = self.sb(stk, "c_Wsc", [64, 1024], BF16)
        self.Ucp = self.sb(stk, "c_Ucp", [128, 1024], BF16)
        self.opre_st = [self.sb(stk, "c_ost0", [128, 8, CH], BF16),
                        self.sb(stk, "c_ost1", [128, 8, CH], BF16)]
        self.EpT_st = [self.sb(stk, "c_est0", [128, 8, CH], BF16),
                       self.sb(stk, "c_est1", [128, 8, CH], BF16)]
        self.s_osp = [self.sem("s_osp0"), self.sem("s_osp1")]
        self.s_esp = [self.sem("s_esp0"), self.sem("s_esp1")]

    def declare_scan_psum(self, stk, tag):
        self.pX = self.ps(stk, "sp_X" + tag, [128, 1024])
        self.pY = self.ps(stk, "sp_Y" + tag, [128, 1024])
        self.pZ = self.ps(stk, "sp_Z" + tag, [128, 1024])
        self.pV = self.ps(stk, "sp_V" + tag, [64, 1024])

    def declare_p1_psum(self, stk, tag):
        self.pS = self.ps(stk, "p_S" + tag, [1, 512])
        self.pQ = self.ps(stk, "p_Q" + tag, [1, 512])
        self.pR1 = self.ps(stk, "p_R1" + tag, [128, 512])
        self.pR2 = self.ps(stk, "p_R2" + tag, [128, 512])
        self.pA = self.ps(stk, "p_A" + tag, [128, 512])
        self.pB = self.ps(stk, "p_B" + tag, [128, 512])
        self.pT = self.ps(stk, "p_T" + tag, [128, 128])
        self.pTb = self.ps(stk, "p_Tb" + tag, [128, 512], BF16)

    def q_src(self, h, cols):
        j = h // 2
        return self.qkvs[64 * (j % 2):64 * (j % 2) + 64, j // 2, cols]
    def k_src(self, h, cols):
        j = h // 2
        return self.qkvs[64 * (j % 2):64 * (j % 2) + 64, 4 + j // 2, cols]

    @staticmethod
    def hcol(h):
        return (h % 2) * 8 + h // 2

    def build_scan_pair(self, ss, tp, s_p1, s_prev):
        SS = self.SS
        tt = tp
        gc0 = ss * (SS // CH) + 2 * tp
        s_fd = self.sem("s_sfd")
        s_sc = self.sem("s_scc")

        def f_bd(g):
            s_p1.wait(g)
            s_prev.wait(g)
            for half in (0, 1):
                cc = slice(tp * 128 + 64 * half, tp * 128 + 64 * half + 64)
                s_fd.inc(g.dma_start(self.dbc[1024 * half:1024 * half + 1024], self.r_G[:, cc]), 16)
                s_fd.inc(g.dma_start(self.dbc[2048 + 1024 * half:3072 + 1024 * half], self.r_Gp[:, cc]), 16)
                s_fd.inc(g.dma_start(self.dbc[4096 + 1024 * half:5120 + 1024 * half], self.r_lamq[:, cc]), 16)
            lamc_src = bass.AP(self.dec2, 64 * SS + tp * 128 + CH - 1, [[SS, 16], [CH, 2]])
            s_fd.inc(g.dma_start(self.dbc[6144:6176], lamc_src, allow_slow_non_contiguous=True), 16)
            s_fd.wait(g)
            for half in (0, 1):
                sl = slice(64 * half, 64 * half + 64)
                s_fd.inc(g.dma_start(self.Grep[sl, :].rearrange("p (a b) -> p a b", b=64),
                                     bass.AP(self.dbc, 1024 * half, [[0, 64], [64, 16], [1, 64]])), 16)
                s_fd.inc(g.dma_start(self.Gprep[sl, :].rearrange("p (a b) -> p a b", b=64),
                                     bass.AP(self.dbc, 2048 + 1024 * half, [[0, 64], [64, 16], [1, 64]])), 16)
                s_fd.inc(g.dma_start(self.lamrep[:, 1024 * half:1024 * half + 1024].rearrange("p (a b) -> p a b", b=64),
                                     bass.AP(self.dbc, 4096 + 1024 * half, [[0, 128], [64, 16], [1, 64]])), 16)
            s_fd.inc(g.dma_start(self.lamCrep[:, :].rearrange("p (a b) -> p a b", b=2),
                                 bass.AP(self.dbc, 6144, [[0, 64], [2, 16], [1, 2]])), 16)
        self.G_(f_bd)
        if getattr(self, "CUT", None) == 0:
            s_cut = self.sem("s_scd")
            def f_cut(v):
                s_fd.wait(v)
                s_cut.inc(v.nop(), 1)
            self.V(f_cut)
            return s_cut

        def f_kk(t):
            s_p1.wait(t)
            s_prev.wait(t)
            for jpar in range(2):
                for cl in range(2):
                    cc = slice(tp * 128 + 64 * cl, tp * 128 + 64 * cl + 64)
                    for h in range(16):
                        if (h // 2) % 2 != jpar:
                            continue
                        t.matmul(self.pZ[64 * cl:64 * cl + 64, ts(h, 64)],
                                 self.k_src(h, cc), self.k_src(h, cc), start=True, stop=True)
                    t.drain()
            s_sc.inc(t.nop(), 1)
        self.P(f_kk)
        if getattr(self, "CUT", None) == 1:
            s_cut = self.sem("s_scd")
            def f_cut(v):
                s_sc.wait(v)
                s_fd.wait(v)
                s_cut.inc(v.nop(), 1)
            self.V(f_cut)
            return s_cut
        gpcol = self.decT[:, tt, 0:16].unsqueeze(2).broadcast_to([128, 16, 64])
        gcol = self.decT[:, tt, 32:48].unsqueeze(2).broadcast_to([128, 16, 64])
        mA = self.masks[:, 0:64].unsqueeze(1).broadcast_to([128, 16, 64])
        mB = self.masks[:, 64:128].unsqueeze(1).broadcast_to([128, 16, 64])
        mO = self.masks[:, 128:192].unsqueeze(1).broadcast_to([128, 16, 64])
        Dw3 = self.Dw[:, :].rearrange("p (a b) -> p a b", b=64)
        def f_da(v):
            s_fd.wait(v)
            s_sc.wait(v)
            v.scalar_tensor_tensor(Dw3, self.Grep[:, :].rearrange("p (a b) -> p a b", b=64),
                                   -1.0, gpcol, ALU.mult, ALU.add)
            v.drain()
            v.tensor_tensor(Dw3, Dw3, mA, ALU.subtract)
            v.drain()
            s_sc.inc(v.nop(), 1)
        self.V(f_da)
        def f_dae(a):
            s_sc.wait(a)
            a.activation(self.Dw[:, :], self.Dw[:, :], ACT.Exp)
            a.drain()
            s_sc.inc(a.nop(), 1)
        self.A(f_dae)
        def f_aa(v):
            s_sc.wait(v)
            v.tensor_tensor(self.Aw[:, :], self.pZ[:, :], self.Dw[:, :], ALU.mult)
            v.drain()
            v.scalar_tensor_tensor(Dw3, self.Gprep[:, :].rearrange("p (a b) -> p a b", b=64),
                                   1.0, gcol, ALU.mult, ALU.subtract)
            v.drain()
            v.tensor_tensor(Dw3, Dw3, mB, ALU.subtract)
            v.drain()
            s_sc.inc(v.nop(), 1)
        self.V(f_aa)
        def f_dbe(a):
            s_sc.wait(a)
            a.activation(self.Dw[:, :], self.Dw[:, :], ACT.Exp)
            a.drain()
            s_sc.inc(a.nop(), 1)
        self.A(f_dbe)
        def f_bb(v):
            s_sc.wait(v)
            v.tensor_tensor(self.Bw[:, :], self.pZ[:, :], self.Dw[:, :], ALU.mult)
            v.drain()
            v.scalar_tensor_tensor(self.Yc[:, :].rearrange("p (a b) -> p a b", b=64),
                                   self.Bw[:, :].rearrange("p (a b) -> p a b", b=64), -1.0,
                                   self.eyed[:, :].unsqueeze(1).broadcast_to([128, 16, 64]),
                                   ALU.mult, ALU.add)
            v.drain()
            s_sc.inc(v.nop(), 1)
        self.V(f_bb)
        def f_kq(t):
            s_sc.wait(t)
            for jpar in range(2):
                for cl in range(2):
                    cc = slice(tp * 128 + 64 * cl, tp * 128 + 64 * cl + 64)
                    for h in range(16):
                        if (h // 2) % 2 != jpar:
                            continue
                        t.matmul(self.pZ[64 * cl:64 * cl + 64, ts(h, 64)],
                                 self.k_src(h, cc), self.q_src(h, cc), start=True, stop=True)
                    t.drain()
            s_sc.inc(t.nop(), 1)
        self.P(f_kq)
        def f_do(v):
            s_fd.wait(v)
            s_sc.wait(v)
            v.scalar_tensor_tensor(Dw3, self.Grep[:, :].rearrange("p (a b) -> p a b", b=64),
                                   1.0, gcol, ALU.mult, ALU.subtract)
            v.drain()
            v.tensor_tensor(Dw3, Dw3, mO, ALU.subtract)
            v.drain()
            s_sc.inc(v.nop(), 1)
        self.V(f_do)
        def f_doe(a):
            s_sc.wait(a)
            a.activation(self.Dw[:, :], self.Dw[:, :], ACT.Exp)
            a.drain()
            s_sc.inc(a.nop(), 1)
        self.A(f_doe)
        def f_qkdt(v):
            s_sc.wait(v)
            v.tensor_tensor(self.QKDT[:, :], self.pZ[:, :], self.Dw[:, :], ALU.mult)
            v.drain()
            s_sc.inc(v.nop(), 1)
        self.V(f_qkdt)
        if getattr(self, "CUT", None) == 2:
            s_cut = self.sem("s_scd")
            def f_cut(v):
                s_sc.wait(v)
                s_fd.wait(v)
                s_cut.inc(v.nop(), 1)
            self.V(f_cut)
            return s_cut

        eyedb = self.eyed[:, :].unsqueeze(1).broadcast_to([128, 16, 64])
        cF, cT = self.Aw, self.Bw
        for lvl in range(5):
            last = (lvl == 4)
            nF = self.Pf if lvl % 2 == 0 else self.Pg
            nT = self.PfT if lvl % 2 == 0 else self.PgT
            def f_sq(t, cF=cF, cT=cT, last=last):
                s_sc.wait(t)
                for cl in range(2):
                    sl_p = slice(64 * cl, 64 * cl + 64)
                    for h in range(16):
                        t.matmul(self.pX[sl_p, ts(h, 64)], cT[sl_p, ts(h, 64)], cF[sl_p, ts(h, 64)],
                                 start=True, stop=True)
                        if not last:
                            t.matmul(self.pY[sl_p, ts(h, 64)], cF[sl_p, ts(h, 64)], cT[sl_p, ts(h, 64)],
                                     start=True, stop=True)
                    t.drain()
                s_sc.inc(t.nop(), 1)
            self.P(f_sq)
            def f_ev(a, nF=nF, nT=nT, last=last):
                s_sc.wait(a)
                if not last:
                    a.activation(nF[:, :], self.pX[:, :], ACT.Copy)
                    a.drain()
                    a.activation(nT[:, :], self.pY[:, :], ACT.Copy)
                    a.drain()
                s_sc.inc(a.nop(), 1)
            self.A(f_ev)
            def f_ip(v):
                s_sc.wait(v)
                v.scalar_tensor_tensor(self.IPb[:, :].rearrange("p (a b) -> p a b", b=64),
                                       self.pX[:, :].rearrange("p (a b) -> p a b", b=64),
                                       1.0, eyedb, ALU.mult, ALU.add)
                v.drain()
                s_sc.inc(v.nop(), 1)
            self.V(f_ip)
            ycur = self.Yc if lvl % 2 == 0 else self.Yn
            ynew = self.Yn if lvl % 2 == 0 else self.Yc
            def f_yr(t, ycur=ycur):
                s_sc.wait(t)
                for cl in range(2):
                    sl_p = slice(64 * cl, 64 * cl + 64)
                    for h in range(16):
                        t.matmul(self.pZ[sl_p, ts(h, 64)], self.IPb[sl_p, ts(h, 64)],
                                 ycur[sl_p, ts(h, 64)], start=True, stop=True)
                    t.drain()
                s_sc.inc(t.nop(), 1)
            self.P(f_yr)
            def f_ye(a, ynew=ynew):
                s_sc.wait(a)
                a.activation(ynew[:, :], self.pZ[:, :], ACT.Copy)
                a.drain()
                s_sc.inc(a.nop(), 1)
            self.A(f_ye)
            cF, cT = nF, nT
        yfin = self.Yn
        if getattr(self, "CUT", None) == 3:
            s_cut = self.sem("s_scd")
            def f_cut(v):
                s_sc.wait(v)
                s_fd.wait(v)
                s_cut.inc(v.nop(), 1)
            self.V(f_cut)
            return s_cut
        def f_rb(v):
            s_sc.wait(v)
            for h in range(16):
                v.tensor_scalar(self.Rw[:, ts(2 * h, 64)], self.vTs[:, tt, ts(h, 64)],
                                self.decT2[:, tt, h:h + 1], None, ALU.mult)
                v.tensor_scalar(self.Rw[:, ts(2 * h + 1, 64)], self.kTs[:, tt, ts(h // 2, 64)],
                                self.decT[:, tt, 96 + h:97 + h], None, ALU.mult)
                v.tensor_scalar(self.KbT[:, ts(h, 64)], self.kTs[:, tt, ts(h // 2, 64)],
                                self.decT2[:, tt, 32 + h:33 + h], None, ALU.mult)
            v.drain()
            s_sc.inc(v.nop(), 1)
        self.V(f_rb)
        if getattr(self, "CUT", None) == 4:
            s_cut = self.sem("s_scd")
            def f_cut(v):
                s_sc.wait(v)
                s_fd.wait(v)
                s_cut.inc(v.nop(), 1)
            self.V(f_cut)
            return s_cut

        for cl in range(2):
            gc = gc0 + cl
            par, npar = gc % 2, (gc + 1) % 2
            sl_p = slice(64 * cl, 64 * cl + 64)
            ccols = slice(tp * 128 + 64 * cl, tp * 128 + 64 * cl + 64)
            gcols = slice(ss * SS + tp * 128 + 64 * cl, ss * SS + tp * 128 + 64 * cl + 64)
            def f_u(t, cl=cl, sl_p=sl_p):
                s_sc.wait(t)
                for h in range(16):
                    t.matmul(self.pY[sl_p, ts(h, 64)], yfin[sl_p, ts(h, 64)],
                             self.Rw[sl_p, ts(2 * h + 1, 64)], start=True, stop=True)
                for h in range(16):
                    t.matmul(self.pX[sl_p, ts(h, 64)], yfin[sl_p, ts(h, 64)],
                             self.Rw[sl_p, ts(2 * h, 64)], start=True, stop=True)
                t.drain()
                for h in range(16):
                    t.matmul(self.pZ[:, ts(h, 64)], self.Rw[sl_p, slice(128 * h, 128 * h + 128)],
                             yfin[sl_p, ts(h, 64)], start=True, stop=True)
                t.drain()
                s_sc.inc(t.nop(), 1)
            self.P(f_u)
            v_u = s_sc.val
            def f_ue(a, cl=cl, sl_p=sl_p, v_u=v_u):
                s_sc.wait(a, v_u)
                a.activation(self.Uvp[sl_p, :], self.pX[sl_p, :], ACT.Copy)
                a.drain()
                a.activation(self.Ukp[sl_p, :], self.pY[sl_p, :], ACT.Copy)
                a.drain()
                s_sc.inc(a.nop(), 1)
            self.A(f_ue)
            def f_un(v, cl=cl, v_u=v_u):
                s_sc.wait(v, v_u)
                v.tensor_scalar(self.nUkT[:, :], self.pZ[64:128, :], -1.0, None, ALU.mult)
                v.drain()
                s_sc.inc(v.nop(), 1)
            self.V(f_un)
            def f_mp(t, cl=cl, sl_p=sl_p):
                s_sc.wait(t)
                for h in range(16):
                    t.matmul(self.pV[:, ts(h, 64)], self.Ukp[sl_p, ts(h, 64)],
                             self.KbT[sl_p, ts(h, 64)], start=True, stop=True)
                for h in range(16):
                    t.matmul(self.pX[0:64, ts(h, 64)], self.Ukp[sl_p, ts(h, 64)],
                             self.QKDT[sl_p, ts(h, 64)], start=True, stop=True)
                t.drain()
                s_sc.inc(t.nop(), 1)
            self.P(f_mp)
            def f_mq2(v, cl=cl, ccols=ccols):
                s_sc.wait(v)
                v.tensor_scalar(self.nMT[:, :], self.pV[:, :], -1.0, None, ALU.mult)
                v.drain()
                for h in range(16):
                    qb = 64 * ((h // 2) % 2)
                    v.tensor_tensor(self.Qt[:, ts(h, 64)], self.q_src(h, ccols),
                                    self.lamrep[qb:qb + 64, slice(1024 * cl + 64 * h, 1024 * cl + 64 * h + 64)],
                                    ALU.mult)
                v.drain()
                v.scalar_tensor_tensor(self.Et[:, :], self.pX[0:64, :], -1.0, self.Qt[:, :],
                                       ALU.mult, ALU.add)
                v.drain()
                s_sc.inc(v.nop(), 1)
            self.V(f_mq2)
            def f_ep(t, cl=cl, par=par):
                s_sc.wait(t)
                for h in range(16):
                    wsl = self.Wb[par][:, h, :]
                    t.matmul(self.pV[:, ts(self.hcol(h), 64)], wsl, self.Et[:, ts(h, 64)],
                             start=True, stop=True)
                t.drain()
                s_sc.inc(t.nop(), 1)
            self.P(f_ep)
            def f_epe2(a, par=par):
                s_sc.wait(a)
                self.s_esp[par].wait(a)
                a.activation(self.EpT_st[par][0:64, :, :],
                             self.pV[:, 0:512].rearrange("p (a b) -> p a b", b=64), ACT.Copy)
                a.drain()
                a.activation(self.EpT_st[par][64:128, :, :],
                             self.pV[:, 512:1024].rearrange("p (a b) -> p a b", b=64), ACT.Copy)
                a.drain()
                s_sc.inc(a.nop(), 1)
            self.A(f_epe2)
            def f_epd(g, par=par, g0=gcols.start):
                s_sc.wait(g)
                ap = bass.AP(self.d_EpT, g0, [[8 * self.T, 128], [self.T, 8], [1, CH]])
                self.s_esp[par].inc(g.dma_start(ap, self.EpT_st[par][:, :, :]), 16)
            self.G_(f_epd)
            def f_uc(t, cl=cl, par=par, sl_p=sl_p):
                s_sc.wait(t)
                for h in range(16):
                    t.matmul(self.pV[:, ts(h, 64)], self.nUkT[:, ts(h, 64)],
                             self.Sst[par][:, h, :], start=True, stop=True)
                t.drain()
                s_sc.inc(t.nop(), 1)
            self.P(f_uc)
            lamCsl = self.lamCrep[:, cl::2]
            def f_uce(v, cl=cl, sl_p=sl_p, par=par, lamCsl=lamCsl):
                s_sc.wait(v)
                v.tensor_tensor(self.Ucp[sl_p, :], self.pV[0:64, :], self.Uvp[sl_p, :], ALU.add)
                v.drain()
                v.tensor_tensor(self.Ssc[:, :].rearrange("p (a b) -> p a b", b=64),
                                self.Sst[par][:, :, :],
                                lamCsl.unsqueeze(2).broadcast_to([64, 16, 64]), ALU.mult)
                v.drain()
                v.tensor_tensor(self.Wsc[:, :].rearrange("p (a b) -> p a b", b=64),
                                self.Wb[par][:, :, :],
                                lamCsl.unsqueeze(2).broadcast_to([64, 16, 64]), ALU.mult)
                v.drain()
                s_sc.inc(v.nop(), 1)
            self.V(f_uce)
            def f_ot(t, cl=cl, par=par, sl_p=sl_p):
                s_sc.wait(t)
                for h in range(16):
                    t.matmul(self.pV[:, ts(self.hcol(h), 64)], self.Ucp[sl_p, ts(h, 64)],
                             self.QKDT[sl_p, ts(h, 64)], start=True, stop=True)
                t.drain()
                for h in range(16):
                    t.matmul(self.pZ[0:64, ts(self.hcol(h), 64)],
                             self.Sst[par][:, h, :],
                             self.Qt[:, ts(h, 64)], start=True, stop=True)
                t.drain()
                s_sc.inc(t.nop(), 1)
            self.P(f_ot)
            def f_otc(a):
                s_sc.wait(a)
                a.activation(self.scr[0:64, 0:1024], self.pV[:, :], ACT.Copy)
                a.drain()
                s_sc.inc(a.nop(), 1)
            self.A(f_otc)
            def f_ote(v, par=par):
                s_sc.wait(v)
                self.s_osp[par].wait(v)
                v.tensor_tensor(self.opre_st[par][0:64, :, :],
                                self.scr[0:64, 0:512].rearrange("p (a b) -> p a b", b=64),
                                self.pZ[0:64, 0:512].rearrange("p (a b) -> p a b", b=64), ALU.add)
                v.drain()
                v.tensor_tensor(self.opre_st[par][64:128, :, :],
                                self.scr[0:64, 512:1024].rearrange("p (a b) -> p a b", b=64),
                                self.pZ[0:64, 512:1024].rearrange("p (a b) -> p a b", b=64), ALU.add)
                v.drain()
                s_sc.inc(v.nop(), 1)
            self.V(f_ote)
            def f_otd(g, par=par, g0=gcols.start):
                s_sc.wait(g)
                ap = bass.AP(self.d_opre, g0, [[8 * self.T, 128], [self.T, 8], [1, CH]])
                self.s_osp[par].inc(g.dma_start(ap, self.opre_st[par][:, :, :]), 16)
            self.G_(f_otd)
            def f_sw(t, cl=cl, par=par, sl_p=sl_p):
                s_sc.wait(t)
                for h in range(16):
                    t.matmul(self.pV[:, ts(h, 64)], self.KbT[sl_p, ts(h, 64)],
                             self.Ucp[sl_p, ts(h, 64)], start=True, stop=True)
                t.drain()
                for h in range(16):
                    t.matmul(self.pX[0:64, ts(h, 64)], self.nMT[:, ts(h, 64)],
                             self.Wb[par][:, h, :], start=True, stop=True)
                t.drain()
                s_sc.inc(t.nop(), 1)
            self.P(f_sw)
            def f_swe(v, npar=npar):
                s_sc.wait(v)
                v.tensor_tensor(self.Sst[npar][:, :, :].rearrange("p a b -> p (a b)"),
                                self.pV[:, :], self.Ssc[:, :], ALU.add)
                v.drain()
                v.tensor_tensor(self.Wb[npar][:, :, :].rearrange("p a b -> p (a b)"),
                                self.pX[0:64, :], self.Wsc[:, :], ALU.add)
                v.drain()
                s_sc.inc(v.nop(), 1)
            self.V(f_swe)
        s_done = self.sem("s_scd")
        def f_fin(v):
            s_sc.wait(v)
            s_done.inc(v.nop(), 1)
        self.V(f_fin)
        return s_done

    def build_scan_superslab(self, ss, s_p1, s_prev):
        s = s_prev
        for tp in range(self.SS // 128):
            s = self.build_scan_pair(ss, tp, s_p1, s)
        return s


    def declare_epi_ws(self, stk):
        ES = self.ES = 256
        self.Sin = self.sb(stk, "e_Sin", [64, 16, DV], F32)
        self.Sinb = self.sb(stk, "e_Sinb", [128, 16, DV], BF16)
        self.wzT = self.sb(stk, "e_wzT", [128, 8, 1024], BF16)
        self.woutT = self.sb(stk, "e_woutT", [128, 8, 1024], BF16)
        self.xe = [self.sb(stk, "e_xe0", [128, 8, ES], F32),
                   self.sb(stk, "e_xe1", [128, 8, ES], F32)]
        self.opre_sl = [self.sb(stk, "e_op0", [128, 8, ES], BF16),
                        self.sb(stk, "e_op1", [128, 8, ES], BF16)]
        self.EpT_sl = [self.sb(stk, "e_ep0", [128, 8, ES], BF16),
                       self.sb(stk, "e_ep1", [128, 8, ES], BF16)]
        self.hTe = self.sb(stk, "e_hT", [128, 8, ES], BF16)
        self.ofin = self.sb(stk, "e_ofin", [128, 8, ES], F32)
        self.szs = self.sb(stk, "e_szs", [128, 8, ES], BF16)
        self.tf = self.sb(stk, "e_tf", [128, 8, ES], F32)
        self.tsq = self.sb(stk, "e_tsq", [128, 8, ES], F32)
        self.r16 = self.sb(stk, "e_r16", [16, ES], F32)
        self.hgn = self.sb(stk, "e_hgn", [128, 8, ES], BF16)
        self.s_eid = [self.sem("s_eid0"), self.sem("s_eid1")]
        self.s_xod = [self.sem("s_xod0"), self.sem("s_xod1")]

    def declare_epi_psum(self, stk, tag):
        self.peD = self.ps(stk, "pe_D" + tag, [64, 1024])
        self.peZ = self.ps(stk, "pe_Z" + tag, [128, 512])
        self.peG = self.ps(stk, "pe_G" + tag, [16, 512])
        self.peR = self.ps(stk, "pe_R" + tag, [128, 512])
        self.peW = self.ps(stk, "pe_W" + tag, [128, 512])
        self.pR1 = self.ps(stk, "pe_R1" + tag, [128, 512])
        self.pR2 = self.ps(stk, "pe_R2" + tag, [128, 512])

    def build_ag2(self, s_scan):
        s_cc = self.sem("s_cc")
        s_ccw = self.sem("s_ccw")
        self.s_sin = self.sem("s_sin")
        nch = self.T // CH
        fin = nch % 2
        def f_cc(g):
            s_scan.wait(g)
            s_cc.inc(g.dma_start(self.cc_in[:, :], self.Sst[fin][:, :, :].rearrange("p a b -> p (a b)")), 16)
            s_cc.wait(g)
            g.collective_compute(
                "AllGather", mybir.AluOpType.bypass,
                replica_groups=[[i, i + 1] for i in range(0, self.NCORE, 2)],
                ins=[self.cc_in.ap().opt()], outs=[self.cc_out.ap().opt()],
            ).then_inc(s_ccw.h, 1)
            s_ccw.val += 1
            s_ccw.wait(g)
            s_cc.inc(g.dma_start(self.Sin[:, :, :].rearrange("p a b -> p (a b)"), self.cc_out[0:64, :]), 16)
            s_cc.inc(g.dma_start(self.scr[0:64, 0:512], self.cc_out[64:128, 0:512]), 16)
            s_cc.inc(g.dma_start(self.scr[0:64, 512:1024], self.cc_out[64:128, 512:1024]), 16)
        self.G_(f_cc)
        def f_si(v):
            s_cc.wait(v)
            v.tensor_scalar(self.Sin[:, :, :].rearrange("p a b -> p (a b)"),
                            self.Sin[:, :, :].rearrange("p a b -> p (a b)"),
                            self.cvec[0:64, 2:3], None, ALU.mult)
            v.drain()
            v.scalar_tensor_tensor(self.Sin[:, 0:8, :].rearrange("p a b -> p (a b)"),
                                   self.scr[0:64, 0:512], self.cvec[0:64, 3:4],
                                   self.Sin[:, 0:8, :].rearrange("p a b -> p (a b)"),
                                   ALU.mult, ALU.add)
            v.drain()
            v.scalar_tensor_tensor(self.Sin[:, 8:16, :].rearrange("p a b -> p (a b)"),
                                   self.scr[0:64, 512:1024], self.cvec[0:64, 3:4],
                                   self.Sin[:, 8:16, :].rearrange("p a b -> p (a b)"),
                                   ALU.mult, ALU.add)
            v.drain()
            v.tensor_copy(self.Sinb[0:64, :, :].rearrange("p a b -> p (a b)"),
                          self.Sin[:, :, :].rearrange("p a b -> p (a b)"))
            v.drain()
            v.tensor_copy(self.Sinb[64:128, :, :].rearrange("p a b -> p (a b)"),
                          self.Sin[:, :, :].rearrange("p a b -> p (a b)"))
            v.drain()
            self.s_sin.inc(v.nop(), 1)
        self.V(f_si)
        return self.s_sin

    def build_epi_slab(self, sl, s_pre):
        ES = self.ES
        w0 = sl * ES
        par = sl % 2
        s_e = self.sem("s_e")
        def f_ind(g):
            s_pre.wait(g)
            if self.s_xod[par].val:
                self.s_xod[par].wait(g)
            ap_x = bass.AP(self.i_x, 3 + w0, [[8 * self.TH, 128], [self.TH, 8], [1, ES]])
            ap_o = bass.AP(self.d_opre, w0, [[8 * self.T, 128], [self.T, 8], [1, ES]])
            ap_e = bass.AP(self.d_EpT, w0, [[8 * self.T, 128], [self.T, 8], [1, ES]])
            self.s_eid[par].inc(g.dma_start(self.xe[par][:, :, :], ap_x), 16)
            self.s_eid[par].inc(g.dma_start(self.opre_sl[par][:, :, :], ap_o), 16)
            self.s_eid[par].inc(g.dma_start(self.EpT_sl[par][:, :, :], ap_e), 16)
        self.G_(f_ind)
        def f_gate(v):
            s_pre.wait(v)
            self.s_eid[par].wait(v)
            s_e.inc(v.nop(), 1)
        self.V(f_gate)
        self.ln_rep_apply(0, 3 + w0, ES, s_e, self.stats, self.statsm, self.hTe, False,
                          xsrc=self.xe[par], x0=0)
        for dp in range(4):
            def f_z(t, dp=dp):
                s_e.wait(t)
                for j in range(2):
                    d = 2 * dp + j
                    for kk in range(8):
                        t.matmul(self.peZ[:, 256 * j:256 * j + ES], self.wzT[:, kk, ts(d, 128)],
                                 self.hTe[:, kk, :], start=(kk == 0), stop=(kk == 7))
                t.drain()
                s_e.inc(t.nop(), 1)
            self.P(f_z)
            def f_zs(a, dp=dp):
                s_e.wait(a)
                a.activation(self.szs[:, 2 * dp:2 * dp + 2, :],
                             self.peZ[:, 0:512].rearrange("p (a b) -> p a b", b=ES), ACT.Silu)
                a.drain()
                s_e.inc(a.nop(), 1)
            self.A(f_zs)
        for cc in range(ES // CH):
            lcols = slice(cc * CH, cc * CH + CH)
            def f_do(t, cc=cc, lcols=lcols, par=par):
                s_e.wait(t)
                for hp in range(2):
                    for h in range(hp, 16, 2):
                        base = 64 * (h % 2)
                        t.matmul(self.peD[:, ts(self.hcol(h), 64)],
                                 self.Sinb[base:base + 64, h, :],
                                 self.EpT_sl[par][base:base + 64, h // 2, lcols],
                                 start=True, stop=True)
                    t.drain()
                s_e.inc(t.nop(), 1)
            self.P(f_do)
            def f_oa(v, cc=cc, lcols=lcols, par=par):
                s_e.wait(v)
                v.tensor_tensor(self.ofin[0:64, :, lcols],
                                self.opre_sl[par][0:64, :, lcols],
                                self.peD[:, 0:512].rearrange("p (a b) -> p a b", b=64),
                                ALU.add)
                v.drain()
                v.tensor_tensor(self.ofin[64:128, :, lcols],
                                self.opre_sl[par][64:128, :, lcols],
                                self.peD[:, 512:1024].rearrange("p (a b) -> p a b", b=64),
                                ALU.add)
                v.drain()
                s_e.inc(v.nop(), 1)
            self.V(f_oa)
        def f_tf(v):
            s_e.wait(v)
            v.tensor_tensor(self.tf[:, :, :].rearrange("p a b -> p (a b)"),
                            self.ofin[:, :, :].rearrange("p a b -> p (a b)"),
                            self.szs[:, :, :].rearrange("p a b -> p (a b)"), ALU.mult)
            v.drain()
            s_e.inc(v.nop(), 1)
        self.V(f_tf)
        def f_sq(a):
            s_e.wait(a)
            for d in range(8):
                a.activation(self.tsq[:, d, :], self.tf[:, d, :], ACT.Square)
            a.drain()
            s_e.inc(a.nop(), 1)
        self.A(f_sq)
        def f_gs(t):
            s_e.wait(t)
            for d in range(8):
                t.matmul(self.peG[:, 0:ES], self.gnsel[:, d, :], self.tsq[:, d, :],
                         start=(d == 0), stop=(d == 7))
            t.drain()
            s_e.inc(t.nop(), 1)
        self.P(f_gs)
        def f_r16(v):
            s_e.wait(v)
            v.tensor_scalar(self.r16[:, :], self.peG[:, 0:ES], 1.0 / 64.0, None, ALU.mult)
            v.drain()
            s_e.inc(v.nop(), 1)
        self.V(f_r16)
        def f_r16a(a):
            s_e.wait(a)
            a.activation(self.r16[:, :], self.r16[:, :], ACT.Ln, bias=self.cvec[0:16, 1:2])
            a.drain()
            a.activation(self.r16[:, :], self.r16[:, :], ACT.Exp, scale=-0.5)
            a.drain()
            s_e.inc(a.nop(), 1)
        self.A(f_r16a)
        for dp in range(4):
            def f_rr(t, dp=dp):
                s_e.wait(t)
                for j in range(2):
                    t.matmul(self.peR[:, 256 * j:256 * j + ES], self.gnselT[:, 2 * dp + j, :],
                             self.r16[:, :], start=True, stop=True)
                t.drain()
                s_e.inc(t.nop(), 1)
            self.P(f_rr)
            def f_hg(v, dp=dp):
                s_e.wait(v)
                for j in range(2):
                    d = 2 * dp + j
                    v.scalar_tensor_tensor(self.hgn[:, d, :], self.tf[:, d, :], self.nw[:, 0:1],
                                           self.peR[:, 256 * j:256 * j + ES], ALU.mult, ALU.mult)
                v.drain()
                s_e.inc(v.nop(), 1)
            self.V(f_hg)
        for dp in range(4):
            def f_wo(t, dp=dp):
                s_e.wait(t)
                for j in range(2):
                    d = 2 * dp + j
                    for kk in range(8):
                        t.matmul(self.peW[:, 256 * j:256 * j + ES], self.woutT[:, kk, ts(d, 128)],
                                 self.hgn[:, kk, :], start=(kk == 0), stop=(kk == 7))
                t.drain()
                s_e.inc(t.nop(), 1)
            self.P(f_wo)
            def f_xu(v, dp=dp, par=par):
                s_e.wait(v)
                for j in range(2):
                    d = 2 * dp + j
                    v.scalar_tensor_tensor(self.xe[par][:, d, :], self.peW[:, 256 * j:256 * j + ES],
                                           self.adaln[:, d, 2:3], self.xe[par][:, d, :],
                                           ALU.mult, ALU.add)
                v.drain()
                s_e.inc(v.nop(), 1)
            self.V(f_xu)
        def f_xod(g, par=par, w0=w0):
            s_e.wait(g)
            ap = bass.AP(self.d_xp, w0, [[8 * self.T, 128], [self.T, 8], [1, ES]])
            self.s_xod[par].inc(g.dma_start(ap, self.xe[par][:, :, :]), 16)
        self.G_(f_xod)
        return s_e

    def quant_emit(self, ch, src, dst, nblk, pre=None, post_sems=()):
        n = nblk * 16
        bm, tq, gq = self.fbm, self.ftq, self.fgq
        aa = gq[:, 0:n]
        acc = gq[:, 512:512 + n]
        r3 = lambda ap: ap.rearrange("p (a b) -> p a b", b=16)
        bcast = lambda ap: ap.unsqueeze(2).broadcast_to([128, nblk, 16])
        def f_q1(v):
            ch.wait(v)
            if pre is not None:
                pre(v)
            v.tensor_reduce(bm[:, 0:nblk], r3(src), mybir.AxisListType.X, ALU.max,
                            apply_absolute_value=True)
            v.drain()
            v.tensor_scalar(bm[:, 64:64 + nblk], bm[:, 0:nblk], 6e-12, None, ALU.max)
            v.drain()
            ch.inc(v.nop(), 1)
        self.V(f_q1)
        def f_q2(a):
            ch.wait(a)
            a.activation(bm[:, 0:nblk], bm[:, 64:64 + nblk], ACT.Ln)
            a.drain()
            a.activation(bm[:, 0:nblk], bm[:, 0:nblk], ACT.Exp, scale=-1.0,
                         bias=self.cvec[:, 5:6])
            a.drain()
            ch.inc(a.nop(), 1)
        self.A(f_q2)
        def f_q3(v):
            ch.wait(v)
            v.tensor_tensor(r3(tq[:, 0:n]), r3(src), bcast(bm[:, 0:nblk]), ALU.mult)
            v.drain()
            v.tensor_scalar(acc, tq[:, 0:n], -1.0, None, ALU.mult)
            v.drain()
            v.tensor_tensor(aa, tq[:, 0:n], acc, ALU.max)
            v.drain()
            v.tensor_scalar(acc, aa, 7.0, None, ALU.is_gt)
            v.drain()
            v.scalar_tensor_tensor(acc, aa, 5.0, acc, ALU.is_gt, ALU.add)
            v.drain()
            v.scalar_tensor_tensor(acc, aa, 3.0, acc, ALU.is_gt, ALU.add)
            v.drain()
            v.scalar_tensor_tensor(acc, aa, 1.0, acc, ALU.is_gt, ALU.add)
            v.drain()
            hi = tq[:, 0:n]
            v.tensor_scalar(hi, aa, 20.0, None, ALU.is_gt)
            v.drain()
            v.tensor_tensor(hi, hi, hi, ALU.add)
            v.drain()
            v.scalar_tensor_tensor(hi, aa, 14.0, hi, ALU.is_gt, ALU.add)
            v.drain()
            v.scalar_tensor_tensor(hi, aa, 10.0, hi, ALU.is_gt, ALU.add)
            v.drain()
            v.scalar_tensor_tensor(acc, hi, 2.0, acc, ALU.mult, ALU.add)
            v.drain()
            v.tensor_scalar(aa, src, 1e13, -1.0, ALU.mult, ALU.max)
            v.drain()
            v.tensor_scalar(aa, aa, 1.0, None, ALU.min)
            v.drain()
            v.tensor_tensor(acc, acc, aa, ALU.mult)
            v.drain()
            v.tensor_scalar(bm[:, 64:64 + nblk], bm[:, 64:64 + nblk], 1.0 / 12.0, None, ALU.mult)
            v.drain()
            v.tensor_tensor(r3(dst), r3(acc), bcast(bm[:, 64:64 + nblk]), ALU.mult)
            v.drain()
            ch.inc(v.nop(), 1)
            for sm, k in post_sems:
                sm.inc(v.nop(), k)
        self.V(f_q3)

    def build_wq(self, stk, s_pre):
        NR, ISHP = self.NR, self.ISHP
        cols = max(NR * 1024, 8 * ISHP)
        wsh = self.sb(stk, "w_sh", [128, cols], F32)
        wqq = self.sb(stk, "w_qq", [128, cols], BF16)
        wts = self.sb(stk, "w_ts", [128, cols], BF16)
        self.fbm = self.sb(stk, "w_bm", [128, 192], F32)
        self.ftq = self.sb(stk, "w_tq", [128, 512], F32)
        self.fgq = self.sb(stk, "w_gq", [128, 1024], F32)
        pWb = self.ps(stk, "p_wT", [128, 1024], BF16)
        s_wd = self.sem("s_wd")
        s_wq = self.sem("s_wq")
        s_ag2 = self.sem("s_ag2")
        specs = ((self.i_wg, self.cc_g_in, "g"), (self.i_wu, self.cc_u_in, "u"),
                 (self.i_wd, self.cc_d_in, "d"))
        for wi, (ih, ccin, kind) in enumerate(specs):
            ncols = NR * 1024 if kind != "d" else 8 * ISHP
            def f_in(g, ih=ih, ncols=ncols, wi=wi):
                if wi == 0:
                    s_pre.wait(g)
                else:
                    s_wq.wait(g)
                s_wd.inc(g.dma_start(wsh[:, 0:ncols], ih[:, :]), 16)
                s_wd.wait(g)
                s_wq.inc(g.nop(), 1)
            self.G_(f_in)
            ntile = NR if kind != "d" else 8
            tw = 1024 if kind != "d" else ISHP
            for it in range(ntile):
                for c0 in range(0, tw, 512):
                    wd = min(512, tw - c0)
                    self.quant_emit(s_wq, wsh[:, it * tw + c0: it * tw + c0 + wd],
                                    wqq[:, it * tw + c0: it * tw + c0 + wd], wd // 16)
            for k in range(8):
                for r0 in range(0, NR, 8):
                    nr = min(8, NR - r0)
                    if kind != "d":
                        srcs = [(r0 + i) * 1024 + 128 * k for i in range(nr)]
                        dst = wts[:, k * ISHP + 128 * r0: k * ISHP + 128 * (r0 + nr)]
                        dst = dst.rearrange("p (a b) -> p a b", b=128)
                    else:
                        srcs = [k * ISHP + 128 * (r0 + i) for i in range(nr)]
                        dst = wts[:, 0:NR * 1024].rearrange("p (a b) -> p a b", b=1024)[
                            :, r0:r0 + nr, 128 * k:128 * k + 128]
                    def f_t(t, srcs=srcs):
                        s_wq.wait(t)
                        for i, sc in enumerate(srcs):
                            t.transpose(pWb[:, 128 * i:128 * i + 128], wqq[:, sc:sc + 128],
                                        self.eyeb[:, :])
                        t.drain()
                        s_wq.inc(t.nop(), 1)
                    self.P(f_t)
                    def f_te(a, dst=dst, nr=nr):
                        s_wq.wait(a)
                        a.activation(dst, pWb[:, 0:128 * nr].rearrange("p (a b) -> p a b", b=128),
                                     ACT.Copy)
                        a.drain()
                        s_wq.inc(a.nop(), 1)
                    self.A(f_te)
            if kind != "d":
                dst = bass.AP(ccin, 0, [[ISHP, 128], [128 * ISHP, 8], [1, ISHP]])
                src = wts[:, 0:8 * ISHP].rearrange("p (a b) -> p a b", b=ISHP)
            else:
                dst = bass.AP(ccin, 0, [[1024, 128], [128 * 1024, NR], [1, 1024]])
                src = wts[:, 0:NR * 1024].rearrange("p (a b) -> p a b", b=1024)
            def f_out(g, dst=dst, src=src):
                s_wq.wait(g)
                s_wd.inc(g.dma_start(dst, src), 16)
                s_wd.wait(g)
                s_wq.inc(g.nop(), 1)
            self.G_(f_out)
        def f_ag(g):
            s_wq.wait(g)
            grp = [list(range(self.NCORE))]
            for cin, gout in ((self.cc_g_in, self.g_gT), (self.cc_u_in, self.g_uT),
                              (self.cc_d_in, self.g_dT)):
                g.collective_compute("AllGather", mybir.AluOpType.bypass,
                                     replica_groups=grp, ins=[cin.ap().opt()],
                                     outs=[gout.ap().opt()]).then_inc(s_ag2.h, 1)
                s_ag2.val += 1
        self.G_(f_ag)
        s_wqd = self.sem("s_wqd")
        def f_fence(v):
            s_wq.wait(v)
            s_wqd.inc(v.nop(), 1)
        self.V(f_fence)
        return s_wqd, s_ag2

    def declare_ffn_ws(self, stk):
        FS, NU, ISHP, NJ = self.FS, self.NU, self.ISHP, self.NJ
        IT = self.NCORE * ISHP
        self.hf = self.sb(stk, "f_hf", [128, 8, FS], F32)
        self.fsq = self.sb(stk, "f_sq", [128, 2, FS], F32)
        self.hq = self.sb(stk, "f_hq", [128, 8, FS], BF16)
        self.hfT = self.sb(stk, "f_hfT", [128, 1024], F32)
        self.hqT = self.sb(stk, "f_hqT", [128, 1024], BF16)
        self.h2q = self.sb(stk, "f_h2q", [128, NU, IT], BF16)
        self.h2qT = self.sb(stk, "f_h2qT", [128, NJ, FS], BF16)
        self.h2p = self.sb(stk, "f_h2p", [128, 512], F32)
        self.xf = self.sb(stk, "f_xf", [128, 8, FS], F32)
        self.s_xfd = self.sem("s_xfd")
        self.s_yod = self.sem("s_yod")
        WW = min(512, ISHP)
        self.wgb = [self.sb(stk, "f_wg0", [128, 8, WW], BF16),
                    self.sb(stk, "f_wg1", [128, 8, WW], BF16)]
        self.wub = [self.sb(stk, "f_wu0", [128, 8, WW], BF16),
                    self.sb(stk, "f_wu1", [128, 8, WW], BF16)]
        self.wdb = [self.sb(stk, "f_wd0", [128, NJ, 128], BF16),
                    self.sb(stk, "f_wd1", [128, NJ, 128], BF16)]
        self.fbm = self.sb(stk, "f_bm", [128, 192], F32)
        self.ftq = self.sb(stk, "f_tq", [128, 512], F32)
        self.fgq = self.sb(stk, "f_gq", [128, 1024], F32)

    def declare_ffn_psum(self, stk, tag):
        self.pF1 = self.ps(stk, "pf_1" + tag, [128, 512])
        self.pF2 = self.ps(stk, "pf_2" + tag, [128, 512])
        self.pR1 = self.ps(stk, "pf_R1" + tag, [128, 512])
        self.pR2 = self.ps(stk, "pf_R2" + tag, [128, 512])
        self.pD = self.ps(stk, "pf_D" + tag, [128, 512])
        self.pTqf = self.ps(stk, "pf_Tqf" + tag, [128, 512])
        self.pTq = self.ps(stk, "pf_Tq" + tag, [128, 1024], BF16)
        self.pS = self.pF1
        self.pQ = self.pF2

    def ln_ffn(self, w0, n, s_pre, xsrc, x0):
        s_loc = self.sem("s_ln2")
        def f_gate(v):
            s_pre.wait(v)
            s_loc.inc(v.nop(), 1)
        self.V(f_gate)
        for dp in range(4):
            def f_sq(a, dp=dp):
                s_loc.wait(a)
                a.activation(self.fsq[:, 0, 0:n], xsrc[:, 2 * dp, x0:x0 + n], ACT.Square)
                a.activation(self.fsq[:, 1, 0:n], xsrc[:, 2 * dp + 1, x0:x0 + n], ACT.Square)
                a.drain()
                s_loc.inc(a.nop(), 1)
            self.A(f_sq)
            def f_mm(t, dp=dp):
                s_loc.wait(t)
                for j in range(2):
                    d = 2 * dp + j
                    t.matmul(self.pS[0:1, 0:n], self.onesf[:, 0:1], xsrc[:, d, x0:x0 + n],
                             start=(d == 0), stop=(d == 7))
                for j in range(2):
                    d = 2 * dp + j
                    t.matmul(self.pQ[0:1, 0:n], self.onesf[:, 0:1], self.fsq[:, j, 0:n],
                             start=(d == 0), stop=(d == 7))
                t.drain()
                s_loc.inc(t.nop(), 1)
            self.P(f_mm)
        def f_vst(v):
            s_loc.wait(v)
            v.tensor_scalar(self.scr[0:1, 0:n], self.pS[0:1, 0:n], 1.0 / 1024.0, None, ALU.mult)
            v.drain()
            v.tensor_tensor(self.scr[0:1, 0:n], self.scr[0:1, 0:n], self.scr[0:1, 0:n], ALU.mult)
            v.drain()
            v.scalar_tensor_tensor(self.scr[0:1, self.SCROFF:self.SCROFF + n], self.pQ[0:1, 0:n],
                                   1.0 / 1024.0, self.scr[0:1, 0:n], ALU.mult, ALU.subtract)
            v.drain()
            s_loc.inc(v.nop(), 1)
        self.V(f_vst)
        def f_ast(a):
            s_loc.wait(a)
            a.activation(self.scr[0:1, 0:n], self.scr[0:1, self.SCROFF:self.SCROFF + n],
                         ACT.Ln, bias=self.cvec[0:1, 0:1])
            a.drain()
            a.activation(self.stats[0:1, w0:w0 + n], self.scr[0:1, 0:n], ACT.Exp, scale=-0.5)
            a.drain()
            s_loc.inc(a.nop(), 1)
        self.A(f_ast)
        def f_vst2(v):
            s_loc.wait(v)
            v.scalar_tensor_tensor(self.statsm[32:33, w0:w0 + n], self.pS[0:1, 0:n], 1.0 / 1024.0,
                                   self.stats[0:1, w0:w0 + n], ALU.mult, ALU.mult)
            v.drain()
            s_loc.inc(v.nop(), 1)
        self.V(f_vst2)
        self.ln_rep_apply(1, w0, n, s_loc, self.stats, self.statsm, self.hf, False, xsrc, x0)
        return s_loc

    def build_ffn_slab(self, u4, s_pre, s_ag2):
        FS, NU, NR, ISHP, NJ = self.FS, self.NU, self.NR, self.ISHP, self.NJ
        w0 = u4 * FS
        def f_xfd(g):
            s_pre.wait(g)
            if self.s_yod.val:
                self.s_yod.wait(g)
            ap = bass.AP(self.d_xp, w0, [[8 * self.T, 128], [self.T, 8], [1, FS]])
            self.s_xfd.inc(g.dma_start(self.xf[:, :, :], ap), 16)
        self.G_(f_xfd)
        s_xg = self.sem("s_xg")
        def f_xg(v):
            s_pre.wait(v)
            self.s_xfd.wait(v)
            s_xg.inc(v.nop(), 1)
        self.V(f_xg)
        s_ln = self.ln_ffn(3 + w0, FS, s_xg, self.xf, 0)
        s_f = self.sem("s_f")
        def f_gate(v):
            s_ln.wait(v)
            s_f.inc(v.nop(), 1)
        self.V(f_gate)
        for u in range(NU):
            for d0 in (0, 4):
                def f_t(t, u=u, d0=d0):
                    s_f.wait(t)
                    for i in range(4):
                        t.transpose(self.pTqf[:, 128 * i:128 * i + 128],
                                    self.hf[:, d0 + i, 128 * u:128 * u + 128], self.eyef[:, :])
                    t.drain()
                    s_f.inc(t.nop(), 1)
                self.P(f_t)
                def f_te(a, d0=d0):
                    s_f.wait(a)
                    a.activation(self.hfT[:, 512 * (d0 // 4):512 * (d0 // 4) + 512],
                                 self.pTqf[:, 0:512], ACT.Copy)
                    a.drain()
                    s_f.inc(a.nop(), 1)
                self.A(f_te)
            self.quant_emit(s_f, self.hfT[:, 0:512], self.hqT[:, 0:512], 32)
            self.quant_emit(s_f, self.hfT[:, 512:1024], self.hqT[:, 512:1024], 32)
            def f_tb(t, u=u):
                s_f.wait(t)
                for d in range(8):
                    t.transpose(self.pTq[:, 128 * d:128 * d + 128],
                                self.hqT[:, 128 * d:128 * d + 128], self.eyeb[:, :])
                t.drain()
                s_f.inc(t.nop(), 1)
            self.P(f_tb)
            def f_tbe(a, u=u):
                s_f.wait(a)
                a.activation(self.hq[:, :, 128 * u:128 * u + 128],
                             self.pTq[:, :].rearrange("p (a b) -> p a b", b=128), ACT.Copy)
                a.drain()
                s_f.inc(a.nop(), 1)
            self.A(f_tbe)
        s_gu = self.sem("s_gu")
        s_wgp = [self.sem("s_wg0"), self.sem("s_wg1")]
        if not hasattr(self, "h_gu"):
            self.h_gu = []
        chunks = [(s, c0) for s in range(self.NCORE) for c0 in range(0, ISHP, 512)]
        for ci, (s, c0) in enumerate(chunks):
            par = len(self.h_gu) % 2
            W = min(512, ISHP - c0)
            free_after = self.h_gu[-2] if len(self.h_gu) >= 2 else None
            def f_wg(g, s=s, c0=c0, W=W, par=par, free_after=free_after):
                s_ag2.wait(g)
                if free_after is not None:
                    g.wait_ge(s_gu.h, free_after)
                off = 1024 * s * ISHP + c0
                ap_g = bass.AP(self.g_gT, off, [[ISHP, 128], [128 * ISHP, 8], [1, W]])
                ap_u = bass.AP(self.g_uT, off, [[ISHP, 128], [128 * ISHP, 8], [1, W]])
                s_wgp[par].inc(g.dma_start(self.wgb[par][:, :, 0:W], ap_g), 16)
                s_wgp[par].inc(g.dma_start(self.wub[par][:, :, 0:W], ap_u), 16)
            self.G_(f_wg)
            for u in range(NU):
                def f_mm(t, par=par, u=u, W=W):
                    s_f.wait(t)
                    s_wgp[par].wait(t)
                    for k in range(8):
                        t.matmul(self.pF1[:, 0:W], self.hq[:, k, 128 * u:128 * u + 128],
                                 self.wgb[par][:, k, 0:W], start=(k == 0), stop=(k == 7))
                    for k in range(8):
                        t.matmul(self.pF2[:, 0:W], self.hq[:, k, 128 * u:128 * u + 128],
                                 self.wub[par][:, k, 0:W], start=(k == 0), stop=(k == 7))
                    t.drain()
                    s_f.inc(t.nop(), 1)
                self.P(f_mm)
                def f_sil(a, W=W):
                    s_f.wait(a)
                    a.activation(self.h2p[:, 0:W], self.pF1[:, 0:W], ACT.Silu)
                    a.drain()
                    s_f.inc(a.nop(), 1)
                self.A(f_sil)
                def pre_mq(v, W=W):
                    v.tensor_tensor(self.h2p[:, 0:W], self.h2p[:, 0:W], self.pF2[:, 0:W], ALU.mult)
                    v.drain()
                self.quant_emit(s_f, self.h2p[:, 0:W],
                                self.h2q[:, u, s * ISHP + c0: s * ISHP + c0 + W], W // 16,
                                pre=pre_mq, post_sems=[(s_gu, 1)])
            self.h_gu.append(s_gu.val)
        for u in range(NU):
            for j0 in range(0, NJ, 8):
                nj = min(8, NJ - j0)
                def f_ht(t, u=u, j0=j0, nj=nj):
                    s_f.wait(t)
                    for i in range(nj):
                        t.transpose(self.pTq[:, 128 * i:128 * i + 128],
                                    self.h2q[:, u, 128 * (j0 + i): 128 * (j0 + i) + 128],
                                    self.eyeb[:, :])
                    t.drain()
                    s_f.inc(t.nop(), 1)
                self.P(f_ht)
                def f_hte(a, u=u, j0=j0, nj=nj):
                    s_f.wait(a)
                    a.activation(self.h2qT[:, j0:j0 + nj, 128 * u:128 * u + 128],
                                 self.pTq[:, 0:128 * nj].rearrange("p (a b) -> p a b", b=128),
                                 ACT.Copy)
                    a.drain()
                    s_f.inc(a.nop(), 1)
                self.A(f_hte)
        s_wdp = [self.sem("s_wdp0"), self.sem("s_wdp1")]
        s_dn = self.sem("s_dn")
        if not hasattr(self, "h_dn"):
            self.h_dn = []
        for d in range(8):
            par = len(self.h_dn) % 2
            free_after = self.h_dn[-2] if len(self.h_dn) >= 2 else None
            def f_wd(g, d=d, par=par, free_after=free_after):
                s_ag2.wait(g)
                if free_after is not None:
                    g.wait_ge(s_dn.h, free_after)
                ap_d = bass.AP(self.g_dT, 128 * d, [[1024, 128], [128 * 1024, NJ], [1, 128]])
                s_wdp[par].inc(g.dma_start(
                    self.wdb[par][:, :, :].rearrange("p a b -> p (a b)"), ap_d), 16)
            self.G_(f_wd)
            def f_dmm(t, d=d, par=par):
                s_f.wait(t)
                s_wdp[par].wait(t)
                for j in range(NJ):
                    t.matmul(self.pD[:, 0:FS], self.wdb[par][:, j, :], self.h2qT[:, j, 0:FS],
                             start=(j == 0), stop=(j == NJ - 1))
                t.drain()
                s_f.inc(t.nop(), 1)
            self.P(f_dmm)
            def f_xu(v, d=d):
                s_f.wait(v)
                v.scalar_tensor_tensor(self.xf[:, d, :], self.pD[:, 0:FS],
                                       self.adaln[:, d, 5:6], self.xf[:, d, :],
                                       ALU.mult, ALU.add)
                v.drain()
                s_f.inc(v.nop(), 1)
                s_dn.inc(v.nop(), 1)
            self.V(f_xu)
            self.h_dn.append(s_dn.val)
        def f_yod(g, w0=w0):
            s_f.wait(g)
            ap = bass.AP(self.o_y, w0, [[8 * self.T, 128], [self.T, 8], [1, FS]])
            self.s_yod.inc(g.dma_start(ap, self.xf[:, :, :]), 16)
        self.G_(f_yod)
        return s_f

    def build_section1(self, sections=1):
        self.declare_global()
        SS, SSH = self.SS, self.SSH
        for name, shape in (("hT", [128, 8 * SSH]), ("mixed", [128, 16 * SSH]),
                            ("qkvs", [128, 16 * SS]), ("dec", [128, SS]), ("dec2", [96, SS]),
                            ("decT", [128, (SS // 128) * 128]), ("decT2", [128, (SS // 128) * 96]),
                            ("stats", [1, self.TH]), ("statsm", [1, self.TH]),
                            ("adaln", [128, 48]), ("kTs", [128, (SS // 128) * KEY]),
                            ("vTs", [128, (SS // 128) * VAL]),
                            ("opre", [128, 8 * self.T]), ("EpT", [128, 8 * self.T]),
                            ("Sst", [64, 16 * DV]), ("Wb", [64, 16 * DK])):
            self.dbg(name, shape)
        with self.nc.Block() as blk:
            self.blk = blk
            with ExitStack() as stkA:
                self.declare_phaseA(stkA)
                if sections >= 2:
                    self.declare_scan_ws(stkA)
                with ExitStack() as stkP0:
                    self.build_p0(stkP0)
                s = self.s_p0
                for ss in range(self.NSS):
                    with ExitStack() as ps1:
                        self.declare_p1_psum(ps1, "_%d" % ss)
                        s, _scv = self.build_p1_superslab(ss, s)
                    if sections >= 2:
                        with ExitStack() as ps2:
                            self.declare_scan_psum(ps2, "_%d" % ss)
                            s = self.build_scan_superslab(ss, s, s)
                self.dump("hT", self.hT[:, :, :].rearrange("p a b -> p (a b)"), s)
                self.dump("qkvs", self.qkvs[:, :, :].rearrange("p a b -> p (a b)"), s)
                self.dump("dec", self.dec[:, :], s)
                self.dump("dec2", self.dec2[:, :], s)
                self.dump("decT", self.decT[:, :, :].rearrange("p a b -> p (a b)"), s)
                self.dump("decT2", self.decT2[:, :, :].rearrange("p a b -> p (a b)"), s)
                self.dump("stats", self.stats[:, :], s)
                self.dump("statsm", self.statsm[32:33, :], s)
                self.dump("adaln", self.adaln[:, :, :].rearrange("p a b -> p (a b)"), s)
                self.dump("kTs", self.kTs[:, :, :].rearrange("p a b -> p (a b)"), s)
                self.dump("vTs", self.vTs[:, :, :].rearrange("p a b -> p (a b)"), s)
                if sections >= 2:
                    if "opre" in self.dbg_bufs or "EpT" in self.dbg_bufs:
                        s_dg = self.sem("s_dmpgate")
                        def f_dg(v):
                            s.wait(v)
                            for sp in self.s_osp + self.s_esp:
                                sp.wait(v)
                            s_dg.inc(v.nop(), 1)
                        self.V(f_dg)
                        self.dump("opre", bass.AP(self.d_opre, 0, [[8 * self.T, 128], [1, 8 * self.T]]), s_dg)
                        self.dump("EpT", bass.AP(self.d_EpT, 0, [[8 * self.T, 128], [1, 8 * self.T]]), s_dg)
                    nch = self.T // CH
                    self.dump("Sst", self.Sst[nch % 2][:, :, :].rearrange("p a b -> p (a b)"), s)
                    self.dump("Wb", self.Wb[nch % 2][:, :, :].rearrange("p a b -> p (a b)"), s)
            s_wqd = s_ag2 = None
            if sections >= 4:
                with ExitStack() as stkW:
                    s_wqd, s_ag2 = self.build_wq(stkW, s)
            if sections >= 3:
                for name, shape in (("ofin", [128, 8 * 256]), ("szs", [128, 8 * 256]),
                                    ("tf", [128, 8 * 256]), ("hgn", [128, 8 * 256]),
                                    ("r16", [16, 256]), ("hTe", [128, 8 * 256])):
                    self.dbg(name, shape)
                with ExitStack() as stkE:
                    self.declare_epi_ws(stkE)
                    self.declare_epi_psum(stkE, "_e")
                    s_wz = self.sem("s_wz")
                    def f_wzd(g):
                        s.wait(g)
                        if s_wqd is not None:
                            s_wqd.wait(g)
                        s_wz.inc(g.dma_start(self.wzT[:, :, :].rearrange("p a b -> p (a b)"),
                                             self.i_wzT[:, :]), 16)
                        s_wz.inc(g.dma_start(self.woutT[:, :, :].rearrange("p a b -> p (a b)"),
                                             self.i_woutT[:, :]), 16)
                    self.G_(f_wzd)
                    s_cc = self.build_ag2(s)
                    s_all = self.sem("s_epi_gate")
                    def f_eg(v):
                        s.wait(v)
                        s_cc.wait(v)
                        self.s_w.wait(v)
                        s_wz.wait(v)
                        for sp in self.s_osp + self.s_esp:
                            sp.wait(v)
                        if s_wqd is not None:
                            s_wqd.wait(v)
                        s_all.inc(v.nop(), 1)
                    self.V(f_eg)
                    s = s_all
                    for sl in range(self.T // 256):
                        s = self.build_epi_slab(sl, s)
                    s_ef = self.sem("s_epi_fin")
                    def f_ef(v):
                        s.wait(v)
                        for sp in self.s_xod:
                            sp.wait(v)
                        s_ef.inc(v.nop(), 1)
                    self.V(f_ef)
                    s = s_ef
            if sections >= 4:
                for name, shape in (("hf", [128, 8 * self.FS]), ("hq", [128, 8 * self.FS]),
                                    ("h2q", [128, self.NU * self.NCORE * self.ISHP]),
                                    ("h2qT", [128, self.NJ * self.FS])):
                    self.dbg(name, shape)
                with ExitStack() as stkF:
                    self.declare_ffn_ws(stkF)
                    self.declare_ffn_psum(stkF, "_f")
                    for u4 in range(self.NFS):
                        s = self.build_ffn_slab(u4, s, s_ag2)
                    for nm in ("hf", "hq", "h2q", "h2qT"):
                        if nm in self.dbg_bufs:
                            self.dump(nm, getattr(self, nm)[:, :, :].rearrange("p a b -> p (a b)"), s)
            if sections == 3 and self.dbg_bufs:
                for nm, tn in (("ofin", "ofin"), ("szs", "szs"), ("tf", "tf"), ("hgn", "hgn"), ("hTe", "hTe")):
                    if nm in self.dbg_bufs:
                        self.dump(nm, getattr(self, tn)[:, :, :].rearrange("p a b -> p (a b)"), s)
                if "r16" in self.dbg_bufs:
                    self.dump("r16", self.r16[:, :], s)
            if sections >= 4:
                def f_yfin(g):
                    s.wait(g)
                    self.s_yod.wait(g)
                self.G_(f_yfin)
            elif sections == 3:
                s_y = self.sem("s_y")
                def f_y3(g):
                    s.wait(g)
                    src = bass.AP(self.d_xp, 0, [[8 * self.T, 128], [1, 8 * self.T]])
                    s_y.inc(g.dma_start(self.o_y[:, :], src), 16)
                    s_y.wait(g)
                self.G_(f_y3)
        self.gstack.close()
        return self.nc


def prep_inputs(core, d, T, ncores=8):
    b, half = core // 2, core % 2
    x = np.asarray(d["L0_attn_hidden_states"][b], np.float32)
    t0 = half * T
    if t0 == 0:
        xs = np.concatenate([np.zeros((3, 1024), np.float32), x[0:T]], 0).T
    else:
        xs = x[t0 - 3: t0 + T].T
    TH = T + 3
    out = {}
    out["x"] = np.ascontiguousarray(xs.reshape(8, 128, TH).transpose(1, 0, 2)).reshape(128, 8 * TH)
    def packT(w):
        wT = np.ascontiguousarray(np.asarray(w, np.float32).T)
        n = wT.shape[1]
        return np.ascontiguousarray(wT.reshape(8, 128, n).transpose(1, 0, 2)).reshape(128, 8 * n)
    out["wqkvT"] = packT(d["L0_attn_W_qkv"])
    wab = np.zeros((64, 1024), np.float32)
    wab[0:16] = np.asarray(d["L0_attn_W_b"])
    wab[32:48] = np.asarray(d["L0_attn_W_a"])
    out["wabT"] = packT(wab)
    out["wzT"] = packT(d["L0_attn_W_z"])
    out["woutT"] = packT(d["L0_attn_W_out"])
    cw = np.asarray(d["L0_attn_conv_w"], np.float32)[:, 0, :]
    out["convw"] = np.ascontiguousarray(cw.reshape(16, 128, 4).transpose(1, 0, 2)).reshape(128, 64)
    wm = np.asarray(d["L0_wmod"], np.float32)
    out["wmod"] = np.ascontiguousarray(wm.reshape(8, 128, 6144).transpose(1, 0, 2)).reshape(128, 8 * 6144)
    out["crow"] = np.ascontiguousarray(np.asarray(d["L0_c"], np.float32)[b].reshape(8, 128).T)
    vec = np.zeros((16, 4), np.float32)
    vec[:, 0] = np.asarray(d["L0_attn_dt_bias"])
    vec[:, 1] = np.asarray(d["L0_attn_A_log"])
    out["vec"] = vec
    nwcol = np.tile(np.asarray(d["L0_attn_norm_weight"], np.float32), 2).reshape(128, 1)
    out["nw"] = np.concatenate([nwcol, np.full((128, 1), float(half), np.float32)], 1)
    r = np.arange(64)
    mA = (r[:, None] > r[None, :])
    mB = (r[:, None] < r[None, :])
    mI = (r[:, None] <= r[None, :])
    bigA = (1e30 * (1 - mA)).astype(np.float32)
    bigB = (1e30 * (1 - mB)).astype(np.float32)
    bigO = (1e30 * (1 - mI) + np.log(8.0)).astype(np.float32)
    m1 = np.concatenate([bigA, bigB, bigO], 1)
    out["masks"] = np.ascontiguousarray(np.concatenate([m1, m1], 0))
    out["eyed"] = np.ascontiguousarray(np.concatenate([np.eye(64), np.eye(64)], 0)).astype(ml_dtypes.bfloat16)
    cv = np.zeros((128, 8), np.float32); cv[:, 0] = 1e-5; cv[:, 1] = 1e-6
    cv[:, 2] = 1.0 if half == 1 else 0.0
    cv[:, 4] = 24.0
    cv[:, 5] = np.log(24.0)
    out["cvec"] = cv
    ISH = 2816 // ncores
    NR = (ISH + 127) // 128
    ISHP = NR * 128
    wg = np.asarray(d["L0_ffn_w_gate"], np.float32)
    wu = np.asarray(d["L0_ffn_w_up"], np.float32)
    wd_ = np.asarray(d["L0_ffn_w_down"], np.float32)
    gsh = np.zeros((ISHP, 1024), np.float32); gsh[0:ISH] = wg[ISH * core: ISH * (core + 1)]
    out["wg_sh"] = np.ascontiguousarray(gsh.reshape(NR, 128, 1024).transpose(1, 0, 2)).reshape(128, NR * 1024)
    ush = np.zeros((ISHP, 1024), np.float32); ush[0:ISH] = wu[ISH * core: ISH * (core + 1)]
    out["wu_sh"] = np.ascontiguousarray(ush.reshape(NR, 128, 1024).transpose(1, 0, 2)).reshape(128, NR * 1024)
    dsh = np.zeros((1024, ISHP), np.float32); dsh[:, 0:ISH] = wd_[:, ISH * core: ISH * (core + 1)]
    out["wd_sh"] = np.ascontiguousarray(dsh.reshape(8, 128, ISHP).transpose(1, 0, 2)).reshape(128, 8 * ISHP)
    gnsel = np.zeros((128, 8, 16), np.float32)
    for h in range(16):
        gnsel[64 * (h % 2):64 * (h % 2) + 64, h // 2, h] = 1.0
    out["gnsel"] = np.ascontiguousarray(gnsel.reshape(128, 128))
    out["gnselT"] = np.ascontiguousarray(gnsel.transpose(2, 1, 0).reshape(16, 1024))
    out["eyeb"] = np.eye(128).astype(ml_dtypes.bfloat16)
    out["eyef"] = np.eye(128, dtype=np.float32)
    out["ones"] = np.ones((128, 129), np.float32)
    return out


def kernel(**inputs):
    T = 4096
    kb = KB(T=T, SS=256, ncores=8)
    nc = kb.build_section1(sections=4)
    in_maps = [prep_inputs(c, inputs, T, ncores=8) for c in range(8)]
    from concourse.bass_utils import run_bass_kernel_spmd
    res = run_bass_kernel_spmd(nc, in_maps, core_ids=list(range(8)))
    out = np.zeros((4, 8192, 1024), np.float32)
    for c in range(8):
        b, half = c // 2, c % 2
        y = np.asarray(res.results[c]["y"]).reshape(128, 8, T)
        out[b, half * T:(half + 1) * T] = y.transpose(2, 1, 0).reshape(T, 1024)
    return out


# version 06 (final)
# speedup vs baseline: 1.1996x; 1.0012x over previous
import numpy as np
import ml_dtypes
from contextlib import ExitStack

import concourse.bass as bass
import concourse.mybir as mybir

F32 = mybir.dt.float32
BF16 = mybir.dt.bfloat16
ALU = mybir.AluOpType
ACT = mybir.ActivationFunctionType

NKH, NVH, DK, DV = 8, 16, 64, 64
KEY, VAL, CDIM, KSZ = 512, 1024, 2048, 4
CH = 64


def ts(i, n):
    return slice(i * n, (i + 1) * n)


def register_custom_ops():
    return


class Sem:
    def __init__(self, nc, stack, name):
        self.h = stack.enter_context(nc.semaphore(name))
        self.val = 0
    def inc(self, instr, n=1):
        self.val += n
        instr.then_inc(self.h, n)
        return instr
    def wait(self, eng, val=None):
        eng.wait_ge(self.h, self.val if val is None else val)


class KB:
    def __init__(self, T=2048, SS=256, debug=(), ncores=8):
        register_custom_ops()
        assert T % SS == 0 and SS % CH == 0 and SS % 128 == 0
        self.T, self.SS = T, SS
        self.TH = T + 3
        self.SSH = SS + 3
        self.NSS = T // SS
        self.NCH = T // CH
        self.NCORE = ncores
        self.ISH = 2816 // ncores
        self.NR = (self.ISH + 127) // 128
        self.ISHP = self.NR * 128
        self.NJ = (ncores * self.ISHP) // 128
        self.FS = min(512, T)
        self.NFS = T // self.FS
        self.NU = self.FS // 128
        self.SCROFF = 544
        self.debug = set(debug)
        self.nc = bass.Bass("TRN2", target_bir_lowering=False)
        self.gstack = ExitStack()
        self.dbg_bufs = {}

    def inp(self, name, shape, dtype=F32):
        return self.nc.dram_tensor(name, shape, dtype, kind="ExternalInput")
    def outp(self, name, shape, dtype=F32):
        return self.nc.dram_tensor(name, shape, dtype, kind="ExternalOutput")
    def sb(self, stack, name, shape, dtype):
        return stack.enter_context(self.nc.sbuf_tensor(name, shape, dtype))
    def ps(self, stack, name, shape, dtype=F32):
        return stack.enter_context(self.nc.psum_tensor(name, shape, dtype))
    def sem(self, name):
        if not hasattr(self, "_semcache"):
            self._semcache = {}
        if name not in self._semcache:
            self._semcache[name] = Sem(self.nc, self.gstack, name)
        return self._semcache[name]
    def V(self, fn): self.blk.vector(fn)
    def A(self, fn): self.blk.scalar(fn)
    def P(self, fn): self.blk.tensor(fn)
    def G_(self, fn): self.blk.gpsimd(fn)
    def S_(self, fn): self.blk.sync(fn)

    def dbg(self, name, shape, dtype=F32):
        if name in self.debug:
            self.dbg_bufs[name] = self.outp("dbg_" + name, shape, dtype)

    def dump(self, name, src_ap, s_after):
        if name not in self.dbg_bufs:
            return
        h = self.dbg_bufs[name]
        s_d = self.sem("s_dbg_" + name)
        def f(g):
            s_after.wait(g)
            s_d.inc(g.dma_start(h[:, :], src_ap), 16)
            s_d.wait(g)
        self.G_(f)

    def declare_global(self):
        T, TH, SS, SSH = self.T, self.TH, self.SS, self.SSH
        gs = self.gstack
        self.i_x = self.inp("x", [128, 8 * TH])
        self.i_wqkvT = self.inp("wqkvT", [128, 8 * 2048])
        self.i_wabT = self.inp("wabT", [128, 8 * 64])
        self.i_wzT = self.inp("wzT", [128, 8 * 1024])
        self.i_woutT = self.inp("woutT", [128, 8 * 1024])
        self.i_convw = self.inp("convw", [128, 16 * 4])
        self.i_wmod = self.inp("wmod", [128, 8 * 6144])
        self.i_crow = self.inp("crow", [128, 8])
        self.i_vec = self.inp("vec", [16, 4])
        self.i_nw = self.inp("nw", [128, 2])
        self.i_masks = self.inp("masks", [128, 3 * 64])
        self.i_eyed = self.inp("eyed", [128, 64], BF16)
        self.i_eyeb = self.inp("eyeb", [128, 128], BF16)
        self.i_eyef = self.inp("eyef", [128, 128])
        self.i_ones = self.inp("ones", [128, 129])
        self.i_cvec = self.inp("cvec", [128, 8])
        self.i_gnsel = self.inp("gnsel", [128, 8 * 16])
        self.i_gnselT = self.inp("gnselT", [16, 8 * 128])
        self.i_wg = self.inp("wg_sh", [128, self.NR * 1024])
        self.i_wu = self.inp("wu_sh", [128, self.NR * 1024])
        self.i_wd = self.inp("wd_sh", [128, 8 * self.ISHP])
        self.o_y = self.outp("y", [128, 8 * T])
        self.dbc = self.nc.dram_tensor("dbc", [8192], F32)
        self.cc_in = self.nc.dram_tensor("cc_in", [64, 1024], F32)
        self.cc_out = self.nc.dram_tensor("cc_out", [128, 1024], F32)
        self.d_xp = self.nc.dram_tensor("d_xp", [128, 8 * T], F32)
        self.d_opre = self.nc.dram_tensor("d_opre", [128, 8 * T], BF16)
        self.d_EpT = self.nc.dram_tensor("d_EpT", [128, 8 * T], BF16)
        PB = self.ISHP
        self.cc_g_in = self.nc.dram_tensor("cc_g_in", [1024, PB], BF16)
        self.cc_u_in = self.nc.dram_tensor("cc_u_in", [1024, PB], BF16)
        self.cc_d_in = self.nc.dram_tensor("cc_d_in", [PB, 1024], BF16)
        self.g_gT = self.nc.dram_tensor("g_gT", [self.NCORE * 1024, PB], BF16)
        self.g_uT = self.nc.dram_tensor("g_uT", [self.NCORE * 1024, PB], BF16)
        self.g_dT = self.nc.dram_tensor("g_dT", [self.NCORE * PB, 1024], BF16)
        self.Sst = [self.sb(gs, "g_S0", [64, 16, DV], F32),
                    self.sb(gs, "g_S1", [64, 16, DV], F32)]
        self.Wb = [self.sb(gs, "g_W0", [64, 16, DK], BF16),
                   self.sb(gs, "g_W1", [64, 16, DK], BF16)]
        self.statpack = self.sb(gs, "g_stats", [33, TH], F32)
        self.stats = self.statpack
        self.statsm = self.statpack
        self.convw = self.sb(gs, "g_convw", [128, 16, 4], F32)
        self.crow = self.sb(gs, "g_crow", [128, 8], F32)
        self.vecs = self.sb(gs, "g_vecs", [16, 4], F32)
        self.nw = self.sb(gs, "g_nw", [128, 2], F32)
        self.masks = self.sb(gs, "g_masks", [128, 3 * 64], F32)
        self.eyed = self.sb(gs, "g_eyed", [128, 64], BF16)
        self.eyeb = self.sb(gs, "g_eyeb", [128, 128], BF16)
        self.eyef = self.sb(gs, "g_eyef", [128, 128], F32)
        self.onesf = self.sb(gs, "g_onesf", [128, 129], F32)
        self.cvec = self.sb(gs, "g_cvec", [128, 8], F32)
        self.gnsel = self.sb(gs, "g_gnsel", [128, 8, 16], F32)
        self.gnselT = self.sb(gs, "g_gnselT", [16, 8, 128], F32)
        self.adaln = self.sb(gs, "g_adaln", [128, 8, 6], F32)
        self.negea = self.sb(gs, "g_negea", [16, 1], F32)
        self.scanmask = self.sb(gs, "g_scanmask", [16, SS], F32)
        self.scr = self.sb(gs, "g_scr", [128, 1088], F32)
        self.scrb = self.sb(gs, "g_scrb", [128, 576], BF16)

    def declare_phaseA(self, stk):
        SS, SSH = self.SS, self.SSH
        self.xs = self.sb(stk, "a_xs", [128, 8, SSH], F32)
        self.wqb = [self.sb(stk, "a_wq0", [128, 8, 128], BF16),
                    self.sb(stk, "a_wq1", [128, 8, 128], BF16)]
        self.wabT = self.sb(stk, "a_wabT", [128, 8, 64], BF16)
        self.hT = self.sb(stk, "a_hT", [128, 8, SSH], BF16)
        self.mixed = self.sb(stk, "a_mixed", [128, 16, SSH], F32)
        self.cv1 = self.sb(stk, "a_cv1", [128, SS], F32)
        self.cv2 = self.sb(stk, "a_cv2", [128, SS], F32)
        self.qkvs = self.sb(stk, "a_qkvs", [128, 16, SS], BF16)
        self.kTs = self.sb(stk, "a_kTs", [128, SS // 128, KEY], BF16)
        self.vTs = self.sb(stk, "a_vTs", [128, SS // 128, VAL], BF16)
        self.dec = self.sb(stk, "a_dec", [128, SS], F32)
        self.dec2 = self.sb(stk, "a_dec2", [96, SS], F32)
        self.decT = self.sb(stk, "a_decT", [128, SS // 128, 128], F32)
        self.decT2 = self.sb(stk, "a_decT2", [128, SS // 128, 96], F32)
        self.sA = self.sb(stk, "a_sA", [16, SS], F32)
        self.sB = self.sb(stk, "a_sB", [16, SS], F32)
        self.sC = self.sb(stk, "a_sC", [16, SS], F32)
        self.r_Gp = self.dec[0:16, :]
        self.r_G = self.dec[32:48, :]
        self.r_lamq = self.dec[64:80, :]
        self.r_blam = self.dec[96:112, :]
        self.r_bet = self.dec2[0:16, :]
        self.r_erev = self.dec2[32:48, :]
        self.r_lamr = self.dec2[64:80, :]

    def build_p0(self, stk):
        s_in = self.s_in = self.sem("s_in")
        s_w = self.s_w = self.sem("s_w")
        s_msk = self.s_msk = self.sem("s_msk")
        s_p0 = self.s_p0 = self.sem("s_p0")
        wmbuf = self.sb(stk, "p0_wmod", [128, 6144], BF16)
        modacc = self.sb(stk, "p0_modacc", [128, 48], F32)
        self.pR1 = self.ps(stk, "p_p0mod", [128, 512])

        def f_sync(s):
            for dst, src in ((self.convw, self.i_convw),
                             (self.vecs, self.i_vec), (self.nw, self.i_nw),
                             (self.masks, self.i_masks), (self.eyed, self.i_eyed), (self.eyeb, self.i_eyeb),
                             (self.eyef, self.i_eyef), (self.onesf, self.i_ones),
                             (self.cvec, self.i_cvec), (self.gnsel, self.i_gnsel), (self.gnselT, self.i_gnselT),
                             (self.crow, self.i_crow)):
                d = dst[:, :, :].rearrange("p a b -> p (a b)") if len(dst.shape) == 3 else dst[:, :]
                s_in.inc(s.dma_start(d, src[:, :]), 16)
        self.S_(f_sync)

        def f_g(g):
            s_w.inc(g.dma_start(self.wabT[:, :, :].rearrange("p a b -> p (a b)"),
                                self.i_wabT[:, :]), 16)
            g.memset(self.scanmask[:, :], 1.0)
            g.drain()
            g.memset(bass.AP(self.scanmask, 0, [[self.SS, 16], [CH, self.SS // CH]]), 0.0)
            g.drain()
            g.memset(self.dec[:, :], 0.0)
            g.drain()
            g.memset(self.dec2[:, :], 0.0)
            g.drain()
            g.memset(self.Sst[0][:, :, :].rearrange("p a b -> p (a b)"), 0.0)
            g.drain()
            s_msk.inc(g.nop(), 1)
        self.G_(f_g)

        def f_wi(v):
            s_in.wait(v)
            for j in range(16):
                v.tensor_copy(self.Wb[0][:, j, :], self.eyed[0:64, :])
            v.drain()
            s_msk.inc(v.nop(), 1)
        self.V(f_wi)
        def f_a(a):
            s_in.wait(a)
            a.activation(self.scr[:, 0:8], self.crow[:, :], ACT.Silu)
            a.drain()
            s_p0.inc(a.nop(), 1)
        self.A(f_a)
        def f_v(v):
            s_p0.wait(v)
            v.tensor_copy(self.scrb[:, 0:8], self.scr[:, 0:8])
            v.drain()
            s_p0.inc(v.nop(), 1)
        self.V(f_v)

        s_wmd = self.sem("s_wmd")
        s_wmc = self.sem("s_wmc")
        for k in range(8):
            def f_gk(g, k=k):
                if k > 0:
                    g.wait_ge(s_wmc.h, 2 * k - 1)
                s_wmd.inc(g.dma_start(wmbuf[:, :], self.i_wmod[:, ts(k, 6144)]), 16)
            self.G_(f_gk)
            def f_tk(t, k=k):
                s_wmd.wait(t)
                s_p0.wait(t, 2)
                if k > 0:
                    t.wait_ge(s_wmc.h, 2 * k)
                for j in range(48):
                    t.matmul(self.pR1[:, j:j + 1], wmbuf[:, ts(j, 128)], self.scrb[:, k:k + 1],
                             start=True, stop=True)
                t.drain()
                s_wmc.inc(t.nop(), 1)
            self.P(f_tk)
            def f_vk(v, k=k):
                s_wmc.wait(v)
                if k == 0:
                    v.tensor_copy(modacc[:, :], self.pR1[:, 0:48])
                else:
                    v.tensor_tensor(modacc[:, :], modacc[:, :], self.pR1[:, 0:48], ALU.add)
                v.drain()
                s_wmc.inc(v.nop(), 1)
            self.V(f_vk)
        s_wm = s_wmc
        def f_v2(v):
            s_wm.wait(v)
            v.tensor_copy(self.adaln[:, :, :].transpose([0, 2, 1]), modacc[:, :])
            v.drain()
            v.tensor_scalar(self.adaln[:, :, 1], self.adaln[:, :, 1], 1.0, None, ALU.add)
            v.tensor_scalar(self.adaln[:, :, 4], self.adaln[:, :, 4], 1.0, None, ALU.add)
            v.drain()
            s_p0.inc(v.nop(), 1)
        self.V(f_v2)
        def f_a2(a):
            s_p0.wait(a)
            a.activation(self.scr[:, 0:8], self.adaln[:, :, 2], ACT.Abs)
            a.activation(self.scr[:, 8:16], self.adaln[:, :, 5], ACT.Abs)
            a.activation(self.scr[:, 16:24], self.adaln[:, :, 2], ACT.Sign)
            a.activation(self.scr[:, 24:32], self.adaln[:, :, 5], ACT.Sign)
            a.activation(self.negea[:, 0:1], self.vecs[:, 1:2], ACT.Exp)
            a.drain()
            s_p0.inc(a.nop(), 1)
        self.A(f_a2)
        def f_v3(v):
            s_p0.wait(v)
            v.tensor_scalar(self.scr[:, 0:16], self.scr[:, 0:16], 0.25, None, ALU.max)
            v.tensor_scalar(self.negea[:, 0:1], self.negea[:, 0:1], -1.0, None, ALU.mult)
            v.drain()
            v.tensor_tensor(self.adaln[:, :, 2], self.scr[:, 0:8], self.scr[:, 16:24], ALU.mult)
            v.tensor_tensor(self.adaln[:, :, 5], self.scr[:, 8:16], self.scr[:, 24:32], ALU.mult)
            v.drain()
            s_p0.inc(v.nop(), 1)
        self.V(f_v3)

    def ln_apply(self, which, w0, n, s_pre, sq_scratch, stats_r, stats_m, dst, halo_mask,
                 xsrc, x0):
        ai_s, ai_c = (0, 1) if which == 0 else (3, 4)
        s_loc = self.sem("s_ln%d" % which)
        def f_sq(a):
            s_pre.wait(a)
            for d in range(8):
                a.activation(sq_scratch[:, d, 0:n], xsrc[:, d, x0:x0 + n], ACT.Square)
            a.drain()
            s_loc.inc(a.nop(), 1)
        self.A(f_sq)
        def f_mm(t):
            s_loc.wait(t)
            s_pre.wait(t)
            for d in range(8):
                t.matmul(self.pS[0:1, 0:n], self.onesf[:, 0:1], xsrc[:, d, x0:x0 + n],
                         start=(d == 0), stop=(d == 7))
            for d in range(8):
                t.matmul(self.pQ[0:1, 0:n], self.onesf[:, 0:1], sq_scratch[:, d, 0:n],
                         start=(d == 0), stop=(d == 7))
            t.drain()
            s_loc.inc(t.nop(), 1)
        self.P(f_mm)
        def f_vst(v):
            s_loc.wait(v)
            v.tensor_scalar(self.scr[0:1, 0:n], self.pS[0:1, 0:n], 1.0 / 1024.0, None, ALU.mult)
            v.drain()
            v.tensor_tensor(self.scr[0:1, 0:n], self.scr[0:1, 0:n], self.scr[0:1, 0:n], ALU.mult)
            v.drain()
            v.scalar_tensor_tensor(self.scr[0:1, self.SCROFF:self.SCROFF + n], self.pQ[0:1, 0:n], 1.0 / 1024.0,
                                   self.scr[0:1, 0:n], ALU.mult, ALU.subtract)
            v.drain()
            s_loc.inc(v.nop(), 1)
        self.V(f_vst)
        def f_ast(a):
            s_loc.wait(a)
            a.activation(self.scr[0:1, 0:n], self.scr[0:1, self.SCROFF:self.SCROFF + n], ACT.Ln, bias=self.cvec[0:1, 0:1])
            a.drain()
            a.activation(stats_r[0:1, w0:w0 + n], self.scr[0:1, 0:n], ACT.Exp, scale=-0.5)
            a.drain()
            s_loc.inc(a.nop(), 1)
        self.A(f_ast)
        def f_vst2(v):
            s_loc.wait(v)
            v.scalar_tensor_tensor(stats_m[32:33, w0:w0 + n], self.pS[0:1, 0:n], 1.0 / 1024.0,
                                   stats_r[0:1, w0:w0 + n], ALU.mult, ALU.mult)
            v.drain()
            s_loc.inc(v.nop(), 1)
        self.V(f_vst2)
        self.ln_rep_apply(which, w0, n, s_loc, stats_r, stats_m, dst, halo_mask, xsrc, x0)
        return s_loc

    def ln_rep_apply(self, which, w0, n, s_loc, stats_r, stats_m, dst, halo_mask, xsrc, x0):
        ai_s, ai_c = (0, 1) if which == 0 else (3, 4)
        def f_rep(t):
            s_loc.wait(t)
            t.matmul(self.pR1[:, 0:n], self.onesf[0:1, 0:128], stats_r[0:1, w0:w0 + n],
                     start=True, stop=True)
            t.drain()
            t.matmul(self.pR2[:, 0:n], self.onesf[32:33, 0:128], stats_m[32:33, w0:w0 + n],
                     start=True, stop=True)
            t.drain()
            s_loc.inc(t.nop(), 1)
        self.P(f_rep)
        def f_h(v):
            s_loc.wait(v)
            for d in range(8):
                v.tensor_tensor(self.scr[:, 0:n], xsrc[:, d, x0:x0 + n], self.pR1[:, 0:n], ALU.mult)
                v.drain()
                v.tensor_tensor(self.scr[:, self.SCROFF:self.SCROFF + n], self.scr[:, 0:n], self.pR2[:, 0:n], ALU.subtract)
                v.drain()
                v.tensor_scalar(dst[:, d, 0:n], self.scr[:, self.SCROFF:self.SCROFF + n],
                                self.adaln[:, d, ai_c:ai_c + 1], self.adaln[:, d, ai_s:ai_s + 1],
                                ALU.mult, ALU.add)
                v.drain()
            if halo_mask:
                for d in range(8):
                    v.tensor_scalar(dst[:, d, 0:3], dst[:, d, 0:3], self.nw[:, 1:2], None, ALU.mult)
                v.drain()
            s_loc.inc(v.nop(), 1)
        self.V(f_h)

    def build_p1_superslab(self, ss, s_pre):
        T, SS, SSH = self.T, self.SS, self.SSH
        w0 = ss * SS
        s_xs = self.sem("s_xs")
        def f_xd(g):
            s_pre.wait(g)
            ap_x = bass.AP(self.i_x, w0, [[8 * self.TH, 128], [self.TH, 8], [1, SSH]])
            s_xs.inc(g.dma_start(self.xs[:, :, :], ap_x), 16)
        self.G_(f_xd)
        s_g0 = self.sem("s_lng")
        def f_xg(v):
            s_pre.wait(v)
            s_xs.wait(v)
            s_g0.inc(v.nop(), 1)
        self.V(f_xg)
        s_ln = self.ln_apply(0, w0, SSH, s_g0, self.mixed, self.stats, self.statsm, self.hT,
                             halo_mask=(ss == 0), xsrc=self.xs, x0=0)
        s_qkv = self.sem("s_qkv")
        s_cv = self.sem("s_cv")
        s_wqp = [self.sem("s_wqa"), self.sem("s_wqb")]
        q_base, cv_base = s_qkv.val, s_cv.val
        for chn in range(16):
            pp = self.pA if chn % 2 == 0 else self.pB
            wb = self.wqb[chn % 2]
            swq = s_wqp[chn % 2]
            def f_wl(g, chn=chn, wb=wb, swq=swq):
                if chn >= 2:
                    g.wait_ge(s_qkv.h, q_base + chn - 1)
                swq.inc(g.dma_start(wb[:, :, :].rearrange("p a b -> p (a b)"),
                                    self.i_wqkvT[:, :].rearrange("p (a b) -> p a b", b=2048)[:, :, ts(chn, 128)]),
                        16)
            self.G_(f_wl)
            def f_mm(t, chn=chn, pp=pp, wb=wb, swq=swq):
                s_ln.wait(t)
                swq.wait(t)
                if chn >= 2:
                    s_cv.wait(t, cv_base + (chn - 2) * 2 + 1)
                for k in range(8):
                    t.matmul(pp[:, 0:SSH], wb[:, k, :], self.hT[:, k, :],
                             start=(k == 0), stop=(k == 7))
                t.drain()
                s_qkv.inc(t.nop(), 1)
            self.P(f_mm)
            def f_cv(v, chn=chn, pp=pp):
                s_qkv.wait(v)
                s_cv.wait(v)
                cw = self.convw
                v.tensor_scalar(self.cv1[:, 0:SS], pp[:, 0:SS],
                                cw[:, chn, 0:1], None, ALU.mult)
                v.drain()
                v.scalar_tensor_tensor(self.cv2[:, 0:SS], pp[:, 1:1 + SS],
                                       cw[:, chn, 1:2], self.cv1[:, 0:SS], ALU.mult, ALU.add)
                v.drain()
                v.scalar_tensor_tensor(self.cv1[:, 0:SS], pp[:, 2:2 + SS],
                                       cw[:, chn, 2:3], self.cv2[:, 0:SS], ALU.mult, ALU.add)
                v.drain()
                v.scalar_tensor_tensor(self.cv2[:, 0:SS], pp[:, 3:3 + SS],
                                       cw[:, chn, 3:4], self.cv1[:, 0:SS], ALU.mult, ALU.add)
                v.drain()
                s_cv.inc(v.nop(), 1)
            self.V(f_cv)
            def f_si(a, chn=chn):
                s_cv.wait(a)
                a.activation(self.qkvs[:, chn, :], self.cv2[:, 0:SS], ACT.Silu)
                a.drain()
                s_cv.inc(a.nop(), 1)
            self.A(f_si)
        s_ab = self.sem("s_ab")
        def f_ab(t):
            s_ln.wait(t)
            self.s_w.wait(t)
            for k in range(8):
                t.matmul(self.pR1[0:64, 0:SS], self.wabT[:, k, :], self.hT[:, k, 3:3 + SS],
                         start=(k == 0), stop=(k == 7))
            t.drain()
            s_ab.inc(t.nop(), 1)
        self.P(f_ab)
        def f_d0(v):
            s_ab.wait(v)
            v.tensor_scalar(self.sA[:, 0:SS], self.pR1[32:48, 0:SS], self.vecs[:, 0:1], None, ALU.add)
            v.drain()
            s_ab.inc(v.nop(), 1)
        self.V(f_d0)
        def f_d1(a):
            s_ab.wait(a)
            a.activation(self.r_bet[:, :], self.pR1[0:16, 0:SS], ACT.Sigmoid)
            a.drain()
            a.activation(self.sB[:, 0:SS], self.sA[:, 0:SS], ACT.Abs)
            a.drain()
            a.activation(self.sB[:, 0:SS], self.sB[:, 0:SS], ACT.Exp, scale=-1.0)
            a.drain()
            s_ab.inc(a.nop(), 1)
        self.A(f_d1)
        def f_d2(v):
            s_ab.wait(v)
            v.tensor_scalar(self.sB[:, 0:SS], self.sB[:, 0:SS], 1.0, None, ALU.add)
            v.drain()
            s_ab.inc(v.nop(), 1)
        self.V(f_d2)
        def f_d3(a):
            s_ab.wait(a)
            a.activation(self.sB[:, 0:SS], self.sB[:, 0:SS], ACT.Ln)
            a.drain()
            s_ab.inc(a.nop(), 1)
        self.A(f_d3)
        def f_d4(v):
            s_ab.wait(v)
            v.tensor_scalar(self.sC[:, 0:SS], self.sA[:, 0:SS], 0.0, None, ALU.max)
            v.drain()
            v.tensor_tensor(self.sC[:, 0:SS], self.sC[:, 0:SS], self.sB[:, 0:SS], ALU.add)
            v.drain()
            v.tensor_scalar(self.sC[:, 0:SS], self.sC[:, 0:SS], self.negea[:, 0:1], None, ALU.mult)
            v.drain()
            s_ab.inc(v.nop(), 1)
        self.V(f_d4)
        def f_d5(a):
            s_ab.wait(a)
            a.activation(self.sA[:, 0:SS], self.pR1[0:16, 0:SS], ACT.Abs)
            a.drain()
            a.activation(self.sA[:, 0:SS], self.sA[:, 0:SS], ACT.Exp, scale=-1.0)
            a.drain()
            s_ab.inc(a.nop(), 1)
        self.A(f_d5)
        def f_d6(v):
            s_ab.wait(v)
            v.tensor_scalar(self.sA[:, 0:SS], self.sA[:, 0:SS], 1.0, None, ALU.add)
            v.drain()
            s_ab.inc(v.nop(), 1)
        self.V(f_d6)
        def f_d7(a):
            s_ab.wait(a)
            a.activation(self.sA[:, 0:SS], self.sA[:, 0:SS], ACT.Ln)
            a.drain()
            s_ab.inc(a.nop(), 1)
        self.A(f_d7)
        def f_d8(v):
            s_ab.wait(v)
            self.s_msk.wait(v)
            v.tensor_scalar(self.sB[:, 0:SS], self.pR1[0:16, 0:SS], -1.0, 0.0, ALU.mult, ALU.max)
            v.drain()
            v.scalar_tensor_tensor(self.sB[:, 0:SS], self.sB[:, 0:SS], -1.0, self.sA[:, 0:SS],
                                   ALU.mult, ALU.subtract)
            v.drain()
            v.tensor_tensor_scan(self.sA[:, 0:SS], self.scanmask[:, :], self.sC[:, 0:SS],
                                 0.0, ALU.mult, ALU.add)
            v.drain()
            v.tensor_copy(self.r_G[:, :], self.sA[:, 0:SS])
            v.drain()
            v.tensor_tensor(self.r_Gp[:, :], self.sA[:, 0:SS], self.sB[:, 0:SS], ALU.add)
            v.drain()
            s_ab.inc(v.nop(), 1)
        self.V(f_d8)
        def f_d9(a):
            s_ab.wait(a)
            a.activation(self.sC[:, 0:SS], self.sA[:, 0:SS], ACT.Exp)
            a.drain()
            s_ab.inc(a.nop(), 1)
        self.A(f_d9)
        def f_d10(v):
            s_ab.wait(v)
            v.tensor_copy(self.r_lamr[:, :], self.sC[:, 0:SS])
            v.drain()
            v.tensor_tensor(self.r_blam[:, :], self.r_bet[:, :], self.sC[:, 0:SS], ALU.mult)
            v.drain()
            v.tensor_scalar(self.r_lamq[:, :], self.sC[:, 0:SS], 0.125, None, ALU.mult)
            v.drain()
            gend = bass.AP(self.dec, 32 * SS + CH - 1, [[SS, 16], [CH, SS // CH], [0, CH]])
            v.tensor_tensor(self.sA[:, 0:SS].rearrange("p (c u) -> p c u", u=CH), gend,
                            self.r_G[:, :].rearrange("p (c u) -> p c u", u=CH), ALU.subtract)
            v.drain()
            s_ab.inc(v.nop(), 1)
        self.V(f_d10)
        def f_d11(a):
            s_ab.wait(a)
            a.activation(self.r_erev[:, :], self.sA[:, 0:SS], ACT.Exp)
            a.drain()
            s_ab.inc(a.nop(), 1)
        self.A(f_d11)
        s_tr = self.sem("s_tr")
        ntt = SS // 128
        for tt in range(ntt):
            def f_tp1(t, tt=tt):
                s_ab.wait(t)
                self.s_msk.wait(t)
                if s_tr.val:
                    s_tr.wait(t)
                t.transpose(self.pT[:, 0:128], self.dec[:, tt * 128:(tt + 1) * 128],
                            self.eyef[:, :])
                t.drain()
                s_tr.inc(t.nop(), 1)
            self.P(f_tp1)
            def f_te1(a, tt=tt):
                s_tr.wait(a)
                a.activation(self.decT[:, tt, :], self.pT[:, 0:128], ACT.Copy)
                a.drain()
                s_tr.inc(a.nop(), 1)
            self.A(f_te1)
            def f_tp1b(t, tt=tt):
                s_tr.wait(t)
                t.transpose(self.pT[:, 0:96], self.dec2[:, tt * 128:(tt + 1) * 128],
                            self.eyef[0:96, 0:96])
                t.drain()
                s_tr.inc(t.nop(), 1)
            self.P(f_tp1b)
            def f_te1b(a, tt=tt):
                s_tr.wait(a)
                a.activation(self.decT2[:, tt, :], self.pT[:, 0:96], ACT.Copy)
                a.drain()
                s_tr.inc(a.nop(), 1)
            self.A(f_te1b)
        for tt in range(ntt):
            for rr0, nrr in ((0, 4), (4, 4), (8, 4)):
                def f_tp2(t, rr0=rr0, nrr=nrr, tt=tt):
                    s_cv.wait(t)
                    s_tr.wait(t)
                    for i in range(nrr):
                        t.transpose(self.pTb[:, 128 * i:128 * i + 128],
                                    self.qkvs[:, 4 + rr0 + i, tt * 128:(tt + 1) * 128],
                                    self.eyeb[:, :])
                    t.drain()
                    s_tr.inc(t.nop(), 1)
                self.P(f_tp2)
                def f_te2(v, rr0=rr0, nrr=nrr, tt=tt):
                    s_tr.wait(v)
                    if rr0 == 0:
                        dst = self.kTs[:, tt, 0:512]
                    elif rr0 == 4:
                        dst = self.vTs[:, tt, 0:512]
                    else:
                        dst = self.vTs[:, tt, 512:1024]
                    v.tensor_copy(dst, self.pTb[:, 0:128 * nrr])
                    v.drain()
                    s_tr.inc(v.nop(), 1)
                self.V(f_te2)
        s_done = self.sem("s_ss")
        def f_fin(v):
            s_tr.wait(v)
            s_cv.wait(v)
            s_ab.wait(v)
            s_done.inc(v.nop(), 1)
        self.V(f_fin)
        return s_done, s_cv


    def declare_scan_ws(self, stk):
        self.Grep = self.sb(stk, "c_Grep", [128, 1024], F32)
        self.Gprep = self.sb(stk, "c_Gprep", [128, 1024], F32)
        self.lamrep = self.sb(stk, "c_lamrep", [128, 2048], F32)
        self.lamCrep = self.sb(stk, "c_lamCrep", [64, 32], F32)
        self.Dw = self.sb(stk, "c_Dw", [128, 1024], F32)
        self.Aw = self.sb(stk, "c_Aw", [128, 1024], BF16)
        self.Bw = self.sb(stk, "c_Bw", [128, 1024], BF16)
        self.QKDT = self.sb(stk, "c_QKDT", [128, 1024], BF16)
        self.Pf = self.sb(stk, "c_Pf", [128, 1024], BF16)
        self.PfT = self.sb(stk, "c_PfT", [128, 1024], BF16)
        self.Pg = self.sb(stk, "c_Pg", [128, 1024], BF16)
        self.PgT = self.sb(stk, "c_PgT", [128, 1024], BF16)
        self.IPb = self.sb(stk, "c_IP", [128, 1024], BF16)
        self.Yc = self.sb(stk, "c_Yc", [128, 1024], BF16)
        self.Yn = self.sb(stk, "c_Yn", [128, 1024], BF16)
        self.Rw = self.sb(stk, "c_Rw", [128, 2048], BF16)
        self.Qt = self.sb(stk, "c_Qt", [64, 1024], F32)
        self.KbT = self.sb(stk, "c_KbT", [128, 1024], BF16)
        self.Uvp = self.sb(stk, "c_Uvp", [128, 1024], BF16)
        self.Ukp = self.sb(stk, "c_Ukp", [128, 1024], BF16)
        self.nUkT = self.sb(stk, "c_nUkT", [64, 1024], F32)
        self.nMT = self.sb(stk, "c_nMT", [64, 1024], BF16)
        self.Et = self.sb(stk, "c_Et", [64, 1024], BF16)
        self.Ssc = self.sb(stk, "c_Ssc", [64, 1024], F32)
        self.Wsc = self.sb(stk, "c_Wsc", [64, 1024], BF16)
        self.Ucp = self.sb(stk, "c_Ucp", [128, 1024], BF16)
        self.opre_st = [self.sb(stk, "c_ost0", [128, 8, CH], BF16),
                        self.sb(stk, "c_ost1", [128, 8, CH], BF16)]
        self.EpT_st = [self.sb(stk, "c_est0", [128, 8, CH], BF16),
                       self.sb(stk, "c_est1", [128, 8, CH], BF16)]
        self.s_osp = [self.sem("s_osp0"), self.sem("s_osp1")]
        self.s_esp = [self.sem("s_esp0"), self.sem("s_esp1")]

    def declare_scan_psum(self, stk, tag):
        self.pX = self.ps(stk, "sp_X" + tag, [128, 1024])
        self.pY = self.ps(stk, "sp_Y" + tag, [128, 1024])
        self.pZ = self.ps(stk, "sp_Z" + tag, [128, 1024])
        self.pV = self.ps(stk, "sp_V" + tag, [64, 1024])

    def declare_p1_psum(self, stk, tag):
        self.pS = self.ps(stk, "p_S" + tag, [1, 512])
        self.pQ = self.ps(stk, "p_Q" + tag, [1, 512])
        self.pR1 = self.ps(stk, "p_R1" + tag, [128, 512])
        self.pR2 = self.ps(stk, "p_R2" + tag, [128, 512])
        self.pA = self.ps(stk, "p_A" + tag, [128, 512])
        self.pB = self.ps(stk, "p_B" + tag, [128, 512])
        self.pT = self.ps(stk, "p_T" + tag, [128, 128])
        self.pTb = self.ps(stk, "p_Tb" + tag, [128, 512], BF16)

    def q_src(self, h, cols):
        j = h // 2
        return self.qkvs[64 * (j % 2):64 * (j % 2) + 64, j // 2, cols]
    def k_src(self, h, cols):
        j = h // 2
        return self.qkvs[64 * (j % 2):64 * (j % 2) + 64, 4 + j // 2, cols]

    @staticmethod
    def hcol(h):
        return (h % 2) * 8 + h // 2

    def build_scan_pair(self, ss, tp, s_p1, s_prev):
        SS = self.SS
        tt = tp
        gc0 = ss * (SS // CH) + 2 * tp
        s_fd = self.sem("s_sfd")
        s_sc = self.sem("s_scc")

        def f_bd(g):
            s_p1.wait(g)
            s_prev.wait(g)
            for half in (0, 1):
                cc = slice(tp * 128 + 64 * half, tp * 128 + 64 * half + 64)
                s_fd.inc(g.dma_start(self.dbc[1024 * half:1024 * half + 1024], self.r_G[:, cc]), 16)
                s_fd.inc(g.dma_start(self.dbc[2048 + 1024 * half:3072 + 1024 * half], self.r_Gp[:, cc]), 16)
                s_fd.inc(g.dma_start(self.dbc[4096 + 1024 * half:5120 + 1024 * half], self.r_lamq[:, cc]), 16)
            lamc_src = bass.AP(self.dec2, 64 * SS + tp * 128 + CH - 1, [[SS, 16], [CH, 2]])
            s_fd.inc(g.dma_start(self.dbc[6144:6176], lamc_src, allow_slow_non_contiguous=True), 16)
            s_fd.wait(g)
            for half in (0, 1):
                sl = slice(64 * half, 64 * half + 64)
                s_fd.inc(g.dma_start(self.Grep[sl, :].rearrange("p (a b) -> p a b", b=64),
                                     bass.AP(self.dbc, 1024 * half, [[0, 64], [64, 16], [1, 64]])), 16)
                s_fd.inc(g.dma_start(self.Gprep[sl, :].rearrange("p (a b) -> p a b", b=64),
                                     bass.AP(self.dbc, 2048 + 1024 * half, [[0, 64], [64, 16], [1, 64]])), 16)
                s_fd.inc(g.dma_start(self.lamrep[:, 1024 * half:1024 * half + 1024].rearrange("p (a b) -> p a b", b=64),
                                     bass.AP(self.dbc, 4096 + 1024 * half, [[0, 128], [64, 16], [1, 64]])), 16)
            s_fd.inc(g.dma_start(self.lamCrep[:, :].rearrange("p (a b) -> p a b", b=2),
                                 bass.AP(self.dbc, 6144, [[0, 64], [2, 16], [1, 2]])), 16)
        self.G_(f_bd)
        if getattr(self, "CUT", None) == 0:
            s_cut = self.sem("s_scd")
            def f_cut(v):
                s_fd.wait(v)
                s_cut.inc(v.nop(), 1)
            self.V(f_cut)
            return s_cut

        def f_kk(t):
            s_p1.wait(t)
            s_prev.wait(t)
            for jpar in range(2):
                for cl in range(2):
                    cc = slice(tp * 128 + 64 * cl, tp * 128 + 64 * cl + 64)
                    for h in range(16):
                        if (h // 2) % 2 != jpar:
                            continue
                        t.matmul(self.pZ[64 * cl:64 * cl + 64, ts(h, 64)],
                                 self.k_src(h, cc), self.k_src(h, cc), start=True, stop=True)
                    t.drain()
            s_sc.inc(t.nop(), 1)
        self.P(f_kk)
        if getattr(self, "CUT", None) == 1:
            s_cut = self.sem("s_scd")
            def f_cut(v):
                s_sc.wait(v)
                s_fd.wait(v)
                s_cut.inc(v.nop(), 1)
            self.V(f_cut)
            return s_cut
        gpcol = self.decT[:, tt, 0:16].unsqueeze(2).broadcast_to([128, 16, 64])
        gcol = self.decT[:, tt, 32:48].unsqueeze(2).broadcast_to([128, 16, 64])
        mA = self.masks[:, 0:64].unsqueeze(1).broadcast_to([128, 16, 64])
        mB = self.masks[:, 64:128].unsqueeze(1).broadcast_to([128, 16, 64])
        mO = self.masks[:, 128:192].unsqueeze(1).broadcast_to([128, 16, 64])
        Dw3 = self.Dw[:, :].rearrange("p (a b) -> p a b", b=64)
        def f_da(v):
            s_fd.wait(v)
            s_sc.wait(v)
            v.scalar_tensor_tensor(Dw3, self.Grep[:, :].rearrange("p (a b) -> p a b", b=64),
                                   -1.0, gpcol, ALU.mult, ALU.add)
            v.drain()
            v.tensor_tensor(Dw3, Dw3, mA, ALU.subtract)
            v.drain()
            s_sc.inc(v.nop(), 1)
        self.V(f_da)
        def f_dae(a):
            s_sc.wait(a)
            a.activation(self.Dw[:, :], self.Dw[:, :], ACT.Exp)
            a.drain()
            s_sc.inc(a.nop(), 1)
        self.A(f_dae)
        def f_aa(v):
            s_sc.wait(v)
            v.tensor_tensor(self.Aw[:, :], self.pZ[:, :], self.Dw[:, :], ALU.mult)
            v.drain()
            v.scalar_tensor_tensor(Dw3, self.Gprep[:, :].rearrange("p (a b) -> p a b", b=64),
                                   1.0, gcol, ALU.mult, ALU.subtract)
            v.drain()
            v.tensor_tensor(Dw3, Dw3, mB, ALU.subtract)
            v.drain()
            s_sc.inc(v.nop(), 1)
        self.V(f_aa)
        def f_dbe(a):
            s_sc.wait(a)
            a.activation(self.Dw[:, :], self.Dw[:, :], ACT.Exp)
            a.drain()
            s_sc.inc(a.nop(), 1)
        self.A(f_dbe)
        def f_bb(v):
            s_sc.wait(v)
            v.tensor_tensor(self.Bw[:, :], self.pZ[:, :], self.Dw[:, :], ALU.mult)
            v.drain()
            v.scalar_tensor_tensor(self.Yc[:, :].rearrange("p (a b) -> p a b", b=64),
                                   self.Bw[:, :].rearrange("p (a b) -> p a b", b=64), -1.0,
                                   self.eyed[:, :].unsqueeze(1).broadcast_to([128, 16, 64]),
                                   ALU.mult, ALU.add)
            v.drain()
            s_sc.inc(v.nop(), 1)
        self.V(f_bb)
        def f_kq(t):
            s_sc.wait(t)
            for jpar in range(2):
                for cl in range(2):
                    cc = slice(tp * 128 + 64 * cl, tp * 128 + 64 * cl + 64)
                    for h in range(16):
                        if (h // 2) % 2 != jpar:
                            continue
                        t.matmul(self.pZ[64 * cl:64 * cl + 64, ts(h, 64)],
                                 self.k_src(h, cc), self.q_src(h, cc), start=True, stop=True)
                    t.drain()
            s_sc.inc(t.nop(), 1)
        self.P(f_kq)
        def f_do(v):
            s_fd.wait(v)
            s_sc.wait(v)
            v.scalar_tensor_tensor(Dw3, self.Grep[:, :].rearrange("p (a b) -> p a b", b=64),
                                   1.0, gcol, ALU.mult, ALU.subtract)
            v.drain()
            v.tensor_tensor(Dw3, Dw3, mO, ALU.subtract)
            v.drain()
            s_sc.inc(v.nop(), 1)
        self.V(f_do)
        def f_doe(a):
            s_sc.wait(a)
            a.activation(self.Dw[:, :], self.Dw[:, :], ACT.Exp)
            a.drain()
            s_sc.inc(a.nop(), 1)
        self.A(f_doe)
        def f_qkdt(v):
            s_sc.wait(v)
            v.tensor_tensor(self.QKDT[:, :], self.pZ[:, :], self.Dw[:, :], ALU.mult)
            v.drain()
            s_sc.inc(v.nop(), 1)
        self.V(f_qkdt)
        if getattr(self, "CUT", None) == 2:
            s_cut = self.sem("s_scd")
            def f_cut(v):
                s_sc.wait(v)
                s_fd.wait(v)
                s_cut.inc(v.nop(), 1)
            self.V(f_cut)
            return s_cut

        eyedb = self.eyed[:, :].unsqueeze(1).broadcast_to([128, 16, 64])
        cF, cT = self.Aw, self.Bw
        for lvl in range(5):
            last = (lvl == 4)
            nF = self.Pf if lvl % 2 == 0 else self.Pg
            nT = self.PfT if lvl % 2 == 0 else self.PgT
            def f_sq(t, cF=cF, cT=cT, last=last):
                s_sc.wait(t)
                for cl in range(2):
                    sl_p = slice(64 * cl, 64 * cl + 64)
                    for h in range(16):
                        t.matmul(self.pX[sl_p, ts(h, 64)], cT[sl_p, ts(h, 64)], cF[sl_p, ts(h, 64)],
                                 start=True, stop=True)
                        if not last:
                            t.matmul(self.pY[sl_p, ts(h, 64)], cF[sl_p, ts(h, 64)], cT[sl_p, ts(h, 64)],
                                     start=True, stop=True)
                    t.drain()
                s_sc.inc(t.nop(), 1)
            self.P(f_sq)
            def f_ev(a, nF=nF, nT=nT, last=last):
                s_sc.wait(a)
                if not last:
                    a.activation(nF[:, :], self.pX[:, :], ACT.Copy)
                    a.drain()
                    a.activation(nT[:, :], self.pY[:, :], ACT.Copy)
                    a.drain()
                s_sc.inc(a.nop(), 1)
            self.A(f_ev)
            def f_ip(v):
                s_sc.wait(v)
                v.scalar_tensor_tensor(self.IPb[:, :].rearrange("p (a b) -> p a b", b=64),
                                       self.pX[:, :].rearrange("p (a b) -> p a b", b=64),
                                       1.0, eyedb, ALU.mult, ALU.add)
                v.drain()
                s_sc.inc(v.nop(), 1)
            self.V(f_ip)
            ycur = self.Yc if lvl % 2 == 0 else self.Yn
            ynew = self.Yn if lvl % 2 == 0 else self.Yc
            def f_yr(t, ycur=ycur):
                s_sc.wait(t)
                for cl in range(2):
                    sl_p = slice(64 * cl, 64 * cl + 64)
                    for h in range(16):
                        t.matmul(self.pZ[sl_p, ts(h, 64)], self.IPb[sl_p, ts(h, 64)],
                                 ycur[sl_p, ts(h, 64)], start=True, stop=True)
                    t.drain()
                s_sc.inc(t.nop(), 1)
            self.P(f_yr)
            def f_ye(a, ynew=ynew):
                s_sc.wait(a)
                a.activation(ynew[:, :], self.pZ[:, :], ACT.Copy)
                a.drain()
                s_sc.inc(a.nop(), 1)
            self.A(f_ye)
            cF, cT = nF, nT
        yfin = self.Yn
        if getattr(self, "CUT", None) == 3:
            s_cut = self.sem("s_scd")
            def f_cut(v):
                s_sc.wait(v)
                s_fd.wait(v)
                s_cut.inc(v.nop(), 1)
            self.V(f_cut)
            return s_cut
        def f_rb(v):
            s_sc.wait(v)
            for h in range(16):
                v.tensor_scalar(self.Rw[:, ts(2 * h, 64)], self.vTs[:, tt, ts(h, 64)],
                                self.decT2[:, tt, h:h + 1], None, ALU.mult)
                v.tensor_scalar(self.Rw[:, ts(2 * h + 1, 64)], self.kTs[:, tt, ts(h // 2, 64)],
                                self.decT[:, tt, 96 + h:97 + h], None, ALU.mult)
                v.tensor_scalar(self.KbT[:, ts(h, 64)], self.kTs[:, tt, ts(h // 2, 64)],
                                self.decT2[:, tt, 32 + h:33 + h], None, ALU.mult)
            v.drain()
            s_sc.inc(v.nop(), 1)
        self.V(f_rb)
        if getattr(self, "CUT", None) == 4:
            s_cut = self.sem("s_scd")
            def f_cut(v):
                s_sc.wait(v)
                s_fd.wait(v)
                s_cut.inc(v.nop(), 1)
            self.V(f_cut)
            return s_cut

        for cl in range(2):
            gc = gc0 + cl
            par, npar = gc % 2, (gc + 1) % 2
            sl_p = slice(64 * cl, 64 * cl + 64)
            ccols = slice(tp * 128 + 64 * cl, tp * 128 + 64 * cl + 64)
            gcols = slice(ss * SS + tp * 128 + 64 * cl, ss * SS + tp * 128 + 64 * cl + 64)
            def f_u(t, cl=cl, sl_p=sl_p):
                s_sc.wait(t)
                for h in range(16):
                    t.matmul(self.pY[sl_p, ts(h, 64)], yfin[sl_p, ts(h, 64)],
                             self.Rw[sl_p, ts(2 * h + 1, 64)], start=True, stop=True)
                for h in range(16):
                    t.matmul(self.pX[sl_p, ts(h, 64)], yfin[sl_p, ts(h, 64)],
                             self.Rw[sl_p, ts(2 * h, 64)], start=True, stop=True)
                t.drain()
                for h in range(16):
                    t.matmul(self.pZ[:, ts(h, 64)], self.Rw[sl_p, slice(128 * h, 128 * h + 128)],
                             yfin[sl_p, ts(h, 64)], start=True, stop=True)
                t.drain()
                s_sc.inc(t.nop(), 1)
            self.P(f_u)
            v_u = s_sc.val
            def f_ue(a, cl=cl, sl_p=sl_p, v_u=v_u):
                s_sc.wait(a, v_u)
                a.activation(self.Uvp[sl_p, :], self.pX[sl_p, :], ACT.Copy)
                a.drain()
                a.activation(self.Ukp[sl_p, :], self.pY[sl_p, :], ACT.Copy)
                a.drain()
                s_sc.inc(a.nop(), 1)
            self.A(f_ue)
            def f_un(v, cl=cl, v_u=v_u):
                s_sc.wait(v, v_u)
                v.tensor_scalar(self.nUkT[:, :], self.pZ[64:128, :], -1.0, None, ALU.mult)
                v.drain()
                s_sc.inc(v.nop(), 1)
            self.V(f_un)
            def f_mp(t, cl=cl, sl_p=sl_p):
                s_sc.wait(t)
                for h in range(16):
                    t.matmul(self.pV[:, ts(h, 64)], self.Ukp[sl_p, ts(h, 64)],
                             self.KbT[sl_p, ts(h, 64)], start=True, stop=True)
                for h in range(16):
                    t.matmul(self.pX[0:64, ts(h, 64)], self.Ukp[sl_p, ts(h, 64)],
                             self.QKDT[sl_p, ts(h, 64)], start=True, stop=True)
                t.drain()
                s_sc.inc(t.nop(), 1)
            self.P(f_mp)
            def f_mq2(v, cl=cl, ccols=ccols):
                s_sc.wait(v)
                v.tensor_scalar(self.nMT[:, :], self.pV[:, :], -1.0, None, ALU.mult)
                v.drain()
                for h in range(16):
                    qb = 64 * ((h // 2) % 2)
                    v.tensor_tensor(self.Qt[:, ts(h, 64)], self.q_src(h, ccols),
                                    self.lamrep[qb:qb + 64, slice(1024 * cl + 64 * h, 1024 * cl + 64 * h + 64)],
                                    ALU.mult)
                v.drain()
                v.scalar_tensor_tensor(self.Et[:, :], self.pX[0:64, :], -1.0, self.Qt[:, :],
                                       ALU.mult, ALU.add)
                v.drain()
                s_sc.inc(v.nop(), 1)
            self.V(f_mq2)
            def f_ep(t, cl=cl, par=par):
                s_sc.wait(t)
                for h in range(16):
                    wsl = self.Wb[par][:, h, :]
                    t.matmul(self.pV[:, ts(self.hcol(h), 64)], wsl, self.Et[:, ts(h, 64)],
                             start=True, stop=True)
                t.drain()
                s_sc.inc(t.nop(), 1)
            self.P(f_ep)
            def f_epe2(a, par=par):
                s_sc.wait(a)
                self.s_esp[par].wait(a)
                a.activation(self.EpT_st[par][0:64, :, :],
                             self.pV[:, 0:512].rearrange("p (a b) -> p a b", b=64), ACT.Copy)
                a.drain()
                a.activation(self.EpT_st[par][64:128, :, :],
                             self.pV[:, 512:1024].rearrange("p (a b) -> p a b", b=64), ACT.Copy)
                a.drain()
                s_sc.inc(a.nop(), 1)
            self.A(f_epe2)
            def f_epd(g, par=par, g0=gcols.start):
                s_sc.wait(g)
                ap = bass.AP(self.d_EpT, g0, [[8 * self.T, 128], [self.T, 8], [1, CH]])
                self.s_esp[par].inc(g.dma_start(ap, self.EpT_st[par][:, :, :]), 16)
            self.G_(f_epd)
            def f_uc(t, cl=cl, par=par, sl_p=sl_p):
                s_sc.wait(t)
                for h in range(16):
                    t.matmul(self.pV[:, ts(h, 64)], self.nUkT[:, ts(h, 64)],
                             self.Sst[par][:, h, :], start=True, stop=True)
                t.drain()
                s_sc.inc(t.nop(), 1)
            self.P(f_uc)
            lamCsl = self.lamCrep[:, cl::2]
            def f_uce(v, cl=cl, sl_p=sl_p, par=par, lamCsl=lamCsl):
                s_sc.wait(v)
                v.tensor_tensor(self.Ucp[sl_p, :], self.pV[0:64, :], self.Uvp[sl_p, :], ALU.add)
                v.drain()
                v.tensor_tensor(self.Ssc[:, :].rearrange("p (a b) -> p a b", b=64),
                                self.Sst[par][:, :, :],
                                lamCsl.unsqueeze(2).broadcast_to([64, 16, 64]), ALU.mult)
                v.drain()
                v.tensor_tensor(self.Wsc[:, :].rearrange("p (a b) -> p a b", b=64),
                                self.Wb[par][:, :, :],
                                lamCsl.unsqueeze(2).broadcast_to([64, 16, 64]), ALU.mult)
                v.drain()
                s_sc.inc(v.nop(), 1)
            self.V(f_uce)
            def f_ot(t, cl=cl, par=par, sl_p=sl_p):
                s_sc.wait(t)
                for h in range(16):
                    t.matmul(self.pV[:, ts(self.hcol(h), 64)], self.Ucp[sl_p, ts(h, 64)],
                             self.QKDT[sl_p, ts(h, 64)], start=True, stop=True)
                t.drain()
                for h in range(16):
                    t.matmul(self.pZ[0:64, ts(self.hcol(h), 64)],
                             self.Sst[par][:, h, :],
                             self.Qt[:, ts(h, 64)], start=True, stop=True)
                t.drain()
                s_sc.inc(t.nop(), 1)
            self.P(f_ot)
            def f_otc(a):
                s_sc.wait(a)
                a.activation(self.scr[0:64, 0:1024], self.pV[:, :], ACT.Copy)
                a.drain()
                s_sc.inc(a.nop(), 1)
            self.A(f_otc)
            def f_ote(v, par=par):
                s_sc.wait(v)
                self.s_osp[par].wait(v)
                v.tensor_tensor(self.opre_st[par][0:64, :, :],
                                self.scr[0:64, 0:512].rearrange("p (a b) -> p a b", b=64),
                                self.pZ[0:64, 0:512].rearrange("p (a b) -> p a b", b=64), ALU.add)
                v.drain()
                v.tensor_tensor(self.opre_st[par][64:128, :, :],
                                self.scr[0:64, 512:1024].rearrange("p (a b) -> p a b", b=64),
                                self.pZ[0:64, 512:1024].rearrange("p (a b) -> p a b", b=64), ALU.add)
                v.drain()
                s_sc.inc(v.nop(), 1)
            self.V(f_ote)
            def f_otd(g, par=par, g0=gcols.start):
                s_sc.wait(g)
                ap = bass.AP(self.d_opre, g0, [[8 * self.T, 128], [self.T, 8], [1, CH]])
                self.s_osp[par].inc(g.dma_start(ap, self.opre_st[par][:, :, :]), 16)
            self.G_(f_otd)
            def f_sw(t, cl=cl, par=par, sl_p=sl_p):
                s_sc.wait(t)
                for h in range(16):
                    t.matmul(self.pV[:, ts(h, 64)], self.KbT[sl_p, ts(h, 64)],
                             self.Ucp[sl_p, ts(h, 64)], start=True, stop=True)
                t.drain()
                for h in range(16):
                    t.matmul(self.pX[0:64, ts(h, 64)], self.nMT[:, ts(h, 64)],
                             self.Wb[par][:, h, :], start=True, stop=True)
                t.drain()
                s_sc.inc(t.nop(), 1)
            self.P(f_sw)
            def f_swe(v, npar=npar):
                s_sc.wait(v)
                v.tensor_tensor(self.Sst[npar][:, :, :].rearrange("p a b -> p (a b)"),
                                self.pV[:, :], self.Ssc[:, :], ALU.add)
                v.drain()
                v.tensor_tensor(self.Wb[npar][:, :, :].rearrange("p a b -> p (a b)"),
                                self.pX[0:64, :], self.Wsc[:, :], ALU.add)
                v.drain()
                s_sc.inc(v.nop(), 1)
            self.V(f_swe)
        s_done = self.sem("s_scd")
        def f_fin(v):
            s_sc.wait(v)
            s_done.inc(v.nop(), 1)
        self.V(f_fin)
        return s_done

    def build_scan_superslab(self, ss, s_p1, s_prev):
        s = s_prev
        for tp in range(self.SS // 128):
            s = self.build_scan_pair(ss, tp, s_p1, s)
        return s


    def declare_epi_ws(self, stk):
        ES = self.ES = 256
        self.Sin = self.sb(stk, "e_Sin", [64, 16, DV], F32)
        self.Sinb = self.sb(stk, "e_Sinb", [128, 16, DV], BF16)
        self.wzT = self.sb(stk, "e_wzT", [128, 8, 1024], BF16)
        self.woutT = self.sb(stk, "e_woutT", [128, 8, 1024], BF16)
        self.xe = [self.sb(stk, "e_xe0", [128, 8, ES], F32),
                   self.sb(stk, "e_xe1", [128, 8, ES], F32)]
        self.opre_sl = [self.sb(stk, "e_op0", [128, 8, ES], BF16),
                        self.sb(stk, "e_op1", [128, 8, ES], BF16)]
        self.EpT_sl = [self.sb(stk, "e_ep0", [128, 8, ES], BF16),
                       self.sb(stk, "e_ep1", [128, 8, ES], BF16)]
        self.hTe = self.sb(stk, "e_hT", [128, 8, ES], BF16)
        self.ofin = self.sb(stk, "e_ofin", [128, 8, ES], F32)
        self.szs = self.sb(stk, "e_szs", [128, 8, ES], BF16)
        self.tf = self.sb(stk, "e_tf", [128, 8, ES], F32)
        self.tsq = self.sb(stk, "e_tsq", [128, 8, ES], F32)
        self.r16 = self.sb(stk, "e_r16", [16, ES], F32)
        self.hgn = self.sb(stk, "e_hgn", [128, 8, ES], BF16)
        self.s_eid = [self.sem("s_eid0"), self.sem("s_eid1")]
        self.s_xod = [self.sem("s_xod0"), self.sem("s_xod1")]

    def declare_epi_psum(self, stk, tag):
        self.peD = self.ps(stk, "pe_D" + tag, [64, 1024])
        self.peZ = self.ps(stk, "pe_Z" + tag, [128, 512])
        self.peG = self.ps(stk, "pe_G" + tag, [16, 512])
        self.peR = self.ps(stk, "pe_R" + tag, [128, 512])
        self.peW = self.ps(stk, "pe_W" + tag, [128, 512])
        self.pR1 = self.ps(stk, "pe_R1" + tag, [128, 512])
        self.pR2 = self.ps(stk, "pe_R2" + tag, [128, 512])

    def build_ag2(self, s_scan):
        s_cc = self.sem("s_cc")
        s_ccw = self.sem("s_ccw")
        self.s_sin = self.sem("s_sin")
        nch = self.T // CH
        fin = nch % 2
        def f_cc(g):
            s_scan.wait(g)
            s_cc.inc(g.dma_start(self.cc_in[:, :], self.Sst[fin][:, :, :].rearrange("p a b -> p (a b)")), 16)
            s_cc.wait(g)
            g.collective_compute(
                "AllGather", mybir.AluOpType.bypass,
                replica_groups=[[i, i + 1] for i in range(0, self.NCORE, 2)],
                ins=[self.cc_in.ap().opt()], outs=[self.cc_out.ap().opt()],
            ).then_inc(s_ccw.h, 1)
            s_ccw.val += 1
            s_ccw.wait(g)
            s_cc.inc(g.dma_start(self.Sin[:, :, :].rearrange("p a b -> p (a b)"), self.cc_out[0:64, :]), 16)
            s_cc.inc(g.dma_start(self.scr[0:64, 0:512], self.cc_out[64:128, 0:512]), 16)
            s_cc.inc(g.dma_start(self.scr[0:64, 512:1024], self.cc_out[64:128, 512:1024]), 16)
        self.G_(f_cc)
        def f_si(v):
            s_cc.wait(v)
            v.tensor_scalar(self.Sin[:, :, :].rearrange("p a b -> p (a b)"),
                            self.Sin[:, :, :].rearrange("p a b -> p (a b)"),
                            self.cvec[0:64, 2:3], None, ALU.mult)
            v.drain()
            v.scalar_tensor_tensor(self.Sin[:, 0:8, :].rearrange("p a b -> p (a b)"),
                                   self.scr[0:64, 0:512], self.cvec[0:64, 3:4],
                                   self.Sin[:, 0:8, :].rearrange("p a b -> p (a b)"),
                                   ALU.mult, ALU.add)
            v.drain()
            v.scalar_tensor_tensor(self.Sin[:, 8:16, :].rearrange("p a b -> p (a b)"),
                                   self.scr[0:64, 512:1024], self.cvec[0:64, 3:4],
                                   self.Sin[:, 8:16, :].rearrange("p a b -> p (a b)"),
                                   ALU.mult, ALU.add)
            v.drain()
            v.tensor_copy(self.Sinb[0:64, :, :].rearrange("p a b -> p (a b)"),
                          self.Sin[:, :, :].rearrange("p a b -> p (a b)"))
            v.drain()
            v.tensor_copy(self.Sinb[64:128, :, :].rearrange("p a b -> p (a b)"),
                          self.Sin[:, :, :].rearrange("p a b -> p (a b)"))
            v.drain()
            self.s_sin.inc(v.nop(), 1)
        self.V(f_si)
        return self.s_sin

    def build_epi_slab(self, sl, s_pre):
        ES = self.ES
        w0 = sl * ES
        par = sl % 2
        s_e = self.sem("s_e")
        def f_ind(g):
            s_pre.wait(g)
            if self.s_xod[par].val:
                self.s_xod[par].wait(g)
            ap_x = bass.AP(self.i_x, 3 + w0, [[8 * self.TH, 128], [self.TH, 8], [1, ES]])
            ap_o = bass.AP(self.d_opre, w0, [[8 * self.T, 128], [self.T, 8], [1, ES]])
            ap_e = bass.AP(self.d_EpT, w0, [[8 * self.T, 128], [self.T, 8], [1, ES]])
            self.s_eid[par].inc(g.dma_start(self.xe[par][:, :, :], ap_x), 16)
            self.s_eid[par].inc(g.dma_start(self.opre_sl[par][:, :, :], ap_o), 16)
            self.s_eid[par].inc(g.dma_start(self.EpT_sl[par][:, :, :], ap_e), 16)
        self.G_(f_ind)
        def f_gate(v):
            s_pre.wait(v)
            self.s_eid[par].wait(v)
            s_e.inc(v.nop(), 1)
        self.V(f_gate)
        self.ln_rep_apply(0, 3 + w0, ES, s_e, self.stats, self.statsm, self.hTe, False,
                          xsrc=self.xe[par], x0=0)
        for dp in range(4):
            def f_z(t, dp=dp):
                s_e.wait(t)
                for j in range(2):
                    d = 2 * dp + j
                    for kk in range(8):
                        t.matmul(self.peZ[:, 256 * j:256 * j + ES], self.wzT[:, kk, ts(d, 128)],
                                 self.hTe[:, kk, :], start=(kk == 0), stop=(kk == 7))
                t.drain()
                s_e.inc(t.nop(), 1)
            self.P(f_z)
            def f_zs(a, dp=dp):
                s_e.wait(a)
                a.activation(self.szs[:, 2 * dp:2 * dp + 2, :],
                             self.peZ[:, 0:512].rearrange("p (a b) -> p a b", b=ES), ACT.Silu)
                a.drain()
                s_e.inc(a.nop(), 1)
            self.A(f_zs)
        for cc in range(ES // CH):
            lcols = slice(cc * CH, cc * CH + CH)
            def f_do(t, cc=cc, lcols=lcols, par=par):
                s_e.wait(t)
                for hp in range(2):
                    for h in range(hp, 16, 2):
                        base = 64 * (h % 2)
                        t.matmul(self.peD[:, ts(self.hcol(h), 64)],
                                 self.Sinb[base:base + 64, h, :],
                                 self.EpT_sl[par][base:base + 64, h // 2, lcols],
                                 start=True, stop=True)
                    t.drain()
                s_e.inc(t.nop(), 1)
            self.P(f_do)
            def f_oa(v, cc=cc, lcols=lcols, par=par):
                s_e.wait(v)
                v.tensor_tensor(self.ofin[0:64, :, lcols],
                                self.opre_sl[par][0:64, :, lcols],
                                self.peD[:, 0:512].rearrange("p (a b) -> p a b", b=64),
                                ALU.add)
                v.drain()
                v.tensor_tensor(self.ofin[64:128, :, lcols],
                                self.opre_sl[par][64:128, :, lcols],
                                self.peD[:, 512:1024].rearrange("p (a b) -> p a b", b=64),
                                ALU.add)
                v.drain()
                s_e.inc(v.nop(), 1)
            self.V(f_oa)
        def f_tf(v):
            s_e.wait(v)
            v.tensor_tensor(self.tf[:, :, :].rearrange("p a b -> p (a b)"),
                            self.ofin[:, :, :].rearrange("p a b -> p (a b)"),
                            self.szs[:, :, :].rearrange("p a b -> p (a b)"), ALU.mult)
            v.drain()
            s_e.inc(v.nop(), 1)
        self.V(f_tf)
        def f_sq(a):
            s_e.wait(a)
            for d in range(8):
                a.activation(self.tsq[:, d, :], self.tf[:, d, :], ACT.Square)
            a.drain()
            s_e.inc(a.nop(), 1)
        self.A(f_sq)
        def f_gs(t):
            s_e.wait(t)
            for d in range(8):
                t.matmul(self.peG[:, 0:ES], self.gnsel[:, d, :], self.tsq[:, d, :],
                         start=(d == 0), stop=(d == 7))
            t.drain()
            s_e.inc(t.nop(), 1)
        self.P(f_gs)
        def f_r16(v):
            s_e.wait(v)
            v.tensor_scalar(self.r16[:, :], self.peG[:, 0:ES], 1.0 / 64.0, None, ALU.mult)
            v.drain()
            s_e.inc(v.nop(), 1)
        self.V(f_r16)
        def f_r16a(a):
            s_e.wait(a)
            a.activation(self.r16[:, :], self.r16[:, :], ACT.Ln, bias=self.cvec[0:16, 1:2])
            a.drain()
            a.activation(self.r16[:, :], self.r16[:, :], ACT.Exp, scale=-0.5)
            a.drain()
            s_e.inc(a.nop(), 1)
        self.A(f_r16a)
        for dp in range(4):
            def f_rr(t, dp=dp):
                s_e.wait(t)
                for j in range(2):
                    t.matmul(self.peR[:, 256 * j:256 * j + ES], self.gnselT[:, 2 * dp + j, :],
                             self.r16[:, :], start=True, stop=True)
                t.drain()
                s_e.inc(t.nop(), 1)
            self.P(f_rr)
            def f_hg(v, dp=dp):
                s_e.wait(v)
                for j in range(2):
                    d = 2 * dp + j
                    v.scalar_tensor_tensor(self.hgn[:, d, :], self.tf[:, d, :], self.nw[:, 0:1],
                                           self.peR[:, 256 * j:256 * j + ES], ALU.mult, ALU.mult)
                v.drain()
                s_e.inc(v.nop(), 1)
            self.V(f_hg)
        for dp in range(4):
            def f_wo(t, dp=dp):
                s_e.wait(t)
                for j in range(2):
                    d = 2 * dp + j
                    for kk in range(8):
                        t.matmul(self.peW[:, 256 * j:256 * j + ES], self.woutT[:, kk, ts(d, 128)],
                                 self.hgn[:, kk, :], start=(kk == 0), stop=(kk == 7))
                t.drain()
                s_e.inc(t.nop(), 1)
            self.P(f_wo)
            def f_xu(v, dp=dp, par=par):
                s_e.wait(v)
                for j in range(2):
                    d = 2 * dp + j
                    v.scalar_tensor_tensor(self.xe[par][:, d, :], self.peW[:, 256 * j:256 * j + ES],
                                           self.adaln[:, d, 2:3], self.xe[par][:, d, :],
                                           ALU.mult, ALU.add)
                v.drain()
                s_e.inc(v.nop(), 1)
            self.V(f_xu)
        def f_xod(g, par=par, w0=w0):
            s_e.wait(g)
            ap = bass.AP(self.d_xp, w0, [[8 * self.T, 128], [self.T, 8], [1, ES]])
            self.s_xod[par].inc(g.dma_start(ap, self.xe[par][:, :, :]), 16)
        self.G_(f_xod)
        return s_e

    def quant_emit(self, ch, src, dst, nblk, pre=None, post_sems=()):
        n = nblk * 16
        bm, tq, gq = self.fbm, self.ftq, self.fgq
        aa = gq[:, 0:n]
        acc = gq[:, 512:512 + n]
        r3 = lambda ap: ap.rearrange("p (a b) -> p a b", b=16)
        bcast = lambda ap: ap.unsqueeze(2).broadcast_to([128, nblk, 16])
        def f_q1(v):
            ch.wait(v)
            if pre is not None:
                pre(v)
            v.tensor_reduce(bm[:, 0:nblk], r3(src), mybir.AxisListType.X, ALU.max,
                            apply_absolute_value=True)
            v.drain()
            v.tensor_scalar(bm[:, 64:64 + nblk], bm[:, 0:nblk], 6e-12, None, ALU.max)
            v.drain()
            ch.inc(v.nop(), 1)
        self.V(f_q1)
        def f_q2(a):
            ch.wait(a)
            a.activation(bm[:, 0:nblk], bm[:, 64:64 + nblk], ACT.Ln)
            a.drain()
            a.activation(bm[:, 0:nblk], bm[:, 0:nblk], ACT.Exp, scale=-1.0,
                         bias=self.cvec[:, 5:6])
            a.drain()
            ch.inc(a.nop(), 1)
        self.A(f_q2)
        def f_q3(v):
            ch.wait(v)
            v.tensor_tensor(r3(tq[:, 0:n]), r3(src), bcast(bm[:, 0:nblk]), ALU.mult)
            v.drain()
            v.tensor_scalar(acc, tq[:, 0:n], -1.0, None, ALU.mult)
            v.drain()
            v.tensor_tensor(aa, tq[:, 0:n], acc, ALU.max)
            v.drain()
            v.tensor_scalar(acc, aa, 7.0, None, ALU.is_gt)
            v.drain()
            v.scalar_tensor_tensor(acc, aa, 5.0, acc, ALU.is_gt, ALU.add)
            v.drain()
            v.scalar_tensor_tensor(acc, aa, 3.0, acc, ALU.is_gt, ALU.add)
            v.drain()
            v.scalar_tensor_tensor(acc, aa, 1.0, acc, ALU.is_gt, ALU.add)
            v.drain()
            hi = tq[:, 0:n]
            v.tensor_scalar(hi, aa, 20.0, None, ALU.is_gt)
            v.drain()
            v.tensor_tensor(hi, hi, hi, ALU.add)
            v.drain()
            v.scalar_tensor_tensor(hi, aa, 14.0, hi, ALU.is_gt, ALU.add)
            v.drain()
            v.scalar_tensor_tensor(hi, aa, 10.0, hi, ALU.is_gt, ALU.add)
            v.drain()
            v.scalar_tensor_tensor(acc, hi, 2.0, acc, ALU.mult, ALU.add)
            v.drain()
            v.tensor_scalar(aa, src, 1e13, -1.0, ALU.mult, ALU.max)
            v.drain()
            v.tensor_scalar(aa, aa, 1.0, None, ALU.min)
            v.drain()
            v.tensor_tensor(acc, acc, aa, ALU.mult)
            v.drain()
            v.tensor_scalar(bm[:, 64:64 + nblk], bm[:, 64:64 + nblk], 1.0 / 12.0, None, ALU.mult)
            v.drain()
            v.tensor_tensor(r3(dst), r3(acc), bcast(bm[:, 64:64 + nblk]), ALU.mult)
            v.drain()
            ch.inc(v.nop(), 1)
            for sm, k in post_sems:
                sm.inc(v.nop(), k)
        self.V(f_q3)

    def build_wq(self, stk, s_pre):
        NR, ISHP = self.NR, self.ISHP
        cols = max(NR * 1024, 8 * ISHP)
        wsh = self.sb(stk, "w_sh", [128, cols], F32)
        wqq = self.sb(stk, "w_qq", [128, cols], BF16)
        wts = self.sb(stk, "w_ts", [128, cols], BF16)
        self.fbm = self.sb(stk, "w_bm", [128, 192], F32)
        self.ftq = self.sb(stk, "w_tq", [128, 512], F32)
        self.fgq = self.sb(stk, "w_gq", [128, 1024], F32)
        pWb = self.ps(stk, "p_wT", [128, 1024], BF16)
        s_wd = self.sem("s_wd")
        s_wq = self.sem("s_wq")
        s_ag2 = self.sem("s_ag2")
        specs = ((self.i_wg, self.cc_g_in, "g"), (self.i_wu, self.cc_u_in, "u"),
                 (self.i_wd, self.cc_d_in, "d"))
        for wi, (ih, ccin, kind) in enumerate(specs):
            ncols = NR * 1024 if kind != "d" else 8 * ISHP
            def f_in(g, ih=ih, ncols=ncols, wi=wi):
                if wi == 0:
                    s_pre.wait(g)
                else:
                    s_wq.wait(g)
                s_wd.inc(g.dma_start(wsh[:, 0:ncols], ih[:, :]), 16)
                s_wd.wait(g)
                s_wq.inc(g.nop(), 1)
            self.G_(f_in)
            ntile = NR if kind != "d" else 8
            tw = 1024 if kind != "d" else ISHP
            for it in range(ntile):
                for c0 in range(0, tw, 512):
                    wd = min(512, tw - c0)
                    self.quant_emit(s_wq, wsh[:, it * tw + c0: it * tw + c0 + wd],
                                    wqq[:, it * tw + c0: it * tw + c0 + wd], wd // 16)
            for k in range(8):
                for r0 in range(0, NR, 8):
                    nr = min(8, NR - r0)
                    if kind != "d":
                        srcs = [(r0 + i) * 1024 + 128 * k for i in range(nr)]
                        dst = wts[:, k * ISHP + 128 * r0: k * ISHP + 128 * (r0 + nr)]
                        dst = dst.rearrange("p (a b) -> p a b", b=128)
                    else:
                        srcs = [k * ISHP + 128 * (r0 + i) for i in range(nr)]
                        dst = wts[:, 0:NR * 1024].rearrange("p (a b) -> p a b", b=1024)[
                            :, r0:r0 + nr, 128 * k:128 * k + 128]
                    def f_t(t, srcs=srcs):
                        s_wq.wait(t)
                        for i, sc in enumerate(srcs):
                            t.transpose(pWb[:, 128 * i:128 * i + 128], wqq[:, sc:sc + 128],
                                        self.eyeb[:, :])
                        t.drain()
                        s_wq.inc(t.nop(), 1)
                    self.P(f_t)
                    def f_te(a, dst=dst, nr=nr):
                        s_wq.wait(a)
                        a.activation(dst, pWb[:, 0:128 * nr].rearrange("p (a b) -> p a b", b=128),
                                     ACT.Copy)
                        a.drain()
                        s_wq.inc(a.nop(), 1)
                    self.A(f_te)
            if kind != "d":
                dst = bass.AP(ccin, 0, [[8 * ISHP, 128], [1, 8 * ISHP]])
                src = wts[:, 0:8 * ISHP]
            else:
                dst = bass.AP(ccin, 0, [[NR * 128, 128], [128, NR], [128 * NR * 128, 8], [1, 128]])
                src = wts[:, 0:NR * 1024].rearrange("p (r d c) -> p r d c", d=8, c=128)
            def f_out(g, dst=dst, src=src):
                s_wq.wait(g)
                s_wd.inc(g.dma_start(dst, src), 16)
                s_wd.wait(g)
                s_wq.inc(g.nop(), 1)
            self.G_(f_out)
        def f_ag(g):
            s_wq.wait(g)
            grp = [list(range(self.NCORE))]
            for cin, gout in ((self.cc_g_in, self.g_gT), (self.cc_u_in, self.g_uT),
                              (self.cc_d_in, self.g_dT)):
                g.collective_compute("AllGather", mybir.AluOpType.bypass,
                                     replica_groups=grp, ins=[cin.ap().opt()],
                                     outs=[gout.ap().opt()]).then_inc(s_ag2.h, 1)
                s_ag2.val += 1
        self.G_(f_ag)
        s_wqd = self.sem("s_wqd")
        def f_fence(v):
            s_wq.wait(v)
            s_wqd.inc(v.nop(), 1)
        self.V(f_fence)
        return s_wqd, s_ag2

    def declare_ffn_ws(self, stk):
        FS, NU, ISHP, NJ = self.FS, self.NU, self.ISHP, self.NJ
        IT = self.NCORE * ISHP
        self.hf = self.sb(stk, "f_hf", [128, 8, FS], F32)
        self.fsq = self.sb(stk, "f_sq", [128, 2, FS], F32)
        self.hq = self.sb(stk, "f_hq", [128, 8, FS], BF16)
        self.hfT = self.sb(stk, "f_hfT", [128, 1024], F32)
        self.hqT = self.sb(stk, "f_hqT", [128, 1024], BF16)
        self.h2q = self.sb(stk, "f_h2q", [128, NU, IT], BF16)
        self.h2qT = self.sb(stk, "f_h2qT", [128, NJ, FS], BF16)
        self.h2p = self.sb(stk, "f_h2p", [128, 512], F32)
        self.xf = self.sb(stk, "f_xf", [128, 8, FS], F32)
        self.s_xfd = self.sem("s_xfd")
        self.s_yod = self.sem("s_yod")
        WW = min(512, ISHP)
        self.wgb = [self.sb(stk, "f_wg0", [128, 8, WW], BF16),
                    self.sb(stk, "f_wg1", [128, 8, WW], BF16)]
        self.wub = [self.sb(stk, "f_wu0", [128, 8, WW], BF16),
                    self.sb(stk, "f_wu1", [128, 8, WW], BF16)]
        self.wdb = [self.sb(stk, "f_wd0", [128, NJ, 128], BF16),
                    self.sb(stk, "f_wd1", [128, NJ, 128], BF16)]
        self.fbm = self.sb(stk, "f_bm", [128, 192], F32)
        self.ftq = self.sb(stk, "f_tq", [128, 512], F32)
        self.fgq = self.sb(stk, "f_gq", [128, 1024], F32)

    def declare_ffn_psum(self, stk, tag):
        self.pF1 = self.ps(stk, "pf_1" + tag, [128, 512])
        self.pF2 = self.ps(stk, "pf_2" + tag, [128, 512])
        self.pR1 = self.ps(stk, "pf_R1" + tag, [128, 512])
        self.pR2 = self.ps(stk, "pf_R2" + tag, [128, 512])
        self.pD = self.ps(stk, "pf_D" + tag, [128, 512])
        self.pTqf = self.ps(stk, "pf_Tqf" + tag, [128, 512])
        self.pTq = self.ps(stk, "pf_Tq" + tag, [128, 1024], BF16)
        self.pS = self.pF1
        self.pQ = self.pF2

    def ln_ffn(self, w0, n, s_pre, xsrc, x0):
        s_loc = self.sem("s_ln2")
        def f_gate(v):
            s_pre.wait(v)
            s_loc.inc(v.nop(), 1)
        self.V(f_gate)
        for dp in range(4):
            def f_sq(a, dp=dp):
                s_loc.wait(a)
                a.activation(self.fsq[:, 0, 0:n], xsrc[:, 2 * dp, x0:x0 + n], ACT.Square)
                a.activation(self.fsq[:, 1, 0:n], xsrc[:, 2 * dp + 1, x0:x0 + n], ACT.Square)
                a.drain()
                s_loc.inc(a.nop(), 1)
            self.A(f_sq)
            def f_mm(t, dp=dp):
                s_loc.wait(t)
                for j in range(2):
                    d = 2 * dp + j
                    t.matmul(self.pS[0:1, 0:n], self.onesf[:, 0:1], xsrc[:, d, x0:x0 + n],
                             start=(d == 0), stop=(d == 7))
                for j in range(2):
                    d = 2 * dp + j
                    t.matmul(self.pQ[0:1, 0:n], self.onesf[:, 0:1], self.fsq[:, j, 0:n],
                             start=(d == 0), stop=(d == 7))
                t.drain()
                s_loc.inc(t.nop(), 1)
            self.P(f_mm)
        def f_vst(v):
            s_loc.wait(v)
            v.tensor_scalar(self.scr[0:1, 0:n], self.pS[0:1, 0:n], 1.0 / 1024.0, None, ALU.mult)
            v.drain()
            v.tensor_tensor(self.scr[0:1, 0:n], self.scr[0:1, 0:n], self.scr[0:1, 0:n], ALU.mult)
            v.drain()
            v.scalar_tensor_tensor(self.scr[0:1, self.SCROFF:self.SCROFF + n], self.pQ[0:1, 0:n],
                                   1.0 / 1024.0, self.scr[0:1, 0:n], ALU.mult, ALU.subtract)
            v.drain()
            s_loc.inc(v.nop(), 1)
        self.V(f_vst)
        def f_ast(a):
            s_loc.wait(a)
            a.activation(self.scr[0:1, 0:n], self.scr[0:1, self.SCROFF:self.SCROFF + n],
                         ACT.Ln, bias=self.cvec[0:1, 0:1])
            a.drain()
            a.activation(self.stats[0:1, w0:w0 + n], self.scr[0:1, 0:n], ACT.Exp, scale=-0.5)
            a.drain()
            s_loc.inc(a.nop(), 1)
        self.A(f_ast)
        def f_vst2(v):
            s_loc.wait(v)
            v.scalar_tensor_tensor(self.statsm[32:33, w0:w0 + n], self.pS[0:1, 0:n], 1.0 / 1024.0,
                                   self.stats[0:1, w0:w0 + n], ALU.mult, ALU.mult)
            v.drain()
            s_loc.inc(v.nop(), 1)
        self.V(f_vst2)
        self.ln_rep_apply(1, w0, n, s_loc, self.stats, self.statsm, self.hf, False, xsrc, x0)
        return s_loc

    def build_ffn_slab(self, u4, s_pre, s_ag2):
        FS, NU, NR, ISHP, NJ = self.FS, self.NU, self.NR, self.ISHP, self.NJ
        w0 = u4 * FS
        def f_xfd(g):
            s_pre.wait(g)
            if self.s_yod.val:
                self.s_yod.wait(g)
            ap = bass.AP(self.d_xp, w0, [[8 * self.T, 128], [self.T, 8], [1, FS]])
            self.s_xfd.inc(g.dma_start(self.xf[:, :, :], ap), 16)
        self.G_(f_xfd)
        s_xg = self.sem("s_xg")
        def f_xg(v):
            s_pre.wait(v)
            self.s_xfd.wait(v)
            s_xg.inc(v.nop(), 1)
        self.V(f_xg)
        s_ln = self.ln_ffn(3 + w0, FS, s_xg, self.xf, 0)
        s_f = self.sem("s_f")
        def f_gate(v):
            s_ln.wait(v)
            s_f.inc(v.nop(), 1)
        self.V(f_gate)
        for u in range(NU):
            for d0 in (0, 4):
                def f_t(t, u=u, d0=d0):
                    s_f.wait(t)
                    for i in range(4):
                        t.transpose(self.pTqf[:, 128 * i:128 * i + 128],
                                    self.hf[:, d0 + i, 128 * u:128 * u + 128], self.eyef[:, :])
                    t.drain()
                    s_f.inc(t.nop(), 1)
                self.P(f_t)
                def f_te(a, d0=d0):
                    s_f.wait(a)
                    a.activation(self.hfT[:, 512 * (d0 // 4):512 * (d0 // 4) + 512],
                                 self.pTqf[:, 0:512], ACT.Copy)
                    a.drain()
                    s_f.inc(a.nop(), 1)
                self.A(f_te)
            self.quant_emit(s_f, self.hfT[:, 0:512], self.hqT[:, 0:512], 32)
            self.quant_emit(s_f, self.hfT[:, 512:1024], self.hqT[:, 512:1024], 32)
            def f_tb(t, u=u):
                s_f.wait(t)
                for d in range(8):
                    t.transpose(self.pTq[:, 128 * d:128 * d + 128],
                                self.hqT[:, 128 * d:128 * d + 128], self.eyeb[:, :])
                t.drain()
                s_f.inc(t.nop(), 1)
            self.P(f_tb)
            def f_tbe(a, u=u):
                s_f.wait(a)
                a.activation(self.hq[:, :, 128 * u:128 * u + 128],
                             self.pTq[:, :].rearrange("p (a b) -> p a b", b=128), ACT.Copy)
                a.drain()
                s_f.inc(a.nop(), 1)
            self.A(f_tbe)
        s_gu = self.sem("s_gu")
        s_wgp = [self.sem("s_wg0"), self.sem("s_wg1")]
        if not hasattr(self, "h_gu"):
            self.h_gu = []
        chunks = [(s, c0) for s in range(self.NCORE) for c0 in range(0, ISHP, 512)]
        for ci, (s, c0) in enumerate(chunks):
            par = len(self.h_gu) % 2
            W = min(512, ISHP - c0)
            free_after = self.h_gu[-2] if len(self.h_gu) >= 2 else None
            def f_wg(g, s=s, c0=c0, W=W, par=par, free_after=free_after):
                s_ag2.wait(g)
                if free_after is not None:
                    g.wait_ge(s_gu.h, free_after)
                off = 1024 * s * ISHP + c0
                ap_g = bass.AP(self.g_gT, off, [[8 * ISHP, 128], [ISHP, 8], [1, W]])
                ap_u = bass.AP(self.g_uT, off, [[8 * ISHP, 128], [ISHP, 8], [1, W]])
                s_wgp[par].inc(g.dma_start(self.wgb[par][:, :, 0:W], ap_g), 16)
                s_wgp[par].inc(g.dma_start(self.wub[par][:, :, 0:W], ap_u), 16)
            self.G_(f_wg)
            for u in range(NU):
                def f_mm(t, par=par, u=u, W=W):
                    s_f.wait(t)
                    s_wgp[par].wait(t)
                    for k in range(8):
                        t.matmul(self.pF1[:, 0:W], self.hq[:, k, 128 * u:128 * u + 128],
                                 self.wgb[par][:, k, 0:W], start=(k == 0), stop=(k == 7))
                    for k in range(8):
                        t.matmul(self.pF2[:, 0:W], self.hq[:, k, 128 * u:128 * u + 128],
                                 self.wub[par][:, k, 0:W], start=(k == 0), stop=(k == 7))
                    t.drain()
                    s_f.inc(t.nop(), 1)
                self.P(f_mm)
                def f_sil(a, W=W):
                    s_f.wait(a)
                    a.activation(self.h2p[:, 0:W], self.pF1[:, 0:W], ACT.Silu)
                    a.drain()
                    s_f.inc(a.nop(), 1)
                self.A(f_sil)
                def pre_mq(v, W=W):
                    v.tensor_tensor(self.h2p[:, 0:W], self.h2p[:, 0:W], self.pF2[:, 0:W], ALU.mult)
                    v.drain()
                self.quant_emit(s_f, self.h2p[:, 0:W],
                                self.h2q[:, u, s * ISHP + c0: s * ISHP + c0 + W], W // 16,
                                pre=pre_mq, post_sems=[(s_gu, 1)])
            self.h_gu.append(s_gu.val)
        for u in range(NU):
            for j0 in range(0, NJ, 8):
                nj = min(8, NJ - j0)
                def f_ht(t, u=u, j0=j0, nj=nj):
                    s_f.wait(t)
                    for i in range(nj):
                        t.transpose(self.pTq[:, 128 * i:128 * i + 128],
                                    self.h2q[:, u, 128 * (j0 + i): 128 * (j0 + i) + 128],
                                    self.eyeb[:, :])
                    t.drain()
                    s_f.inc(t.nop(), 1)
                self.P(f_ht)
                def f_hte(a, u=u, j0=j0, nj=nj):
                    s_f.wait(a)
                    a.activation(self.h2qT[:, j0:j0 + nj, 128 * u:128 * u + 128],
                                 self.pTq[:, 0:128 * nj].rearrange("p (a b) -> p a b", b=128),
                                 ACT.Copy)
                    a.drain()
                    s_f.inc(a.nop(), 1)
                self.A(f_hte)
        s_wdp = [self.sem("s_wdp0"), self.sem("s_wdp1")]
        s_dn = self.sem("s_dn")
        if not hasattr(self, "h_dn"):
            self.h_dn = []
        for d in range(8):
            par = len(self.h_dn) % 2
            free_after = self.h_dn[-2] if len(self.h_dn) >= 2 else None
            def f_wd(g, d=d, par=par, free_after=free_after):
                s_ag2.wait(g)
                if free_after is not None:
                    g.wait_ge(s_dn.h, free_after)
                for s in range(self.NCORE):
                    ap_d = bass.AP(self.g_dT, s * ISHP * 1024 + d * NR * 16384,
                                   [[NR * 128, 128], [1, NR * 128]])
                    s_wdp[par].inc(g.dma_start(
                        self.wdb[par][:, s * NR:(s + 1) * NR, :].rearrange("p a b -> p (a b)"),
                        ap_d), 16)
            self.G_(f_wd)
            def f_dmm(t, d=d, par=par):
                s_f.wait(t)
                s_wdp[par].wait(t)
                for j in range(NJ):
                    t.matmul(self.pD[:, 0:FS], self.wdb[par][:, j, :], self.h2qT[:, j, 0:FS],
                             start=(j == 0), stop=(j == NJ - 1))
                t.drain()
                s_f.inc(t.nop(), 1)
            self.P(f_dmm)
            def f_xu(v, d=d):
                s_f.wait(v)
                v.scalar_tensor_tensor(self.xf[:, d, :], self.pD[:, 0:FS],
                                       self.adaln[:, d, 5:6], self.xf[:, d, :],
                                       ALU.mult, ALU.add)
                v.drain()
                s_f.inc(v.nop(), 1)
                s_dn.inc(v.nop(), 1)
            self.V(f_xu)
            self.h_dn.append(s_dn.val)
        def f_yod(g, w0=w0):
            s_f.wait(g)
            ap = bass.AP(self.o_y, w0, [[8 * self.T, 128], [self.T, 8], [1, FS]])
            self.s_yod.inc(g.dma_start(ap, self.xf[:, :, :]), 16)
        self.G_(f_yod)
        return s_f

    def build_section1(self, sections=1):
        self.declare_global()
        SS, SSH = self.SS, self.SSH
        for name, shape in (("hT", [128, 8 * SSH]), ("mixed", [128, 16 * SSH]),
                            ("qkvs", [128, 16 * SS]), ("dec", [128, SS]), ("dec2", [96, SS]),
                            ("decT", [128, (SS // 128) * 128]), ("decT2", [128, (SS // 128) * 96]),
                            ("stats", [1, self.TH]), ("statsm", [1, self.TH]),
                            ("adaln", [128, 48]), ("kTs", [128, (SS // 128) * KEY]),
                            ("vTs", [128, (SS // 128) * VAL]),
                            ("opre", [128, 8 * self.T]), ("EpT", [128, 8 * self.T]),
                            ("Sst", [64, 16 * DV]), ("Wb", [64, 16 * DK])):
            self.dbg(name, shape)
        with self.nc.Block() as blk:
            self.blk = blk
            with ExitStack() as stkA:
                self.declare_phaseA(stkA)
                if sections >= 2:
                    self.declare_scan_ws(stkA)
                with ExitStack() as stkP0:
                    self.build_p0(stkP0)
                s = self.s_p0
                for ss in range(self.NSS):
                    with ExitStack() as ps1:
                        self.declare_p1_psum(ps1, "_%d" % ss)
                        s, _scv = self.build_p1_superslab(ss, s)
                    if sections >= 2:
                        with ExitStack() as ps2:
                            self.declare_scan_psum(ps2, "_%d" % ss)
                            s = self.build_scan_superslab(ss, s, s)
                self.dump("hT", self.hT[:, :, :].rearrange("p a b -> p (a b)"), s)
                self.dump("qkvs", self.qkvs[:, :, :].rearrange("p a b -> p (a b)"), s)
                self.dump("dec", self.dec[:, :], s)
                self.dump("dec2", self.dec2[:, :], s)
                self.dump("decT", self.decT[:, :, :].rearrange("p a b -> p (a b)"), s)
                self.dump("decT2", self.decT2[:, :, :].rearrange("p a b -> p (a b)"), s)
                self.dump("stats", self.stats[:, :], s)
                self.dump("statsm", self.statsm[32:33, :], s)
                self.dump("adaln", self.adaln[:, :, :].rearrange("p a b -> p (a b)"), s)
                self.dump("kTs", self.kTs[:, :, :].rearrange("p a b -> p (a b)"), s)
                self.dump("vTs", self.vTs[:, :, :].rearrange("p a b -> p (a b)"), s)
                if sections >= 2:
                    if "opre" in self.dbg_bufs or "EpT" in self.dbg_bufs:
                        s_dg = self.sem("s_dmpgate")
                        def f_dg(v):
                            s.wait(v)
                            for sp in self.s_osp + self.s_esp:
                                sp.wait(v)
                            s_dg.inc(v.nop(), 1)
                        self.V(f_dg)
                        self.dump("opre", bass.AP(self.d_opre, 0, [[8 * self.T, 128], [1, 8 * self.T]]), s_dg)
                        self.dump("EpT", bass.AP(self.d_EpT, 0, [[8 * self.T, 128], [1, 8 * self.T]]), s_dg)
                    nch = self.T // CH
                    self.dump("Sst", self.Sst[nch % 2][:, :, :].rearrange("p a b -> p (a b)"), s)
                    self.dump("Wb", self.Wb[nch % 2][:, :, :].rearrange("p a b -> p (a b)"), s)
            s_wqd = s_ag2 = None
            if sections >= 4:
                with ExitStack() as stkW:
                    s_wqd, s_ag2 = self.build_wq(stkW, s)
            if sections >= 3:
                for name, shape in (("ofin", [128, 8 * 256]), ("szs", [128, 8 * 256]),
                                    ("tf", [128, 8 * 256]), ("hgn", [128, 8 * 256]),
                                    ("r16", [16, 256]), ("hTe", [128, 8 * 256])):
                    self.dbg(name, shape)
                with ExitStack() as stkE:
                    self.declare_epi_ws(stkE)
                    self.declare_epi_psum(stkE, "_e")
                    s_wz = self.sem("s_wz")
                    def f_wzd(g):
                        s.wait(g)
                        if s_wqd is not None:
                            s_wqd.wait(g)
                        s_wz.inc(g.dma_start(self.wzT[:, :, :].rearrange("p a b -> p (a b)"),
                                             self.i_wzT[:, :]), 16)
                        s_wz.inc(g.dma_start(self.woutT[:, :, :].rearrange("p a b -> p (a b)"),
                                             self.i_woutT[:, :]), 16)
                    self.G_(f_wzd)
                    s_cc = self.build_ag2(s)
                    s_all = self.sem("s_epi_gate")
                    def f_eg(v):
                        s.wait(v)
                        s_cc.wait(v)
                        self.s_w.wait(v)
                        s_wz.wait(v)
                        for sp in self.s_osp + self.s_esp:
                            sp.wait(v)
                        if s_wqd is not None:
                            s_wqd.wait(v)
                        s_all.inc(v.nop(), 1)
                    self.V(f_eg)
                    s = s_all
                    for sl in range(self.T // 256):
                        s = self.build_epi_slab(sl, s)
                    s_ef = self.sem("s_epi_fin")
                    def f_ef(v):
                        s.wait(v)
                        for sp in self.s_xod:
                            sp.wait(v)
                        s_ef.inc(v.nop(), 1)
                    self.V(f_ef)
                    s = s_ef
            if sections >= 4:
                for name, shape in (("hf", [128, 8 * self.FS]), ("hq", [128, 8 * self.FS]),
                                    ("h2q", [128, self.NU * self.NCORE * self.ISHP]),
                                    ("h2qT", [128, self.NJ * self.FS])):
                    self.dbg(name, shape)
                with ExitStack() as stkF:
                    self.declare_ffn_ws(stkF)
                    self.declare_ffn_psum(stkF, "_f")
                    for u4 in range(self.NFS):
                        s = self.build_ffn_slab(u4, s, s_ag2)
                    for nm in ("hf", "hq", "h2q", "h2qT"):
                        if nm in self.dbg_bufs:
                            self.dump(nm, getattr(self, nm)[:, :, :].rearrange("p a b -> p (a b)"), s)
            if sections == 3 and self.dbg_bufs:
                for nm, tn in (("ofin", "ofin"), ("szs", "szs"), ("tf", "tf"), ("hgn", "hgn"), ("hTe", "hTe")):
                    if nm in self.dbg_bufs:
                        self.dump(nm, getattr(self, tn)[:, :, :].rearrange("p a b -> p (a b)"), s)
                if "r16" in self.dbg_bufs:
                    self.dump("r16", self.r16[:, :], s)
            if sections >= 4:
                def f_yfin(g):
                    s.wait(g)
                    self.s_yod.wait(g)
                self.G_(f_yfin)
            elif sections == 3:
                s_y = self.sem("s_y")
                def f_y3(g):
                    s.wait(g)
                    src = bass.AP(self.d_xp, 0, [[8 * self.T, 128], [1, 8 * self.T]])
                    s_y.inc(g.dma_start(self.o_y[:, :], src), 16)
                    s_y.wait(g)
                self.G_(f_y3)
        self.gstack.close()
        return self.nc


def prep_inputs(core, d, T, ncores=8):
    b, half = core // 2, core % 2
    x = np.asarray(d["L0_attn_hidden_states"][b], np.float32)
    t0 = half * T
    if t0 == 0:
        xs = np.concatenate([np.zeros((3, 1024), np.float32), x[0:T]], 0).T
    else:
        xs = x[t0 - 3: t0 + T].T
    TH = T + 3
    out = {}
    out["x"] = np.ascontiguousarray(xs.reshape(8, 128, TH).transpose(1, 0, 2)).reshape(128, 8 * TH)
    def packT(w):
        wT = np.ascontiguousarray(np.asarray(w, np.float32).T)
        n = wT.shape[1]
        return np.ascontiguousarray(wT.reshape(8, 128, n).transpose(1, 0, 2)).reshape(128, 8 * n)
    out["wqkvT"] = packT(d["L0_attn_W_qkv"])
    wab = np.zeros((64, 1024), np.float32)
    wab[0:16] = np.asarray(d["L0_attn_W_b"])
    wab[32:48] = np.asarray(d["L0_attn_W_a"])
    out["wabT"] = packT(wab)
    out["wzT"] = packT(d["L0_attn_W_z"])
    out["woutT"] = packT(d["L0_attn_W_out"])
    cw = np.asarray(d["L0_attn_conv_w"], np.float32)[:, 0, :]
    out["convw"] = np.ascontiguousarray(cw.reshape(16, 128, 4).transpose(1, 0, 2)).reshape(128, 64)
    wm = np.asarray(d["L0_wmod"], np.float32)
    out["wmod"] = np.ascontiguousarray(wm.reshape(8, 128, 6144).transpose(1, 0, 2)).reshape(128, 8 * 6144)
    out["crow"] = np.ascontiguousarray(np.asarray(d["L0_c"], np.float32)[b].reshape(8, 128).T)
    vec = np.zeros((16, 4), np.float32)
    vec[:, 0] = np.asarray(d["L0_attn_dt_bias"])
    vec[:, 1] = np.asarray(d["L0_attn_A_log"])
    out["vec"] = vec
    nwcol = np.tile(np.asarray(d["L0_attn_norm_weight"], np.float32), 2).reshape(128, 1)
    out["nw"] = np.concatenate([nwcol, np.full((128, 1), float(half), np.float32)], 1)
    r = np.arange(64)
    mA = (r[:, None] > r[None, :])
    mB = (r[:, None] < r[None, :])
    mI = (r[:, None] <= r[None, :])
    bigA = (1e30 * (1 - mA)).astype(np.float32)
    bigB = (1e30 * (1 - mB)).astype(np.float32)
    bigO = (1e30 * (1 - mI) + np.log(8.0)).astype(np.float32)
    m1 = np.concatenate([bigA, bigB, bigO], 1)
    out["masks"] = np.ascontiguousarray(np.concatenate([m1, m1], 0))
    out["eyed"] = np.ascontiguousarray(np.concatenate([np.eye(64), np.eye(64)], 0)).astype(ml_dtypes.bfloat16)
    cv = np.zeros((128, 8), np.float32); cv[:, 0] = 1e-5; cv[:, 1] = 1e-6
    cv[:, 2] = 1.0 if half == 1 else 0.0
    cv[:, 4] = 24.0
    cv[:, 5] = np.log(24.0)
    out["cvec"] = cv
    ISH = 2816 // ncores
    NR = (ISH + 127) // 128
    ISHP = NR * 128
    wg = np.asarray(d["L0_ffn_w_gate"], np.float32)
    wu = np.asarray(d["L0_ffn_w_up"], np.float32)
    wd_ = np.asarray(d["L0_ffn_w_down"], np.float32)
    gsh = np.zeros((ISHP, 1024), np.float32); gsh[0:ISH] = wg[ISH * core: ISH * (core + 1)]
    out["wg_sh"] = np.ascontiguousarray(gsh.reshape(NR, 128, 1024).transpose(1, 0, 2)).reshape(128, NR * 1024)
    ush = np.zeros((ISHP, 1024), np.float32); ush[0:ISH] = wu[ISH * core: ISH * (core + 1)]
    out["wu_sh"] = np.ascontiguousarray(ush.reshape(NR, 128, 1024).transpose(1, 0, 2)).reshape(128, NR * 1024)
    dsh = np.zeros((1024, ISHP), np.float32); dsh[:, 0:ISH] = wd_[:, ISH * core: ISH * (core + 1)]
    out["wd_sh"] = np.ascontiguousarray(dsh.reshape(8, 128, ISHP).transpose(1, 0, 2)).reshape(128, 8 * ISHP)
    gnsel = np.zeros((128, 8, 16), np.float32)
    for h in range(16):
        gnsel[64 * (h % 2):64 * (h % 2) + 64, h // 2, h] = 1.0
    out["gnsel"] = np.ascontiguousarray(gnsel.reshape(128, 128))
    out["gnselT"] = np.ascontiguousarray(gnsel.transpose(2, 1, 0).reshape(16, 1024))
    out["eyeb"] = np.eye(128).astype(ml_dtypes.bfloat16)
    out["eyef"] = np.eye(128, dtype=np.float32)
    out["ones"] = np.ones((128, 129), np.float32)
    return out


def kernel(**inputs):
    T = 4096
    kb = KB(T=T, SS=256, ncores=8)
    nc = kb.build_section1(sections=4)
    in_maps = [prep_inputs(c, inputs, T, ncores=8) for c in range(8)]
    from concourse.bass_utils import run_bass_kernel_spmd
    res = run_bass_kernel_spmd(nc, in_maps, core_ids=list(range(8)))
    out = np.zeros((4, 8192, 1024), np.float32)
    for c in range(8):
        b, half = c // 2, c % 2
        y = np.asarray(res.results[c]["y"]).reshape(128, 8, T)
        out[b, half * T:(half + 1) * T] = y.transpose(2, 1, 0).reshape(T, 1024)
    return out
